# version 01
# baseline (speedup 1.0000x reference)
import math
from contextlib import ExitStack

import numpy as np
import ml_dtypes

import concourse.bass as bass
import concourse.mybir as mybir
from concourse.bass import IndirectOffsetOnAxis
from concourse.bass_utils import run_bass_kernel_spmd

F32 = mybir.dt.float32
BF16 = mybir.dt.bfloat16
I32 = mybir.dt.int32
U32 = mybir.dt.uint32
U8 = mybir.dt.uint8
AF = mybir.ActivationFunctionType
ALU = mybir.AluOpType
AX = mybir.AxisListType.X

D = 1024
TOK = 2048
NB = D // 128
DH = 4096
NBH = DH // 128
NEXP = 8
KSEL = 2048
NTOKG = 16384
EPS = 1e-5

ENGINES = ("sync", "scalar", "vector", "tensor", "gpsimd")


class Prog:

    def __init__(self, nc, ctx):
        self.nc = nc
        self.ctx = ctx
        self.esem = {e: ctx.enter_context(nc.semaphore(f"E_{e}"))
                     for e in ENGINES}
        self.ecount = {e: 0 for e in ENGINES}
        self.steps = {e: [] for e in ENGINES}
        self.last_w = {}
        self.readers = {}
        self.custom = {}
        self.nsteps = 0

    def _h(self, sid):
        return self.esem[sid] if sid in self.esem else self.custom[sid]

    def step(self, engine, reads, writes, fn, dma=False, own_sem=None):
        waits = {}

        def addw(sid, val):
            waits[sid] = max(waits.get(sid, 0), val)

        for k in reads:
            if k in self.last_w:
                addw(*self.last_w[k])
        for k in writes:
            if k in self.last_w:
                addw(*self.last_w[k])
            for r in self.readers.get(k, ()):
                addw(*r)
        inc_n = 16 if dma else 1
        if own_sem is not None:
            h = self.ctx.enter_context(self.nc.semaphore(own_sem))
            self.custom[own_sem] = h
            sid, self_val = own_sem, inc_n
        else:
            sid = engine
            self.ecount[engine] += inc_n
            self_val = self.ecount[engine]
        self.steps[engine].append((sorted(waits.items()), fn, inc_n, own_sem))
        self.nsteps += 1
        for k in writes:
            self.last_w[k] = (sid, self_val)
            self.readers[k] = []
        for k in reads:
            self.readers.setdefault(k, []).append((sid, self_val))
        return (sid, self_val)

    def emit(self, block, final_keys=()):
        fw = {}
        for k in final_keys:
            sid, v = self.last_w[k]
            fw[sid] = max(fw.get(sid, 0), v)
        self.steps["gpsimd"].append(
            (sorted(fw.items()), None, 0, None))
        for ename in ENGINES:
            items = self.steps[ename]
            if not items:
                continue
            section = getattr(block, ename)

            def make(items=items, ename=ename):
                def body(eng):
                    for waits, fn, inc_n, own in items:
                        for sid, val in waits:
                            eng.wait_ge(self._h(sid), val)
                        if fn is None:
                            continue
                        ins = fn(eng)
                        if inc_n:
                            if own is not None:
                                ins.then_inc(self.custom[own], inc_n)
                            else:
                                ins.then_inc(self.esem[ename], inc_n)
                return body
            section(make())


class Arena:

    def __init__(self, tensor, ncols):
        self.t = tensor
        self.n = ncols
        self.free = [(0, ncols)]
        self.owner_keys = {}

    def alloc(self, cols, keys_hint):
        for i, (s, e) in enumerate(self.free):
            if e - s >= cols:
                if s + cols == e:
                    self.free.pop(i)
                else:
                    self.free[i] = (s + cols, e)
                rng = (s, s + cols)
                legacy = set()
                drop = []
                for r, ks in self.owner_keys.items():
                    if r[0] < rng[1] and rng[0] < r[1]:
                        legacy |= ks
                        drop.append(r)
                for r in drop:
                    del self.owner_keys[r]
                self.owner_keys[rng] = set(keys_hint)
                return rng, legacy
        raise RuntimeError(f"arena OOM: want {cols}, free={self.free}")

    def freea(self, rng, keys=()):
        if rng in self.owner_keys:
            self.owner_keys[rng] = self.owner_keys[rng] | set(keys)
        else:
            self.owner_keys[rng] = set(keys)
        self.free.append(rng)
        self.free.sort()
        out = []
        for s, e in self.free:
            if out and out[-1][1] == s:
                out[-1] = (out[-1][0], e)
            else:
                out.append((s, e))
        self.free = [tuple(x) for x in out]


def cdiv(a, b):
    return (a + b - 1) // b



DBG = {}


def build_program(phases=(1, 2, 3, 4)):
    nc = bass.Bass("TRN2", target_bir_lowering=False)
    ap = lambda t: t.ap()

    I = {}
    def inp(name, shape, dt=F32):
        I[name] = ap(nc.dram_tensor(name, shape, dt, kind="ExternalInput"))
        return I[name]

    x_in = inp("x_in", [TOK, D])
    inp("wqT", [D, D]); inp("wkT", [D, D]); inp("wvT", [D, D])
    inp("ropeC", [512, TOK]); inp("ropeS", [512, TOK])
    inp("w1T", [D, DH]); inp("w2T", [DH, D])
    inp("convW", [128, 3 * NBH]); inp("convB", [128, NBH])
    inp("winT", [D, D]); inp("wgT", [D, D]); inp("woT", [D, D])
    inp("b_in", [128, NB]); inp("b_g", [128, NB]); inp("b_o", [128, NB])
    inp("A_Tin", [16, 16]); inp("A_nat", [16, 16])
    inp("BmT", [D, 16]); inp("CmT", [16, D]); inp("i16rep", [128, 16])
    inp("cmask", [128, 4])
    inp("Ws_in", [D, NEXP]); inp("bs_in", [128, 1])
    inp("W1e", [D, DH]); inp("b1e", [128, NBH])
    inp("W2e", [DH, D]); inp("b2e", [128, NB])
    inp("ident_bf", [128, 128], BF16); inp("ident_f32", [128, 128])
    inp("consts_in", [128, 8])
    inp("egrp", [128, NEXP]); inp("e8to128", [NEXP, 128])
    inp("ltmask", [128, 128])
    inp("halo_idx", [128, 1], U32); inp("halo_msk", [2, 1])
    inp("tokid", [128, 16])
    inp("iota_row", [1, TOK]); inp("b2row", [1, D]); inp("ebase", [128, 1])
    inp("plidx", [128, 1], U32); inp("myrows", [128, 1], U32)

    out_hbm = ap(nc.dram_tensor("out_sh", [TOK, D], F32, kind="ExternalOutput"))
    dbg_hbm = ap(nc.dram_tensor("dbg_out", [128, 8192], F32, kind="ExternalOutput"))

    Dm = {}
    def dram(name, shape, dt):
        Dm[name] = ap(nc.dram_tensor(name, shape, dt))
        return Dm[name]

    dram("ot_spill", [TOK, D], F32)
    dram("qr_spill", [NB * 128, TOK], BF16)
    dram("k_bounce", [D, TOK], BF16)
    dram("v_bounce", [TOK, D], BF16)
    dram("k_ag", [4 * D, TOK], BF16)
    dram("v_ag", [4 * TOK, D], BF16)
    dram("w1q_dram", [D, DH], BF16)
    dram("w2q_dram", [DH, D], BF16)
    dram("halo_bounce", [2, D], F32)
    dram("row_bounce", [1, 4096], F32)
    dram("halo_ag", [16, D], F32)
    dram("state_bounce", [1, 16], F32)
    dram("pow_dram", [16, 32], F32)
    dram("state_ag", [8, 16], F32)
    dram("probs_bounce", [NEXP, TOK], F32)
    dram("probs_ag", [8 * NEXP, TOK], F32)
    dram("act_bounce", [TOK, D], BF16)
    dram("act_ag", [NTOKG, D], BF16)
    dram("moe_bounce", [KSEL, D], F32)
    dram("moe_ag", [NTOKG + 128, D], F32)
    dram("cum_dram", [NEXP * 128, 128], F32)
    dram("ce_dram", [NEXP, 128], F32)
    dram("w1e_dram", [D, DH], BF16)
    dram("w2e_dram", [DH, D], BF16)
    dram("xint_dram", [D, KSEL], BF16)
    dram("evals_dram", [NTOKG, 1], F32)
    dram("ridx_dram", [72, TOK], F32)
    dram("moe_row", [1, 4096], F32)

    ctx = ExitStack()
    P = Prog(nc, ctx)
    sbuf = lambda n, s, d: ctx.enter_context(nc.sbuf_tensor(n, s, d))
    psum = lambda n, s, d: ctx.enter_context(nc.psum_tensor(n, s, d))

    X = sbuf("X", [128, NB * TOK], F32)
    st = sbuf("st", [16, 2048], F32)
    idb = sbuf("idb", [128, 128], BF16)
    idf = sbuf("idf", [128, 128], F32)
    consts = sbuf("consts", [128, 8], F32)
    biasbuf = sbuf("biasbuf", [128, 8 * NB + 5 * NBH], F32)
    OF = dict(BIN=0, BG=NB, BO=2 * NB, B2=3 * NB, CB=4 * NB,
              CW=4 * NB + NBH, B1=4 * NB + 4 * NBH)
    AF32 = Arena(sbuf("AF32", [128, 12544], F32), 12544)
    AB16 = Arena(sbuf("AB16", [128, 35072], BF16), 35072)

    sq2 = sbuf("sq2", [128, 2 * 512], F32)
    onesrow = sbuf("onesrow", [1, 128], F32)
    i32buf = sbuf("i32buf", [128, 1024], I32)

    PS = [psum(f"ps{i}", [128, 512], F32) for i in range(8)]

    sh = dict(nc=nc, P=P, I=I, Dm=Dm, X=X, st=st, idb=idb, idf=idf,
              consts=consts, biasbuf=biasbuf, OF=OF, AF32=AF32, AB16=AB16,
              PS=PS, out_hbm=out_hbm, sq2=sq2, onesrow=onesrow, i32buf=i32buf,
              dbg_hbm=dbg_hbm)
    P.step("vector", [], ["onesrow"], lambda e: e.memset(onesrow[:], 1.0))

    rng, legacy = AF32.alloc(4096, ["xtok"])
    xtok = AF32.t[:, rng[0]:rng[0] + 4096]
    for nm, dst, src in (
            ("idb", idb[:], I["ident_bf"]), ("idf", idf[:], I["ident_f32"]),
            ("consts", consts[:], I["consts_in"])):
        P.step("sync", [], [nm], lambda e, d=dst, s=src: e.dma_start(out=d, in_=s), dma=True)
    for nm, sl, src in (
            ("bb0", (OF["BIN"], NB), "b_in"), ("bb1", (OF["BG"], NB), "b_g"),
            ("bb2", (OF["BO"], NB), "b_o"), ("bb3", (OF["B2"], NB), "b2e"),
            ("bb4", (OF["CB"], NBH), "convB"), ("bb5", (OF["CW"], 3 * NBH), "convW"),
            ("bb6", (OF["B1"], NBH), "b1e")):
        P.step("sync", [], [nm],
               lambda e, a=sl[0], n=sl[1], s=src: e.dma_start(
                   out=biasbuf[:, a:a + n], in_=I[s]), dma=True)

    for ch in range(8):
        P.step("sync", (list(legacy) if ch == 0 else []), [("xtok", ch % 2)],
               lambda e, ch=ch: e.dma_start(
                   out=xtok.rearrange("p (a d) -> p a d", a=4)[:, (ch % 2) * 2:(ch % 2) * 2 + 2, :],
                   in_=x_in.rearrange("(a b) d -> b a d", b=128)[:, 2 * ch:2 * ch + 2, :]),
               dma=True)
        for ai in range(2):
            a = ch * 2 + ai
            for j in range(NB):
                k = a * NB + j
                P.step("tensor", [("xtok", ch % 2), "idf"], [("PS", k % 2)],
                       lambda e, ch=ch, ai=ai, j=j, k=k: e.transpose(
                           PS[k % 2][:, 0:128],
                           xtok[:, ((ch % 2) * 2 + ai) * D + 128 * j:
                                ((ch % 2) * 2 + ai) * D + 128 * j + 128], idf[:]))
                P.step("vector", [("PS", k % 2)], [("X", j)],
                       lambda e, a=a, j=j, k=k: e.tensor_copy(
                           X[:, j * TOK + 128 * a: j * TOK + 128 * a + 128],
                           PS[k % 2][:, 0:128]))
    AF32.freea(rng, [("xtok", 0), ("xtok", 1)])

    with ctx:
        with nc.Block() as block:
            if 1 in phases:
                emit_l0_attn(sh)
            if 2 in phases:
                emit_l0_ffn(sh)
            if 3 in phases:
                emit_l1_attn(sh)
            if 4 in phases:
                emit_moe(sh)
            else:
                emit_debug_out(sh)
            fk = ["out_hbm"] + [("dbg", n) for (n, _, _) in DBG.get("names", [])]
            P.emit(block, final_keys=fk)
    return nc


def emit_debug_out(sh):
    emit_store_out(sh, ln4=False)


def emit_store_out(sh, ln4=False):
    P, X, idf, PS, AF32 = sh["P"], sh["X"], sh["idf"], sh["PS"], sh["AF32"]
    rng, legacy = AF32.alloc(4096, ["xtok2"])
    xtok = AF32.t[:, rng[0]:rng[0] + 4096]
    first = True
    for ch in range(8):
        for ai in range(2):
            a = ch * 2 + ai
            for j in range(NB):
                k = a * NB + j
                P.step("tensor", [("X", j), "idf"], [("PS", 2 + k % 2)],
                       lambda e, a=a, j=j, k=k: e.transpose(
                           PS[2 + k % 2][:, 0:128],
                           X[:, j * TOK + 128 * a: j * TOK + 128 * a + 128], idf[:]))
                w = [("xtok2", ch % 2)] + (list(legacy) if first else [])
                first = False
                P.step("vector", [("PS", 2 + k % 2)], w,
                       lambda e, ch=ch, ai=ai, j=j, k=k: e.tensor_copy(
                           xtok[:, ((ch % 2) * 2 + ai) * D + 128 * j:
                                ((ch % 2) * 2 + ai) * D + 128 * j + 128],
                           PS[2 + k % 2][:, 0:128]))
        P.step("gpsimd", [("xtok2", ch % 2)], ["out_hbm"],
               lambda e, ch=ch: e.dma_start(
                   out=sh["out_hbm"].rearrange("(a b) d -> b a d", b=128)[:, 2 * ch:2 * ch + 2, :],
                   in_=xtok.rearrange("p (a d) -> p a d", a=4)[:, (ch % 2) * 2:(ch % 2) * 2 + 2, :]),
               dma=True)
    AF32.freea(rng)



def wload(sh, tag, dram_ap, nin, nout):
    P, AB16 = sh["P"], sh["AB16"]
    cols = nin * nout * 128
    rng, legacy = AB16.alloc(cols, [tag])
    t = AB16.t[:, rng[0]:rng[0] + cols]
    P.step("gpsimd", [], [tag] + list(legacy),
           lambda e: e.dma_start(
               out=t.rearrange("p (m j c) -> p m j c", m=nout, j=nin),
               in_=dram_ap.rearrange("(j p) (m c) -> p m j c", p=128, c=128)),
           dma=True)
    return t, tag, rng




def dbg_dump(sh, key_reads, src_ap, col0, name):
    if not DBG.get("on"):
        return
    P = sh["P"]
    np_, nw = src_ap.shape[0], src_ap.shape[-1]
    P.step("gpsimd", list(key_reads), [("dbg", name)],
           lambda e: e.dma_start(
               out=sh["dbg_hbm"][0:np_, col0:col0 + nw], in_=src_ap),
           dma=True)
    DBG.setdefault("names", []).append((name, col0, src_ap.shape[-1]))
    P.final_extra = ("dbg", name)


def get_xb(sh):
    P, X, AB16 = sh["P"], sh["X"], sh["AB16"]
    if sh.get("xb_rng") is None:
        rng, legacy = AB16.alloc(NB * TOK, ["xballoc"])
        sh["xb_rng"] = rng
        sh["Xb"] = AB16.t[:, rng[0]:rng[0] + NB * TOK]
        first_extra = list(legacy) + ["xballoc"]
    else:
        first_extra = []
    Xb = sh["Xb"]
    for j in range(NB):
        P.step("vector", [("X", j)], [("Xb", j)] + (first_extra if j == 0 else []),
               lambda e, j=j: e.tensor_copy(Xb[:, j * TOK:(j + 1) * TOK],
                                            X[:, j * TOK:(j + 1) * TOK]))
    return Xb


def free_xb(sh):
    if sh.get("xb_rng") is not None:
        sh["AB16"].freea(sh["xb_rng"], [("Xb", j) for j in range(NB)] + ["xballoc"])
        sh["xb_rng"] = None


def proj(sh, tag, wsb, wkey, evac, nin=NB, nout=NB, tokw=512, src=None,
         srckeyfn=None, psbanks=(4, 5), srctok=TOK, ntch=None):
    P, PS = sh["P"], sh["PS"]
    src = sh["Xb"] if src is None else src
    if srckeyfn is None:
        srckeyfn = lambda j, tch: ("Xb", j)
    ntch = (srctok // tokw) if ntch is None else ntch
    g = 0
    for m in range(nout):
        for tch in range(ntch):
            bank = psbanks[g % 2]
            ps = PS[bank]

            def mk(m=m, tch=tch, ps=ps):
                def f(e):
                    for j in range(nin):
                        ins = e.matmul(
                            ps[:, 0:tokw],
                            wsb[:, (m * nin + j) * 128:(m * nin + j) * 128 + 128],
                            src[:, j * srctok + tch * tokw:
                                j * srctok + tch * tokw + tokw],
                            start=(j == 0), stop=(j == nin - 1))
                    return ins
                return f
            rkeys = [wkey] + [srckeyfn(j, tch) for j in range(nin)]
            P.step("tensor", rkeys, [("PS", bank)], mk())
            evac(m, tch, ps[:, 0:tokw], bank)
            g += 1


def featln(sh, tag, chunk_done=None):
    P, X, st, PS, consts = sh["P"], sh["X"], sh["st"], sh["PS"], sh["consts"]
    sq2, onesrow = sh["sq2"], sh["onesrow"]
    for tch in range(4):
        for j in range(NB):
            k = tch * NB + j
            P.step("scalar", [("X", j)], [(tag + "sq", k % 2), "BUFsq2"],
                   lambda e, j=j, k=k, tch=tch: e.activation(
                       sq2[:, (k % 2) * 512:(k % 2) * 512 + 512],
                       X[:, j * TOK + tch * 512: j * TOK + tch * 512 + 512],
                       AF.Square))
            P.step("tensor", [("X", j), "consts"], [("PS", 4)],
                   lambda e, j=j, tch=tch: e.matmul(
                       PS[4][0:1, 0:512], consts[:, 0:1],
                       X[:, j * TOK + tch * 512: j * TOK + tch * 512 + 512],
                       start=(j == 0), stop=(j == NB - 1)))
            P.step("tensor", [(tag + "sq", k % 2), "consts"], [("PS", 5)],
                   lambda e, j=j, k=k: e.matmul(
                       PS[5][0:1, 0:512], consts[:, 0:1],
                       sq2[:, (k % 2) * 512:(k % 2) * 512 + 512],
                       start=(j == 0), stop=(j == NB - 1)))
        P.step("vector", [("PS", 4)], [(tag, "mu")],
               lambda e: e.tensor_scalar_mul(
                   st[0:1, 1024:1536], PS[4][0:1, 0:512], 1.0 / D))
        P.step("vector", [("PS", 5)], [(tag, "e2")],
               lambda e: e.tensor_scalar_mul(
                   st[0:1, 1536:2048], PS[5][0:1, 0:512], 1.0 / D))
        P.step("vector", [(tag, "mu")], [(tag, "m2")],
               lambda e: e.tensor_mul(st[0:1, 512:1024], st[0:1, 1024:1536],
                                      st[0:1, 1024:1536]))
        P.step("vector", [(tag, "m2"), (tag, "e2")], [(tag, "var")],
               lambda e: e.tensor_sub(st[0:1, 1536:2048], st[0:1, 1536:2048],
                                      st[0:1, 512:1024]))
        P.step("scalar", [(tag, "var"), "consts"], [(tag, "lg")],
               lambda e: e.activation(st[0:1, 512:1024], st[0:1, 1536:2048],
                                      AF.Ln, bias=consts[0:1, 1:2]))
        P.step("scalar", [(tag, "lg")], [(tag, "rstd")],
               lambda e: e.activation(st[0:1, 0:512], st[0:1, 512:1024],
                                      AF.Exp, scale=-0.5))
        P.step("vector", [(tag, "rstd"), (tag, "mu")], [(tag, "b")],
               lambda e: e.tensor_mul(st[0:1, 512:1024], st[0:1, 1024:1536],
                                      st[0:1, 0:512]))
        P.step("tensor", [(tag, "rstd"), "onesrow"], [("PS", 4)],
               lambda e: e.matmul(PS[4][:, 0:512], onesrow[0:1, 0:128],
                                  st[0:1, 0:512], start=True, stop=True))
        P.step("tensor", [(tag, "b"), "onesrow"], [("PS", 5)],
               lambda e: e.matmul(PS[5][:, 0:512], onesrow[0:1, 0:128],
                                  st[0:1, 512:1024], start=True, stop=True))
        for j in range(NB):
            sl = slice(j * TOK + tch * 512, j * TOK + tch * 512 + 512)
            P.step("vector", [("PS", 4)], [("X", j)],
                   lambda e, sl=sl: e.tensor_mul(X[:, sl], X[:, sl],
                                                 PS[4][:, 0:512]))
            P.step("vector", [("PS", 5)], [("X", j)],
                   lambda e, sl=sl: e.tensor_sub(X[:, sl], X[:, sl],
                                                 PS[5][:, 0:512]))
        if chunk_done is not None:
            chunk_done(tch)



def emit_l1_attn(sh):
    P, nc, I, Dm, X, st, PS = (sh["P"], sh["nc"], sh["I"], sh["Dm"], sh["X"],
                               sh["st"], sh["PS"])
    AF32, AB16, consts = sh["AF32"], sh["AB16"], sh["consts"]
    biasbuf, OF = sh["biasbuf"], sh["OF"]

    get_xb(sh)
    win_sb, win_k, win_r = wload(sh, "winsb", I["winT"], NB, NB)
    wg_sb, wg_k, wg_r = wload(sh, "wgsb", I["wgT"], NB, NB)

    rA, lgA = AF32.alloc(768, ["l1small"])
    base = rA[0]
    A_ = AF32.t
    bmsb = A_[:, base:base + 128]
    powB = A_[:, base + 128:base + 336]
    powBT = A_[:, base + 336:base + 544]
    wstack = A_[:, base + 544:base + 560]
    sstack = A_[:, base + 560:base + 561]
    hin = A_[:, base + 561:base + 562]
    i16 = A_[:, base + 576:base + 592]
    cmk = A_[:, base + 592:base + 596]
    prep = A_[:, base + 600:base + 632]
    P.step("sync", [], ["l1bm"] + list(lgA),
           lambda e: e.dma_start(
               out=bmsb.rearrange("p (j c) -> p j c", j=NB),
               in_=I["BmT"].rearrange("(j p) c -> p j c", p=128)), dma=True)
    P.step("sync", [], [("pow", 0)],
           lambda e: e.dma_start(out=powB[0:16, 0:16], in_=I["A_Tin"]), dma=True)
    P.step("sync", [], [("powT", 0)],
           lambda e: e.dma_start(out=powBT[0:16, 0:16], in_=I["A_nat"]), dma=True)
    P.step("sync", [], ["i16"],
           lambda e: e.dma_start(out=i16[:], in_=I["i16rep"]), dma=True)
    P.step("sync", [], ["cmk"],
           lambda e: e.dma_start(out=cmk[:], in_=I["cmask"]), dma=True)

    for s in range(12):
        P.step("tensor", [("pow", s), ("powT", s)], [("PS", 6)],
               lambda e, s=s: e.matmul(PS[6][0:16, 0:16],
                                       powBT[0:16, s * 16:s * 16 + 16],
                                       powB[0:16, s * 16:s * 16 + 16],
                                       start=True, stop=True))
        P.step("vector", [("PS", 6)], [("pow", s + 1)],
               lambda e, s=s: e.tensor_copy(
                   powB[0:16, (s + 1) * 16:(s + 1) * 16 + 16], PS[6][0:16, 0:16]))
        P.step("tensor", [("pow", s), ("powT", s)], [("PS", 7)],
               lambda e, s=s: e.matmul(PS[7][0:16, 0:16],
                                       powB[0:16, s * 16:s * 16 + 16],
                                       powBT[0:16, s * 16:s * 16 + 16],
                                       start=True, stop=True))
        P.step("vector", [("PS", 7)], [("powT", s + 1)],
               lambda e, s=s: e.tensor_copy(
                   powBT[0:16, (s + 1) * 16:(s + 1) * 16 + 16], PS[7][0:16, 0:16]))

    rU, lgU = AF32.alloc(2048, ["U16z"])
    U16 = AF32.t[:, rU[0]:rU[0] + 2048]
    rG, lgG = AF32.alloc(8192, ["gtile"])
    gtile = AF32.t[:, rG[0]:rG[0] + 4096]
    itile = AF32.t[:, rG[0] + 4096:rG[0] + 8192]
    lgG = list(lgG)

    for tch in range(4):
        g = 0
        for m in range(NB):
            for wsb, wkey, which in ((wg_sb, wg_k, "g"), (win_sb, win_k, "i")):
                bank = (4, 5)[g % 2]

                def mk(m=m, wsb=wsb, bank=bank, tch=tch):
                    def f(e):
                        for j in range(NB):
                            ins = e.matmul(
                                PS[bank][:, 0:512],
                                wsb[:, (m * NB + j) * 128:(m * NB + j) * 128 + 128],
                                sh["Xb"][:, j * TOK + tch * 512:
                                         j * TOK + tch * 512 + 512],
                                start=(j == 0), stop=(j == NB - 1))
                        return ins
                    return f
                P.step("tensor", [wkey] + [("Xb", j) for j in range(NB)],
                       [("PS", bank)], mk())
                if which == "g":
                    P.step("scalar", [("PS", bank), "bb1"],
                           [("gt", m)] + (lgG if (m, tch) == (0, 0) else []),
                           lambda e, m=m, bank=bank: e.activation(
                               gtile[:, m * 512:m * 512 + 512],
                               PS[bank][:, 0:512], AF.Sigmoid,
                               bias=biasbuf[:, OF["BG"] + m:OF["BG"] + m + 1]))
                else:
                    P.step("scalar", [("PS", bank), "bb0"], [("it", m)],
                           lambda e, m=m, bank=bank: e.activation(
                               itile[:, m * 512:m * 512 + 512],
                               PS[bank][:, 0:512], AF.Identity,
                               bias=biasbuf[:, OF["BIN"] + m:OF["BIN"] + m + 1]))
                g += 1
        for m in range(NB):
            P.step("vector", [("gt", m), ("it", m)], [("ut", m)],
                   lambda e, m=m: e.tensor_mul(itile[:, m * 512:m * 512 + 512],
                                               itile[:, m * 512:m * 512 + 512],
                                               gtile[:, m * 512:m * 512 + 512]))

        def mkub(tch=tch):
            def f(e):
                for j in range(NB):
                    ins = e.matmul(PS[6][0:16, 0:512],
                                   bmsb[:, j * 16:j * 16 + 16],
                                   itile[:, j * 512:j * 512 + 512],
                                   start=(j == 0), stop=(j == NB - 1))
                return ins
            return f
        P.step("tensor", [("ut", j) for j in range(NB)] + ["l1bm"],
               [("PS", 6)], mkub())
        P.step("vector", [("PS", 6)],
               [("U16", tch)] + (list(lgU) if tch == 0 else []),
               lambda e, tch=tch: e.tensor_copy(
                   U16[0:16, tch * 512:tch * 512 + 512], PS[6][0:16, 0:512]))

    AF32.freea(rG, [(k, m) for k in ("gt", "it", "ut") for m in range(NB)])
    AB16.freea(win_r, [win_k])
    AB16.freea(wg_r, [wg_k])
    UKEYS = [("U16", t) for t in range(4)]

    for s in range(11):
        shift = 1 << s
        spans = []
        end = 2048
        while end > shift:
            spans.append((max(shift, end - 512), end))
            end = max(shift, end - 512)
        for (start, end) in spans:
            P.step("tensor", [("pow", s)] + UKEYS, [("PS", 6)],
                   lambda e, s=s, start=start, end=end, shift=shift: e.matmul(
                       PS[6][0:16, 0:end - start],
                       powB[0:16, s * 16:s * 16 + 16],
                       U16[0:16, start - shift:end - shift],
                       start=True, stop=True))
            P.step("vector", [("PS", 6)], UKEYS,
                   lambda e, start=start, end=end: e.tensor_add(
                       U16[0:16, start:end], U16[0:16, start:end],
                       PS[6][0:16, 0:end - start]))

    P.step("sync", UKEYS, ["state_bounce"],
           lambda e: e.dma_start(out=Dm["state_bounce"][:, :],
                                 in_=U16[0:16, 2047:2048]), dma=True)
    P.step("gpsimd", ["state_bounce"], ["state_ag"],
           lambda e: e.collective_compute(
               "AllGather", ALU.bypass,
               replica_groups=[[0, 1, 2, 3, 4, 5, 6, 7]],
               ins=[Dm["state_bounce"][:, :]], outs=[Dm["state_ag"][:, :]]),
           own_sem="cc_state")
    P.step("sync", ["state_ag"], ["sstack"],
           lambda e: e.dma_start(
               out=sstack[:, 0:1],
               in_=Dm["state_ag"].rearrange("a (b c) -> (a b) c", c=1)),
           dma=True)
    P.step("sync", [("pow", 11), ("pow", 12)], ["pow_dram"],
           lambda e: e.dma_start(out=Dm["pow_dram"][0:16, 0:32],
                                 in_=powB[0:16, 176:208]), dma=True)
    for r in range(8):
        P.step("sync", ["pow_dram"], [("powrep", r)],
               lambda e, r=r: e.dma_start(
                   out=prep[16 * r:16 * r + 16, 0:32],
                   in_=Dm["pow_dram"][0:16, 0:32]), dma=True)
    P.step("vector", ["i16", "cmk"], [("ws", 0)],
           lambda e: e.tensor_scalar_mul(wstack[:], i16[:], cmk[:, 0:1]))
    P.step("vector", [("ws", 0)] + [("powrep", r) for r in range(8)], [("ws", 1)],
           lambda e: e.scalar_tensor_tensor(
               wstack[:], prep[:, 0:16], cmk[:, 1:2], wstack[:],
               ALU.mult, ALU.add))
    P.step("vector", [("ws", 1)], [("ws", 2)],
           lambda e: e.scalar_tensor_tensor(
               wstack[:], prep[:, 16:32], cmk[:, 2:3], wstack[:],
               ALU.mult, ALU.add))
    P.step("tensor", [("ws", 2), "sstack"], [("PS", 7)],
           lambda e: e.matmul(PS[7][0:16, 0:1], wstack[:], sstack[:, 0:1],
                              start=True, stop=True))
    P.step("vector", [("PS", 7)], ["hin"],
           lambda e: e.tensor_copy(hin[0:16, 0:1], PS[7][0:16, 0:1]))

    rC, lgC = AF32.alloc(2048, ["C16z"])
    C16 = AF32.t[:, rC[0]:rC[0] + 2048]
    P.step("vector", [], [("C16", -1)] + list(lgC),
           lambda e: e.memset(C16[0:16, :], 0.0))
    P.step("tensor", [("pow", 0), "hin", ("C16", -1)], [("PS", 7)],
           lambda e: e.matmul(PS[7][0:16, 0:1], powB[0:16, 0:16],
                              hin[0:16, 0:1], start=True, stop=True))
    P.step("vector", [("PS", 7), ("C16", -1)], [("C16", 0)],
           lambda e: e.tensor_copy(C16[0:16, 0:1], PS[7][0:16, 0:1]))
    for s in range(11):
        shift = 1 << s
        lo, hi = shift, min(2048, 2 * shift)
        for start in range(lo, hi, 512):
            end = min(start + 512, hi)
            P.step("tensor", [("pow", s), ("C16", s)], [("PS", 7)],
                   lambda e, s=s, start=start, end=end, shift=shift: e.matmul(
                       PS[7][0:16, 0:end - start],
                       powB[0:16, s * 16:s * 16 + 16],
                       C16[0:16, start - shift:end - shift],
                       start=True, stop=True))
            P.step("vector", [("PS", 7), ("C16", s)], [("C16", s + 1)],
                   lambda e, start=start, end=end: e.tensor_add(
                       C16[0:16, start:end], C16[0:16, start:end],
                       PS[7][0:16, 0:end - start]))
    P.step("vector", [("C16", 11)] + UKEYS, UKEYS,
           lambda e: e.tensor_add(U16[0:16, :], U16[0:16, :], C16[0:16, :]))
    AF32.freea(rC, [("C16", s) for s in range(-1, 12)])

    rH, lgH = AB16.alloc(2048, ["Hbz"])
    Hb = AB16.t[:, rH[0]:rH[0] + 2048]
    P.step("vector", UKEYS, ["Hb"] + list(lgH),
           lambda e: e.tensor_copy(Hb[0:16, :], U16[0:16, :]))
    rcm, lgcm = AB16.alloc(1024, ["cmsbz"])
    cmsb = AB16.t[:, rcm[0]:rcm[0] + 1024]
    P.step("gpsimd", [], ["cmsb"] + list(lgcm),
           lambda e: e.dma_start(out=cmsb[0:16, :], in_=I["CmT"][:, :]), dma=True)
    wo_sb, wo_k, wo_r = wload(sh, "wosb", I["woT"], NB, NB)
    ryt, lgyt = AB16.alloc(NB * 512, ["yTz"])
    yTc = AB16.t[:, ryt[0]:ryt[0] + NB * 512]
    first_yt = [True]
    for tch in range(4):
        for m in range(NB):
            P.step("tensor", ["cmsb", "Hb"], [("PS", (4, 5)[m % 2])],
                   lambda e, m=m, tch=tch: e.matmul(
                       PS[(4, 5)[m % 2]][:, 0:512],
                       cmsb[0:16, m * 128:m * 128 + 128],
                       Hb[0:16, tch * 512:tch * 512 + 512],
                       start=True, stop=True))
            w = [("yTc", m)] + (list(lgyt) if first_yt[0] else [])
            first_yt[0] = False
            P.step("vector", [("PS", (4, 5)[m % 2])], w,
                   lambda e, m=m: e.tensor_copy(
                       yTc[:, m * 512:m * 512 + 512],
                       PS[(4, 5)[m % 2]][:, 0:512]))
        for m in range(NB):
            bank = (6, 7)[m % 2]

            def mkwo(m=m, bank=bank):
                def f(e):
                    for j in range(NB):
                        ins = e.matmul(
                            PS[bank][:, 0:512],
                            wo_sb[:, (m * NB + j) * 128:(m * NB + j) * 128 + 128],
                            yTc[:, j * 512:j * 512 + 512],
                            start=(j == 0), stop=(j == NB - 1))
                    return ins
                return f
            P.step("tensor", [wo_k] + [("yTc", j) for j in range(NB)],
                   [("PS", bank)], mkwo())
            sl = slice(m * TOK + tch * 512, m * TOK + tch * 512 + 512)
            P.step("vector", [("PS", bank), ("X", m), "bb2"], [("X", m)],
                   lambda e, m=m, sl=sl, bank=bank: e.scalar_tensor_tensor(
                       X[:, sl], PS[bank][:, 0:512],
                       biasbuf[:, OF["BO"] + m:OF["BO"] + m + 1],
                       X[:, sl], ALU.add, ALU.add))
    AF32.freea(rU, [("U16", t) for t in range(4)])
    AF32.freea(rA, [("pow", s) for s in range(13)] +
               [("powT", s) for s in range(13)] +
               ["l1bm", "i16", "cmk", "sstack", "hin",
                ("ws", 0), ("ws", 1), ("ws", 2)] +
               [("powrep", r) for r in range(8)])
    AB16.freea(rH, ["Hb"])
    AB16.freea(rcm, ["cmsb"])
    AB16.freea(ryt, [("yTc", m) for m in range(NB)])
    AB16.freea(wo_r, [wo_k])

    featln(sh, "ln3")


def emit_l0_attn(sh):
    P, nc, I, Dm, X, PS = sh["P"], sh["nc"], sh["I"], sh["Dm"], sh["X"], sh["PS"]
    AF32, AB16, sq2 = sh["AF32"], sh["AB16"], sh["sq2"]
    idb, idf = sh["idb"], sh["idf"]

    Xb = get_xb(sh)

    rS, lgS = AF32.alloc(8192, ["qkslab"])
    qsl = AF32.t[:, rS[0]:rS[0] + 8192]
    rT, lgT = AF32.alloc(4096, ["ropetmp"])
    rt1 = AF32.t[:, rT[0]:rT[0] + 1024]
    rt2 = AF32.t[:, rT[0] + 1024:rT[0] + 2048]
    rtc = AF32.t[:, rT[0] + 2048:rT[0] + 3072]
    rts = AF32.t[:, rT[0] + 3072:rT[0] + 4096]
    rQKB, lgQKB = AB16.alloc(4096, ["qkb"])
    qkb = AB16.t[:, rQKB[0]:rQKB[0] + 4096]
    first_tbl = [True]
    first_cast = [True]

    def qkproj(tg, wname, spill, spkey):
        wsb, wkey, wrng = wload(sh, tg + "w", I[wname], NB, NB)
        g = 0
        for H in range(2):
            for m in range(NB):
                for tc in range(2):
                    bank = (4, 5)[g % 2]
                    g += 1
                    tch = 2 * H + tc

                    def mkp(m=m, tch=tch, bank=bank):
                        def f(e):
                            for j in range(NB):
                                ins = e.matmul(
                                    PS[bank][:, 0:512],
                                    wsb[:, (m * NB + j) * 128:
                                        (m * NB + j) * 128 + 128],
                                    Xb[:, j * TOK + tch * 512:
                                       j * TOK + tch * 512 + 512],
                                    start=(j == 0), stop=(j == NB - 1))
                            return ins
                        return f
                    P.step("tensor", [wkey] + [("Xb", j) for j in range(NB)],
                           [("PS", bank)], mkp())
                    wk = [("qs", m, tc)]
                    if (tg, H, m, tc) == ("q", 0, 0, 0):
                        wk += list(lgS)
                    P.step("vector", [("PS", bank)], wk,
                           lambda e, m=m, tc=tc, bank=bank: e.tensor_copy(
                               qsl[:, m * 1024 + tc * 512:
                                   m * 1024 + tc * 512 + 512],
                               PS[bank][:, 0:512]))
            for hh in range(4):
                te = qsl[:, hh * 1024:hh * 1024 + 1024]
                to = qsl[:, (hh + 4) * 1024:(hh + 4) * 1024 + 1024]
                kTE = [("qs", hh, 0), ("qs", hh, 1)]
                kTO = [("qs", hh + 4, 0), ("qs", hh + 4, 1)]
                lt = list(lgT) if first_tbl[0] else []
                first_tbl[0] = False
                P.step("sync", [], [("RTC",)] + lt,
                       lambda e, hh=hh, H=H: e.dma_start(
                           out=rtc[:, :],
                           in_=I["ropeC"][hh * 128:(hh + 1) * 128,
                                          H * 1024:H * 1024 + 1024]), dma=True)
                P.step("sync", [], [("RTS",)],
                       lambda e, hh=hh, H=H: e.dma_start(
                           out=rts[:, :],
                           in_=I["ropeS"][hh * 128:(hh + 1) * 128,
                                          H * 1024:H * 1024 + 1024]), dma=True)
                P.step("vector", kTE + [("RTC",)], [("RT1",)],
                       lambda e, te=te: e.tensor_mul(rt1, te, rtc))
                P.step("vector", kTE + [("RTS",)], [("RT2",)],
                       lambda e, te=te: e.tensor_mul(rt2, te, rts))
                P.step("vector", kTO + [("RTS",)], kTE,
                       lambda e, te=te, to=to: e.tensor_mul(te, to, rts))
                P.step("vector", [("RT1",)] + kTE, kTE,
                       lambda e, te=te: e.tensor_add(te, te, rt1))
                P.step("vector", kTO + [("RTC",)], [("RT1",)],
                       lambda e, to=to: e.tensor_mul(rt1, to, rtc))
                P.step("vector", [("RT1",), ("RT2",)], kTO,
                       lambda e, to=to: e.tensor_sub(to, rt1, rt2))
            for j in range(NB):
                slot = j % 2
                wk = [("qkb", slot)] + (list(lgQKB) if first_cast[0] else [])
                first_cast[0] = False
                P.step("vector", [("qs", j, 0), ("qs", j, 1)], wk,
                       lambda e, j=j, slot=slot: e.tensor_copy(
                           qkb[:, slot * 2048:slot * 2048 + 1024],
                           qsl[:, j * 1024:(j + 1) * 1024]))
                P.step("sync", [("qkb", slot)], [(spkey, j, H)],
                       lambda e, j=j, slot=slot, spill=spill, H=H: e.dma_start(
                           out=spill[j * 128:(j + 1) * 128,
                                     H * 1024:H * 1024 + 1024],
                           in_=qkb[:, slot * 2048:slot * 2048 + 1024]),
                       dma=True)
        AB16.freea(wrng, [wkey])

    qkproj("q", "wqT", Dm["qr_spill"], "qsp")
    qkproj("k", "wkT", Dm["k_bounce"], "ksp")
    P.step("gpsimd", [("ksp", j, H) for j in range(NB) for H in range(2)],
           ["k_ag"],
           lambda e: e.collective_compute(
               "AllGather", ALU.bypass,
               replica_groups=[[0, 1, 2, 3], [4, 5, 6, 7]],
               ins=[Dm["k_bounce"][:, :]], outs=[Dm["k_ag"][:, :]]),
           own_sem="cc_k")
    AF32.freea(rS, [("qs", m, t) for m in range(NB) for t in range(2)])
    AF32.freea(rT, [("RT1",), ("RT2",), ("RTC",), ("RTS",)])
    AB16.freea(rQKB, [("qkb", 0), ("qkb", 1)])

    wvsb, wvk, wvr = wload(sh, "vw", I["wvT"], NB, NB)
    rVS, lgVS = AB16.alloc(4096, ["vstage"])
    vstage = AB16.t[:, rVS[0]:rVS[0] + 4096]
    rVT, lgVT = AB16.alloc(2048, ["vtok"])
    vtok = AB16.t[:, rVT[0]:rVT[0] + 2048]
    first_vs, first_vt = [True], [True]
    for tch in range(4):
        for m in range(NB):
            bank = (4, 5)[m % 2]

            def mkv(m=m, tch=tch, bank=bank):
                def f(e):
                    for j in range(NB):
                        ins = e.matmul(
                            PS[bank][:, 0:512],
                            wvsb[:, (m * NB + j) * 128:(m * NB + j) * 128 + 128],
                            Xb[:, j * TOK + tch * 512:j * TOK + tch * 512 + 512],
                            start=(j == 0), stop=(j == NB - 1))
                    return ins
                return f
            P.step("tensor", [wvk] + [("Xb", j) for j in range(NB)],
                   [("PS", bank)], mkv())
            wk = [("vst", m)] + (list(lgVS) if first_vs[0] else [])
            first_vs[0] = False
            P.step("scalar", [("PS", bank)], wk,
                   lambda e, m=m, bank=bank: e.activation(
                       vstage[:, m * 512:m * 512 + 512], PS[bank][:, 0:512],
                       AF.Copy))
        for a in range(4):
            pb = (6, 7)[a % 2]

            def mkt(a=a, pb=pb):
                def f(e):
                    pv = PS[pb].bitcast(BF16)
                    for m in range(NB):
                        ins = e.transpose(
                            pv[:, m * 128:(m + 1) * 128],
                            vstage[:, m * 512 + a * 128:m * 512 + a * 128 + 128],
                            idb[:])
                    return ins
                return f
            P.step("tensor", [("vst", m) for m in range(NB)] + ["idb"],
                   [("PS", pb)], mkt())
            slot = a % 2
            wk = [("vtok", slot)] + (list(lgVT) if first_vt[0] else [])
            first_vt[0] = False
            P.step("vector", [("PS", pb)], wk,
                   lambda e, pb=pb, slot=slot: e.tensor_copy(
                       vtok[:, slot * 1024:slot * 1024 + 1024],
                       PS[pb].bitcast(BF16)[:, 0:1024]))
            P.step("sync", [("vtok", slot)], [("vsp", tch, a)],
                   lambda e, tch=tch, a=a, slot=slot: e.dma_start(
                       out=Dm["v_bounce"].rearrange("(a p) c -> p a c", p=128)
                       [:, tch * 4 + a:tch * 4 + a + 1, :],
                       in_=vtok.rearrange("p (s c) -> p s c", s=2)
                       [:, slot:slot + 1, :]), dma=True)
    P.step("gpsimd", [("vsp", t, a) for t in range(4) for a in range(4)],
           ["v_ag"],
           lambda e: e.collective_compute(
               "AllGather", ALU.bypass,
               replica_groups=[[0, 1, 2, 3], [4, 5, 6, 7]],
               ins=[Dm["v_bounce"][:, :]], outs=[Dm["v_ag"][:, :]]),
           own_sem="cc_v")
    AB16.freea(wvr, [wvk])
    AB16.freea(rVS, [("vst", m) for m in range(NB)])
    AB16.freea(rVT, [("vtok", 0), ("vtok", 1)])
    free_xb(sh)

    Xv = X.bitcast(BF16)
    kslab = Xv[:, 0:16384]
    vslab = Xv[:, 16384:32768]
    rQR, lgQR = AB16.alloc(16384, ["qres"])
    qres = AB16.t[:, rQR[0]:rQR[0] + 16384]
    rSC, lgSC = AB16.alloc(4096, ["scsl"])
    scsl = [AB16.t[:, rSC[0]:rSC[0] + 2048],
            AB16.t[:, rSC[0] + 2048:rSC[0] + 4096]]
    rAT, lgAT = AB16.alloc(1024, ["atmp"])
    atmp = [AB16.t[:, rAT[0]:rAT[0] + 512], AB16.t[:, rAT[0] + 512:rAT[0] + 1024]]
    rO, lgO = AF32.alloc(6144, ["otile"])
    oti = [AF32.t[:, rO[0]:rO[0] + 1024], AF32.t[:, rO[0] + 1024:rO[0] + 2048]]
    otr = [AF32.t[:, rO[0] + 2048:rO[0] + 3072], AF32.t[:, rO[0] + 3072:rO[0] + 4096]]
    orb = [AF32.t[:, rO[0] + 4096:rO[0] + 5120], AF32.t[:, rO[0] + 5120:rO[0] + 6144]]
    first_sc, first_at, first_ot = [True], [True], [True]
    AVB = {(0, 0): 2, (0, 1): 3, (1, 0): 6, (1, 1): 7}

    P.step("sync", [("qsp", j, H) for j in range(NB) for H in range(2)],
           ["qres"] + list(lgQR),
           lambda e: e.dma_start(
               out=qres.rearrange("p (j t) -> p j t", j=NB),
               in_=Dm["qr_spill"].rearrange("(j p) t -> p j t", p=128)),
           dma=True)

    for qt in range(4):
        P.step("sync", ["k_ag"],
               [("kslab",), ("X", 0), ("X", 1), ("X", 2), ("X", 3)],
               lambda e, qt=qt: e.dma_start(
                   out=kslab.rearrange("p (j u) -> p j u", j=NB),
                   in_=Dm["k_ag"][qt * 1024:(qt + 1) * 1024, :]
                   .rearrange("(j p) t -> p j t", p=128)), dma=True)
        P.step("scalar", ["v_ag"],
               [("vslab",), ("X", 4), ("X", 5), ("X", 6), ("X", 7)],
               lambda e, qt=qt: e.dma_start(
                   out=vslab.rearrange("p (g c) -> p g c", g=16),
                   in_=Dm["v_ag"][qt * 2048:(qt + 1) * 2048, :]
                   .rearrange("(g p) c -> p g c", p=128)), dma=True)
        for p in range(8):
            q0 = p * 256
            if qt >= 1:
                for T in (0, 1):
                    t = 2 * p + T
                    P.step("gpsimd", [("ots", t)], [("OTR", T)],
                           lambda e, t=t, T=T: e.dma_start(
                               out=otr[T][:, :],
                               in_=Dm["ot_spill"][t * 128:(t + 1) * 128, :]),
                           dma=True)
            for kc in range(4):
                for T in (0, 1):
                    bank = T

                    def mks(kc=kc, T=T, bank=bank, q0=q0):
                        def f(e):
                            for j in range(NB):
                                ins = e.matmul(
                                    PS[bank][:, 0:512],
                                    qres[:, j * 2048 + q0 + T * 128:
                                         j * 2048 + q0 + T * 128 + 128],
                                    kslab[:, j * 2048 + kc * 512:
                                          j * 2048 + kc * 512 + 512],
                                    start=(j == 0), stop=(j == NB - 1))
                            return ins
                        return f
                    P.step("tensor", [("qres",), ("kslab",)],
                           [("PS", bank)], mks())
                    wk = [("sc", T, kc)] + (list(lgSC) if first_sc[0] else [])
                    first_sc[0] = False
                    P.step("scalar", [("PS", bank)], wk,
                           lambda e, T=T, kc=kc, bank=bank: e.activation(
                               scsl[T][:, kc * 512:kc * 512 + 512],
                               PS[bank][:, 0:512], AF.Copy))
                    P.step("vector", [("sc", T, kc)], [("fmx", T, kc)],
                           lambda e, T=T, kc=kc: e.tensor_reduce(
                               sq2[:, T * 4 + kc:T * 4 + kc + 1],
                               scsl[T][:, kc * 512:kc * 512 + 512], AX, ALU.max))
            for T in (0, 1):
                P.step("vector", [("fmx", T, kc) for kc in range(4)], [("fM", T)],
                       lambda e, T=T: e.tensor_reduce(
                           sq2[:, 16 + T:17 + T], sq2[:, T * 4:T * 4 + 4],
                           AX, ALU.max))
                P.step("vector", [("fM", T)], [("fnm", T)],
                       lambda e, T=T: e.tensor_scalar_mul(
                           sq2[:, 18 + T:19 + T], sq2[:, 16 + T:17 + T],
                           -1.0 / 32.0))
                for kc in range(4):
                    tb = kc % 2
                    wk = [("fat", tb), ("fds", T, kc)] + \
                        (list(lgAT) if first_at[0] else [])
                    first_at[0] = False
                    P.step("scalar", [("sc", T, kc), ("fnm", T)], wk,
                           lambda e, T=T, kc=kc, tb=tb: e.activation(
                               atmp[tb], scsl[T][:, kc * 512:kc * 512 + 512],
                               AF.Exp, scale=1.0 / 32.0,
                               bias=sq2[:, 18 + T:19 + T],
                               accum_out=sq2[:, 20 + T * 4 + kc:
                                             21 + T * 4 + kc]))
                    pb = 4 + kc % 2

                    def mkt(tb=tb, pb=pb):
                        def f(e):
                            pv = PS[pb].bitcast(BF16)
                            for i in range(4):
                                ins = e.transpose(
                                    pv[:, i * 128:(i + 1) * 128],
                                    atmp[tb][:, i * 128:(i + 1) * 128], idb[:])
                            return ins
                        return f
                    P.step("tensor", [("fat", tb), "idb"], [("PS", pb)], mkt())
                    P.step("vector", [("PS", pb)], [("sc", T, kc)],
                           lambda e, T=T, kc=kc, pb=pb: e.tensor_copy(
                               scsl[T][:, kc * 512:kc * 512 + 512],
                               PS[pb].bitcast(BF16)[:, 0:512]))
                P.step("vector", [("fds", T, kc) for kc in range(4)], [("fD", T)],
                       lambda e, T=T: e.tensor_reduce(
                           sq2[:, 36 + T:37 + T], sq2[:, 20 + T * 4:24 + T * 4],
                           AX, ALU.add))
            for T in (0, 1):
                for dvh in (0, 1):
                    bank = AVB[(T, dvh)]
                    for gg in range(2):
                        def mka(T=T, dvh=dvh, bank=bank, gg=gg):
                            def f(e):
                                for g in range(gg * 8, gg * 8 + 8):
                                    ins = e.matmul(
                                        PS[bank][:, 0:512],
                                        scsl[T][:, g * 128:(g + 1) * 128],
                                        vslab[:, g * 1024 + dvh * 512:
                                              g * 1024 + dvh * 512 + 512],
                                        start=(g == 0), stop=(g == 15))
                                return ins
                            return f
                        P.step("tensor",
                               [("sc", T, kc) for kc in range(gg * 2, gg * 2 + 2)]
                               + [("vslab",)], [("PS", bank)], mka())
            for T in (0, 1):
                t = 2 * p + T
                cb = 72 + T * 8
                if qt == 0:
                    for dvh in (0, 1):
                        wk = [("OT", T, dvh)] + \
                            (list(lgO) if first_ot[0] else [])
                        first_ot[0] = False
                        P.step("vector", [("PS", AVB[(T, dvh)])], wk,
                               lambda e, T=T, dvh=dvh: e.tensor_copy(
                                   oti[T][:, dvh * 512:dvh * 512 + 512],
                                   PS[AVB[(T, dvh)]][:, 0:512]))
                    P.step("vector", [("fM", T)], [("frm", t)],
                           lambda e, T=T, t=t: e.tensor_copy(
                               sq2[:, 40 + t:41 + t], sq2[:, 16 + T:17 + T]))
                    P.step("vector", [("fD", T)], [("frd", t)],
                           lambda e, T=T, t=t: e.tensor_copy(
                               sq2[:, 56 + t:57 + t], sq2[:, 36 + T:37 + T]))
                else:
                    P.step("vector", [("frm", t), ("fM", T)], [("fmm", T)],
                           lambda e, t=t, cb=cb, T=T: e.tensor_tensor(
                               sq2[:, cb:cb + 1], sq2[:, 40 + t:41 + t],
                               sq2[:, 16 + T:17 + T], ALU.max))
                    P.step("vector", [("fmm", T), ("frm", t)], [("fe1a", T)],
                           lambda e, t=t, cb=cb: e.tensor_sub(
                               sq2[:, cb + 1:cb + 2], sq2[:, 40 + t:41 + t],
                               sq2[:, cb:cb + 1]))
                    P.step("scalar", [("fe1a", T)], [("fe1", T)],
                           lambda e, cb=cb: e.activation(
                               sq2[:, cb + 1:cb + 2], sq2[:, cb + 1:cb + 2],
                               AF.Exp, scale=1.0 / 32.0))
                    P.step("vector", [("fmm", T), ("fM", T)], [("fe2a", T)],
                           lambda e, cb=cb, T=T: e.tensor_sub(
                               sq2[:, cb + 2:cb + 3], sq2[:, 16 + T:17 + T],
                               sq2[:, cb:cb + 1]))
                    P.step("scalar", [("fe2a", T)], [("fe2", T)],
                           lambda e, cb=cb: e.activation(
                               sq2[:, cb + 2:cb + 3], sq2[:, cb + 2:cb + 3],
                               AF.Exp, scale=1.0 / 32.0))
                    P.step("vector", [("fmm", T)], [("frm", t)],
                           lambda e, t=t, cb=cb: e.tensor_copy(
                               sq2[:, 40 + t:41 + t], sq2[:, cb:cb + 1]))
                    P.step("vector", [("fe1", T), ("frd", t)], [("fda", T)],
                           lambda e, t=t, cb=cb: e.tensor_mul(
                               sq2[:, cb + 3:cb + 4], sq2[:, 56 + t:57 + t],
                               sq2[:, cb + 1:cb + 2]))
                    P.step("vector", [("fe2", T), ("fD", T)], [("fdb", T)],
                           lambda e, cb=cb, T=T: e.tensor_mul(
                               sq2[:, cb + 5:cb + 6], sq2[:, 36 + T:37 + T],
                               sq2[:, cb + 2:cb + 3]))
                    P.step("vector", [("fda", T), ("fdb", T)], [("frd", t)],
                           lambda e, t=t, cb=cb: e.tensor_add(
                               sq2[:, 56 + t:57 + t], sq2[:, cb + 3:cb + 4],
                               sq2[:, cb + 5:cb + 6]))
                    for dvh in (0, 1):
                        bank = AVB[(T, dvh)]
                        P.step("vector", [("PS", bank), ("fe2", T)],
                               [("OT", T, dvh)],
                               lambda e, bank=bank, dvh=dvh, T=T, cb=cb:
                               e.tensor_scalar_mul(
                                   oti[T][:, dvh * 512:dvh * 512 + 512],
                                   PS[bank][:, 0:512], sq2[:, cb + 2:cb + 3]))
                        P.step("vector", [("OTR", T), ("fe1", T)],
                               [("OTR", T)],
                               lambda e, dvh=dvh, T=T, cb=cb:
                               e.tensor_scalar_mul(
                                   otr[T][:, dvh * 512:dvh * 512 + 512],
                                   otr[T][:, dvh * 512:dvh * 512 + 512],
                                   sq2[:, cb + 1:cb + 2]))
                        P.step("vector", [("OTR", T)], [("OT", T, dvh)],
                               lambda e, dvh=dvh, T=T: e.tensor_add(
                                   oti[T][:, dvh * 512:dvh * 512 + 512],
                                   oti[T][:, dvh * 512:dvh * 512 + 512],
                                   otr[T][:, dvh * 512:dvh * 512 + 512]))
                if qt == 3:
                    P.step("vector", [("frd", t)], [("fr", T)],
                           lambda e, t=t, cb=cb: e.reciprocal(
                               sq2[:, cb + 4:cb + 5], sq2[:, 56 + t:57 + t]))
                    P.step("sync", [], [("xrw", T)],
                           lambda e, t=t, T=T: e.dma_start(
                               out=orb[T][:, :],
                               in_=sh["I"]["x_in"][t * 128:(t + 1) * 128, :]),
                           dma=True)
                    for dvh in (0, 1):
                        P.step("vector", [("fr", T)], [("OT", T, dvh)],
                               lambda e, dvh=dvh, T=T, cb=cb:
                               e.tensor_scalar_mul(
                                   oti[T][:, dvh * 512:dvh * 512 + 512],
                                   oti[T][:, dvh * 512:dvh * 512 + 512],
                                   sq2[:, cb + 4:cb + 5]))
                    P.step("vector", [("xrw", T), ("OT", T, 0), ("OT", T, 1)],
                           [("OT", T, 0), ("OT", T, 1)],
                           lambda e, T=T: e.tensor_add(
                               oti[T][:, :], oti[T][:, :], orb[T][:, :]))
                P.step("gpsimd", [("OT", T, 0), ("OT", T, 1)], [("ots", t)],
                       lambda e, t=t, T=T: e.dma_start(
                           out=Dm["ot_spill"][t * 128:(t + 1) * 128, :],
                           in_=oti[T][:, :]), dma=True)

    for a in range(16):
        sl = a % 2
        P.step("sync", [("ots", a)], [("orb2", sl), ("xrw", sl)],
               lambda e, a=a, sl=sl: e.dma_start(
                   out=orb[sl][:, :],
                   in_=Dm["ot_spill"][a * 128:(a + 1) * 128, :]), dma=True)
        for j in range(NB):
            pb = 4 + j % 2
            P.step("tensor", [("orb2", sl), "idf"], [("PS", pb)],
                   lambda e, j=j, pb=pb, sl=sl: e.transpose(
                       PS[pb][:, 0:128], orb[sl][:, j * 128:j * 128 + 128],
                       idf[:]))
            P.step("vector", [("PS", pb)],
                   [("X", j), ("kslab",) if j < 4 else ("vslab",)],
                   lambda e, j=j, pb=pb, a=a: e.tensor_copy(
                       X[:, j * TOK + a * 128:j * TOK + a * 128 + 128],
                       PS[pb][:, 0:128]))

    AB16.freea(rQR, [("qres",)])
    AB16.freea(rSC, [("sc", T, kc) for T in (0, 1) for kc in range(4)])
    AB16.freea(rAT, [("fat", 0), ("fat", 1)])
    AF32.freea(rO, [("OT", s, d) for s in (0, 1) for d in (0, 1)] +
               [("OTR", 0), ("OTR", 1), ("orb2", 0), ("orb2", 1),
                ("xrw", 0), ("xrw", 1)])
    featln(sh, "ln1")



def emit_l0_ffn(sh):
    P, nc, I, Dm, X, st, PS = (sh["P"], sh["nc"], sh["I"], sh["Dm"], sh["X"],
                               sh["st"], sh["PS"])
    AF32, AB16, consts, onesrow = sh["AF32"], sh["AB16"], sh["consts"], sh["onesrow"]
    biasbuf, OF, idf, idb, i32buf = (sh["biasbuf"], sh["OF"], sh["idf"],
                                     sh["idb"], sh["i32buf"])
    free_xb(sh)

    rRB, lgRB = AF32.alloc(2944, ["rowB"])
    rowB = AF32.t[:, rRB[0]:rRB[0] + 2944]
    rWS, lgWS = AF32.alloc(8192, ["wq_tile"])
    wtile = AF32.t[:, rWS[0]:rWS[0] + 4096]
    rowA = AF32.t[:, rWS[0] + 4096:rWS[0] + 8192]
    rWB, lgWB = AB16.alloc(8192, ["wq_sgn"])
    sgn = AB16.t[:, rWB[0]:rWB[0] + 4096]
    qout = AB16.t[:, rWB[0] + 4096:rWB[0] + 8192]
    racc = sh["sq2"]
    wqs = biasbuf

    wmeta = dict(w1=(I["w1T"], Dm["w1q_dram"], NB, DH, D * DH),
                 w2=(I["w2T"], Dm["w2q_dram"], NBH, D, DH * D))
    for wn, (din, dout, nt, tc, total) in wmeta.items():
        for t in range(nt):
            P.step("sync", ["BUFwt"], [("wqt", wn, t % 2), "BUFwt"] +
                   (list(lgWS) + list(lgRB) if (wn, t) == ("w1", 0) else []),
                   lambda e, t=t, din=din, tc=tc: e.dma_start(
                       out=wtile[:, 0:tc], in_=din[t * 128:(t + 1) * 128, :]),
                   dma=True)
            P.step("vector", [("wqt", wn, t % 2), "BUFwt"], [("wqr1", wn)],
                   lambda e, tc=tc: e.tensor_reduce(
                       wqs[:, 193:194], wtile[:, 0:tc], AX, ALU.add,
                       apply_absolute_value=True))
            if t == 0:
                P.step("vector", [("wqr1", wn)], [("wacc", wn)],
                       lambda e: e.tensor_copy(wqs[:, 192:193], wqs[:, 193:194]))
            else:
                P.step("vector", [("wqr1", wn), ("wacc", wn)], [("wacc", wn)],
                       lambda e: e.tensor_add(wqs[:, 192:193], wqs[:, 192:193],
                                              wqs[:, 193:194]))
        P.step("tensor", [("wacc", wn), "consts"], [("PS", 2)],
               lambda e: e.matmul(PS[2][0:1, 0:1], consts[:, 0:1],
                                  wqs[:, 192:193], start=True, stop=True))
        P.step("vector", [("PS", 2)], [("wmean", wn)],
               lambda e, total=total: e.tensor_scalar_mul(
                   wqs[0:1, 195:196], PS[2][0:1, 0:1], 1.0 / total))
        P.step("vector", [("wmean", wn)], [("wmeanc", wn)],
               lambda e: e.tensor_scalar_max(wqs[0:1, 195:196],
                                             wqs[0:1, 195:196], 1e-5))
        P.step("vector", [("wmeanc", wn)], [("wthr", wn)],
               lambda e: e.tensor_scalar_mul(wqs[0:1, 196:197],
                                             wqs[0:1, 195:196], 0.5))
        io = 0 if wn == "w1" else 1
        P.step("vector", [("wmeanc", wn)], [("wsc", wn)],
               lambda e, io=io: e.tensor_scalar_mul(
                   rowB[0:1, 2064 + io:2065 + io], wqs[0:1, 195:196],
                   1.0 / 127.0))
        P.step("tensor", [("wthr", wn), "onesrow"], [("PS", 2)],
               lambda e: e.matmul(PS[2][:, 0:1], onesrow[0:1, 0:128],
                                  wqs[0:1, 196:197], start=True, stop=True))
        P.step("vector", [("PS", 2)], [("thrb", wn)],
               lambda e: e.tensor_copy(wqs[:, 194:195], PS[2][:, 0:1]))
        for t in range(nt):
            P.step("sync", ["BUFwt"], [("wqt", wn, t % 2), "BUFwt"],
                   lambda e, t=t, din=din, tc=tc: e.dma_start(
                       out=wtile[:, 0:tc], in_=din[t * 128:(t + 1) * 128, :]),
                   dma=True)
            P.step("scalar", [("wqt", wn, t % 2)],
                   [("wsgn", wn, t % 2)] + (list(lgWB) if (wn, t) == ("w1", 0) else []),
                   lambda e, tc=tc: e.activation(sgn[:, 0:tc], wtile[:, 0:tc],
                                                 AF.Sign))
            P.step("scalar", [("wsgn", wn, t % 2)], [("wqt", wn, t % 2), "BUFwt"],
                   lambda e, tc=tc: e.activation(wtile[:, 0:tc], wtile[:, 0:tc],
                                                 AF.Abs))
            P.step("vector", [("wqt", wn, t % 2), ("thrb", wn)],
                   [("wqt", wn, t % 2), "BUFwt"],
                   lambda e, tc=tc: e.tensor_scalar(
                       wtile[:, 0:tc], wtile[:, 0:tc], wqs[:, 194:195], None,
                       ALU.is_ge))
            P.step("vector", [("wqt", wn, t % 2), ("wsgn", wn, t % 2), "BUFwt"],
                   [("wqd", wn, t % 2)],
                   lambda e, tc=tc: e.tensor_mul(qout[:, 0:tc], wtile[:, 0:tc],
                                                 sgn[:, 0:tc]))
            P.step("sync", [("wqd", wn, t % 2)], [("wqdr", wn)],
                   lambda e, t=t, dout=dout, tc=tc: e.dma_start(
                       out=dout[t * 128:(t + 1) * 128, :], in_=qout[:, 0:tc]),
                   dma=True)
    AB16.freea(rWB, [("wsgn", w, p) for w in ("w1", "w2") for p in (0, 1)] +
               [("wqd", w, p) for w in ("w1", "w2") for p in (0, 1)])

    for j in range(NB):
        P.step("sync", [("X", j)], [("hb", j)],
               lambda e, j=j: e.dma_start(
                   out=Dm["halo_bounce"][0:1, j * 128:(j + 1) * 128],
                   in_=X[:, j * TOK:j * TOK + 1]), dma=True)
        P.step("sync", [("X", j)], [("hb2", j)],
               lambda e, j=j: e.dma_start(
                   out=Dm["halo_bounce"][1:2, j * 128:(j + 1) * 128],
                   in_=X[:, j * TOK + 2047:j * TOK + 2048]), dma=True)
    P.step("gpsimd",
           [("hb", j) for j in range(NB)] + [("hb2", j) for j in range(NB)],
           ["halo_ag"],
           lambda e: e.collective_compute(
               "AllGather", ALU.bypass, replica_groups=[[0, 1, 2, 3, 4, 5, 6, 7]],
               ins=[Dm["halo_bounce"][:, :]], outs=[Dm["halo_ag"][:, :]]),
           own_sem="cc_halo")
    rHG, lgHG = AF32.alloc(1024, ["hgath"])
    hg = AF32.t[:, rHG[0]:rHG[0] + 1024]
    P.step("sync", [], ["hidx", "BUFi32"],
           lambda e: e.dma_start(out=i32buf.bitcast(U32)[:, 0:1],
                                 in_=I["halo_idx"]), dma=True)
    P.step("gpsimd", ["halo_ag", "hidx", "BUFi32"], [("hg",)] + list(lgHG),
           lambda e: e.indirect_dma_start(
               out=hg[:, :], out_offset=None, in_=Dm["halo_ag"][:, :],
               in_offset=IndirectOffsetOnAxis(i32buf.bitcast(U32)[:, 0:1], 0)),
           dma=True)
    P.step("sync", [], ["hmsk"],
           lambda e: e.dma_start(out=st[0:2, 8:9], in_=I["halo_msk"]), dma=True)
    P.step("vector", [("hg",)], [("hamax",)],
           lambda e: e.tensor_reduce(st[0:2, 4:5], hg[0:2, :], AX, ALU.max,
                                     apply_absolute_value=True))
    P.step("vector", [("hamax",)], [("hamaxc",)],
           lambda e: e.tensor_scalar_max(st[0:2, 5:6], st[0:2, 4:5], 1e-5))
    P.step("vector", [("hamaxc",)], [("hrec",)],
           lambda e: e.reciprocal(st[0:2, 6:7], st[0:2, 5:6]))
    P.step("vector", [("hrec",), "hmsk"], [("hsq",)],
           lambda e: e.scalar_tensor_tensor(st[0:2, 7:8], st[0:2, 6:7], 127.0,
                                            st[0:2, 8:9], ALU.mult, ALU.mult))

    rXQ, lgXQ = AB16.alloc(NB * 2050, ["xq"])
    xq = AB16.t[:, rXQ[0]:rXQ[0] + NB * 2050]
    XQ0 = rXQ[0]
    rHQ, lgHQ = AB16.alloc(1024, ["hq"])
    hq = AB16.t[:, rHQ[0]:rHQ[0] + 1024]
    P.step("vector", [("hsq",), ("hg",)], [("hqi",), "BUFi32"],
           lambda e: e.tensor_scalar(i32buf[0:2, 0:1024], hg[0:2, :],
                                     st[0:2, 7:8], None, ALU.mult))
    P.step("vector", [("hqi",)], [("hq",)] + list(lgHQ) + list(lgHG),
           lambda e: e.tensor_copy(hq[0:2, :], i32buf[0:2, 0:1024]))
    AF32.freea(rHG, [("hg",)])

    macc = wtile[:, 0:2048]
    t0 = wtile[:, 2048:4096]
    t0b = rowA[:, 0:2048]
    P.step("scalar", [("X", 0), "BUFwt"], [("mt", 6), "BUFwt"],
           lambda e: e.activation(macc, X[:, 0:2048], AF.Abs))
    for j in range(1, NB):
        buf = t0 if j % 2 else t0b
        bk = "BUFwt" if j % 2 else "BUFrowA"
        P.step("scalar", [("X", j)], [("mta", j % 2), bk],
               lambda e, j=j, buf=buf: e.activation(
                   buf, X[:, j * TOK:j * TOK + 2048], AF.Abs))
        P.step("vector", [("mta", j % 2), ("mt", 6), bk], [("mt", 6), ("mtt", j % 2)],
               lambda e, buf=buf: e.tensor_max(macc, macc, buf))
    for a in range(16):
        P.step("tensor", [("mt", 6), "idf"], [("PS", 3)],
               lambda e, a=a: e.transpose(PS[3][:, 0:128],
                                          macc[:, a * 128:a * 128 + 128], idf[:]))
        P.step("vector", [("PS", 3)], [("amx", a), "BUFsq2"],
               lambda e, a=a: e.tensor_reduce(racc[:, 4 + a:5 + a],
                                              PS[3][:, 0:128], AX, ALU.max))
    P.step("tensor", [("amx", a) for a in range(16)] + ["idf"], [("PS", 3)],
           lambda e: e.transpose(PS[3][0:16, 0:128], racc[:, 4:20], idf[:]))
    P.step("vector", [("PS", 3), "BUFwt"], [("amxT",), "BUFwt"],
           lambda e: e.tensor_copy(t0[0:16, 0:128], PS[3][0:16, 0:128]))
    P.step("sync", [("amxT",), "BUFwt"], ["amxd"],
           lambda e: e.dma_start(out=Dm["row_bounce"][0:1, 0:2048],
                                 in_=t0[0:16, 0:128]), dma=True)
    P.step("sync", ["amxd", "BUFrowA"], [("amaxrow",), "BUFrowA"],
           lambda e: e.dma_start(out=rowA[0:1, 0:2048],
                                 in_=Dm["row_bounce"][0:1, 0:2048]), dma=True)
    P.step("vector", [("amaxrow",), "BUFrowA"], [("amaxrow",), "BUFrowA"],
           lambda e: e.tensor_scalar_max(rowA[0:1, 0:2048], rowA[0:1, 0:2048],
                                         1e-5))
    P.step("vector", [("amaxrow",), "BUFrowA"], [("srow",), "BUFrowA"],
           lambda e: e.reciprocal(rowA[0:1, 2048:4096], rowA[0:1, 0:2048]))
    P.step("vector", [("srow",), "BUFrowA"], [("srow",), "BUFrowA"],
           lambda e: e.tensor_scalar_mul(rowA[0:1, 2048:4096],
                                         rowA[0:1, 2048:4096], 127.0))
    P.step("vector", [("amaxrow",), ("wsc", "w1"), "BUFrowA"], [("s1core",), "BUFrowA"],
           lambda e: e.tensor_scalar(rowB[0:1, 1:2049], rowA[0:1, 0:2048],
                                     rowB[0:1, 2064:2065], None, ALU.mult))
    P.step("tensor", [("hamaxc",), "idf"], [("PS", 3)],
           lambda e: e.transpose(PS[3][0:1, 0:2], st[0:2, 5:6], idf[0:2, 0:2]))
    P.step("vector", [("PS", 3), ("wsc", "w1")], [("hs1row",)],
           lambda e: e.tensor_scalar(st[0:1, 10:12], PS[3][0:1, 0:2],
                                     rowB[0:1, 2064:2065], None, ALU.mult))
    P.step("vector", [("hs1row",)], [("s1halo",)],
           lambda e: e.tensor_copy(rowB[0:1, 0:1], st[0:1, 10:11]))
    P.step("vector", [("hs1row",)], [("s1halo2",)],
           lambda e: e.tensor_copy(rowB[0:1, 2049:2050], st[0:1, 11:12]))
    for tch in range(4):
        P.step("tensor", [("srow",), "onesrow", "BUFrowA"], [("PS", 2)],
               lambda e, tch=tch: e.matmul(
                   PS[2][:, 0:512], onesrow[0:1, 0:128],
                   rowA[0:1, 2048 + tch * 512:2048 + tch * 512 + 512],
                   start=True, stop=True))
        sbuf_ = t0 if tch % 2 else t0b
        sbk = "BUFwt" if tch % 2 else "BUFrowA"
        P.step("vector", [("PS", 2), sbk], [("sbc", tch), sbk],
               lambda e, sbuf_=sbuf_: e.tensor_copy(sbuf_[:, 0:512], PS[2][:, 0:512]))
        for j in range(NB):
            P.step("vector", [("sbc", tch), ("X", j), sbk], [("xqi", j, tch), "BUFi32"],
                   lambda e, j=j, tch=tch, sbuf_=sbuf_: e.tensor_mul(
                       i32buf[:, 0:512],
                       X[:, j * TOK + tch * 512:j * TOK + tch * 512 + 512],
                       sbuf_[:, 0:512]))
            P.step("vector", [("xqi", j, tch)],
                   [("xq", j, tch)] + (list(lgXQ) if (j, tch) == (0, 0) else []),
                   lambda e, j=j, tch=tch: e.tensor_copy(
                       xq[:, j * 2050 + 1 + tch * 512:j * 2050 + 1 + tch * 512 + 512],
                       i32buf[:, 0:512]))
    for j in range(NB):
        P.step("tensor", [("hq",), "idb"], [("PS", 3)],
               lambda e, j=j: e.transpose(
                   PS[3].bitcast(BF16)[0:128, 0:2], hq[0:2, j * 128:(j + 1) * 128],
                   idb[0:2, 0:2]))
        P.step("vector", [("PS", 3)], [("xqh", j)],
               lambda e, j=j: e.tensor_copy(
                   xq[:, j * 2050:j * 2050 + 2050:2049],
                   PS[3].bitcast(BF16)[:, 0:2]))
    AB16.freea(rHQ, [("hq",)])
    AF32.freea(rWS, [("wqt", w, p) for w in ("w1", "w2") for p in (0, 1)] +
               ["BUFwt", "BUFrowA", ("amaxrow",), ("srow",), ("mt", 6),
                ("mta", 0), ("mta", 1), ("mtt", 0), ("mtt", 1), ("amxT",)] +
               [("sbc", tc) for tc in range(4)])

    XQKEYS = ([("xq", j, tc) for j in range(NB) for tc in range(4)] +
              [("xqh", j) for j in range(NB)])
    rH, lgH = AF32.alloc(NBH * 258, ["hext"])
    hext = AF32.t[:, rH[0]:rH[0] + NBH * 258]
    rsm, lgsm = AF32.alloc(1312, ["smalls"])
    s1bc = AF32.t[:, rsm[0]:rsm[0] + 258]
    s2bc = AF32.t[:, rsm[0] + 258:rsm[0] + 514]
    s2obc = AF32.t[:, rsm[0] + 514:rsm[0] + 770]
    tmpev = AF32.t[:, rsm[0] + 770:rsm[0] + 1282]
    sgbuf = [i32buf.bitcast(F32)[:, 512:768], i32buf.bitcast(F32)[:, 768:1024]]
    rCV, lgCV = AB16.alloc(NBH * 256, ["cvo"])
    cvo = AB16.t[:, rCV[0]:rCV[0] + NBH * 256]
    xq2 = cvo
    rw1s, lgw1s = AB16.alloc(2 * NB * 128, ["w1s"])
    w1s = [AB16.t[:, rw1s[0] + i * NB * 128:rw1s[0] + (i + 1) * NB * 128]
           for i in range(2)]
    rw2s, lgw2s = AB16.alloc(2 * 16 * 128, ["w2s"])
    w2s = [AB16.t[:, rw2s[0] + i * 16 * 128:rw2s[0] + (i + 1) * 16 * 128]
           for i in range(2)]
    first = dict(h=True, c=True, q2=True, w1=True, w2=True, sm=True)
    CW0, CW1, CW2 = OF["CW"], OF["CW"] + NBH, OF["CW"] + 2 * NBH

    for t in range(8):
        off = t * 256
        P.step("tensor", [("s1core",), ("s1halo",), ("s1halo2",), "onesrow"],
               [("PS", 2)],
               lambda e, off=off: e.matmul(
                   PS[2][:, 0:258], onesrow[0:1, 0:128],
                   rowB[0:1, off:off + 258], start=True, stop=True))
        P.step("vector", [("PS", 2)],
               [("s1bc",)] + (list(lgsm) if first["sm"] else []),
               lambda e: e.tensor_copy(s1bc[:, 0:258], PS[2][:, 0:258]))
        first["sm"] = False
        for fb in range(NBH):
            bank = (0, 1)[fb % 2]
            P.step("gpsimd", [("wqdr", "w1"), ("w1c", fb % 2)],
                   [("w1l", fb % 2)] + (list(lgw1s) if first["w1"] else []),
                   lambda e, fb=fb: e.dma_start(
                       out=w1s[fb % 2].rearrange("p (j c) -> p j c", j=NB),
                       in_=Dm["w1q_dram"].rearrange("(j p) q -> p j q", p=128)
                       [:, :, fb * 128:(fb + 1) * 128]), dma=True)
            first["w1"] = False

            def mk1(t=t, fb=fb, bank=bank):
                def f(e):
                    for j in range(NB):
                        ins = e.matmul(
                            PS[bank][:, 0:258],
                            w1s[fb % 2][:, j * 128:(j + 1) * 128],
                            xq[:, j * 2050 + t * 256:j * 2050 + t * 256 + 258],
                            start=(j == 0), stop=(j == NB - 1))
                    return ins
                return f
            P.step("tensor", [("w1l", fb % 2)] + XQKEYS,
                   [("PS", bank), ("w1c", fb % 2)], mk1())
            P.step("vector", [("PS", bank), ("s1bc",)],
                   [("hx", fb)] + (list(lgH) if first["h"] else []),
                   lambda e, fb=fb, bank=bank: e.tensor_mul(
                       hext[:, fb * 258:fb * 258 + 258], PS[bank][:, 0:258],
                       s1bc[:, 0:258]))
            first["h"] = False
        if t == 7:
            dbg_dump(sh, [("hx", 0)], hext[:, 0:258], 3328, "hext0_t7")
            dbg_dump(sh, [("s1bc",)], s1bc[:, 0:258], 3600, "s1bc_t7")
            dbg_dump(sh, [("xq", 0, 3), ("xqh", 0)],
                     sh["AB16"].t[:, XQ0 + 1536:XQ0 + 2050], 3872, "xqtail")
        for fb in range(NBH):
            cs = slice(fb * 256, fb * 256 + 256)
            P.step("vector", [("hx", fb), "bb5", "bb4"],
                   [("cv", fb)] + (list(lgCV) if first["c"] else []),
                   lambda e, fb=fb, cs=cs: e.tensor_scalar(
                       cvo[:, cs], hext[:, fb * 258:fb * 258 + 256],
                       biasbuf[:, CW0 + fb:CW0 + fb + 1],
                       biasbuf[:, OF["CB"] + fb:OF["CB"] + fb + 1],
                       ALU.mult, ALU.add))
            first["c"] = False
            P.step("vector", [("cv", fb)], [("cv", fb)],
                   lambda e, fb=fb, cs=cs: e.scalar_tensor_tensor(
                       cvo[:, cs], hext[:, fb * 258 + 1:fb * 258 + 257],
                       biasbuf[:, CW1 + fb:CW1 + fb + 1], cvo[:, cs],
                       ALU.mult, ALU.add))
            P.step("vector", [("cv", fb)], [("cv", fb)],
                   lambda e, fb=fb, cs=cs: e.scalar_tensor_tensor(
                       cvo[:, cs], hext[:, fb * 258 + 2:fb * 258 + 258],
                       biasbuf[:, CW2 + fb:CW2 + fb + 1], cvo[:, cs],
                       ALU.mult, ALU.add))
            P.step("scalar", [("cv", fb), ("sgt", fb % 2)], [("sg", fb % 2), "BUFi32"],
                   lambda e, fb=fb, cs=cs: e.activation(
                       sgbuf[fb % 2], cvo[:, cs], AF.Sigmoid))
            P.step("vector", [("sg", fb % 2), ("cv", fb)],
                   [("cv", fb), ("sgt", fb % 2)],
                   lambda e, fb=fb, cs=cs: e.tensor_mul(
                       cvo[:, cs], cvo[:, cs], sgbuf[fb % 2]))
        mx = sh["sq2"][:, 0:256]
        mabs2 = [sh["sq2"][:, 256:512], sh["sq2"][:, 644:900]]
        P.step("scalar", [("cv", 0), "BUFsq2"], [("mx",), "BUFsq2"],
               lambda e: e.activation(mx, cvo[:, 0:256], AF.Abs))
        for fb in range(1, NBH):
            P.step("scalar", [("cv", fb), ("mabs", fb % 2)], [("mabs", fb % 2)],
                   lambda e, fb=fb: e.activation(
                       mabs2[fb % 2], cvo[:, fb * 256:fb * 256 + 256], AF.Abs))
            P.step("vector", [("mabs", fb % 2), ("mx",)], [("mx",), ("mabs", fb % 2)],
                   lambda e, fb=fb: e.tensor_max(mx, mx, mabs2[fb % 2]))
        for a in range(2):
            P.step("tensor", [("mx",), "idf"], [("PS", 3)],
                   lambda e, a=a: e.transpose(PS[3][:, 0:128],
                                              mx[:, a * 128:a * 128 + 128], idf[:]))
            P.step("vector", [("PS", 3)], [("am2t", a), "BUFsq2"],
                   lambda e, a=a: e.tensor_reduce(sh["sq2"][:, 512 + a:513 + a],
                                                  PS[3][:, 0:128], AX, ALU.max))
        P.step("tensor", [("am2t", 0), ("am2t", 1), "idf"], [("PS", 3)],
               lambda e: e.transpose(PS[3][0:2, 0:128], sh["sq2"][:, 512:514],
                                     idf[:]))
        P.step("vector", [("PS", 3)], [("am2T",), "BUFsq2"],
               lambda e: e.tensor_copy(sh["sq2"][0:2, 516:644], PS[3][0:2, 0:128]))
        P.step("sync", [("am2T",)], ["am2d"],
               lambda e: e.dma_start(out=Dm["row_bounce"][0:1, 2048:2304],
                                     in_=sh["sq2"][0:2, 516:644]), dma=True)
        P.step("sync", ["am2d"], [("am2row",)],
               lambda e: e.dma_start(out=rowB[0:1, 2080:2336],
                                     in_=Dm["row_bounce"][0:1, 2048:2304]),
               dma=True)
        P.step("vector", [("am2row",)], [("am2row",)],
               lambda e: e.tensor_scalar_max(rowB[0:1, 2080:2336],
                                             rowB[0:1, 2080:2336], 1e-5))
        P.step("vector", [("am2row",)], [("s2row",)],
               lambda e: e.reciprocal(rowB[0:1, 2336:2592], rowB[0:1, 2080:2336]))
        P.step("vector", [("s2row",)], [("s2row",)],
               lambda e: e.tensor_scalar_mul(rowB[0:1, 2336:2592],
                                             rowB[0:1, 2336:2592], 127.0))
        P.step("vector", [("am2row",), ("wsc", "w2")], [("s2orow",)],
               lambda e: e.tensor_scalar(rowB[0:1, 2592:2848], rowB[0:1, 2080:2336],
                                         rowB[0:1, 2065:2066], None, ALU.mult))
        P.step("tensor", [("s2row",), "onesrow"], [("PS", 2)],
               lambda e: e.matmul(PS[2][:, 0:256], onesrow[0:1, 0:128],
                                  rowB[0:1, 2336:2592], start=True, stop=True))
        P.step("vector", [("PS", 2)], [("s2bc",)],
               lambda e: e.tensor_copy(s2bc[:, 0:256], PS[2][:, 0:256]))
        P.step("tensor", [("s2orow",), "onesrow"], [("PS", 2)],
               lambda e: e.matmul(PS[2][:, 0:256], onesrow[0:1, 0:128],
                                  rowB[0:1, 2592:2848], start=True, stop=True))
        P.step("vector", [("PS", 2)], [("s2obc",)],
               lambda e: e.tensor_copy(s2obc[:, 0:256], PS[2][:, 0:256]))
        if t == 7:
            dbg_dump(sh, [("am2row",)], rowB[0:1, 2080:2336], 0, "am2row")
            dbg_dump(sh, [("s2orow",)], rowB[0:1, 2592:2848], 256, "s2orow")
            dbg_dump(sh, [("s2bc",)], s2bc[:, 0:256], 512, "s2bc")
        for fb in range(NBH):
            cs = slice(fb * 256, fb * 256 + 256)
            P.step("vector", [("cv", fb), ("s2bc",)], [("q2i", fb), "BUFi32"],
                   lambda e, fb=fb, cs=cs: e.tensor_tensor(
                       i32buf[:, 0:256], cvo[:, cs], s2bc[:, 0:256], ALU.mult))
            P.step("vector", [("q2i", fb)], [("cv", fb)],
                   lambda e, fb=fb: e.tensor_copy(
                       xq2[:, fb * 256:fb * 256 + 256], i32buf[:, 0:256]))
        if t == 7:
            dbg_dump(sh, [("cv", fb) for fb in range(NBH)],
                     cvo[:, 0:2048], 1024, "xq2_t0")
        for m in range(NB):
            bank = (4, 5)[m % 2]
            for hf in range(2):
                P.step("gpsimd", [("wqdr", "w2"), ("w2c", hf)],
                       [("w2l", hf)] + (list(lgw2s) if first["w2"] else []),
                       lambda e, m=m, hf=hf: e.dma_start(
                           out=w2s[hf].rearrange("p (k c) -> p k c", k=16),
                           in_=Dm["w2q_dram"].rearrange("(k p) q -> p k q", p=128)
                           [:, hf * 16:hf * 16 + 16, m * 128:(m + 1) * 128]),
                       dma=True)
                first["w2"] = False

                def mk2(m=m, bank=bank, hf=hf):
                    def f(e):
                        for kk in range(16):
                            ins = e.matmul(
                                PS[bank][:, 0:256],
                                w2s[hf][:, kk * 128:(kk + 1) * 128],
                                xq2[:, (hf * 16 + kk) * 256:(hf * 16 + kk) * 256 + 256],
                                start=(hf == 0 and kk == 0),
                                stop=(hf == 1 and kk == 15))
                        return ins
                    return f
                P.step("tensor", [("w2l", hf)] + [("cv", fb) for fb in range(NBH)],
                       [("PS", bank), ("w2c", hf)], mk2())
            P.step("vector", [("PS", bank), ("s2obc",)], [("tmpev", m % 2)],
                   lambda e, bank=bank, m=m: e.tensor_mul(
                       tmpev[:, (m % 2) * 256:(m % 2) * 256 + 256],
                       PS[bank][:, 0:256], s2obc[:, 0:256]))
            if t == 7 and m == 0:
                dbg_dump(sh, [("tmpev", m % 2)],
                         tmpev[:, 0:256], 3072, "ffnout_m0t0")
            P.step("vector", [("tmpev", m % 2), ("X", m)], [("X", m)],
                   lambda e, m=m, t=t: e.tensor_add(
                       X[:, m * TOK + t * 256:m * TOK + t * 256 + 256],
                       X[:, m * TOK + t * 256:m * TOK + t * 256 + 256],
                       tmpev[:, (m % 2) * 256:(m % 2) * 256 + 256]))

    AF32.freea(rH, [("hx", fb) for fb in range(NBH)])
    AF32.freea(rsm, [("s1bc",), ("s2bc",), ("s2obc",), ("tmpev", 0), ("tmpev", 1)])
    AF32.freea(rRB, [("s1core",), ("s1halo",), ("s1halo2",), ("am2row",),
                     ("s2row",), ("s2orow",), ("wsc", "w1"), ("wsc", "w2")])
    AB16.freea(rCV, [("cv", fb) for fb in range(NBH)] + [("mx",)])
    AB16.freea(rw1s, [("w1l", 0), ("w1l", 1), ("w1c", 0), ("w1c", 1)])
    AB16.freea(rw2s, [("w2l", 0), ("w2l", 1), ("w2c", 0), ("w2c", 1)])
    AB16.freea(rXQ, [("xq", j, tc) for j in range(NB) for tc in range(4)] +
               [("xqh", j) for j in range(NB)])

    featln(sh, "ln2")





def emit_moe(sh):
    P, nc, I, Dm, X, PS = sh["P"], sh["nc"], sh["I"], sh["Dm"], sh["X"], sh["PS"]
    AF32, AB16, sq2 = sh["AF32"], sh["AB16"], sh["sq2"]
    idb, idf, consts, onesrow = sh["idb"], sh["idf"], sh["consts"], sh["onesrow"]
    i32buf, biasbuf, OF = sh["i32buf"], sh["biasbuf"], sh["OF"]
    NIT = 48
    free_xb(sh)

    if not DBG.get("no_wcast"):
        rWB_, lgWB_ = AB16.alloc(4096, ["mwcb"])
        wcb = [AB16.t[:, rWB_[0]:rWB_[0] + 2048],
               AB16.t[:, rWB_[0] + 2048:rWB_[0] + 4096]]
        wjobs = []
        for t in range(8):
            for h in range(2):
                wjobs.append(("w1e_dram",
                              I["W1e"][t * 128:(t + 1) * 128,
                                       h * 2048:h * 2048 + 2048],
                              Dm["w1e_dram"][t * 128:(t + 1) * 128,
                                             h * 2048:h * 2048 + 2048], 2048))
        for t in range(32):
            wjobs.append(("w2e_dram",
                          I["W2e"][t * 128:(t + 1) * 128, :],
                          Dm["w2e_dram"][t * 128:(t + 1) * 128, :], 1024))
        for i, (dk, sap, dap, w) in enumerate(wjobs):
            sl = i % 2
            wk = [("mwcb", sl)] + (list(lgWB_) if i == 0 else [])
            P.step("gpsimd", [], wk,
                   lambda e, sap=sap, sl=sl, w=w: e.dma_start(
                       out=wcb[sl][:, 0:w], in_=sap), dma=True)
            P.step("sync", [("mwcb", sl)], [dk],
                   lambda e, dap=dap, sl=sl, w=w: e.dma_start(
                       out=dap, in_=wcb[sl][:, 0:w]), dma=True)
        AB16.freea(rWB_, [("mwcb", 0), ("mwcb", 1)])

    rSM, lgSM = AF32.alloc(1024, ["msm"])
    s0 = rSM[0]
    SM = AF32.t
    col = lambda c: SM[:, s0 + c:s0 + c + 1]
    c64 = lambda c: SM[0:64, s0 + c:s0 + c + 1]
    rBG, lgBG = AF32.alloc(8192, ["mbg"])
    b0 = rBG[0]
    R = [AF32.t[:, b0 + i * 2048:b0 + (i + 1) * 2048] for i in range(4)]
    rOB, lgOB = AF32.alloc(2048, ["mob"])
    obuf = [AF32.t[:, rOB[0]:rOB[0] + 1024], AF32.t[:, rOB[0] + 1024:rOB[0] + 2048]]
    rBB, lgBB = AF32.alloc(1024, ["mb2"])
    b2bc = AF32.t[:, rBB[0]:rBB[0] + 1024]

    P.step("vector", [], [("mzb",)] + list(lgOB),
           lambda e: e.memset(obuf[0][:, :], 0.0))
    P.step("scalar", [("mzb",)], ["moe_pad"],
           lambda e: e.dma_start(out=Dm["moe_ag"][NTOKG:NTOKG + 128, :],
                                 in_=obuf[0][:, :]), dma=True)

    for nm, c, src, npart in (("megr", 256, "egrp", 128), ("me8", 264, "e8to128", 8),
                              ("mltm", 392, "ltmask", 128), ("mws", 576, "Ws_in", None),
                              ("mtok", 96, "tokid", 128), ("mbs", 12, "bs_in", 128),
                              ("mebs", 112, "ebase", 128)):
        if nm == "mws":
            P.step("sync", [], [nm] + (list(lgSM) if nm == "megr" else []),
                   lambda e: e.dma_start(
                       out=SM[:, s0 + 576:s0 + 640].rearrange(
                           "p (j ee) -> p j ee", j=NB),
                       in_=I["Ws_in"].rearrange("(j p) ee -> p j ee", p=128)),
                   dma=True)
        else:
            w = src
            nc_ = {"egrp": 8, "e8to128": 128, "ltmask": 128, "tokid": 16,
                   "bs_in": 1, "ebase": 1}[src]
            P.step("sync", [], [nm] + (list(lgSM) if nm == "megr" else []),
                   lambda e, c=c, w=w, npart=npart, nc_=nc_: e.dma_start(
                       out=SM[0:npart, s0 + c:s0 + c + nc_], in_=I[w]), dma=True)

    if not DBG.get("no_actexp"):
        rXT, lgXT = AB16.alloc(2048, ["mxtb"])
        xtb = [AB16.t[:, rXT[0]:rXT[0] + 1024], AB16.t[:, rXT[0] + 1024:rXT[0] + 2048]]
        for a in range(16):
            sl = a % 2
            for hf in range(2):
                pb = 4 + hf

                def mkt(a=a, hf=hf, pb=pb):
                    def f(e):
                        for j in range(4 * hf, 4 * hf + 4):
                            ins = e.transpose(
                                PS[pb][:, (j - 4 * hf) * 128:(j - 4 * hf) * 128 + 128],
                                X[:, j * TOK + a * 128:j * TOK + a * 128 + 128],
                                idf[:])
                        return ins
                    return f
                P.step("tensor", [("X", j) for j in range(4 * hf, 4 * hf + 4)] +
                       ["idf"], [("PS", pb)], mkt())
                wk = [("mxtb", sl)] + (list(lgXT) if a == 0 and hf == 0 else [])
                P.step("vector", [("PS", pb)], wk,
                       lambda e, sl=sl, hf=hf, pb=pb: e.tensor_copy(
                           xtb[sl][:, hf * 512:hf * 512 + 512], PS[pb][:, 0:512]))
            P.step("sync", [("mxtb", sl)], [("mab", a)],
                   lambda e, a=a, sl=sl: e.dma_start(
                       out=Dm["act_bounce"][a * 128:(a + 1) * 128, :],
                       in_=xtb[sl][:, :]), dma=True)
        P.step("gpsimd", [("mab", a) for a in range(16)], ["act_ag"],
               lambda e: e.collective_compute(
                   "AllGather", ALU.bypass,
                   replica_groups=[[0, 1, 2, 3, 4, 5, 6, 7]],
                   ins=[Dm["act_bounce"][:, :]], outs=[Dm["act_ag"][:, :]]),
               own_sem="cc_act")
    if not DBG.get("no_actexp"):
        AB16.freea(rXT, [("mxtb", 0), ("mxtb", 1)])

    probsT = R[1]
    drow = R[2][0:1, :]
    rrow = R[3][0:1, :]
    for tch in range(4):
        def mkl(tch=tch):
            def f(e):
                for j in range(NB):
                    ins = e.matmul(
                        PS[0][0:8, 0:512],
                        SM[:, s0 + 576 + j * 8:s0 + 576 + j * 8 + 8],
                        X[:, j * TOK + tch * 512:j * TOK + tch * 512 + 512],
                        start=(j == 0), stop=(j == NB - 1))
                return ins
            return f
        P.step("tensor", ["mws"] + [("X", j) for j in range(NB)],
               [("PS", 0)], mkl())
        wk = [("mpt0", tch)] + (list(lgBG) if tch == 0 else [])
        P.step("vector", [("PS", 0)], wk,
               lambda e, tch=tch: e.tensor_copy(
                   probsT[0:8, tch * 512:tch * 512 + 512], PS[0][0:8, 0:512]))
        P.step("scalar", [("mpt0", tch), "mbs"], [("mpt", tch)],
               lambda e, tch=tch: e.activation(
                   probsT[0:8, tch * 512:tch * 512 + 512],
                   probsT[0:8, tch * 512:tch * 512 + 512],
                   AF.Exp, bias=SM[0:8, s0 + 12:s0 + 13]))
        P.step("tensor", [("mpt", tch), "consts"], [("PS", 1)],
               lambda e, tch=tch: e.matmul(
                   PS[1][0:1, 0:512], consts[0:8, 0:1],
                   probsT[0:8, tch * 512:tch * 512 + 512],
                   start=True, stop=True))
        P.step("vector", [("PS", 1)], [("mdr", tch)],
               lambda e, tch=tch: e.tensor_copy(
                   drow[:, tch * 512:tch * 512 + 512], PS[1][0:1, 0:512]))
    P.step("vector", [("mdr", t) for t in range(4)], [("mrr",)],
           lambda e: e.reciprocal(rrow[:, :], drow[:, :]))
    for tch in range(4):
        P.step("tensor", [("mrr",), "onesrow"], [("PS", 1)],
               lambda e, tch=tch: e.matmul(
                   PS[1][0:8, 0:512], onesrow[0:1, 0:8],
                   rrow[:, tch * 512:tch * 512 + 512], start=True, stop=True))
        P.step("vector", [("PS", 1), ("mpt", tch)], [("mpn", tch)],
               lambda e, tch=tch: e.tensor_mul(
                   probsT[0:8, tch * 512:tch * 512 + 512],
                   probsT[0:8, tch * 512:tch * 512 + 512], PS[1][0:8, 0:512]))
    P.step("sync", [("mpn", t) for t in range(4)], ["probs_bounce"],
           lambda e: e.dma_start(out=Dm["probs_bounce"][:, :],
                                 in_=probsT[0:8, :]), dma=True)
    P.step("gpsimd", ["probs_bounce"], ["probs_ag"],
           lambda e: e.collective_compute(
               "AllGather", ALU.bypass,
               replica_groups=[[0, 1, 2, 3, 4, 5, 6, 7]],
               ins=[Dm["probs_bounce"][:, :]], outs=[Dm["probs_ag"][:, :]]),
           own_sem="cc_probs")
    PL = R[0]
    P.step("sync", [], ["mpli", "BUFi32"],
           lambda e: e.dma_start(out=i32buf.bitcast(U32)[:, 160:161],
                                 in_=I["plidx"]), dma=True)
    P.step("sync", [], ["mmyr", "BUFi32"],
           lambda e: e.dma_start(out=i32buf.bitcast(U32)[:, 161:162],
                                 in_=I["myrows"]), dma=True)
    P.step("gpsimd", ["probs_ag", "mpli", "BUFi32"], [("mPL",)],
           lambda e: e.indirect_dma_start(
               out=PL[0:64, :], out_offset=None,
               in_=Dm["probs_ag"][:, :],
               in_offset=IndirectOffsetOnAxis(
                   i32buf.bitcast(U32)[0:64, 160:161], 0)), dma=True)

    dbg_dump(sh, [("mPL",), ("mR0",)], PL[0:64, :], 0, "mPLd")
    if DBG.get("moe_stage") == 1:
        AF32.freea(rBG, [("mPL",), ("mR0",)])
        emit_store_out(sh)
        return

    SS = R[1]
    P.step("vector", [], [("mlo",)] + list(lgSM),
           lambda e: e.memset(c64(0), 0.0))
    P.step("vector", [], [("mhi",)] + list(lgSM),
           lambda e: e.memset(c64(1), 1.0))
    for it in range(NIT):
        P.step("vector", [("mlo",), ("mhi",)], [("mmid",)],
               lambda e: e.tensor_add(c64(2), c64(0), c64(1)))
        P.step("vector", [("mmid",)], [("mmid",)],
               lambda e: e.tensor_scalar_mul(c64(2), c64(2), 0.5))
        P.step("vector", [("mmid",)], [("mnmid",)],
               lambda e: e.tensor_scalar_mul(c64(3), c64(2), -1.0))
        P.step("scalar", [("mPL",), ("mnmid",)], [("mS1",), ("mc1",)],
               lambda e: e.activation(SS[0:64, :], PL[0:64, :], AF.Sign,
                                      bias=c64(3), accum_out=c64(4)))
        P.step("scalar", [("mS1",)], [("mS2",), ("mc2",)],
               lambda e: e.activation(SS[0:64, :], SS[0:64, :], AF.Abs,
                                      accum_out=c64(5)))
        P.step("vector", [("mc1",), ("mc2",)], [("mcnt",)],
               lambda e: e.tensor_add(c64(6), c64(4), c64(5)))
        P.step("vector", [("mcnt",)], [("mcnt",)],
               lambda e: e.tensor_scalar_mul(c64(6), c64(6), 0.5))
        P.step("tensor", [("mcnt",), "megr"], [("PS", 2)],
               lambda e: e.matmul(PS[2][0:8, 0:1],
                                  SM[0:64, s0 + 256:s0 + 264], c64(6),
                                  start=True, stop=True))
        P.step("vector", [("PS", 2)], [("mg8",)],
               lambda e: e.tensor_copy(SM[0:8, s0 + 9:s0 + 10],
                                       PS[2][0:8, 0:1]))
        P.step("tensor", [("mg8",), "me8"], [("PS", 3)],
               lambda e: e.matmul(PS[3][0:64, 0:1],
                                  SM[0:8, s0 + 264:s0 + 328],
                                  SM[0:8, s0 + 9:s0 + 10],
                                  start=True, stop=True))
        P.step("vector", [("PS", 3)], [("mg64",)],
               lambda e: e.tensor_copy(c64(10), PS[3][0:64, 0:1]))
        P.step("vector", [("mg64",)], [("mfl",)],
               lambda e: e.tensor_scalar(c64(7), c64(10), 2047.5, None,
                                         ALU.is_ge))
        P.step("vector", [("mmid",), ("mlo",)], [("mt1",)],
               lambda e: e.tensor_sub(c64(8), c64(2), c64(0)))
        P.step("vector", [("mt1",), ("mfl",)], [("mt1b",)],
               lambda e: e.tensor_mul(c64(8), c64(8), c64(7)))
        P.step("vector", [("mt1b",), ("mlo",)], [("mlo",)],
               lambda e: e.tensor_add(c64(0), c64(0), c64(8)))
        P.step("vector", [("mhi",), ("mmid",)], [("mt2",)],
               lambda e: e.tensor_sub(c64(9), c64(1), c64(2)))
        P.step("vector", [("mt2",), ("mfl",)], [("mt2b",)],
               lambda e: e.tensor_mul(c64(9), c64(9), c64(7)))
        P.step("vector", [("mt2b",), ("mmid",)], [("mhi",)],
               lambda e: e.tensor_add(c64(1), c64(2), c64(9)))

    if DBG.get("moe_stage") == 2:
        dbg_dump(sh, [("mhi",), ("mlo",)], SM[:, s0:s0 + 16], 6144, "msmS2")
        AF32.freea(rBG, [("mPL",), ("mR0",)])
        emit_store_out(sh)
        return

    mgt, meq, cumq = R[1], R[2], R[3]
    P.step("vector", [("mPL",), ("mhi",), ("mS2",)], [("mgt",)],
           lambda e: e.tensor_scalar(mgt[0:64, :], PL[0:64, :], c64(1), None,
                                     ALU.is_gt))
    P.step("vector", [("mPL",), ("mhi",), ("mrr",)], [("mge",)],
           lambda e: e.tensor_scalar(meq[0:64, :], PL[0:64, :], c64(1), None,
                                     ALU.is_ge))
    P.step("vector", [("mge",), ("mgt",)], [("meq",)],
           lambda e: e.tensor_sub(meq[0:64, :], meq[0:64, :], mgt[0:64, :]))
    P.step("vector", [("mgt",)], [("mgr",)],
           lambda e: e.tensor_reduce(c64(4), mgt[0:64, :], AX, ALU.add))
    P.step("tensor", [("mgr",), "megr"], [("PS", 2)],
           lambda e: e.matmul(PS[2][0:8, 0:1], SM[0:64, s0 + 256:s0 + 264],
                              c64(4), start=True, stop=True))
    P.step("vector", [("PS", 2)], [("mg8b",)],
           lambda e: e.tensor_copy(SM[0:8, s0 + 9:s0 + 10], PS[2][0:8, 0:1]))
    P.step("tensor", [("mg8b",), "me8"], [("PS", 3)],
           lambda e: e.matmul(PS[3][0:64, 0:1], SM[0:8, s0 + 264:s0 + 328],
                              SM[0:8, s0 + 9:s0 + 10], start=True, stop=True))
    P.step("vector", [("PS", 3)], [("mG64",)],
           lambda e: e.tensor_copy(c64(10), PS[3][0:64, 0:1]))
    P.step("vector", [("mG64",)], [("mCm1",)],
           lambda e: e.tensor_scalar_mul(c64(11), c64(10), -1.0))
    P.step("vector", [("mCm1",)], [("mCm1b",)],
           lambda e: e.tensor_scalar(c64(11), c64(11), 2047.0, None, ALU.add))
    P.step("vector", [("meq",)], [("mcq",)],
           lambda e: e.tensor_tensor_scan(cumq[0:64, :], meq[0:64, :],
                                          meq[0:64, :], 0.0, ALU.add,
                                          ALU.bypass))
    P.step("vector", [("mcq",)], [("mrt",)],
           lambda e: e.tensor_copy(c64(13), cumq[0:64, 2047:2048]))
    P.step("tensor", [("mrt",), "mltm"], [("PS", 2)],
           lambda e: e.matmul(PS[2][0:64, 0:1], SM[0:64, s0 + 392:s0 + 456],
                              c64(13), start=True, stop=True))
    P.step("vector", [("PS", 2)], [("moff",)],
           lambda e: e.tensor_copy(c64(14), PS[2][0:64, 0:1]))
    P.step("vector", [("mcq",), ("meq",)], [("mq1",)],
           lambda e: e.tensor_sub(cumq[0:64, :], cumq[0:64, :], meq[0:64, :]))
    P.step("vector", [("mq1",), ("moff",)], [("mq2",)],
           lambda e: e.tensor_scalar(cumq[0:64, :], cumq[0:64, :], c64(14),
                                     None, ALU.add))
    P.step("vector", [("mq2",), ("mCm1b",)], [("mcond",)],
           lambda e: e.tensor_scalar(cumq[0:64, :], cumq[0:64, :], c64(11),
                                     None, ALU.is_le))
    P.step("vector", [("mcond",), ("meq",)], [("mseq",)],
           lambda e: e.tensor_mul(meq[0:64, :], meq[0:64, :], cumq[0:64, :]))
    P.step("vector", [("mseq",), ("mgt",)], [("msel",)],
           lambda e: e.tensor_add(meq[0:64, :], meq[0:64, :], mgt[0:64, :]))
    sel = meq
    dbg_dump(sh, [("msel",)], sel[0:64, :], 2048, "mseld")
    P.step("vector", [("msel",), ("mPL",), ("mcond",)], [("mval",)],
           lambda e: e.tensor_mul(cumq[0:64, :], PL[0:64, :], sel[0:64, :]))
    val = cumq
    P.step("vector", [("msel",), ("mval",)], [("mscn",), ("mR0",)],
           lambda e: e.tensor_tensor_scan(PL[0:64, :], sel[0:64, :],
                                          sel[0:64, :], 0.0, ALU.add,
                                          ALU.bypass))
    scn = PL
    P.step("vector", [("mscn",)], [("mrt2",)],
           lambda e: e.tensor_copy(c64(13), scn[0:64, 2047:2048]))
    P.step("tensor", [("mrt2",), "mltm"], [("PS", 2)],
           lambda e: e.matmul(PS[2][0:64, 0:1], SM[0:64, s0 + 392:s0 + 456],
                              c64(13), start=True, stop=True))
    P.step("vector", [("PS", 2)], [("moff2",)],
           lambda e: e.tensor_copy(c64(14), PS[2][0:64, 0:1]))
    P.step("vector", [("mscn",), ("msel",)], [("mex1",), ("mR0",)],
           lambda e: e.tensor_sub(scn[0:64, :], scn[0:64, :], sel[0:64, :]))
    P.step("vector", [("mex1",), ("moff2",)], [("mex2",), ("mR0",)],
           lambda e: e.tensor_scalar(scn[0:64, :], scn[0:64, :], c64(14),
                                     None, ALU.add))
    dbg_dump(sh, [("mex2",), ("mR0",)], scn[0:64, :], 4096, "mexd")
    P.step("vector", [("mex2",), ("msel",), ("mgr",)], [("mgin",)],
           lambda e: e.tensor_add(mgt[0:64, :], scn[0:64, :], sel[0:64, :]))
    gin = mgt
    if DBG.get("moe_stage") == 3:
        dbg_dump(sh, [("mgin",)], gin[0:64, :], 6144, "mginS3")
        AF32.freea(rBG, [("mPL",), ("mR0",)])
        emit_store_out(sh)
        return

    P.step("sync", [("mgin",)], ["cum_dram"],
           lambda e: e.dma_start(
               out=Dm["cum_dram"][0:128, :].rearrange("(r q) c -> r (q c)",
                                                      q=16),
               in_=gin[0:8, :]), dma=True)
    P.step("vector", [("mgin",)], [("mcb",)],
           lambda e: e.tensor_copy(SM[0:8, s0 + 16:s0 + 32],
                                   gin[0:8, 127:2048:128]))
    P.step("sync", [("mcb",)], ["mcbd"],
           lambda e: e.dma_start(
               out=Dm["moe_row"][0:1, 0:128].rearrange("a (r q) -> (a r) q",
                                                       r=8),
               in_=SM[0:8, s0 + 16:s0 + 32]), dma=True)
    P.step("sync", ["mcbd"], [("mcbc",)],
           lambda e: e.dma_start(
               out=SM[:, s0 + 15:s0 + 16],
               in_=Dm["moe_row"][0:1, 0:128].rearrange("a (p q) -> (a p) q",
                                                       p=128)), dma=True)
    P.step("vector", [("mex2",), "mebs"], [("mrx1",), ("mR0",)],
           lambda e: e.tensor_scalar(scn[0:64, :], scn[0:64, :],
                                     SM[0:64, s0 + 112:s0 + 113], None,
                                     ALU.add))
    P.step("vector", [("mrx1",)], [("mrx2",), ("mR0",)],
           lambda e: e.tensor_scalar(scn[0:64, :], scn[0:64, :], -16384.0,
                                     None, ALU.add))
    P.step("vector", [("mrx2",), ("msel",)], [("mrx3",), ("mR0",)],
           lambda e: e.tensor_mul(scn[0:64, :], scn[0:64, :], sel[0:64, :]))
    P.step("vector", [("mrx3",)], [("mrx4",), ("mR0",)],
           lambda e: e.tensor_scalar(scn[0:64, :], scn[0:64, :], 16384.0,
                                     None, ALU.add))
    P.step("sync", [("mrx4",)], ["mrxd"],
           lambda e: e.dma_start(out=Dm["ridx_dram"][0:64, :],
                                 in_=scn[0:64, :]), dma=True)
    P.step("gpsimd", ["mrxd", "mmyr", "BUFi32"], [("mrst",), ("mgin",)],
           lambda e: e.indirect_dma_start(
               out=gin[0:8, :], out_offset=None,
               in_=Dm["ridx_dram"][0:64, :],
               in_offset=IndirectOffsetOnAxis(
                   i32buf.bitcast(U32)[0:8, 161:162], 0)), dma=True)
    def mkrt():
        def f_(e):
            for a in range(16):
                ins = e.transpose(PS[2][:, a * 8:a * 8 + 8],
                                  gin[0:8, a * 128:(a + 1) * 128],
                                  idf[0:8, 0:8])
            return ins
        return f_
    P.step("tensor", [("mrst",), "idf"], [("PS", 2)], mkrt())
    P.step("vector", [("PS", 2)], [("mrxm",)],
           lambda e: e.tensor_copy(SM[:, s0 + 128:s0 + 256],
                                   PS[2][:, 0:128]))
    P.step("vector", [("mrxm",)], [("mridxi",), "BUFi32"],
           lambda e: e.tensor_copy(i32buf[:, 0:128],
                                   SM[:, s0 + 128:s0 + 256]))
    P.step("sync", [("mval",)], ["evals"],
           lambda e: e.dma_start(
               out=Dm["evals_dram"].rearrange("(r t) c -> r (t c)", r=8),
               in_=val[0:8, :]), dma=True)
    P.step("sync", [("mrx3",)], [("iotar",), ("msel",)],
           lambda e: e.dma_start(out=R[2][0:1, 0:2048], in_=I["iota_row"]),
           dma=True)
    for tch in range(4):
        P.step("tensor", [("iotar",), "onesrow"], [("PS", 1)],
               lambda e, tch=tch: e.matmul(
                   PS[1][:, 0:512], onesrow[0:1, 0:128],
                   R[2][0:1, tch * 512:tch * 512 + 512],
                   start=True, stop=True))
        P.step("vector", [("PS", 1), ("mrxm",)],
               [("mpm", tch), ("mrst",), ("mgin",)],
               lambda e, tch=tch: e.tensor_copy(
                   R[1][:, tch * 512:tch * 512 + 512], PS[1][:, 0:512]))
    P.step("vector", [("mpm", t) for t in range(4)] + [("mcbc",)], [("mcmp",)],
           lambda e: e.tensor_scalar(R[1][:, :], R[1][:, :],
                                     SM[:, s0 + 15:s0 + 16], None, ALU.is_ge))
    for tch in range(4):
        P.step("tensor", [("mcmp",), "consts"], [("PS", 0)],
               lambda e, tch=tch: e.matmul(
                   PS[0][0:1, 0:512], consts[:, 0:1],
                   R[1][:, tch * 512:tch * 512 + 512], start=True, stop=True))
        P.step("vector", [("PS", 0), ("mrxd",)], [("mnf", tch), ("mrx4",)],
               lambda e, tch=tch: e.tensor_copy(
                   R[0][0:1, tch * 512:tch * 512 + 512], PS[0][0:1, 0:512]))
    def mknt():
        def f_(e):
            for f in range(16):
                ins = e.transpose(PS[3][:, f:f + 1],
                                  R[0][0:1, f * 128:(f + 1) * 128],
                                  idf[0:1, 0:1])
            return ins
        return f_
    P.step("tensor", [("mnf", t) for t in range(4)] + ["idf"], [("PS", 3)],
           mknt())
    P.step("vector", [("PS", 3)], [("mnfm",)],
           lambda e: e.tensor_copy(SM[:, s0 + 32:s0 + 48], PS[3][:, 0:16]))
    P.step("vector", [("mnfm",)], [("mnfi",), "BUFi32"],
           lambda e: e.tensor_copy(i32buf[:, 176:192], SM[:, s0 + 32:s0 + 48]))
    gb = [SM[:, s0 + 128:s0 + 256], SM[:, s0 + 640:s0 + 768]]
    for f in range(16):
        sl = f % 2
        P.step("gpsimd", [("mnfi",), "cum_dram", "BUFi32", ("mridxi",)],
               [("mgb", sl)],
               lambda e, f=f, sl=sl: e.indirect_dma_start(
                   out=gb[sl][:, :], out_offset=None,
                   in_=Dm["cum_dram"][0:128, :],
                   in_offset=IndirectOffsetOnAxis(
                       i32buf.bitcast(U32)[:, 176 + f:177 + f], 0)), dma=True)
        P.step("vector", [("mgb", sl), "mtok"], [("mgb", sl)],
               lambda e, f=f, sl=sl: e.tensor_scalar(
                   gb[sl][:, :], gb[sl][:, :], SM[:, s0 + 96 + f:s0 + 97 + f],
                   None, ALU.is_le))
        P.step("vector", [("mgb", sl)], [("mpc", f)],
               lambda e, f=f, sl=sl: e.tensor_reduce(
                   SM[:, s0 + 48 + f:s0 + 49 + f], gb[sl][:, :], AX, ALU.add))
    P.step("vector", [("mpc", f) for f in range(16)] + [("mnfm",)], [("midx",)],
           lambda e: e.scalar_tensor_tensor(
               SM[:, s0 + 64:s0 + 80], SM[:, s0 + 32:s0 + 48], 128.0,
               SM[:, s0 + 48:s0 + 64], ALU.mult, ALU.add))
    P.step("vector", [("midx",)], [("midxi",), "BUFi32"],
           lambda e: e.tensor_copy(i32buf[:, 208:224], SM[:, s0 + 64:s0 + 80]))
    for f in range(16):
        P.step("gpsimd", [("midxi",), "evals", "BUFi32"], [("mvc", f)],
               lambda e, f=f: e.indirect_dma_start(
                   out=SM[:, s0 + 80 + f:s0 + 81 + f], out_offset=None,
                   in_=Dm["evals_dram"][:, :],
                   in_offset=IndirectOffsetOnAxis(
                       i32buf.bitcast(U32)[:, 208 + f:209 + f], 0)), dma=True)

    dbg_dump(sh, [("mhi",), ("mG64",), ("mCm1b",)], SM[:, s0:s0 + 16],
             6144, "msm")
    dbg_dump(sh, [("mnfm",), ("mpc", 15), ("midx",), ("mvc", 15)],
             SM[:, s0 + 32:s0 + 112], 6160, "msm2")
    dbg_dump(sh, [("mridxi",), ("midxi",), ("mnfi",)],
             i32buf.bitcast(F32)[:, 0:224], 6240, "mi32")

    if DBG.get("moe_stage") == 4:
        AF32.freea(rBG, [("mPL",), ("mR0",)])
        emit_store_out(sh)
        return

    rXG, lgXG = AB16.alloc(2048, ["mxg"])
    xg = [AB16.t[:, rXG[0]:rXG[0] + 1024], AB16.t[:, rXG[0] + 1024:rXG[0] + 2048]]
    rXS, lgXS = AB16.alloc(2048, ["mxs"])
    xts = [AB16.t[:, rXS[0]:rXS[0] + 1024], AB16.t[:, rXS[0] + 1024:rXS[0] + 2048]]
    for f in range(16):
        sl = f % 2
        wk = [("mxg", sl)] + (list(lgXG) if f == 0 else [])
        P.step("gpsimd", [("midxi",), "act_ag", "BUFi32"], wk,
               lambda e, f=f, sl=sl: e.indirect_dma_start(
                   out=xg[sl][:, :], out_offset=None,
                   in_=Dm["act_ag"][:, :],
                   in_offset=IndirectOffsetOnAxis(
                       i32buf.bitcast(U32)[:, 208 + f:209 + f], 0)), dma=True)
        for hf in range(2):
            pb = 4 + hf

            def mkx(sl=sl, hf=hf, pb=pb):
                def f_(e):
                    pv = PS[pb].bitcast(BF16)
                    for j in range(4 * hf, 4 * hf + 4):
                        ins = e.transpose(
                            pv[:, (j - 4 * hf) * 128:(j - 4 * hf) * 128 + 128],
                            xg[sl][:, j * 128:(j + 1) * 128], idb[:])
                    return ins
                return f_
            P.step("tensor", [("mxg", sl), "idb"], [("PS", pb)], mkx())
            wk = [("mxs", sl)] + (list(lgXS) if f == 0 and hf == 0 else [])
            P.step("vector", [("PS", pb)], wk,
                   lambda e, sl=sl, hf=hf, pb=pb: e.tensor_copy(
                       xts[sl][:, hf * 512:hf * 512 + 512],
                       PS[pb].bitcast(BF16)[:, 0:512]))
        P.step("sync", [("mxs", sl)], [("mxid", f)],
               lambda e, f=f, sl=sl: e.dma_start(
                   out=Dm["xint_dram"].rearrange("(j p) t -> p j t", p=128)
                   [:, :, f * 128:(f + 1) * 128],
                   in_=xts[sl].rearrange("p (j t) -> p j t", j=NB)), dma=True)
    AB16.freea(rXG, [("mxg", 0), ("mxg", 1)])
    AB16.freea(rXS, [("mxs", 0), ("mxs", 1)])
    AF32.freea(rBG, [("mPL",), ("mS1",), ("mS2",), ("mgt",), ("mge",),
                     ("meq",), ("mseq",), ("msel",), ("mcq",), ("mq1",),
                     ("mq2",), ("mcond",), ("mval",), ("mscn",), ("mex1",),
                     ("mex2",), ("mgin",), ("mrst",), ("mrx1",), ("mrx2",),
                     ("mrx3",), ("mrx4",)] +
               [("mpt", t) for t in range(4)] + [("mpn", t) for t in range(4)] +
               [("mdr", t) for t in range(4)] + [("mrr",)] +
               [("mpm", t) for t in range(4)] + [("mcmp",)] +
               [("mnf", t) for t in range(4)])

    P.step("sync", [], [("mb2r",)] + list(lgBB),
           lambda e: e.dma_start(out=b2bc[0:1, 0:1024], in_=I["b2row"]),
           dma=True)
    for hf in range(2):
        P.step("tensor", [("mb2r",), "onesrow"], [("PS", 1)],
               lambda e, hf=hf: e.matmul(
                   PS[1][:, 0:512], onesrow[0:1, 0:128],
                   b2bc[0:1, hf * 512:hf * 512 + 512], start=True, stop=True))
        P.step("vector", [("PS", 1)], [("mb2", hf)],
               lambda e, hf=hf: e.tensor_copy(
                   b2bc[:, hf * 512:hf * 512 + 512], PS[1][:, 0:512]))
    rHT, lgHT = AB16.alloc(NBH * 512, ["mht"])
    hT = AB16.t[:, rHT[0]:rHT[0] + NBH * 512]
    rXL, lgXL = AB16.alloc(8192, ["mxl"])
    xtl = [AB16.t[:, rXL[0]:rXL[0] + 4096], AB16.t[:, rXL[0] + 4096:rXL[0] + 8192]]
    rW1, lgW1 = AB16.alloc(2048, ["mw1b"])
    w1b = [AB16.t[:, rW1[0]:rW1[0] + 1024], AB16.t[:, rW1[0] + 1024:rW1[0] + 2048]]
    rW2, lgW2 = AB16.alloc(2048, ["mw2b"])
    w2b = [AB16.t[:, rW2[0]:rW2[0] + 1024], AB16.t[:, rW2[0] + 1024:rW2[0] + 2048]]
    first = dict(xl=True, w1=True, w2=True, ht=True, ob=True)
    for tl in range(4):
        wk = [("mxtl", tl % 2)] + (list(lgXL) if first["xl"] else [])
        first["xl"] = False
        P.step("sync", [("mxid", f) for f in range(16)], wk,
               lambda e, tl=tl: e.dma_start(
                   out=xtl[tl % 2].rearrange("p (j t) -> p j t", j=NB),
                   in_=Dm["xint_dram"].rearrange("(j p) t -> p j t", p=128)
                   [:, :, tl * 512:(tl + 1) * 512]), dma=True)
        for fb in range(NBH):
            wk = [("mw1", fb % 2)] + (list(lgW1) if first["w1"] else [])
            first["w1"] = False
            P.step("sync", ["w1e_dram"], wk,
                   lambda e, fb=fb: e.dma_start(
                       out=w1b[fb % 2].rearrange("p (j c) -> p j c", j=NB),
                       in_=Dm["w1e_dram"].rearrange("(j p) c -> p j c", p=128)
                       [:, :, fb * 128:(fb + 1) * 128]), dma=True)

            def mk1(tl=tl, fb=fb):
                def f_(e):
                    for j in range(NB):
                        ins = e.matmul(
                            PS[fb % 2][:, 0:512],
                            w1b[fb % 2][:, j * 128:(j + 1) * 128],
                            xtl[tl % 2][:, j * 512:(j + 1) * 512],
                            start=(j == 0), stop=(j == NB - 1))
                    return ins
                return f_
            P.step("tensor", [("mw1", fb % 2), ("mxtl", tl % 2)],
                   [("PS", fb % 2)], mk1())
            wk = [("mhT", fb)] + (list(lgHT) if first["ht"] else [])
            first["ht"] = False
            P.step("scalar", [("PS", fb % 2), "bb6"], wk,
                   lambda e, fb=fb: e.activation(
                       hT[:, fb * 512:fb * 512 + 512], PS[fb % 2][:, 0:512],
                       AF.Gelu_apprx_tanh,
                       bias=biasbuf[:, OF["B1"] + fb:OF["B1"] + fb + 1]))
        for fb in range(NBH):
            wk = [("mw2", fb % 2)] + (list(lgW2) if first["w2"] else [])
            first["w2"] = False
            P.step("scalar", ["w2e_dram"], wk,
                   lambda e, fb=fb: e.dma_start(
                       out=w2b[fb % 2][:, :],
                       in_=Dm["w2e_dram"][fb * 128:(fb + 1) * 128, :]),
                   dma=True)

            def mk2(fb=fb):
                def f_(e):
                    for tc4 in range(4):
                        for dvh in range(2):
                            ins = e.matmul(
                                PS[tc4 * 2 + dvh][:, 0:512],
                                hT[:, fb * 512 + tc4 * 128:
                                   fb * 512 + tc4 * 128 + 128],
                                w2b[fb % 2][:, dvh * 512:dvh * 512 + 512],
                                start=(fb == 0), stop=(fb == NBH - 1))
                    return ins
                return f_
            P.step("tensor", [("mw2", fb % 2), ("mhT", fb)],
                   [("PS", b) for b in range(8)], mk2())
        for tc4 in range(4):
            osl = tc4 % 2
            for dvh in range(2):
                wk = [("mo", osl, dvh)] + \
                    ([("mzb",)] + list() if first["ob"] else [])
                first["ob"] = False
                P.step("vector", [("PS", tc4 * 2 + dvh), ("mb2", dvh)], wk,
                       lambda e, tc4=tc4, dvh=dvh, osl=osl: e.tensor_add(
                           obuf[osl][:, dvh * 512:dvh * 512 + 512],
                           PS[tc4 * 2 + dvh][:, 0:512],
                           b2bc[:, dvh * 512:dvh * 512 + 512]))
                P.step("vector", [("mo", osl, dvh), ("mvc", tl * 4 + tc4)],
                       [("mo", osl, dvh)],
                       lambda e, tl=tl, tc4=tc4, dvh=dvh, osl=osl:
                       e.tensor_scalar_mul(
                           obuf[osl][:, dvh * 512:dvh * 512 + 512],
                           obuf[osl][:, dvh * 512:dvh * 512 + 512],
                           SM[:, s0 + 80 + tl * 4 + tc4:
                              s0 + 81 + tl * 4 + tc4]))
            P.step("sync", [("mo", osl, 0), ("mo", osl, 1)],
                   [("mob", tl * 4 + tc4)],
                   lambda e, tl=tl, tc4=tc4, osl=osl: e.dma_start(
                       out=Dm["moe_bounce"][(tl * 4 + tc4) * 128:
                                            (tl * 4 + tc4) * 128 + 128, :],
                       in_=obuf[osl][:, :]), dma=True)
    AB16.freea(rHT, [("mhT", fb) for fb in range(NBH)])
    AB16.freea(rXL, [("mxtl", 0), ("mxtl", 1)])
    AB16.freea(rW1, [("mw1", 0), ("mw1", 1)])
    AB16.freea(rW2, [("mw2", 0), ("mw2", 1)])

    P.step("gpsimd", [("mob", a) for a in range(16)], ["moe_ag_c"],
           lambda e: e.collective_compute(
               "AllGather", ALU.bypass,
               replica_groups=[[0, 1, 2, 3, 4, 5, 6, 7]],
               ins=[Dm["moe_bounce"][:, :]], outs=[Dm["moe_ag"][0:NTOKG, :]]),
           own_sem="cc_moe")

    rXI, lgXI = AF32.alloc(2048, ["mxti"])
    xti = [AF32.t[:, rXI[0]:rXI[0] + 1024], AF32.t[:, rXI[0] + 1024:rXI[0] + 2048]]
    rLN, lgLN = AF32.alloc(1024, ["mlnt"])
    lnt = AF32.t[:, rLN[0]:rLN[0] + 1024]
    first_xi = [True]
    for a in range(16):
        sl = a % 2
        for hf in range(2):
            pb = 4 + hf

            def mko(a=a, hf=hf, pb=pb):
                def f_(e):
                    for j in range(4 * hf, 4 * hf + 4):
                        ins = e.transpose(
                            PS[pb][:, (j - 4 * hf) * 128:(j - 4 * hf) * 128 + 128],
                            X[:, j * TOK + a * 128:j * TOK + a * 128 + 128],
                            idf[:])
                    return ins
                return f_
            P.step("tensor", [("X", j) for j in range(4 * hf, 4 * hf + 4)] +
                   ["idf"], [("PS", pb)], mko())
            wk = [("mxt", sl)] + (list(lgXI) if first_xi[0] else [])
            first_xi[0] = False
            P.step("vector", [("PS", pb)], wk,
                   lambda e, sl=sl, hf=hf, pb=pb: e.tensor_copy(
                       xti[sl][:, hf * 512:hf * 512 + 512], PS[pb][:, 0:512]))
        for ee in range(NEXP):
            P.step("gpsimd", ["moe_ag_c", "moe_pad", ("mridxi",), "BUFi32"],
                   [("mxt", sl)],
                   lambda e, a=a, ee=ee, sl=sl: e.indirect_dma_start(
                       out=xti[sl][:, :], out_offset=None,
                       in_=Dm["moe_ag"][:, :],
                       in_offset=IndirectOffsetOnAxis(
                           i32buf.bitcast(U32)[:, a * 8 + ee:a * 8 + ee + 1],
                           0),
                       compute_op=ALU.add), dma=True)
        cb4 = 520 + sl * 4
        P.step("vector", [("mxt", sl)], [("ml4m", sl)],
               lambda e, sl=sl, cb4=cb4: e.tensor_reduce(
                   col(cb4), xti[sl][:, :], AX, ALU.add))
        P.step("vector", [("ml4m", sl)], [("ml4m", sl)],
               lambda e, cb4=cb4: e.tensor_scalar_mul(col(cb4), col(cb4),
                                                      1.0 / D))
        P.step("vector", [("mxt", sl), ("ml4m", sl)], [("mxt", sl)],
               lambda e, sl=sl, cb4=cb4: e.tensor_scalar(
                   xti[sl][:, :], xti[sl][:, :], col(cb4), None,
                   ALU.subtract))
        P.step("scalar", [("mxt", sl)], [("ml4s", sl), ("mlntk",)],
               lambda e, sl=sl, cb4=cb4: e.activation(
                   lnt[:, :], xti[sl][:, :], AF.Square,
                   accum_out=col(cb4 + 1)))
        P.step("vector", [("ml4s", sl)], [("ml4v", sl)],
               lambda e, cb4=cb4: e.tensor_scalar_mul(col(cb4 + 2),
                                                      col(cb4 + 1), 1.0 / D))
        P.step("scalar", [("ml4v", sl), "consts"], [("ml4l", sl)],
               lambda e, cb4=cb4: e.activation(col(cb4 + 2), col(cb4 + 2),
                                               AF.Ln, bias=consts[:, 1:2]))
        P.step("scalar", [("ml4l", sl)], [("ml4r", sl)],
               lambda e, cb4=cb4: e.activation(col(cb4 + 2), col(cb4 + 2),
                                               AF.Exp, scale=-0.5))
        P.step("vector", [("mxt", sl), ("ml4r", sl)], [("mxt", sl)],
               lambda e, sl=sl, cb4=cb4: e.tensor_scalar_mul(
                   xti[sl][:, :], xti[sl][:, :], col(cb4 + 2)))
        P.step("sync", [("mxt", sl)], ["out_hbm"],
               lambda e, a=a, sl=sl: e.dma_start(
                   out=sh["out_hbm"][a * 128:(a + 1) * 128, :],
                   in_=xti[sl][:, :]), dma=True)



def _rope_tables():
    i = np.arange(512, dtype=np.float64)
    theta = np.power(10000.0, -2.0 * (i - 1.0) / D)
    pos = np.arange(8192, dtype=np.float64)
    ang = (pos[:, None] * theta[None, :])
    return (np.cos(ang.astype(np.float32).astype(np.float64)),
            np.sin(ang.astype(np.float32).astype(np.float64)))


def _host_inputs(inputs):
    perm = np.r_[np.arange(0, D, 2), np.arange(1, D, 2)]
    f32 = np.float32
    x = np.asarray(inputs["L0_attn_x"], f32).reshape(NTOKG, D)

    wqT = np.ascontiguousarray(np.asarray(inputs["L0_attn_wq"], f32).T[:, perm])
    wkT = np.ascontiguousarray(np.asarray(inputs["L0_attn_wk"], f32).T[:, perm])
    wvT = np.ascontiguousarray(np.asarray(inputs["L0_attn_wv"], f32).T)
    w1T = np.ascontiguousarray(np.asarray(inputs["L0_ffn_w1"], f32).T)
    w2T = np.ascontiguousarray(np.asarray(inputs["L0_ffn_w2"], f32).T)
    cw = np.asarray(inputs["L0_ffn_conv_w"], f32).reshape(DH, 3)
    convW = np.zeros((128, 3 * NBH), f32)
    for t in range(3):
        convW[:, t * NBH:(t + 1) * NBH] = cw[:, t].reshape(NBH, 128).T
    convB = np.asarray(inputs["L0_ffn_conv_b"], f32).reshape(NBH, 128).T.copy()
    winT = np.ascontiguousarray(np.asarray(inputs["L1_attn_W_in"], f32)[D:, :].T)
    wgT = np.ascontiguousarray(np.asarray(inputs["L1_attn_W_gate"], f32).T)
    woT = np.ascontiguousarray(np.asarray(inputs["L1_attn_W_out"], f32).T)
    b_in = np.asarray(inputs["L1_attn_b_in"], f32)[D:].reshape(NB, 128).T.copy()
    b_g = np.asarray(inputs["L1_attn_b_gate"], f32).reshape(NB, 128).T.copy()
    b_o = np.asarray(inputs["L1_attn_b_out"], f32).reshape(NB, 128).T.copy()
    A_T = np.ascontiguousarray(np.asarray(inputs["L1_attn_A"], f32).T)
    BmT = np.ascontiguousarray(np.asarray(inputs["L1_attn_Bm"], f32).T)
    CmT = np.ascontiguousarray(np.asarray(inputs["L1_attn_Cm"], f32).T)
    Ws = np.ascontiguousarray(np.asarray(inputs["L1_ffn_Ws"], f32))
    bs = np.asarray(inputs["L1_ffn_bs"], f32)
    bst = np.zeros((128, 1), f32)
    bst[:NEXP, 0] = bs
    W1 = np.asarray(inputs["L1_ffn_W1"], f32)
    b1 = np.asarray(inputs["L1_ffn_b1"], f32)
    W2 = np.asarray(inputs["L1_ffn_W2"], f32)
    b2 = np.asarray(inputs["L1_ffn_b2"], f32)

    cos, sin = _rope_tables()
    cos = cos.astype(f32); sin = sin.astype(f32)

    ident_bf = np.eye(128, dtype=ml_dtypes.bfloat16)
    ident_f32 = np.eye(128, dtype=f32)
    consts = np.zeros((128, 8), f32)
    consts[:, 0] = 1.0
    consts[:, 1] = EPS
    consts[:, 2] = 1.0
    egrp = np.zeros((128, NEXP), f32)
    for e in range(NEXP):
        egrp[8 * e:8 * e + 8, e] = 1.0
    e8to128 = np.zeros((NEXP, 128), f32)
    for e in range(NEXP):
        for r in range(8):
            e8to128[e, 8 * e + r] = 1.0
    ltm = np.zeros((128, 128), f32)
    for e in range(NEXP):
        for r1 in range(8):
            for r2 in range(r1 + 1, 8):
                ltm[8 * e + r1, 8 * e + r2] = 1.0
    iota_row = np.arange(TOK, dtype=f32).reshape(1, TOK)
    tokid = np.zeros((128, 16), f32)
    for p in range(128):
        for f in range(16):
            tokid[p, f] = p + 128 * f

    maps = []
    for c in range(NCORES_ALL):
        r = c % 4
        m = dict(
            x_in=x[c * TOK:(c + 1) * TOK], wqT=wqT, wkT=wkT, wvT=wvT,
            ropeC=np.ascontiguousarray(cos[r * TOK:(r + 1) * TOK].T),
            ropeS=np.ascontiguousarray(sin[r * TOK:(r + 1) * TOK].T),
            w1T=w1T, w2T=w2T, convW=convW, convB=convB,
            winT=winT, wgT=wgT, woT=woT, b_in=b_in, b_g=b_g, b_o=b_o,
            A_Tin=A_T, A_nat=np.ascontiguousarray(np.asarray(inputs["L1_attn_A"], f32)),
            BmT=BmT, CmT=CmT, i16rep=np.tile(np.eye(16, dtype=f32), (8, 1)),
            Ws_in=Ws, bs_in=bst,
            W1e=np.ascontiguousarray(W1[c]),
            b1e=b1[c].reshape(NBH, 128).T.copy(),
            W2e=np.ascontiguousarray(W2[c]),
            b2e=b2[c].reshape(NB, 128).T.copy(),
            ident_bf=ident_bf, ident_f32=ident_f32, consts_in=consts,
            egrp=egrp, e8to128=e8to128, ltmask=ltm, tokid=tokid,
            iota_row=iota_row,
            b2row=np.ascontiguousarray(b2[c].reshape(1, D)),
        )
        ebase = np.zeros((128, 1), f32)
        plidx = np.zeros((128, 1), np.uint32)
        myrows = np.zeros((128, 1), np.uint32)
        for eh in range(NEXP):
            te = (eh + c) % NEXP
            ebase[8 * eh:8 * eh + 8, 0] = 2048.0 * te
            for rr in range(8):
                plidx[8 * eh + rr, 0] = 8 * rr + te
            myrows[eh, 0] = 8 * eh + c
        m["ebase"] = ebase
        m["plidx"] = plidx
        m["myrows"] = myrows
        cm = np.zeros((128, 4), f32)
        for cp in range(NCORES_ALL):
            if cp // 4 != c // 4:
                continue
            d = r - 1 - (cp % 4)
            if d < 0 or d > 2:
                continue
            cm[16 * cp:16 * cp + 16, d] = 1.0
        m["cmask"] = cm
        hidx = np.zeros((128, 1), np.uint32)
        hmsk = np.zeros((2, 1), f32)
        if r > 0:
            hidx[0, 0] = 2 * (c - 1) + 1
            hmsk[0, 0] = 1.0
        if r < 3:
            hidx[1, 0] = 2 * (c + 1)
            hmsk[1, 0] = 1.0
        m["halo_idx"] = hidx
        m["halo_msk"] = hmsk
        maps.append(m)
    return maps


NCORES_ALL = 8
_NC_CACHE = {}


def kernel(_phases=(1, 2, 3, 4), **inputs):
    maps = _host_inputs(inputs)
    nc = build_program(phases=tuple(_phases))
    res = run_bass_kernel_spmd(nc, maps, core_ids=list(range(NCORES_ALL)))
    out = np.concatenate([res.results[c]["out_sh"] for c in range(NCORES_ALL)],
                         axis=0)
    return out.reshape(2, 8192, D)


# version 02
# speedup vs baseline: 1.0230x; 1.0230x over previous
import math
from contextlib import ExitStack

import numpy as np
import ml_dtypes

import concourse.bass as bass
import concourse.mybir as mybir
from concourse.bass import IndirectOffsetOnAxis
from concourse.bass_utils import run_bass_kernel_spmd

F32 = mybir.dt.float32
BF16 = mybir.dt.bfloat16
I32 = mybir.dt.int32
U32 = mybir.dt.uint32
U8 = mybir.dt.uint8
AF = mybir.ActivationFunctionType
ALU = mybir.AluOpType
AX = mybir.AxisListType.X

D = 1024
TOK = 2048
NB = D // 128
DH = 4096
NBH = DH // 128
NEXP = 8
KSEL = 2048
NTOKG = 16384
EPS = 1e-5

ENGINES = ("sync", "scalar", "vector", "tensor", "gpsimd")


class Prog:

    def __init__(self, nc, ctx):
        self.nc = nc
        self.ctx = ctx
        self.esem = {e: ctx.enter_context(nc.semaphore(f"E_{e}"))
                     for e in ENGINES}
        self.ecount = {e: 0 for e in ENGINES}
        self.steps = {e: [] for e in ENGINES}
        self.last_w = {}
        self.readers = {}
        self.custom = {}
        self.nsteps = 0

    def _h(self, sid):
        return self.esem[sid] if sid in self.esem else self.custom[sid]

    def step(self, engine, reads, writes, fn, dma=False, own_sem=None):
        waits = {}

        def addw(sid, val):
            waits[sid] = max(waits.get(sid, 0), val)

        for k in reads:
            if k in self.last_w:
                addw(*self.last_w[k])
        for k in writes:
            if k in self.last_w:
                addw(*self.last_w[k])
            for r in self.readers.get(k, ()):
                addw(*r)
        inc_n = 16 if dma else 1
        if own_sem is not None:
            h = self.ctx.enter_context(self.nc.semaphore(own_sem))
            self.custom[own_sem] = h
            sid, self_val = own_sem, inc_n
        else:
            sid = engine
            self.ecount[engine] += inc_n
            self_val = self.ecount[engine]
        self.steps[engine].append((sorted(waits.items()), fn, inc_n, own_sem))
        self.nsteps += 1
        for k in writes:
            self.last_w[k] = (sid, self_val)
            self.readers[k] = []
        for k in reads:
            self.readers.setdefault(k, []).append((sid, self_val))
        return (sid, self_val)

    def emit(self, block, final_keys=()):
        fw = {}
        for k in final_keys:
            sid, v = self.last_w[k]
            fw[sid] = max(fw.get(sid, 0), v)
        self.steps["gpsimd"].append(
            (sorted(fw.items()), None, 0, None))
        for ename in ENGINES:
            items = self.steps[ename]
            if not items:
                continue
            section = getattr(block, ename)

            def make(items=items, ename=ename):
                def body(eng):
                    for waits, fn, inc_n, own in items:
                        for sid, val in waits:
                            eng.wait_ge(self._h(sid), val)
                        if fn is None:
                            continue
                        ins = fn(eng)
                        if inc_n:
                            if own is not None:
                                ins.then_inc(self.custom[own], inc_n)
                            else:
                                ins.then_inc(self.esem[ename], inc_n)
                return body
            section(make())


class Arena:

    def __init__(self, tensor, ncols):
        self.t = tensor
        self.n = ncols
        self.free = [(0, ncols)]
        self.owner_keys = {}

    def alloc(self, cols, keys_hint):
        for i, (s, e) in enumerate(self.free):
            if e - s >= cols:
                if s + cols == e:
                    self.free.pop(i)
                else:
                    self.free[i] = (s + cols, e)
                rng = (s, s + cols)
                legacy = set()
                drop = []
                for r, ks in self.owner_keys.items():
                    if r[0] < rng[1] and rng[0] < r[1]:
                        legacy |= ks
                        drop.append(r)
                for r in drop:
                    del self.owner_keys[r]
                self.owner_keys[rng] = set(keys_hint)
                return rng, legacy
        raise RuntimeError(f"arena OOM: want {cols}, free={self.free}")

    def freea(self, rng, keys=()):
        if rng in self.owner_keys:
            self.owner_keys[rng] = self.owner_keys[rng] | set(keys)
        else:
            self.owner_keys[rng] = set(keys)
        self.free.append(rng)
        self.free.sort()
        out = []
        for s, e in self.free:
            if out and out[-1][1] == s:
                out[-1] = (out[-1][0], e)
            else:
                out.append((s, e))
        self.free = [tuple(x) for x in out]


def cdiv(a, b):
    return (a + b - 1) // b



DBG = {}


def build_program(phases=(1, 2, 3, 4)):
    nc = bass.Bass("TRN2", target_bir_lowering=False)
    ap = lambda t: t.ap()

    I = {}
    def inp(name, shape, dt=F32):
        I[name] = ap(nc.dram_tensor(name, shape, dt, kind="ExternalInput"))
        return I[name]

    x_in = inp("x_in", [TOK, D])
    inp("wqT", [D, D]); inp("wkT", [D, D]); inp("wvT", [D, D])
    inp("ropeC", [512, TOK]); inp("ropeS", [512, TOK])
    inp("w1T", [D, DH]); inp("w2T", [DH, D])
    inp("convW", [128, 3 * NBH]); inp("convB", [128, NBH])
    inp("winT", [D, D]); inp("wgT", [D, D]); inp("woT", [D, D])
    inp("b_in", [128, NB]); inp("b_g", [128, NB]); inp("b_o", [128, NB])
    inp("A_Tin", [16, 16]); inp("A_nat", [16, 16])
    inp("BmT", [D, 16]); inp("CmT", [16, D]); inp("i16rep", [128, 16])
    inp("cmask", [128, 4])
    inp("Ws_in", [D, NEXP]); inp("bs_in", [128, 1])
    inp("W1e", [D, DH]); inp("b1e", [128, NBH])
    inp("W2e", [DH, D]); inp("b2e", [128, NB])
    inp("ident_bf", [128, 128], BF16); inp("ident_f32", [128, 128])
    inp("consts_in", [128, 8])
    inp("egrp", [128, NEXP]); inp("e8to128", [NEXP, 128])
    inp("ltmask", [128, 128])
    inp("halo_idx", [128, 1], U32); inp("halo_msk", [2, 1])
    inp("tokid", [128, 16])
    inp("iota_row", [1, TOK]); inp("b2row", [1, D]); inp("ebase", [128, 1])
    inp("plidx", [128, 1], U32); inp("myrows", [128, 1], U32)

    out_hbm = ap(nc.dram_tensor("out_sh", [TOK, D], F32, kind="ExternalOutput"))
    dbg_hbm = ap(nc.dram_tensor("dbg_out", [128, 8192], F32, kind="ExternalOutput"))

    Dm = {}
    def dram(name, shape, dt):
        Dm[name] = ap(nc.dram_tensor(name, shape, dt))
        return Dm[name]

    dram("ot_spill", [TOK, D], F32)
    dram("qr_spill", [NB * 128, TOK], BF16)
    dram("k_bounce", [D, TOK], BF16)
    dram("v_bounce", [TOK, D], BF16)
    dram("k_ag", [4 * D, TOK], BF16)
    dram("v_ag", [4 * TOK, D], BF16)
    dram("w1q_dram", [D, DH], BF16)
    dram("w2q_dram", [DH, D], BF16)
    dram("halo_bounce", [2, D], F32)
    dram("row_bounce", [1, 4096], F32)
    dram("halo_ag", [16, D], F32)
    dram("state_bounce", [1, 16], F32)
    dram("pow_dram", [16, 32], F32)
    dram("state_ag", [8, 16], F32)
    dram("probs_bounce", [NEXP, TOK], F32)
    dram("probs_ag", [8 * NEXP, TOK], F32)
    dram("act_bounce", [TOK, D], BF16)
    dram("act_ag", [NTOKG, D], BF16)
    dram("moe_bounce", [KSEL, D], F32)
    dram("moe_ag", [NTOKG + 128, D], F32)
    dram("cum_dram", [NEXP * 128, 128], F32)
    dram("ce_dram", [NEXP, 128], F32)
    dram("w1e_dram", [D, DH], BF16)
    dram("w2e_dram", [DH, D], BF16)
    dram("xint_dram", [D, KSEL], BF16)
    dram("evals_dram", [NTOKG, 1], F32)
    dram("ridx_dram", [72, TOK], F32)
    dram("moe_row", [1, 4096], F32)

    ctx = ExitStack()
    P = Prog(nc, ctx)
    sbuf = lambda n, s, d: ctx.enter_context(nc.sbuf_tensor(n, s, d))
    psum = lambda n, s, d: ctx.enter_context(nc.psum_tensor(n, s, d))

    X = sbuf("X", [128, NB * TOK], F32)
    st = sbuf("st", [16, 2048], F32)
    idb = sbuf("idb", [128, 128], BF16)
    idf = sbuf("idf", [128, 128], F32)
    consts = sbuf("consts", [128, 8], F32)
    biasbuf = sbuf("biasbuf", [128, 8 * NB + 5 * NBH], F32)
    OF = dict(BIN=0, BG=NB, BO=2 * NB, B2=3 * NB, CB=4 * NB,
              CW=4 * NB + NBH, B1=4 * NB + 4 * NBH)
    AF32 = Arena(sbuf("AF32", [128, 12544], F32), 12544)
    AB16 = Arena(sbuf("AB16", [128, 35072], BF16), 35072)

    sq2 = sbuf("sq2", [128, 2 * 512], F32)
    onesrow = sbuf("onesrow", [1, 128], F32)
    i32buf = sbuf("i32buf", [128, 1024], I32)

    PS = [psum(f"ps{i}", [128, 512], F32) for i in range(8)]

    sh = dict(nc=nc, P=P, I=I, Dm=Dm, X=X, st=st, idb=idb, idf=idf,
              consts=consts, biasbuf=biasbuf, OF=OF, AF32=AF32, AB16=AB16,
              PS=PS, out_hbm=out_hbm, sq2=sq2, onesrow=onesrow, i32buf=i32buf,
              dbg_hbm=dbg_hbm)
    P.step("vector", [], ["onesrow"], lambda e: e.memset(onesrow[:], 1.0))

    rng, legacy = AF32.alloc(4096, ["xtok"])
    xtok = AF32.t[:, rng[0]:rng[0] + 4096]
    for nm, dst, src in (
            ("idb", idb[:], I["ident_bf"]), ("idf", idf[:], I["ident_f32"]),
            ("consts", consts[:], I["consts_in"])):
        P.step("sync", [], [nm], lambda e, d=dst, s=src: e.dma_start(out=d, in_=s), dma=True)
    for nm, sl, src in (
            ("bb0", (OF["BIN"], NB), "b_in"), ("bb1", (OF["BG"], NB), "b_g"),
            ("bb2", (OF["BO"], NB), "b_o"), ("bb3", (OF["B2"], NB), "b2e"),
            ("bb4", (OF["CB"], NBH), "convB"), ("bb5", (OF["CW"], 3 * NBH), "convW"),
            ("bb6", (OF["B1"], NBH), "b1e")):
        P.step("sync", [], [nm],
               lambda e, a=sl[0], n=sl[1], s=src: e.dma_start(
                   out=biasbuf[:, a:a + n], in_=I[s]), dma=True)

    for ch in range(8):
        P.step("sync", (list(legacy) if ch == 0 else []), [("xtok", ch % 2)],
               lambda e, ch=ch: e.dma_start(
                   out=xtok.rearrange("p (a d) -> p a d", a=4)[:, (ch % 2) * 2:(ch % 2) * 2 + 2, :],
                   in_=x_in.rearrange("(a b) d -> b a d", b=128)[:, 2 * ch:2 * ch + 2, :]),
               dma=True)
        for ai in range(2):
            a = ch * 2 + ai
            for j in range(NB):
                k = a * NB + j
                P.step("tensor", [("xtok", ch % 2), "idf"], [("PS", k % 2)],
                       lambda e, ch=ch, ai=ai, j=j, k=k: e.transpose(
                           PS[k % 2][:, 0:128],
                           xtok[:, ((ch % 2) * 2 + ai) * D + 128 * j:
                                ((ch % 2) * 2 + ai) * D + 128 * j + 128], idf[:]))
                P.step("vector", [("PS", k % 2)], [("X", j)],
                       lambda e, a=a, j=j, k=k: e.tensor_copy(
                           X[:, j * TOK + 128 * a: j * TOK + 128 * a + 128],
                           PS[k % 2][:, 0:128]))
    AF32.freea(rng, [("xtok", 0), ("xtok", 1)])

    with ctx:
        with nc.Block() as block:
            if 2 in phases:
                emit_l0_ffn_wquant(sh)
            if 1 in phases:
                emit_l0_attn(sh)
            if 2 in phases:
                drain_wq(sh)
                emit_l0_ffn(sh)
            if 3 in phases:
                emit_l1_attn(sh)
            if 4 in phases:
                emit_moe(sh)
            else:
                emit_debug_out(sh)
            fk = ["out_hbm"] + [("dbg", n) for (n, _, _) in DBG.get("names", [])]
            P.emit(block, final_keys=fk)
    return nc


def emit_debug_out(sh):
    emit_store_out(sh, ln4=False)


def emit_store_out(sh, ln4=False):
    P, X, idf, PS, AF32 = sh["P"], sh["X"], sh["idf"], sh["PS"], sh["AF32"]
    rng, legacy = AF32.alloc(4096, ["xtok2"])
    xtok = AF32.t[:, rng[0]:rng[0] + 4096]
    first = True
    for ch in range(8):
        for ai in range(2):
            a = ch * 2 + ai
            for j in range(NB):
                k = a * NB + j
                P.step("tensor", [("X", j), "idf"], [("PS", 2 + k % 2)],
                       lambda e, a=a, j=j, k=k: e.transpose(
                           PS[2 + k % 2][:, 0:128],
                           X[:, j * TOK + 128 * a: j * TOK + 128 * a + 128], idf[:]))
                w = [("xtok2", ch % 2)] + (list(legacy) if first else [])
                first = False
                P.step("vector", [("PS", 2 + k % 2)], w,
                       lambda e, ch=ch, ai=ai, j=j, k=k: e.tensor_copy(
                           xtok[:, ((ch % 2) * 2 + ai) * D + 128 * j:
                                ((ch % 2) * 2 + ai) * D + 128 * j + 128],
                           PS[2 + k % 2][:, 0:128]))
        P.step("gpsimd", [("xtok2", ch % 2)], ["out_hbm"],
               lambda e, ch=ch: e.dma_start(
                   out=sh["out_hbm"].rearrange("(a b) d -> b a d", b=128)[:, 2 * ch:2 * ch + 2, :],
                   in_=xtok.rearrange("p (a d) -> p a d", a=4)[:, (ch % 2) * 2:(ch % 2) * 2 + 2, :]),
               dma=True)
    AF32.freea(rng)



def wload(sh, tag, dram_ap, nin, nout):
    P, AB16 = sh["P"], sh["AB16"]
    cols = nin * nout * 128
    rng, legacy = AB16.alloc(cols, [tag])
    t = AB16.t[:, rng[0]:rng[0] + cols]
    P.step("gpsimd", [], [tag] + list(legacy),
           lambda e: e.dma_start(
               out=t.rearrange("p (m j c) -> p m j c", m=nout, j=nin),
               in_=dram_ap.rearrange("(j p) (m c) -> p m j c", p=128, c=128)),
           dma=True)
    return t, tag, rng




def dbg_dump(sh, key_reads, src_ap, col0, name):
    if not DBG.get("on"):
        return
    P = sh["P"]
    np_, nw = src_ap.shape[0], src_ap.shape[-1]
    P.step("gpsimd", list(key_reads), [("dbg", name)],
           lambda e: e.dma_start(
               out=sh["dbg_hbm"][0:np_, col0:col0 + nw], in_=src_ap),
           dma=True)
    DBG.setdefault("names", []).append((name, col0, src_ap.shape[-1]))
    P.final_extra = ("dbg", name)


def get_xb(sh):
    P, X, AB16 = sh["P"], sh["X"], sh["AB16"]
    if sh.get("xb_rng") is None:
        rng, legacy = AB16.alloc(NB * TOK, ["xballoc"])
        sh["xb_rng"] = rng
        sh["Xb"] = AB16.t[:, rng[0]:rng[0] + NB * TOK]
        first_extra = list(legacy) + ["xballoc"]
    else:
        first_extra = []
    Xb = sh["Xb"]
    for j in range(NB):
        P.step("vector", [("X", j)], [("Xb", j)] + (first_extra if j == 0 else []),
               lambda e, j=j: e.tensor_copy(Xb[:, j * TOK:(j + 1) * TOK],
                                            X[:, j * TOK:(j + 1) * TOK]))
    return Xb


def free_xb(sh):
    if sh.get("xb_rng") is not None:
        sh["AB16"].freea(sh["xb_rng"], [("Xb", j) for j in range(NB)] + ["xballoc"])
        sh["xb_rng"] = None


def proj(sh, tag, wsb, wkey, evac, nin=NB, nout=NB, tokw=512, src=None,
         srckeyfn=None, psbanks=(4, 5), srctok=TOK, ntch=None):
    P, PS = sh["P"], sh["PS"]
    src = sh["Xb"] if src is None else src
    if srckeyfn is None:
        srckeyfn = lambda j, tch: ("Xb", j)
    ntch = (srctok // tokw) if ntch is None else ntch
    g = 0
    for m in range(nout):
        for tch in range(ntch):
            bank = psbanks[g % 2]
            ps = PS[bank]

            def mk(m=m, tch=tch, ps=ps):
                def f(e):
                    for j in range(nin):
                        ins = e.matmul(
                            ps[:, 0:tokw],
                            wsb[:, (m * nin + j) * 128:(m * nin + j) * 128 + 128],
                            src[:, j * srctok + tch * tokw:
                                j * srctok + tch * tokw + tokw],
                            start=(j == 0), stop=(j == nin - 1))
                    return ins
                return f
            rkeys = [wkey] + [srckeyfn(j, tch) for j in range(nin)]
            P.step("tensor", rkeys, [("PS", bank)], mk())
            evac(m, tch, ps[:, 0:tokw], bank)
            g += 1


def featln(sh, tag, chunk_done=None):
    P, X, st, PS, consts = sh["P"], sh["X"], sh["st"], sh["PS"], sh["consts"]
    sq2, onesrow = sh["sq2"], sh["onesrow"]
    for tch in range(4):
        for j in range(NB):
            k = tch * NB + j
            P.step("scalar", [("X", j)], [(tag + "sq", k % 2), "BUFsq2"],
                   lambda e, j=j, k=k, tch=tch: e.activation(
                       sq2[:, (k % 2) * 512:(k % 2) * 512 + 512],
                       X[:, j * TOK + tch * 512: j * TOK + tch * 512 + 512],
                       AF.Square))
            P.step("tensor", [("X", j), "consts"], [("PS", 4)],
                   lambda e, j=j, tch=tch: e.matmul(
                       PS[4][0:1, 0:512], consts[:, 0:1],
                       X[:, j * TOK + tch * 512: j * TOK + tch * 512 + 512],
                       start=(j == 0), stop=(j == NB - 1)))
            P.step("tensor", [(tag + "sq", k % 2), "consts"], [("PS", 5)],
                   lambda e, j=j, k=k: e.matmul(
                       PS[5][0:1, 0:512], consts[:, 0:1],
                       sq2[:, (k % 2) * 512:(k % 2) * 512 + 512],
                       start=(j == 0), stop=(j == NB - 1)))
        P.step("vector", [("PS", 4)], [(tag, "mu")],
               lambda e: e.tensor_scalar_mul(
                   st[0:1, 1024:1536], PS[4][0:1, 0:512], 1.0 / D))
        P.step("vector", [("PS", 5)], [(tag, "e2")],
               lambda e: e.tensor_scalar_mul(
                   st[0:1, 1536:2048], PS[5][0:1, 0:512], 1.0 / D))
        P.step("vector", [(tag, "mu")], [(tag, "m2")],
               lambda e: e.tensor_mul(st[0:1, 512:1024], st[0:1, 1024:1536],
                                      st[0:1, 1024:1536]))
        P.step("vector", [(tag, "m2"), (tag, "e2")], [(tag, "var")],
               lambda e: e.tensor_sub(st[0:1, 1536:2048], st[0:1, 1536:2048],
                                      st[0:1, 512:1024]))
        P.step("scalar", [(tag, "var"), "consts"], [(tag, "lg")],
               lambda e: e.activation(st[0:1, 512:1024], st[0:1, 1536:2048],
                                      AF.Ln, bias=consts[0:1, 1:2]))
        P.step("scalar", [(tag, "lg")], [(tag, "rstd")],
               lambda e: e.activation(st[0:1, 0:512], st[0:1, 512:1024],
                                      AF.Exp, scale=-0.5))
        P.step("vector", [(tag, "rstd"), (tag, "mu")], [(tag, "b")],
               lambda e: e.tensor_mul(st[0:1, 512:1024], st[0:1, 1024:1536],
                                      st[0:1, 0:512]))
        P.step("tensor", [(tag, "rstd"), "onesrow"], [("PS", 4)],
               lambda e: e.matmul(PS[4][:, 0:512], onesrow[0:1, 0:128],
                                  st[0:1, 0:512], start=True, stop=True))
        P.step("tensor", [(tag, "b"), "onesrow"], [("PS", 5)],
               lambda e: e.matmul(PS[5][:, 0:512], onesrow[0:1, 0:128],
                                  st[0:1, 512:1024], start=True, stop=True))
        for j in range(NB):
            sl = slice(j * TOK + tch * 512, j * TOK + tch * 512 + 512)
            P.step("vector", [("PS", 4)], [("X", j)],
                   lambda e, sl=sl: e.tensor_mul(X[:, sl], X[:, sl],
                                                 PS[4][:, 0:512]))
            P.step("vector", [("PS", 5)], [("X", j)],
                   lambda e, sl=sl: e.tensor_sub(X[:, sl], X[:, sl],
                                                 PS[5][:, 0:512]))
        if chunk_done is not None:
            chunk_done(tch)



def emit_l1_attn(sh):
    P, nc, I, Dm, X, st, PS = (sh["P"], sh["nc"], sh["I"], sh["Dm"], sh["X"],
                               sh["st"], sh["PS"])
    AF32, AB16, consts = sh["AF32"], sh["AB16"], sh["consts"]
    biasbuf, OF = sh["biasbuf"], sh["OF"]

    get_xb(sh)
    win_sb, win_k, win_r = wload(sh, "winsb", I["winT"], NB, NB)
    wg_sb, wg_k, wg_r = wload(sh, "wgsb", I["wgT"], NB, NB)

    rA, lgA = AF32.alloc(768, ["l1small"])
    base = rA[0]
    A_ = AF32.t
    bmsb = A_[:, base:base + 128]
    powB = A_[:, base + 128:base + 336]
    powBT = A_[:, base + 336:base + 544]
    wstack = A_[:, base + 544:base + 560]
    sstack = A_[:, base + 560:base + 561]
    hin = A_[:, base + 561:base + 562]
    i16 = A_[:, base + 576:base + 592]
    cmk = A_[:, base + 592:base + 596]
    prep = A_[:, base + 600:base + 632]
    P.step("sync", [], ["l1bm"] + list(lgA),
           lambda e: e.dma_start(
               out=bmsb.rearrange("p (j c) -> p j c", j=NB),
               in_=I["BmT"].rearrange("(j p) c -> p j c", p=128)), dma=True)
    P.step("sync", [], [("pow", 0)],
           lambda e: e.dma_start(out=powB[0:16, 0:16], in_=I["A_Tin"]), dma=True)
    P.step("sync", [], [("powT", 0)],
           lambda e: e.dma_start(out=powBT[0:16, 0:16], in_=I["A_nat"]), dma=True)
    P.step("sync", [], ["i16"],
           lambda e: e.dma_start(out=i16[:], in_=I["i16rep"]), dma=True)
    P.step("sync", [], ["cmk"],
           lambda e: e.dma_start(out=cmk[:], in_=I["cmask"]), dma=True)

    for s in range(12):
        P.step("tensor", [("pow", s), ("powT", s)], [("PS", 6)],
               lambda e, s=s: e.matmul(PS[6][0:16, 0:16],
                                       powBT[0:16, s * 16:s * 16 + 16],
                                       powB[0:16, s * 16:s * 16 + 16],
                                       start=True, stop=True))
        P.step("vector", [("PS", 6)], [("pow", s + 1)],
               lambda e, s=s: e.tensor_copy(
                   powB[0:16, (s + 1) * 16:(s + 1) * 16 + 16], PS[6][0:16, 0:16]))
        P.step("tensor", [("pow", s), ("powT", s)], [("PS", 7)],
               lambda e, s=s: e.matmul(PS[7][0:16, 0:16],
                                       powB[0:16, s * 16:s * 16 + 16],
                                       powBT[0:16, s * 16:s * 16 + 16],
                                       start=True, stop=True))
        P.step("vector", [("PS", 7)], [("powT", s + 1)],
               lambda e, s=s: e.tensor_copy(
                   powBT[0:16, (s + 1) * 16:(s + 1) * 16 + 16], PS[7][0:16, 0:16]))

    rU, lgU = AF32.alloc(2048, ["U16z"])
    U16 = AF32.t[:, rU[0]:rU[0] + 2048]
    rG, lgG = AF32.alloc(8192, ["gtile"])
    gtile = AF32.t[:, rG[0]:rG[0] + 4096]
    itile = AF32.t[:, rG[0] + 4096:rG[0] + 8192]
    lgG = list(lgG)

    for tch in range(4):
        g = 0
        for m in range(NB):
            for wsb, wkey, which in ((wg_sb, wg_k, "g"), (win_sb, win_k, "i")):
                bank = (4, 5)[g % 2]

                def mk(m=m, wsb=wsb, bank=bank, tch=tch):
                    def f(e):
                        for j in range(NB):
                            ins = e.matmul(
                                PS[bank][:, 0:512],
                                wsb[:, (m * NB + j) * 128:(m * NB + j) * 128 + 128],
                                sh["Xb"][:, j * TOK + tch * 512:
                                         j * TOK + tch * 512 + 512],
                                start=(j == 0), stop=(j == NB - 1))
                        return ins
                    return f
                P.step("tensor", [wkey] + [("Xb", j) for j in range(NB)],
                       [("PS", bank)], mk())
                if which == "g":
                    P.step("scalar", [("PS", bank), "bb1"],
                           [("gt", m)] + (lgG if (m, tch) == (0, 0) else []),
                           lambda e, m=m, bank=bank: e.activation(
                               gtile[:, m * 512:m * 512 + 512],
                               PS[bank][:, 0:512], AF.Sigmoid,
                               bias=biasbuf[:, OF["BG"] + m:OF["BG"] + m + 1]))
                else:
                    P.step("scalar", [("PS", bank), "bb0"], [("it", m)],
                           lambda e, m=m, bank=bank: e.activation(
                               itile[:, m * 512:m * 512 + 512],
                               PS[bank][:, 0:512], AF.Identity,
                               bias=biasbuf[:, OF["BIN"] + m:OF["BIN"] + m + 1]))
                g += 1
        for m in range(NB):
            P.step("vector", [("gt", m), ("it", m)], [("ut", m)],
                   lambda e, m=m: e.tensor_mul(itile[:, m * 512:m * 512 + 512],
                                               itile[:, m * 512:m * 512 + 512],
                                               gtile[:, m * 512:m * 512 + 512]))

        def mkub(tch=tch):
            def f(e):
                for j in range(NB):
                    ins = e.matmul(PS[6][0:16, 0:512],
                                   bmsb[:, j * 16:j * 16 + 16],
                                   itile[:, j * 512:j * 512 + 512],
                                   start=(j == 0), stop=(j == NB - 1))
                return ins
            return f
        P.step("tensor", [("ut", j) for j in range(NB)] + ["l1bm"],
               [("PS", 6)], mkub())
        P.step("vector", [("PS", 6)],
               [("U16", tch)] + (list(lgU) if tch == 0 else []),
               lambda e, tch=tch: e.tensor_copy(
                   U16[0:16, tch * 512:tch * 512 + 512], PS[6][0:16, 0:512]))

    AF32.freea(rG, [(k, m) for k in ("gt", "it", "ut") for m in range(NB)])
    AB16.freea(win_r, [win_k])
    AB16.freea(wg_r, [wg_k])
    UKEYS = [("U16", t) for t in range(4)]

    for s in range(11):
        shift = 1 << s
        spans = []
        end = 2048
        while end > shift:
            spans.append((max(shift, end - 512), end))
            end = max(shift, end - 512)
        for (start, end) in spans:
            P.step("tensor", [("pow", s)] + UKEYS, [("PS", 6)],
                   lambda e, s=s, start=start, end=end, shift=shift: e.matmul(
                       PS[6][0:16, 0:end - start],
                       powB[0:16, s * 16:s * 16 + 16],
                       U16[0:16, start - shift:end - shift],
                       start=True, stop=True))
            P.step("vector", [("PS", 6)], UKEYS,
                   lambda e, start=start, end=end: e.tensor_add(
                       U16[0:16, start:end], U16[0:16, start:end],
                       PS[6][0:16, 0:end - start]))

    P.step("sync", UKEYS, ["state_bounce"],
           lambda e: e.dma_start(out=Dm["state_bounce"][:, :],
                                 in_=U16[0:16, 2047:2048]), dma=True)
    P.step("gpsimd", ["state_bounce"], ["state_ag"],
           lambda e: e.collective_compute(
               "AllGather", ALU.bypass,
               replica_groups=[[0, 1, 2, 3, 4, 5, 6, 7]],
               ins=[Dm["state_bounce"][:, :]], outs=[Dm["state_ag"][:, :]]),
           own_sem="cc_state")
    P.step("sync", ["state_ag"], ["sstack"],
           lambda e: e.dma_start(
               out=sstack[:, 0:1],
               in_=Dm["state_ag"].rearrange("a (b c) -> (a b) c", c=1)),
           dma=True)
    P.step("sync", [("pow", 11), ("pow", 12)], ["pow_dram"],
           lambda e: e.dma_start(out=Dm["pow_dram"][0:16, 0:32],
                                 in_=powB[0:16, 176:208]), dma=True)
    for r in range(8):
        P.step("sync", ["pow_dram"], [("powrep", r)],
               lambda e, r=r: e.dma_start(
                   out=prep[16 * r:16 * r + 16, 0:32],
                   in_=Dm["pow_dram"][0:16, 0:32]), dma=True)
    P.step("vector", ["i16", "cmk"], [("ws", 0)],
           lambda e: e.tensor_scalar_mul(wstack[:], i16[:], cmk[:, 0:1]))
    P.step("vector", [("ws", 0)] + [("powrep", r) for r in range(8)], [("ws", 1)],
           lambda e: e.scalar_tensor_tensor(
               wstack[:], prep[:, 0:16], cmk[:, 1:2], wstack[:],
               ALU.mult, ALU.add))
    P.step("vector", [("ws", 1)], [("ws", 2)],
           lambda e: e.scalar_tensor_tensor(
               wstack[:], prep[:, 16:32], cmk[:, 2:3], wstack[:],
               ALU.mult, ALU.add))
    P.step("tensor", [("ws", 2), "sstack"], [("PS", 7)],
           lambda e: e.matmul(PS[7][0:16, 0:1], wstack[:], sstack[:, 0:1],
                              start=True, stop=True))
    P.step("vector", [("PS", 7)], ["hin"],
           lambda e: e.tensor_copy(hin[0:16, 0:1], PS[7][0:16, 0:1]))

    rC, lgC = AF32.alloc(2048, ["C16z"])
    C16 = AF32.t[:, rC[0]:rC[0] + 2048]
    P.step("vector", [], [("C16", -1)] + list(lgC),
           lambda e: e.memset(C16[0:16, :], 0.0))
    P.step("tensor", [("pow", 0), "hin", ("C16", -1)], [("PS", 7)],
           lambda e: e.matmul(PS[7][0:16, 0:1], powB[0:16, 0:16],
                              hin[0:16, 0:1], start=True, stop=True))
    P.step("vector", [("PS", 7), ("C16", -1)], [("C16", 0)],
           lambda e: e.tensor_copy(C16[0:16, 0:1], PS[7][0:16, 0:1]))
    for s in range(11):
        shift = 1 << s
        lo, hi = shift, min(2048, 2 * shift)
        for start in range(lo, hi, 512):
            end = min(start + 512, hi)
            P.step("tensor", [("pow", s), ("C16", s)], [("PS", 7)],
                   lambda e, s=s, start=start, end=end, shift=shift: e.matmul(
                       PS[7][0:16, 0:end - start],
                       powB[0:16, s * 16:s * 16 + 16],
                       C16[0:16, start - shift:end - shift],
                       start=True, stop=True))
            P.step("vector", [("PS", 7), ("C16", s)], [("C16", s + 1)],
                   lambda e, start=start, end=end: e.tensor_add(
                       C16[0:16, start:end], C16[0:16, start:end],
                       PS[7][0:16, 0:end - start]))
    P.step("vector", [("C16", 11)] + UKEYS, UKEYS,
           lambda e: e.tensor_add(U16[0:16, :], U16[0:16, :], C16[0:16, :]))
    AF32.freea(rC, [("C16", s) for s in range(-1, 12)])

    rH, lgH = AB16.alloc(2048, ["Hbz"])
    Hb = AB16.t[:, rH[0]:rH[0] + 2048]
    P.step("vector", UKEYS, ["Hb"] + list(lgH),
           lambda e: e.tensor_copy(Hb[0:16, :], U16[0:16, :]))
    rcm, lgcm = AB16.alloc(1024, ["cmsbz"])
    cmsb = AB16.t[:, rcm[0]:rcm[0] + 1024]
    P.step("gpsimd", [], ["cmsb"] + list(lgcm),
           lambda e: e.dma_start(out=cmsb[0:16, :], in_=I["CmT"][:, :]), dma=True)
    wo_sb, wo_k, wo_r = wload(sh, "wosb", I["woT"], NB, NB)
    ryt, lgyt = AB16.alloc(NB * 512, ["yTz"])
    yTc = AB16.t[:, ryt[0]:ryt[0] + NB * 512]
    first_yt = [True]
    for tch in range(4):
        for m in range(NB):
            P.step("tensor", ["cmsb", "Hb"], [("PS", (4, 5)[m % 2])],
                   lambda e, m=m, tch=tch: e.matmul(
                       PS[(4, 5)[m % 2]][:, 0:512],
                       cmsb[0:16, m * 128:m * 128 + 128],
                       Hb[0:16, tch * 512:tch * 512 + 512],
                       start=True, stop=True))
            w = [("yTc", m)] + (list(lgyt) if first_yt[0] else [])
            first_yt[0] = False
            P.step("vector", [("PS", (4, 5)[m % 2])], w,
                   lambda e, m=m: e.tensor_copy(
                       yTc[:, m * 512:m * 512 + 512],
                       PS[(4, 5)[m % 2]][:, 0:512]))
        for m in range(NB):
            bank = (6, 7)[m % 2]

            def mkwo(m=m, bank=bank):
                def f(e):
                    for j in range(NB):
                        ins = e.matmul(
                            PS[bank][:, 0:512],
                            wo_sb[:, (m * NB + j) * 128:(m * NB + j) * 128 + 128],
                            yTc[:, j * 512:j * 512 + 512],
                            start=(j == 0), stop=(j == NB - 1))
                    return ins
                return f
            P.step("tensor", [wo_k] + [("yTc", j) for j in range(NB)],
                   [("PS", bank)], mkwo())
            sl = slice(m * TOK + tch * 512, m * TOK + tch * 512 + 512)
            P.step("vector", [("PS", bank), ("X", m), "bb2"], [("X", m)],
                   lambda e, m=m, sl=sl, bank=bank: e.scalar_tensor_tensor(
                       X[:, sl], PS[bank][:, 0:512],
                       biasbuf[:, OF["BO"] + m:OF["BO"] + m + 1],
                       X[:, sl], ALU.add, ALU.add))
    AF32.freea(rU, [("U16", t) for t in range(4)])
    AF32.freea(rA, [("pow", s) for s in range(13)] +
               [("powT", s) for s in range(13)] +
               ["l1bm", "i16", "cmk", "sstack", "hin",
                ("ws", 0), ("ws", 1), ("ws", 2)] +
               [("powrep", r) for r in range(8)])
    AB16.freea(rH, ["Hb"])
    AB16.freea(rcm, ["cmsb"])
    AB16.freea(ryt, [("yTc", m) for m in range(NB)])
    AB16.freea(wo_r, [wo_k])

    featln(sh, "ln3")


def emit_l0_attn(sh):
    P, nc, I, Dm, X, PS = sh["P"], sh["nc"], sh["I"], sh["Dm"], sh["X"], sh["PS"]
    AF32, AB16, sq2 = sh["AF32"], sh["AB16"], sh["sq2"]
    idb, idf = sh["idb"], sh["idf"]

    Xb = get_xb(sh)

    rS, lgS = AF32.alloc(8192, ["qkslab"])
    qsl = AF32.t[:, rS[0]:rS[0] + 8192]
    rT, lgT = AF32.alloc(2048, ["ropetmp"])
    rt1 = AF32.t[:, rT[0]:rT[0] + 512]
    rt2 = AF32.t[:, rT[0] + 512:rT[0] + 1024]
    rtc = AF32.t[:, rT[0] + 1024:rT[0] + 1536]
    rts = AF32.t[:, rT[0] + 1536:rT[0] + 2048]
    rQKB, lgQKB = AB16.alloc(4096, ["qkb"])
    qkb = AB16.t[:, rQKB[0]:rQKB[0] + 4096]
    first_tbl = [True]
    first_cast = [True]
    first_qs = [True]

    def qkproj(tg, wname, spill, spkey):
        wsb, wkey, wrng = wload(sh, tg + "w", I[wname], NB, NB)
        g = 0
        for H in range(2):
            for m in range(NB):
                for tc in range(2):
                    bank = (4, 5)[g % 2]
                    g += 1
                    tch = 2 * H + tc

                    def mkp(m=m, tch=tch, bank=bank):
                        def f(e):
                            for j in range(NB):
                                ins = e.matmul(
                                    PS[bank][:, 0:512],
                                    wsb[:, (m * NB + j) * 128:
                                        (m * NB + j) * 128 + 128],
                                    Xb[:, j * TOK + tch * 512:
                                       j * TOK + tch * 512 + 512],
                                    start=(j == 0), stop=(j == NB - 1))
                            return ins
                        return f
                    P.step("tensor", [wkey] + [("Xb", j) for j in range(NB)],
                           [("PS", bank)], mkp())
                    wk = [("qs", m, tc)]
                    if first_qs[0]:
                        wk += list(lgS)
                        first_qs[0] = False
                    P.step("vector", [("PS", bank)], wk,
                           lambda e, m=m, tc=tc, bank=bank: e.tensor_copy(
                               qsl[:, m * 1024 + tc * 512:
                                   m * 1024 + tc * 512 + 512],
                               PS[bank][:, 0:512]))
                pump_wq(sh)
            for hh in range(4):
                for q4 in range(2):
                    c0 = hh * 1024 + q4 * 512
                    te = qsl[:, c0:c0 + 512]
                    to = qsl[:, 4096 + c0:4096 + c0 + 512]
                    kTE = [("qs", hh, q4)]
                    kTO = [("qs", hh + 4, q4)]
                    lt = list(lgT) if first_tbl[0] else []
                    first_tbl[0] = False
                    t0c = H * 1024 + q4 * 512
                    P.step("sync", [], [("RTC",)] + lt,
                           lambda e, hh=hh, t0c=t0c: e.dma_start(
                               out=rtc[:, :],
                               in_=I["ropeC"][hh * 128:(hh + 1) * 128,
                                              t0c:t0c + 512]), dma=True)
                    P.step("sync", [], [("RTS",)],
                           lambda e, hh=hh, t0c=t0c: e.dma_start(
                               out=rts[:, :],
                               in_=I["ropeS"][hh * 128:(hh + 1) * 128,
                                              t0c:t0c + 512]), dma=True)
                    P.step("vector", kTE + [("RTC",)], [("RT1",)],
                           lambda e, te=te: e.tensor_mul(rt1, te, rtc))
                    P.step("vector", kTE + [("RTS",)], [("RT2",)],
                           lambda e, te=te: e.tensor_mul(rt2, te, rts))
                    P.step("vector", kTO + [("RTS",)], kTE,
                           lambda e, te=te, to=to: e.tensor_mul(te, to, rts))
                    P.step("vector", [("RT1",)] + kTE, kTE,
                           lambda e, te=te: e.tensor_add(te, te, rt1))
                    P.step("vector", kTO + [("RTC",)], [("RT1",)],
                           lambda e, to=to: e.tensor_mul(rt1, to, rtc))
                    P.step("vector", [("RT1",), ("RT2",)], kTO,
                           lambda e, to=to: e.tensor_sub(to, rt1, rt2))
            for j in range(NB):
                slot = j % 2
                wk = [("qkb", slot)] + (list(lgQKB) if first_cast[0] else [])
                first_cast[0] = False
                P.step("vector", [("qs", j, 0), ("qs", j, 1)], wk,
                       lambda e, j=j, slot=slot: e.tensor_copy(
                           qkb[:, slot * 2048:slot * 2048 + 1024],
                           qsl[:, j * 1024:(j + 1) * 1024]))
                P.step("sync", [("qkb", slot)], [(spkey, j, H)],
                       lambda e, j=j, slot=slot, spill=spill, H=H: e.dma_start(
                           out=spill[j * 128:(j + 1) * 128,
                                     H * 1024:H * 1024 + 1024],
                           in_=qkb[:, slot * 2048:slot * 2048 + 1024]),
                       dma=True)
        AB16.freea(wrng, [wkey])

    qkproj("k", "wkT", Dm["k_bounce"], "ksp")
    P.step("gpsimd", [("ksp", j, H) for j in range(NB) for H in range(2)],
           ["k_ag"],
           lambda e: e.collective_compute(
               "AllGather", ALU.bypass,
               replica_groups=[[0, 1, 2, 3], [4, 5, 6, 7]],
               ins=[Dm["k_bounce"][:, :]], outs=[Dm["k_ag"][:, :]]),
           own_sem="cc_k")
    AB16.freea(rQKB, [("qkb", 0), ("qkb", 1)])

    wvsb, wvk, wvr = wload(sh, "vw", I["wvT"], NB, NB)
    rVS, lgVS = AB16.alloc(4096, ["vstage"])
    vstage = AB16.t[:, rVS[0]:rVS[0] + 4096]
    rVT, lgVT = AB16.alloc(2048, ["vtok"])
    vtok = AB16.t[:, rVT[0]:rVT[0] + 2048]
    first_vs, first_vt = [True], [True]
    for tch in range(4):
        for m in range(NB):
            bank = (4, 5)[m % 2]

            def mkv(m=m, tch=tch, bank=bank):
                def f(e):
                    for j in range(NB):
                        ins = e.matmul(
                            PS[bank][:, 0:512],
                            wvsb[:, (m * NB + j) * 128:(m * NB + j) * 128 + 128],
                            Xb[:, j * TOK + tch * 512:j * TOK + tch * 512 + 512],
                            start=(j == 0), stop=(j == NB - 1))
                    return ins
                return f
            P.step("tensor", [wvk] + [("Xb", j) for j in range(NB)],
                   [("PS", bank)], mkv())
            pump_wq(sh)
            wk = [("vst", m)] + (list(lgVS) if first_vs[0] else [])
            first_vs[0] = False
            P.step("scalar", [("PS", bank)], wk,
                   lambda e, m=m, bank=bank: e.activation(
                       vstage[:, m * 512:m * 512 + 512], PS[bank][:, 0:512],
                       AF.Copy))
        for a in range(4):
            pb = (6, 7)[a % 2]

            def mkt(a=a, pb=pb):
                def f(e):
                    pv = PS[pb].bitcast(BF16)
                    for m in range(NB):
                        ins = e.transpose(
                            pv[:, m * 128:(m + 1) * 128],
                            vstage[:, m * 512 + a * 128:m * 512 + a * 128 + 128],
                            idb[:])
                    return ins
                return f
            P.step("tensor", [("vst", m) for m in range(NB)] + ["idb"],
                   [("PS", pb)], mkt())
            slot = a % 2
            wk = [("vtok", slot)] + (list(lgVT) if first_vt[0] else [])
            first_vt[0] = False
            P.step("vector", [("PS", pb)], wk,
                   lambda e, pb=pb, slot=slot: e.tensor_copy(
                       vtok[:, slot * 1024:slot * 1024 + 1024],
                       PS[pb].bitcast(BF16)[:, 0:1024]))
            P.step("sync", [("vtok", slot)], [("vsp", tch, a)],
                   lambda e, tch=tch, a=a, slot=slot: e.dma_start(
                       out=Dm["v_bounce"].rearrange("(a p) c -> p a c", p=128)
                       [:, tch * 4 + a:tch * 4 + a + 1, :],
                       in_=vtok.rearrange("p (s c) -> p s c", s=2)
                       [:, slot:slot + 1, :]), dma=True)
    P.step("gpsimd", [("vsp", t, a) for t in range(4) for a in range(4)],
           ["v_ag"],
           lambda e: e.collective_compute(
               "AllGather", ALU.bypass,
               replica_groups=[[0, 1, 2, 3], [4, 5, 6, 7]],
               ins=[Dm["v_bounce"][:, :]], outs=[Dm["v_ag"][:, :]]),
           own_sem="cc_v")
    AB16.freea(wvr, [wvk])
    AB16.freea(rVS, [("vst", m) for m in range(NB)])
    AB16.freea(rVT, [("vtok", 0), ("vtok", 1)])

    rQKB, lgQKB = AB16.alloc(4096, ["qkb"])
    qkb = AB16.t[:, rQKB[0]:rQKB[0] + 4096]
    first_cast[0] = True
    qkproj("q", "wqT", Dm["qr_spill"], "qsp")
    AF32.freea(rS, [("qs", m, t) for m in range(NB) for t in range(2)])
    AF32.freea(rT, [("RT1",), ("RT2",), ("RTC",), ("RTS",)])
    AB16.freea(rQKB, [("qkb", 0), ("qkb", 1)])
    free_xb(sh)

    Xv = X.bitcast(BF16)
    kslab = Xv[:, 0:16384]
    vslab = Xv[:, 16384:32768]
    rQR, lgQR = AB16.alloc(16384, ["qres"])
    qres = AB16.t[:, rQR[0]:rQR[0] + 16384]
    rSC, lgSC = AB16.alloc(4096, ["scsl"])
    scsl = [AB16.t[:, rSC[0]:rSC[0] + 2048],
            AB16.t[:, rSC[0] + 2048:rSC[0] + 4096]]
    rAT, lgAT = AB16.alloc(1024, ["atmp"])
    atmp = [AB16.t[:, rAT[0]:rAT[0] + 512], AB16.t[:, rAT[0] + 512:rAT[0] + 1024]]
    rO, lgO = AF32.alloc(6144, ["otile"])
    oti = [AF32.t[:, rO[0]:rO[0] + 1024], AF32.t[:, rO[0] + 1024:rO[0] + 2048]]
    otr = [AF32.t[:, rO[0] + 2048:rO[0] + 3072], AF32.t[:, rO[0] + 3072:rO[0] + 4096]]
    orb = [AF32.t[:, rO[0] + 4096:rO[0] + 5120], AF32.t[:, rO[0] + 5120:rO[0] + 6144]]
    first_sc, first_at, first_ot = [True], [True], [True]
    AVB = {(0, 0): 2, (0, 1): 3, (1, 0): 6, (1, 1): 7}

    P.step("sync", [("qsp", j, H) for j in range(NB) for H in range(2)],
           ["qres"] + list(lgQR),
           lambda e: e.dma_start(
               out=qres.rearrange("p (j t) -> p j t", j=NB),
               in_=Dm["qr_spill"].rearrange("(j p) t -> p j t", p=128)),
           dma=True)

    for qt in range(4):
        P.step("sync", ["k_ag"],
               [("kslab",), ("X", 0), ("X", 1), ("X", 2), ("X", 3)],
               lambda e, qt=qt: e.dma_start(
                   out=kslab.rearrange("p (j u) -> p j u", j=NB),
                   in_=Dm["k_ag"][qt * 1024:(qt + 1) * 1024, :]
                   .rearrange("(j p) t -> p j t", p=128)), dma=True)
        P.step("scalar", ["v_ag"],
               [("vslab",), ("X", 4), ("X", 5), ("X", 6), ("X", 7)],
               lambda e, qt=qt: e.dma_start(
                   out=vslab.rearrange("p (g c) -> p g c", g=16),
                   in_=Dm["v_ag"][qt * 2048:(qt + 1) * 2048, :]
                   .rearrange("(g p) c -> p g c", p=128)), dma=True)
        for p in range(8):
            q0 = p * 256
            pump_wq(sh, 2)
            if qt >= 1:
                for T in (0, 1):
                    t = 2 * p + T
                    P.step("gpsimd", [("ots", t)], [("OTR", T)],
                           lambda e, t=t, T=T: e.dma_start(
                               out=otr[T][:, :],
                               in_=Dm["ot_spill"][t * 128:(t + 1) * 128, :]),
                           dma=True)
            for kc in range(4):
                for T in (0, 1):
                    bank = T

                    def mks(kc=kc, T=T, bank=bank, q0=q0):
                        def f(e):
                            for j in range(NB):
                                ins = e.matmul(
                                    PS[bank][:, 0:512],
                                    qres[:, j * 2048 + q0 + T * 128:
                                         j * 2048 + q0 + T * 128 + 128],
                                    kslab[:, j * 2048 + kc * 512:
                                          j * 2048 + kc * 512 + 512],
                                    start=(j == 0), stop=(j == NB - 1))
                            return ins
                        return f
                    P.step("tensor", [("qres",), ("kslab",)],
                           [("PS", bank)], mks())
                    wk = [("sc", T, kc)] + (list(lgSC) if first_sc[0] else [])
                    first_sc[0] = False
                    P.step("scalar", [("PS", bank)], wk,
                           lambda e, T=T, kc=kc, bank=bank: e.activation(
                               scsl[T][:, kc * 512:kc * 512 + 512],
                               PS[bank][:, 0:512], AF.Copy))
                    P.step("vector", [("sc", T, kc)], [("fmx", T, kc)],
                           lambda e, T=T, kc=kc: e.tensor_reduce(
                               sq2[:, T * 4 + kc:T * 4 + kc + 1],
                               scsl[T][:, kc * 512:kc * 512 + 512], AX, ALU.max))
            for T in (0, 1):
                P.step("vector", [("fmx", T, kc) for kc in range(4)], [("fM", T)],
                       lambda e, T=T: e.tensor_reduce(
                           sq2[:, 16 + T:17 + T], sq2[:, T * 4:T * 4 + 4],
                           AX, ALU.max))
                P.step("vector", [("fM", T)], [("fnm", T)],
                       lambda e, T=T: e.tensor_scalar_mul(
                           sq2[:, 18 + T:19 + T], sq2[:, 16 + T:17 + T],
                           -1.0 / 32.0))
                for kc in range(4):
                    tb = kc % 2
                    wk = [("fat", tb), ("fds", T, kc)] + \
                        (list(lgAT) if first_at[0] else [])
                    first_at[0] = False
                    P.step("scalar", [("sc", T, kc), ("fnm", T)], wk,
                           lambda e, T=T, kc=kc, tb=tb: e.activation(
                               atmp[tb], scsl[T][:, kc * 512:kc * 512 + 512],
                               AF.Exp, scale=1.0 / 32.0,
                               bias=sq2[:, 18 + T:19 + T],
                               accum_out=sq2[:, 20 + T * 4 + kc:
                                             21 + T * 4 + kc]))
                    pb = 4 + kc % 2

                    def mkt(tb=tb, pb=pb):
                        def f(e):
                            pv = PS[pb].bitcast(BF16)
                            for i in range(4):
                                ins = e.transpose(
                                    pv[:, i * 128:(i + 1) * 128],
                                    atmp[tb][:, i * 128:(i + 1) * 128], idb[:])
                            return ins
                        return f
                    P.step("tensor", [("fat", tb), "idb"], [("PS", pb)], mkt())
                    P.step("vector", [("PS", pb)], [("sc", T, kc)],
                           lambda e, T=T, kc=kc, pb=pb: e.tensor_copy(
                               scsl[T][:, kc * 512:kc * 512 + 512],
                               PS[pb].bitcast(BF16)[:, 0:512]))
                P.step("vector", [("fds", T, kc) for kc in range(4)], [("fD", T)],
                       lambda e, T=T: e.tensor_reduce(
                           sq2[:, 36 + T:37 + T], sq2[:, 20 + T * 4:24 + T * 4],
                           AX, ALU.add))
            for T in (0, 1):
                for dvh in (0, 1):
                    bank = AVB[(T, dvh)]
                    for gg in range(2):
                        def mka(T=T, dvh=dvh, bank=bank, gg=gg):
                            def f(e):
                                for g in range(gg * 8, gg * 8 + 8):
                                    ins = e.matmul(
                                        PS[bank][:, 0:512],
                                        scsl[T][:, g * 128:(g + 1) * 128],
                                        vslab[:, g * 1024 + dvh * 512:
                                              g * 1024 + dvh * 512 + 512],
                                        start=(g == 0), stop=(g == 15))
                                return ins
                            return f
                        P.step("tensor",
                               [("sc", T, kc) for kc in range(gg * 2, gg * 2 + 2)]
                               + [("vslab",)], [("PS", bank)], mka())
            for T in (0, 1):
                t = 2 * p + T
                cb = 72 + T * 8
                if qt == 0:
                    for dvh in (0, 1):
                        wk = [("OT", T, dvh)] + \
                            (list(lgO) if first_ot[0] else [])
                        first_ot[0] = False
                        P.step("vector", [("PS", AVB[(T, dvh)])], wk,
                               lambda e, T=T, dvh=dvh: e.tensor_copy(
                                   oti[T][:, dvh * 512:dvh * 512 + 512],
                                   PS[AVB[(T, dvh)]][:, 0:512]))
                    P.step("vector", [("fM", T)], [("frm", t)],
                           lambda e, T=T, t=t: e.tensor_copy(
                               sq2[:, 40 + t:41 + t], sq2[:, 16 + T:17 + T]))
                    P.step("vector", [("fD", T)], [("frd", t)],
                           lambda e, T=T, t=t: e.tensor_copy(
                               sq2[:, 56 + t:57 + t], sq2[:, 36 + T:37 + T]))
                else:
                    P.step("vector", [("frm", t), ("fM", T)], [("fmm", T)],
                           lambda e, t=t, cb=cb, T=T: e.tensor_tensor(
                               sq2[:, cb:cb + 1], sq2[:, 40 + t:41 + t],
                               sq2[:, 16 + T:17 + T], ALU.max))
                    P.step("vector", [("fmm", T), ("frm", t)], [("fe1a", T)],
                           lambda e, t=t, cb=cb: e.tensor_sub(
                               sq2[:, cb + 1:cb + 2], sq2[:, 40 + t:41 + t],
                               sq2[:, cb:cb + 1]))
                    P.step("scalar", [("fe1a", T)], [("fe1", T)],
                           lambda e, cb=cb: e.activation(
                               sq2[:, cb + 1:cb + 2], sq2[:, cb + 1:cb + 2],
                               AF.Exp, scale=1.0 / 32.0))
                    P.step("vector", [("fmm", T), ("fM", T)], [("fe2a", T)],
                           lambda e, cb=cb, T=T: e.tensor_sub(
                               sq2[:, cb + 2:cb + 3], sq2[:, 16 + T:17 + T],
                               sq2[:, cb:cb + 1]))
                    P.step("scalar", [("fe2a", T)], [("fe2", T)],
                           lambda e, cb=cb: e.activation(
                               sq2[:, cb + 2:cb + 3], sq2[:, cb + 2:cb + 3],
                               AF.Exp, scale=1.0 / 32.0))
                    P.step("vector", [("fmm", T)], [("frm", t)],
                           lambda e, t=t, cb=cb: e.tensor_copy(
                               sq2[:, 40 + t:41 + t], sq2[:, cb:cb + 1]))
                    P.step("vector", [("fe1", T), ("frd", t)], [("fda", T)],
                           lambda e, t=t, cb=cb: e.tensor_mul(
                               sq2[:, cb + 3:cb + 4], sq2[:, 56 + t:57 + t],
                               sq2[:, cb + 1:cb + 2]))
                    P.step("vector", [("fe2", T), ("fD", T)], [("fdb", T)],
                           lambda e, cb=cb, T=T: e.tensor_mul(
                               sq2[:, cb + 5:cb + 6], sq2[:, 36 + T:37 + T],
                               sq2[:, cb + 2:cb + 3]))
                    P.step("vector", [("fda", T), ("fdb", T)], [("frd", t)],
                           lambda e, t=t, cb=cb: e.tensor_add(
                               sq2[:, 56 + t:57 + t], sq2[:, cb + 3:cb + 4],
                               sq2[:, cb + 5:cb + 6]))
                    for dvh in (0, 1):
                        bank = AVB[(T, dvh)]
                        P.step("vector", [("PS", bank), ("fe2", T)],
                               [("OT", T, dvh)],
                               lambda e, bank=bank, dvh=dvh, T=T, cb=cb:
                               e.tensor_scalar_mul(
                                   oti[T][:, dvh * 512:dvh * 512 + 512],
                                   PS[bank][:, 0:512], sq2[:, cb + 2:cb + 3]))
                        P.step("vector", [("OTR", T), ("fe1", T)],
                               [("OTR", T)],
                               lambda e, dvh=dvh, T=T, cb=cb:
                               e.tensor_scalar_mul(
                                   otr[T][:, dvh * 512:dvh * 512 + 512],
                                   otr[T][:, dvh * 512:dvh * 512 + 512],
                                   sq2[:, cb + 1:cb + 2]))
                        P.step("vector", [("OTR", T)], [("OT", T, dvh)],
                               lambda e, dvh=dvh, T=T: e.tensor_add(
                                   oti[T][:, dvh * 512:dvh * 512 + 512],
                                   oti[T][:, dvh * 512:dvh * 512 + 512],
                                   otr[T][:, dvh * 512:dvh * 512 + 512]))
                if qt == 3:
                    P.step("vector", [("frd", t)], [("fr", T)],
                           lambda e, t=t, cb=cb: e.reciprocal(
                               sq2[:, cb + 4:cb + 5], sq2[:, 56 + t:57 + t]))
                    P.step("sync", [], [("xrw", T)],
                           lambda e, t=t, T=T: e.dma_start(
                               out=orb[T][:, :],
                               in_=sh["I"]["x_in"][t * 128:(t + 1) * 128, :]),
                           dma=True)
                    for dvh in (0, 1):
                        P.step("vector", [("fr", T)], [("OT", T, dvh)],
                               lambda e, dvh=dvh, T=T, cb=cb:
                               e.tensor_scalar_mul(
                                   oti[T][:, dvh * 512:dvh * 512 + 512],
                                   oti[T][:, dvh * 512:dvh * 512 + 512],
                                   sq2[:, cb + 4:cb + 5]))
                    P.step("vector", [("xrw", T), ("OT", T, 0), ("OT", T, 1)],
                           [("OT", T, 0), ("OT", T, 1)],
                           lambda e, T=T: e.tensor_add(
                               oti[T][:, :], oti[T][:, :], orb[T][:, :]))
                P.step("gpsimd", [("OT", T, 0), ("OT", T, 1)], [("ots", t)],
                       lambda e, t=t, T=T: e.dma_start(
                           out=Dm["ot_spill"][t * 128:(t + 1) * 128, :],
                           in_=oti[T][:, :]), dma=True)

    for a in range(16):
        sl = a % 2
        P.step("sync", [("ots", a)], [("orb2", sl), ("xrw", sl)],
               lambda e, a=a, sl=sl: e.dma_start(
                   out=orb[sl][:, :],
                   in_=Dm["ot_spill"][a * 128:(a + 1) * 128, :]), dma=True)
        for j in range(NB):
            pb = 4 + j % 2
            P.step("tensor", [("orb2", sl), "idf"], [("PS", pb)],
                   lambda e, j=j, pb=pb, sl=sl: e.transpose(
                       PS[pb][:, 0:128], orb[sl][:, j * 128:j * 128 + 128],
                       idf[:]))
            P.step("vector", [("PS", pb)],
                   [("X", j), ("kslab",) if j < 4 else ("vslab",)],
                   lambda e, j=j, pb=pb, a=a: e.tensor_copy(
                       X[:, j * TOK + a * 128:j * TOK + a * 128 + 128],
                       PS[pb][:, 0:128]))

    AB16.freea(rQR, [("qres",)])
    AB16.freea(rSC, [("sc", T, kc) for T in (0, 1) for kc in range(4)])
    AB16.freea(rAT, [("fat", 0), ("fat", 1)])
    AF32.freea(rO, [("OT", s, d) for s in (0, 1) for d in (0, 1)] +
               [("OTR", 0), ("OTR", 1), ("orb2", 0), ("orb2", 1),
                ("xrw", 0), ("xrw", 1)])
    featln(sh, "ln1")



def emit_l0_ffn_wquant(sh):
    P, I, Dm = sh["P"], sh["I"], sh["Dm"]
    AF32, AB16 = sh["AF32"], sh["AB16"]
    onesrow, biasbuf = sh["onesrow"], sh["biasbuf"]
    wqs = biasbuf
    rWT, lgWT = AF32.alloc(2048, ["wqtile2"])
    wqt = AF32.t[:, rWT[0]:rWT[0] + 2048]
    rSG, lgSG = AB16.alloc(4096, ["wq_sg2"])
    sgn = AB16.t[:, rSG[0]:rSG[0] + 2048]
    qout = AB16.t[:, rSG[0] + 2048:rSG[0] + 4096]
    RB = Dm["row_bounce"]
    th = []

    def chunks(wn):
        out = []
        if wn == "w1":
            for t in range(8):
                for hh in range(2):
                    out.append((I["w1T"][t * 128:(t + 1) * 128,
                                         hh * 2048:hh * 2048 + 2048],
                                Dm["w1q_dram"][t * 128:(t + 1) * 128,
                                               hh * 2048:hh * 2048 + 2048],
                                2048))
        else:
            for t in range(32):
                out.append((I["w2T"][t * 128:(t + 1) * 128, :],
                            Dm["w2q_dram"][t * 128:(t + 1) * 128, :], 1024))
        return out

    for io, (wn, total) in enumerate((("w1", D * DH), ("w2", DH * D))):
        for ci, (sap, dap, w) in enumerate(chunks(wn)):
            def p1(sap=sap, w=w, first=(io, ci) == (0, 0), ci=ci, wn=wn):
                wk = [("wqt2",)] + (list(lgWT) if first else [])
                P.step("gpsimd", [], wk,
                       lambda e: e.dma_start(out=wqt[:, 0:w], in_=sap),
                       dma=True)
                P.step("vector", [("wqt2",)], [("wq1c", wn)],
                       lambda e: e.tensor_reduce(wqs[:, 193:194], wqt[:, 0:w],
                                                 AX, ALU.add,
                                                 apply_absolute_value=True))
                if ci == 0:
                    P.step("vector", [("wq1c", wn)], [("wq1a", wn)],
                           lambda e: e.tensor_copy(wqs[:, 192:193],
                                                   wqs[:, 193:194]))
                else:
                    P.step("vector", [("wq1c", wn), ("wq1a", wn)],
                           [("wq1a", wn)],
                           lambda e: e.tensor_add(wqs[:, 192:193],
                                                  wqs[:, 192:193],
                                                  wqs[:, 193:194]))
            th.append(p1)

        def stats(wn=wn, total=total, io=io):
            P.step("gpsimd", [("wq1a", wn)], [("wqpd", wn)],
                   lambda e: e.dma_start(
                       out=RB[0:1, 2304:2432].rearrange("a (p c) -> (a p) c",
                                                        p=128),
                       in_=wqs[:, 192:193]), dma=True)
            P.step("gpsimd", [("wqpd", wn)], [("wqpr", wn), ("wqt2",)],
                   lambda e: e.dma_start(out=wqt[0:1, 0:128],
                                         in_=RB[0:1, 2304:2432]), dma=True)
            P.step("vector", [("wqpr", wn)], [("wqm", wn)],
                   lambda e: e.tensor_reduce(wqs[0:1, 195:196],
                                             wqt[0:1, 0:128], AX, ALU.add))
            P.step("vector", [("wqm", wn)], [("wqm", wn)],
                   lambda e: e.tensor_scalar_mul(wqs[0:1, 195:196],
                                                 wqs[0:1, 195:196],
                                                 1.0 / total))
            P.step("vector", [("wqm", wn)], [("wqm3", wn)],
                   lambda e: e.tensor_scalar_max(wqs[0:1, 195:196],
                                                 wqs[0:1, 195:196], 1e-5))
            P.step("vector", [("wqm3", wn)], [("wth", wn)],
                   lambda e: e.tensor_scalar_mul(wqs[0:1, 196:197],
                                                 wqs[0:1, 195:196], 0.5))
            P.step("vector", [("wqm3", wn)], [("wsc", wn)],
                   lambda e: e.tensor_scalar_mul(wqs[0:1, 197 + io:198 + io],
                                                 wqs[0:1, 195:196],
                                                 1.0 / 127.0))
            P.step("vector", [("wth", wn), "onesrow"],
                   [("wthr2", wn), ("wqt2",)],
                   lambda e: e.tensor_scalar_mul(wqt[0:1, 128:256],
                                                 onesrow[0:1, 0:128],
                                                 wqs[0:1, 196:197]))
            P.step("gpsimd", [("wthr2", wn)], [("wthd", wn)],
                   lambda e: e.dma_start(out=RB[0:1, 2432:2560],
                                         in_=wqt[0:1, 128:256]), dma=True)
            P.step("gpsimd", [("wthd", wn)], [("wthb", wn)],
                   lambda e: e.dma_start(
                       out=wqs[:, 194:195],
                       in_=RB[0:1, 2432:2560].rearrange("a (p c) -> (a p) c",
                                                        p=128)), dma=True)
        th.append(stats)

        for ci, (sap, dap, w) in enumerate(chunks(wn)):
            def p2(sap=sap, dap=dap, w=w, wn=wn,
                   first=(io, ci) == (0, 0)):
                P.step("gpsimd", [("wthb", wn)], [("wqt2",)],
                       lambda e: e.dma_start(out=wqt[:, 0:w], in_=sap),
                       dma=True)
                wk = [("wsg2",)] + (list(lgSG) if first else [])
                P.step("scalar", [("wqt2",)], wk,
                       lambda e: e.activation(sgn[:, 0:w], wqt[:, 0:w],
                                              AF.Sign))
                P.step("scalar", [("wsg2",)], [("wab2",), ("wqt2",)],
                       lambda e: e.activation(wqt[:, 0:w], wqt[:, 0:w],
                                              AF.Abs))
                P.step("vector", [("wab2",), ("wthb", wn)],
                       [("wge2",), ("wqt2",)],
                       lambda e: e.tensor_scalar(wqt[:, 0:w], wqt[:, 0:w],
                                                 wqs[:, 194:195], None,
                                                 ALU.is_ge))
                P.step("vector", [("wge2",), ("wsg2",)], [("wqo2",)],
                       lambda e: e.tensor_mul(qout[:, 0:w], wqt[:, 0:w],
                                              sgn[:, 0:w]))
                P.step("gpsimd", [("wqo2",)], [("wqdr", wn)],
                       lambda e: e.dma_start(out=dap, in_=qout[:, 0:w]),
                       dma=True)
            th.append(p2)

    def fin():
        AF32.freea(rWT, [("wqt2",)])
        AB16.freea(rSG, [("wsg2",), ("wqo2",)])
    th.append(fin)
    sh["wqthunks"] = th


def pump_wq(sh, n=1):
    th = sh.get("wqthunks")
    if not th:
        return
    for _ in range(n):
        if not th:
            break
        th.pop(0)()


def drain_wq(sh):
    th = sh.get("wqthunks")
    while th:
        th.pop(0)()


def emit_l0_ffn(sh):
    P, nc, I, Dm, X, st, PS = (sh["P"], sh["nc"], sh["I"], sh["Dm"], sh["X"],
                               sh["st"], sh["PS"])
    AF32, AB16, consts, onesrow = sh["AF32"], sh["AB16"], sh["consts"], sh["onesrow"]
    biasbuf, OF, idf, idb, i32buf = (sh["biasbuf"], sh["OF"], sh["idf"],
                                     sh["idb"], sh["i32buf"])
    free_xb(sh)

    rRB, lgRB = AF32.alloc(2944, ["rowB"])
    rowB = AF32.t[:, rRB[0]:rRB[0] + 2944]
    rWS, lgWS = AF32.alloc(8192, ["wq_tile"])
    wtile = AF32.t[:, rWS[0]:rWS[0] + 4096]
    rowA = AF32.t[:, rWS[0] + 4096:rWS[0] + 8192]
    racc = sh["sq2"]
    wqs = biasbuf



    for j in range(NB):
        P.step("sync", [("X", j)], [("hb", j)],
               lambda e, j=j: e.dma_start(
                   out=Dm["halo_bounce"][0:1, j * 128:(j + 1) * 128],
                   in_=X[:, j * TOK:j * TOK + 1]), dma=True)
        P.step("sync", [("X", j)], [("hb2", j)],
               lambda e, j=j: e.dma_start(
                   out=Dm["halo_bounce"][1:2, j * 128:(j + 1) * 128],
                   in_=X[:, j * TOK + 2047:j * TOK + 2048]), dma=True)
    P.step("gpsimd",
           [("hb", j) for j in range(NB)] + [("hb2", j) for j in range(NB)],
           ["halo_ag"],
           lambda e: e.collective_compute(
               "AllGather", ALU.bypass, replica_groups=[[0, 1, 2, 3, 4, 5, 6, 7]],
               ins=[Dm["halo_bounce"][:, :]], outs=[Dm["halo_ag"][:, :]]),
           own_sem="cc_halo")
    rHG, lgHG = AF32.alloc(1024, ["hgath"])
    hg = AF32.t[:, rHG[0]:rHG[0] + 1024]
    P.step("sync", [], ["hidx", "BUFi32"],
           lambda e: e.dma_start(out=i32buf.bitcast(U32)[:, 0:1],
                                 in_=I["halo_idx"]), dma=True)
    P.step("gpsimd", ["halo_ag", "hidx", "BUFi32"], [("hg",)] + list(lgHG),
           lambda e: e.indirect_dma_start(
               out=hg[:, :], out_offset=None, in_=Dm["halo_ag"][:, :],
               in_offset=IndirectOffsetOnAxis(i32buf.bitcast(U32)[:, 0:1], 0)),
           dma=True)
    P.step("sync", [], ["hmsk"],
           lambda e: e.dma_start(out=st[0:2, 8:9], in_=I["halo_msk"]), dma=True)
    P.step("vector", [("hg",)], [("hamax",)],
           lambda e: e.tensor_reduce(st[0:2, 4:5], hg[0:2, :], AX, ALU.max,
                                     apply_absolute_value=True))
    P.step("vector", [("hamax",)], [("hamaxc",)],
           lambda e: e.tensor_scalar_max(st[0:2, 5:6], st[0:2, 4:5], 1e-5))
    P.step("vector", [("hamaxc",)], [("hrec",)],
           lambda e: e.reciprocal(st[0:2, 6:7], st[0:2, 5:6]))
    P.step("vector", [("hrec",), "hmsk"], [("hsq",)],
           lambda e: e.scalar_tensor_tensor(st[0:2, 7:8], st[0:2, 6:7], 127.0,
                                            st[0:2, 8:9], ALU.mult, ALU.mult))

    rXQ, lgXQ = AB16.alloc(NB * 2050, ["xq"])
    xq = AB16.t[:, rXQ[0]:rXQ[0] + NB * 2050]
    XQ0 = rXQ[0]
    rHQ, lgHQ = AB16.alloc(1024, ["hq"])
    hq = AB16.t[:, rHQ[0]:rHQ[0] + 1024]
    P.step("vector", [("hsq",), ("hg",)], [("hqi",), "BUFi32"],
           lambda e: e.tensor_scalar(i32buf[0:2, 0:1024], hg[0:2, :],
                                     st[0:2, 7:8], None, ALU.mult))
    P.step("vector", [("hqi",)], [("hq",)] + list(lgHQ) + list(lgHG),
           lambda e: e.tensor_copy(hq[0:2, :], i32buf[0:2, 0:1024]))
    AF32.freea(rHG, [("hg",)])

    macc = wtile[:, 0:2048]
    t0 = wtile[:, 2048:4096]
    t0b = rowA[:, 0:2048]
    P.step("scalar", [("X", 0), "BUFwt"],
           [("mt", 6), "BUFwt"] + list(lgWS) + list(lgRB),
           lambda e: e.activation(macc, X[:, 0:2048], AF.Abs))
    for j in range(1, NB):
        buf = t0 if j % 2 else t0b
        bk = "BUFwt" if j % 2 else "BUFrowA"
        P.step("scalar", [("X", j)], [("mta", j % 2), bk],
               lambda e, j=j, buf=buf: e.activation(
                   buf, X[:, j * TOK:j * TOK + 2048], AF.Abs))
        P.step("vector", [("mta", j % 2), ("mt", 6), bk], [("mt", 6), ("mtt", j % 2)],
               lambda e, buf=buf: e.tensor_max(macc, macc, buf))
    for a in range(16):
        P.step("tensor", [("mt", 6), "idf"], [("PS", 3)],
               lambda e, a=a: e.transpose(PS[3][:, 0:128],
                                          macc[:, a * 128:a * 128 + 128], idf[:]))
        P.step("vector", [("PS", 3)], [("amx", a), "BUFsq2"],
               lambda e, a=a: e.tensor_reduce(racc[:, 4 + a:5 + a],
                                              PS[3][:, 0:128], AX, ALU.max))
    P.step("tensor", [("amx", a) for a in range(16)] + ["idf"], [("PS", 3)],
           lambda e: e.transpose(PS[3][0:16, 0:128], racc[:, 4:20], idf[:]))
    P.step("vector", [("PS", 3), "BUFwt"], [("amxT",), "BUFwt"],
           lambda e: e.tensor_copy(t0[0:16, 0:128], PS[3][0:16, 0:128]))
    P.step("sync", [("amxT",), "BUFwt"], ["amxd"],
           lambda e: e.dma_start(out=Dm["row_bounce"][0:1, 0:2048],
                                 in_=t0[0:16, 0:128]), dma=True)
    P.step("sync", ["amxd", "BUFrowA"], [("amaxrow",), "BUFrowA"],
           lambda e: e.dma_start(out=rowA[0:1, 0:2048],
                                 in_=Dm["row_bounce"][0:1, 0:2048]), dma=True)
    P.step("vector", [("amaxrow",), "BUFrowA"], [("amaxrow",), "BUFrowA"],
           lambda e: e.tensor_scalar_max(rowA[0:1, 0:2048], rowA[0:1, 0:2048],
                                         1e-5))
    P.step("vector", [("amaxrow",), "BUFrowA"], [("srow",), "BUFrowA"],
           lambda e: e.reciprocal(rowA[0:1, 2048:4096], rowA[0:1, 0:2048]))
    P.step("vector", [("srow",), "BUFrowA"], [("srow",), "BUFrowA"],
           lambda e: e.tensor_scalar_mul(rowA[0:1, 2048:4096],
                                         rowA[0:1, 2048:4096], 127.0))
    P.step("vector", [("amaxrow",), ("wsc", "w1"), "BUFrowA"], [("s1core",), "BUFrowA"],
           lambda e: e.tensor_scalar(rowB[0:1, 1:2049], rowA[0:1, 0:2048],
                                     wqs[0:1, 197:198], None, ALU.mult))
    P.step("tensor", [("hamaxc",), "idf"], [("PS", 3)],
           lambda e: e.transpose(PS[3][0:1, 0:2], st[0:2, 5:6], idf[0:2, 0:2]))
    P.step("vector", [("PS", 3), ("wsc", "w1")], [("hs1row",)],
           lambda e: e.tensor_scalar(st[0:1, 10:12], PS[3][0:1, 0:2],
                                     wqs[0:1, 197:198], None, ALU.mult))
    P.step("vector", [("hs1row",)], [("s1halo",)],
           lambda e: e.tensor_copy(rowB[0:1, 0:1], st[0:1, 10:11]))
    P.step("vector", [("hs1row",)], [("s1halo2",)],
           lambda e: e.tensor_copy(rowB[0:1, 2049:2050], st[0:1, 11:12]))
    for tch in range(4):
        P.step("tensor", [("srow",), "onesrow", "BUFrowA"], [("PS", 2)],
               lambda e, tch=tch: e.matmul(
                   PS[2][:, 0:512], onesrow[0:1, 0:128],
                   rowA[0:1, 2048 + tch * 512:2048 + tch * 512 + 512],
                   start=True, stop=True))
        sbuf_ = t0 if tch % 2 else t0b
        sbk = "BUFwt" if tch % 2 else "BUFrowA"
        P.step("vector", [("PS", 2), sbk], [("sbc", tch), sbk],
               lambda e, sbuf_=sbuf_: e.tensor_copy(sbuf_[:, 0:512], PS[2][:, 0:512]))
        for j in range(NB):
            P.step("vector", [("sbc", tch), ("X", j), sbk], [("xqi", j, tch), "BUFi32"],
                   lambda e, j=j, tch=tch, sbuf_=sbuf_: e.tensor_mul(
                       i32buf[:, 0:512],
                       X[:, j * TOK + tch * 512:j * TOK + tch * 512 + 512],
                       sbuf_[:, 0:512]))
            P.step("vector", [("xqi", j, tch)],
                   [("xq", j, tch)] + (list(lgXQ) if (j, tch) == (0, 0) else []),
                   lambda e, j=j, tch=tch: e.tensor_copy(
                       xq[:, j * 2050 + 1 + tch * 512:j * 2050 + 1 + tch * 512 + 512],
                       i32buf[:, 0:512]))
    for j in range(NB):
        P.step("tensor", [("hq",), "idb"], [("PS", 3)],
               lambda e, j=j: e.transpose(
                   PS[3].bitcast(BF16)[0:128, 0:2], hq[0:2, j * 128:(j + 1) * 128],
                   idb[0:2, 0:2]))
        P.step("vector", [("PS", 3)], [("xqh", j)],
               lambda e, j=j: e.tensor_copy(
                   xq[:, j * 2050:j * 2050 + 2050:2049],
                   PS[3].bitcast(BF16)[:, 0:2]))
    AB16.freea(rHQ, [("hq",)])
    AF32.freea(rWS, [("wqt", w, p) for w in ("w1", "w2") for p in (0, 1)] +
               ["BUFwt", "BUFrowA", ("amaxrow",), ("srow",), ("mt", 6),
                ("mta", 0), ("mta", 1), ("mtt", 0), ("mtt", 1), ("amxT",)] +
               [("sbc", tc) for tc in range(4)])

    XQKEYS = ([("xq", j, tc) for j in range(NB) for tc in range(4)] +
              [("xqh", j) for j in range(NB)])
    rH, lgH = AF32.alloc(NBH * 258, ["hext"])
    hext = AF32.t[:, rH[0]:rH[0] + NBH * 258]
    rsm, lgsm = AF32.alloc(1312, ["smalls"])
    s1bc = AF32.t[:, rsm[0]:rsm[0] + 258]
    s2bc = AF32.t[:, rsm[0] + 258:rsm[0] + 514]
    s2obc = AF32.t[:, rsm[0] + 514:rsm[0] + 770]
    tmpev = AF32.t[:, rsm[0] + 770:rsm[0] + 1282]
    sgbuf = [i32buf.bitcast(F32)[:, 512:768], i32buf.bitcast(F32)[:, 768:1024]]
    rCV, lgCV = AB16.alloc(NBH * 256, ["cvo"])
    cvo = AB16.t[:, rCV[0]:rCV[0] + NBH * 256]
    xq2 = cvo
    rw1s, lgw1s = AB16.alloc(2 * NB * 128, ["w1s"])
    w1s = [AB16.t[:, rw1s[0] + i * NB * 128:rw1s[0] + (i + 1) * NB * 128]
           for i in range(2)]
    rw2s, lgw2s = AB16.alloc(2 * 16 * 128, ["w2s"])
    w2s = [AB16.t[:, rw2s[0] + i * 16 * 128:rw2s[0] + (i + 1) * 16 * 128]
           for i in range(2)]
    first = dict(h=True, c=True, q2=True, w1=True, w2=True, sm=True)
    CW0, CW1, CW2 = OF["CW"], OF["CW"] + NBH, OF["CW"] + 2 * NBH

    for t in range(8):
        off = t * 256
        P.step("tensor", [("s1core",), ("s1halo",), ("s1halo2",), "onesrow"],
               [("PS", 2)],
               lambda e, off=off: e.matmul(
                   PS[2][:, 0:258], onesrow[0:1, 0:128],
                   rowB[0:1, off:off + 258], start=True, stop=True))
        P.step("vector", [("PS", 2)],
               [("s1bc",)] + (list(lgsm) if first["sm"] else []),
               lambda e: e.tensor_copy(s1bc[:, 0:258], PS[2][:, 0:258]))
        first["sm"] = False
        for fb in range(NBH):
            bank = (0, 1)[fb % 2]
            P.step("gpsimd", [("wqdr", "w1"), ("w1c", fb % 2)],
                   [("w1l", fb % 2)] + (list(lgw1s) if first["w1"] else []),
                   lambda e, fb=fb: e.dma_start(
                       out=w1s[fb % 2].rearrange("p (j c) -> p j c", j=NB),
                       in_=Dm["w1q_dram"].rearrange("(j p) q -> p j q", p=128)
                       [:, :, fb * 128:(fb + 1) * 128]), dma=True)
            first["w1"] = False

            def mk1(t=t, fb=fb, bank=bank):
                def f(e):
                    for j in range(NB):
                        ins = e.matmul(
                            PS[bank][:, 0:258],
                            w1s[fb % 2][:, j * 128:(j + 1) * 128],
                            xq[:, j * 2050 + t * 256:j * 2050 + t * 256 + 258],
                            start=(j == 0), stop=(j == NB - 1))
                    return ins
                return f
            P.step("tensor", [("w1l", fb % 2)] + XQKEYS,
                   [("PS", bank), ("w1c", fb % 2)], mk1())
            P.step("vector", [("PS", bank), ("s1bc",)],
                   [("hx", fb)] + (list(lgH) if first["h"] else []),
                   lambda e, fb=fb, bank=bank: e.tensor_mul(
                       hext[:, fb * 258:fb * 258 + 258], PS[bank][:, 0:258],
                       s1bc[:, 0:258]))
            first["h"] = False
        if t == 7:
            dbg_dump(sh, [("hx", 0)], hext[:, 0:258], 3328, "hext0_t7")
            dbg_dump(sh, [("s1bc",)], s1bc[:, 0:258], 3600, "s1bc_t7")
            dbg_dump(sh, [("xq", 0, 3), ("xqh", 0)],
                     sh["AB16"].t[:, XQ0 + 1536:XQ0 + 2050], 3872, "xqtail")
        for fb in range(NBH):
            cs = slice(fb * 256, fb * 256 + 256)
            P.step("vector", [("hx", fb), "bb5", "bb4"],
                   [("cv", fb)] + (list(lgCV) if first["c"] else []),
                   lambda e, fb=fb, cs=cs: e.tensor_scalar(
                       cvo[:, cs], hext[:, fb * 258:fb * 258 + 256],
                       biasbuf[:, CW0 + fb:CW0 + fb + 1],
                       biasbuf[:, OF["CB"] + fb:OF["CB"] + fb + 1],
                       ALU.mult, ALU.add))
            first["c"] = False
            P.step("vector", [("cv", fb)], [("cv", fb)],
                   lambda e, fb=fb, cs=cs: e.scalar_tensor_tensor(
                       cvo[:, cs], hext[:, fb * 258 + 1:fb * 258 + 257],
                       biasbuf[:, CW1 + fb:CW1 + fb + 1], cvo[:, cs],
                       ALU.mult, ALU.add))
            P.step("vector", [("cv", fb)], [("cv", fb)],
                   lambda e, fb=fb, cs=cs: e.scalar_tensor_tensor(
                       cvo[:, cs], hext[:, fb * 258 + 2:fb * 258 + 258],
                       biasbuf[:, CW2 + fb:CW2 + fb + 1], cvo[:, cs],
                       ALU.mult, ALU.add))
            P.step("scalar", [("cv", fb), ("sgt", fb % 2)], [("sg", fb % 2), "BUFi32"],
                   lambda e, fb=fb, cs=cs: e.activation(
                       sgbuf[fb % 2], cvo[:, cs], AF.Sigmoid))
            P.step("vector", [("sg", fb % 2), ("cv", fb)],
                   [("cv", fb), ("sgt", fb % 2)],
                   lambda e, fb=fb, cs=cs: e.tensor_mul(
                       cvo[:, cs], cvo[:, cs], sgbuf[fb % 2]))
        mx = sh["sq2"][:, 0:256]
        mabs2 = [sh["sq2"][:, 256:512], sh["sq2"][:, 644:900]]
        P.step("scalar", [("cv", 0), "BUFsq2"], [("mx",), "BUFsq2"],
               lambda e: e.activation(mx, cvo[:, 0:256], AF.Abs))
        for fb in range(1, NBH):
            P.step("scalar", [("cv", fb), ("mabs", fb % 2)], [("mabs", fb % 2)],
                   lambda e, fb=fb: e.activation(
                       mabs2[fb % 2], cvo[:, fb * 256:fb * 256 + 256], AF.Abs))
            P.step("vector", [("mabs", fb % 2), ("mx",)], [("mx",), ("mabs", fb % 2)],
                   lambda e, fb=fb: e.tensor_max(mx, mx, mabs2[fb % 2]))
        for a in range(2):
            P.step("tensor", [("mx",), "idf"], [("PS", 3)],
                   lambda e, a=a: e.transpose(PS[3][:, 0:128],
                                              mx[:, a * 128:a * 128 + 128], idf[:]))
            P.step("vector", [("PS", 3)], [("am2t", a), "BUFsq2"],
                   lambda e, a=a: e.tensor_reduce(sh["sq2"][:, 512 + a:513 + a],
                                                  PS[3][:, 0:128], AX, ALU.max))
        P.step("tensor", [("am2t", 0), ("am2t", 1), "idf"], [("PS", 3)],
               lambda e: e.transpose(PS[3][0:2, 0:128], sh["sq2"][:, 512:514],
                                     idf[:]))
        P.step("vector", [("PS", 3)], [("am2T",), "BUFsq2"],
               lambda e: e.tensor_copy(sh["sq2"][0:2, 516:644], PS[3][0:2, 0:128]))
        P.step("sync", [("am2T",)], ["am2d"],
               lambda e: e.dma_start(out=Dm["row_bounce"][0:1, 2048:2304],
                                     in_=sh["sq2"][0:2, 516:644]), dma=True)
        P.step("sync", ["am2d"], [("am2row",)],
               lambda e: e.dma_start(out=rowB[0:1, 2080:2336],
                                     in_=Dm["row_bounce"][0:1, 2048:2304]),
               dma=True)
        P.step("vector", [("am2row",)], [("am2row",)],
               lambda e: e.tensor_scalar_max(rowB[0:1, 2080:2336],
                                             rowB[0:1, 2080:2336], 1e-5))
        P.step("vector", [("am2row",)], [("s2row",)],
               lambda e: e.reciprocal(rowB[0:1, 2336:2592], rowB[0:1, 2080:2336]))
        P.step("vector", [("s2row",)], [("s2row",)],
               lambda e: e.tensor_scalar_mul(rowB[0:1, 2336:2592],
                                             rowB[0:1, 2336:2592], 127.0))
        P.step("vector", [("am2row",), ("wsc", "w2")], [("s2orow",)],
               lambda e: e.tensor_scalar(rowB[0:1, 2592:2848], rowB[0:1, 2080:2336],
                                         wqs[0:1, 198:199], None, ALU.mult))
        P.step("tensor", [("s2row",), "onesrow"], [("PS", 2)],
               lambda e: e.matmul(PS[2][:, 0:256], onesrow[0:1, 0:128],
                                  rowB[0:1, 2336:2592], start=True, stop=True))
        P.step("vector", [("PS", 2)], [("s2bc",)],
               lambda e: e.tensor_copy(s2bc[:, 0:256], PS[2][:, 0:256]))
        P.step("tensor", [("s2orow",), "onesrow"], [("PS", 2)],
               lambda e: e.matmul(PS[2][:, 0:256], onesrow[0:1, 0:128],
                                  rowB[0:1, 2592:2848], start=True, stop=True))
        P.step("vector", [("PS", 2)], [("s2obc",)],
               lambda e: e.tensor_copy(s2obc[:, 0:256], PS[2][:, 0:256]))
        if t == 7:
            dbg_dump(sh, [("am2row",)], rowB[0:1, 2080:2336], 0, "am2row")
            dbg_dump(sh, [("s2orow",)], rowB[0:1, 2592:2848], 256, "s2orow")
            dbg_dump(sh, [("s2bc",)], s2bc[:, 0:256], 512, "s2bc")
        for fb in range(NBH):
            cs = slice(fb * 256, fb * 256 + 256)
            P.step("vector", [("cv", fb), ("s2bc",)], [("q2i", fb), "BUFi32"],
                   lambda e, fb=fb, cs=cs: e.tensor_tensor(
                       i32buf[:, 0:256], cvo[:, cs], s2bc[:, 0:256], ALU.mult))
            P.step("vector", [("q2i", fb)], [("cv", fb)],
                   lambda e, fb=fb: e.tensor_copy(
                       xq2[:, fb * 256:fb * 256 + 256], i32buf[:, 0:256]))
        if t == 7:
            dbg_dump(sh, [("cv", fb) for fb in range(NBH)],
                     cvo[:, 0:2048], 1024, "xq2_t0")
        for m in range(NB):
            bank = (4, 5)[m % 2]
            for hf in range(2):
                P.step("sync", [("wqdr", "w2"), ("w2c", hf)],
                       [("w2l", hf)] + (list(lgw2s) if first["w2"] else []),
                       lambda e, m=m, hf=hf: e.dma_start(
                           out=w2s[hf].rearrange("p (k c) -> p k c", k=16),
                           in_=Dm["w2q_dram"].rearrange("(k p) q -> p k q", p=128)
                           [:, hf * 16:hf * 16 + 16, m * 128:(m + 1) * 128]),
                       dma=True)
                first["w2"] = False

                def mk2(m=m, bank=bank, hf=hf):
                    def f(e):
                        for kk in range(16):
                            ins = e.matmul(
                                PS[bank][:, 0:256],
                                w2s[hf][:, kk * 128:(kk + 1) * 128],
                                xq2[:, (hf * 16 + kk) * 256:(hf * 16 + kk) * 256 + 256],
                                start=(hf == 0 and kk == 0),
                                stop=(hf == 1 and kk == 15))
                        return ins
                    return f
                P.step("tensor", [("w2l", hf)] + [("cv", fb) for fb in range(NBH)],
                       [("PS", bank), ("w2c", hf)], mk2())
            P.step("vector", [("PS", bank), ("s2obc",)], [("tmpev", m % 2)],
                   lambda e, bank=bank, m=m: e.tensor_mul(
                       tmpev[:, (m % 2) * 256:(m % 2) * 256 + 256],
                       PS[bank][:, 0:256], s2obc[:, 0:256]))
            if t == 7 and m == 0:
                dbg_dump(sh, [("tmpev", m % 2)],
                         tmpev[:, 0:256], 3072, "ffnout_m0t0")
            P.step("vector", [("tmpev", m % 2), ("X", m)], [("X", m)],
                   lambda e, m=m, t=t: e.tensor_add(
                       X[:, m * TOK + t * 256:m * TOK + t * 256 + 256],
                       X[:, m * TOK + t * 256:m * TOK + t * 256 + 256],
                       tmpev[:, (m % 2) * 256:(m % 2) * 256 + 256]))

    AF32.freea(rH, [("hx", fb) for fb in range(NBH)])
    AF32.freea(rsm, [("s1bc",), ("s2bc",), ("s2obc",), ("tmpev", 0), ("tmpev", 1)])
    AF32.freea(rRB, [("s1core",), ("s1halo",), ("s1halo2",), ("am2row",),
                     ("s2row",), ("s2orow",), ("wsc", "w1"), ("wsc", "w2")])
    AB16.freea(rCV, [("cv", fb) for fb in range(NBH)] + [("mx",)])
    AB16.freea(rw1s, [("w1l", 0), ("w1l", 1), ("w1c", 0), ("w1c", 1)])
    AB16.freea(rw2s, [("w2l", 0), ("w2l", 1), ("w2c", 0), ("w2c", 1)])
    AB16.freea(rXQ, [("xq", j, tc) for j in range(NB) for tc in range(4)] +
               [("xqh", j) for j in range(NB)])

    featln(sh, "ln2")





def emit_moe(sh):
    P, nc, I, Dm, X, PS = sh["P"], sh["nc"], sh["I"], sh["Dm"], sh["X"], sh["PS"]
    AF32, AB16, sq2 = sh["AF32"], sh["AB16"], sh["sq2"]
    idb, idf, consts, onesrow = sh["idb"], sh["idf"], sh["consts"], sh["onesrow"]
    i32buf, biasbuf, OF = sh["i32buf"], sh["biasbuf"], sh["OF"]
    NIT = 36
    free_xb(sh)

    if not DBG.get("no_wcast"):
        rWB_, lgWB_ = AB16.alloc(4096, ["mwcb"])
        wcb = [AB16.t[:, rWB_[0]:rWB_[0] + 2048],
               AB16.t[:, rWB_[0] + 2048:rWB_[0] + 4096]]
        wjobs = []
        for t in range(8):
            for h in range(2):
                wjobs.append(("w1e_dram",
                              I["W1e"][t * 128:(t + 1) * 128,
                                       h * 2048:h * 2048 + 2048],
                              Dm["w1e_dram"][t * 128:(t + 1) * 128,
                                             h * 2048:h * 2048 + 2048], 2048))
        for t in range(32):
            wjobs.append(("w2e_dram",
                          I["W2e"][t * 128:(t + 1) * 128, :],
                          Dm["w2e_dram"][t * 128:(t + 1) * 128, :], 1024))
        for i, (dk, sap, dap, w) in enumerate(wjobs):
            sl = i % 2
            wk = [("mwcb", sl)] + (list(lgWB_) if i == 0 else [])
            P.step("gpsimd", [], wk,
                   lambda e, sap=sap, sl=sl, w=w: e.dma_start(
                       out=wcb[sl][:, 0:w], in_=sap), dma=True)
            P.step("sync", [("mwcb", sl)], [dk],
                   lambda e, dap=dap, sl=sl, w=w: e.dma_start(
                       out=dap, in_=wcb[sl][:, 0:w]), dma=True)
        AB16.freea(rWB_, [("mwcb", 0), ("mwcb", 1)])

    rSM, lgSM = AF32.alloc(1024, ["msm"])
    s0 = rSM[0]
    SM = AF32.t
    col = lambda c: SM[:, s0 + c:s0 + c + 1]
    c64 = lambda c: SM[0:64, s0 + c:s0 + c + 1]
    rBG, lgBG = AF32.alloc(8192, ["mbg"])
    b0 = rBG[0]
    R = [AF32.t[:, b0 + i * 2048:b0 + (i + 1) * 2048] for i in range(4)]
    rOB, lgOB = AF32.alloc(2048, ["mob"])
    obuf = [AF32.t[:, rOB[0]:rOB[0] + 1024], AF32.t[:, rOB[0] + 1024:rOB[0] + 2048]]
    rBB, lgBB = AF32.alloc(1024, ["mb2"])
    b2bc = AF32.t[:, rBB[0]:rBB[0] + 1024]

    P.step("vector", [], [("mzb",)] + list(lgOB),
           lambda e: e.memset(obuf[0][:, :], 0.0))
    P.step("scalar", [("mzb",)], ["moe_pad"],
           lambda e: e.dma_start(out=Dm["moe_ag"][NTOKG:NTOKG + 128, :],
                                 in_=obuf[0][:, :]), dma=True)

    for nm, c, src, npart in (("megr", 256, "egrp", 128), ("me8", 264, "e8to128", 8),
                              ("mltm", 392, "ltmask", 128), ("mws", 576, "Ws_in", None),
                              ("mtok", 96, "tokid", 128), ("mbs", 12, "bs_in", 128),
                              ("mebs", 112, "ebase", 128)):
        if nm == "mws":
            P.step("sync", [], [nm] + (list(lgSM) if nm == "megr" else []),
                   lambda e: e.dma_start(
                       out=SM[:, s0 + 576:s0 + 640].rearrange(
                           "p (j ee) -> p j ee", j=NB),
                       in_=I["Ws_in"].rearrange("(j p) ee -> p j ee", p=128)),
                   dma=True)
        else:
            w = src
            nc_ = {"egrp": 8, "e8to128": 128, "ltmask": 128, "tokid": 16,
                   "bs_in": 1, "ebase": 1}[src]
            P.step("sync", [], [nm] + (list(lgSM) if nm == "megr" else []),
                   lambda e, c=c, w=w, npart=npart, nc_=nc_: e.dma_start(
                       out=SM[0:npart, s0 + c:s0 + c + nc_], in_=I[w]), dma=True)

    if not DBG.get("no_actexp"):
        rXT, lgXT = AB16.alloc(2048, ["mxtb"])
        xtb = [AB16.t[:, rXT[0]:rXT[0] + 1024], AB16.t[:, rXT[0] + 1024:rXT[0] + 2048]]
        for a in range(16):
            sl = a % 2
            for hf in range(2):
                pb = 4 + hf

                def mkt(a=a, hf=hf, pb=pb):
                    def f(e):
                        for j in range(4 * hf, 4 * hf + 4):
                            ins = e.transpose(
                                PS[pb][:, (j - 4 * hf) * 128:(j - 4 * hf) * 128 + 128],
                                X[:, j * TOK + a * 128:j * TOK + a * 128 + 128],
                                idf[:])
                        return ins
                    return f
                P.step("tensor", [("X", j) for j in range(4 * hf, 4 * hf + 4)] +
                       ["idf"], [("PS", pb)], mkt())
                wk = [("mxtb", sl)] + (list(lgXT) if a == 0 and hf == 0 else [])
                P.step("vector", [("PS", pb)], wk,
                       lambda e, sl=sl, hf=hf, pb=pb: e.tensor_copy(
                           xtb[sl][:, hf * 512:hf * 512 + 512], PS[pb][:, 0:512]))
            P.step("sync", [("mxtb", sl)], [("mab", a)],
                   lambda e, a=a, sl=sl: e.dma_start(
                       out=Dm["act_bounce"][a * 128:(a + 1) * 128, :],
                       in_=xtb[sl][:, :]), dma=True)
        P.step("gpsimd", [("mab", a) for a in range(16)], ["act_ag"],
               lambda e: e.collective_compute(
                   "AllGather", ALU.bypass,
                   replica_groups=[[0, 1, 2, 3, 4, 5, 6, 7]],
                   ins=[Dm["act_bounce"][:, :]], outs=[Dm["act_ag"][:, :]]),
               own_sem="cc_act")
    if not DBG.get("no_actexp"):
        AB16.freea(rXT, [("mxtb", 0), ("mxtb", 1)])

    probsT = R[1]
    drow = R[2][0:1, :]
    rrow = R[3][0:1, :]
    for tch in range(4):
        def mkl(tch=tch):
            def f(e):
                for j in range(NB):
                    ins = e.matmul(
                        PS[0][0:8, 0:512],
                        SM[:, s0 + 576 + j * 8:s0 + 576 + j * 8 + 8],
                        X[:, j * TOK + tch * 512:j * TOK + tch * 512 + 512],
                        start=(j == 0), stop=(j == NB - 1))
                return ins
            return f
        P.step("tensor", ["mws"] + [("X", j) for j in range(NB)],
               [("PS", 0)], mkl())
        wk = [("mpt0", tch)] + (list(lgBG) if tch == 0 else [])
        P.step("vector", [("PS", 0)], wk,
               lambda e, tch=tch: e.tensor_copy(
                   probsT[0:8, tch * 512:tch * 512 + 512], PS[0][0:8, 0:512]))
        P.step("scalar", [("mpt0", tch), "mbs"], [("mpt", tch)],
               lambda e, tch=tch: e.activation(
                   probsT[0:8, tch * 512:tch * 512 + 512],
                   probsT[0:8, tch * 512:tch * 512 + 512],
                   AF.Exp, bias=SM[0:8, s0 + 12:s0 + 13]))
        P.step("tensor", [("mpt", tch), "consts"], [("PS", 1)],
               lambda e, tch=tch: e.matmul(
                   PS[1][0:1, 0:512], consts[0:8, 0:1],
                   probsT[0:8, tch * 512:tch * 512 + 512],
                   start=True, stop=True))
        P.step("vector", [("PS", 1)], [("mdr", tch)],
               lambda e, tch=tch: e.tensor_copy(
                   drow[:, tch * 512:tch * 512 + 512], PS[1][0:1, 0:512]))
    P.step("vector", [("mdr", t) for t in range(4)], [("mrr",)],
           lambda e: e.reciprocal(rrow[:, :], drow[:, :]))
    for tch in range(4):
        P.step("tensor", [("mrr",), "onesrow"], [("PS", 1)],
               lambda e, tch=tch: e.matmul(
                   PS[1][0:8, 0:512], onesrow[0:1, 0:8],
                   rrow[:, tch * 512:tch * 512 + 512], start=True, stop=True))
        P.step("vector", [("PS", 1), ("mpt", tch)], [("mpn", tch)],
               lambda e, tch=tch: e.tensor_mul(
                   probsT[0:8, tch * 512:tch * 512 + 512],
                   probsT[0:8, tch * 512:tch * 512 + 512], PS[1][0:8, 0:512]))
    P.step("sync", [("mpn", t) for t in range(4)], ["probs_bounce"],
           lambda e: e.dma_start(out=Dm["probs_bounce"][:, :],
                                 in_=probsT[0:8, :]), dma=True)
    P.step("gpsimd", ["probs_bounce"], ["probs_ag"],
           lambda e: e.collective_compute(
               "AllGather", ALU.bypass,
               replica_groups=[[0, 1, 2, 3, 4, 5, 6, 7]],
               ins=[Dm["probs_bounce"][:, :]], outs=[Dm["probs_ag"][:, :]]),
           own_sem="cc_probs")
    PL = R[0]
    P.step("sync", [], ["mpli", "BUFi32"],
           lambda e: e.dma_start(out=i32buf.bitcast(U32)[:, 160:161],
                                 in_=I["plidx"]), dma=True)
    P.step("sync", [], ["mmyr", "BUFi32"],
           lambda e: e.dma_start(out=i32buf.bitcast(U32)[:, 161:162],
                                 in_=I["myrows"]), dma=True)
    P.step("gpsimd", ["probs_ag", "mpli", "BUFi32"], [("mPL",)],
           lambda e: e.indirect_dma_start(
               out=PL[0:64, :], out_offset=None,
               in_=Dm["probs_ag"][:, :],
               in_offset=IndirectOffsetOnAxis(
                   i32buf.bitcast(U32)[0:64, 160:161], 0)), dma=True)

    dbg_dump(sh, [("mPL",), ("mR0",)], PL[0:64, :], 0, "mPLd")
    if DBG.get("moe_stage") == 1:
        AF32.freea(rBG, [("mPL",), ("mR0",)])
        emit_store_out(sh)
        return

    SS = R[1]
    P.step("vector", [], [("mlo",)] + list(lgSM),
           lambda e: e.memset(c64(0), 0.0))
    P.step("vector", [], [("mhi",)] + list(lgSM),
           lambda e: e.memset(c64(1), 1.0))
    for it in range(NIT):
        P.step("vector", [("mlo",), ("mhi",)], [("mmid",)],
               lambda e: e.tensor_add(c64(2), c64(0), c64(1)))
        P.step("vector", [("mmid",)], [("mmid",)],
               lambda e: e.tensor_scalar_mul(c64(2), c64(2), 0.5))
        P.step("vector", [("mmid",)], [("mnmid",)],
               lambda e: e.tensor_scalar_mul(c64(3), c64(2), -1.0))
        P.step("scalar", [("mPL",), ("mnmid",)], [("mS1",), ("mc1",)],
               lambda e: e.activation(SS[0:64, :], PL[0:64, :], AF.Sign,
                                      bias=c64(3), accum_out=c64(4)))
        P.step("scalar", [("mS1",)], [("mS2",), ("mc2",)],
               lambda e: e.activation(SS[0:64, :], SS[0:64, :], AF.Abs,
                                      accum_out=c64(5)))
        P.step("vector", [("mc1",), ("mc2",)], [("mcnt",)],
               lambda e: e.tensor_add(c64(6), c64(4), c64(5)))
        P.step("vector", [("mcnt",)], [("mcnt",)],
               lambda e: e.tensor_scalar_mul(c64(6), c64(6), 0.5))
        P.step("tensor", [("mcnt",), "megr"], [("PS", 2)],
               lambda e: e.matmul(PS[2][0:8, 0:1],
                                  SM[0:64, s0 + 256:s0 + 264], c64(6),
                                  start=True, stop=True))
        P.step("vector", [("PS", 2)], [("mg8",)],
               lambda e: e.tensor_copy(SM[0:8, s0 + 9:s0 + 10],
                                       PS[2][0:8, 0:1]))
        P.step("tensor", [("mg8",), "me8"], [("PS", 3)],
               lambda e: e.matmul(PS[3][0:64, 0:1],
                                  SM[0:8, s0 + 264:s0 + 328],
                                  SM[0:8, s0 + 9:s0 + 10],
                                  start=True, stop=True))
        P.step("vector", [("PS", 3)], [("mg64",)],
               lambda e: e.tensor_copy(c64(10), PS[3][0:64, 0:1]))
        P.step("vector", [("mg64",)], [("mfl",)],
               lambda e: e.tensor_scalar(c64(7), c64(10), 2047.5, None,
                                         ALU.is_ge))
        P.step("vector", [("mmid",), ("mlo",)], [("mt1",)],
               lambda e: e.tensor_sub(c64(8), c64(2), c64(0)))
        P.step("vector", [("mt1",), ("mfl",)], [("mt1b",)],
               lambda e: e.tensor_mul(c64(8), c64(8), c64(7)))
        P.step("vector", [("mt1b",), ("mlo",)], [("mlo",)],
               lambda e: e.tensor_add(c64(0), c64(0), c64(8)))
        P.step("vector", [("mhi",), ("mmid",)], [("mt2",)],
               lambda e: e.tensor_sub(c64(9), c64(1), c64(2)))
        P.step("vector", [("mt2",), ("mfl",)], [("mt2b",)],
               lambda e: e.tensor_mul(c64(9), c64(9), c64(7)))
        P.step("vector", [("mt2b",), ("mmid",)], [("mhi",)],
               lambda e: e.tensor_add(c64(1), c64(2), c64(9)))

    if DBG.get("moe_stage") == 2:
        dbg_dump(sh, [("mhi",), ("mlo",)], SM[:, s0:s0 + 16], 6144, "msmS2")
        AF32.freea(rBG, [("mPL",), ("mR0",)])
        emit_store_out(sh)
        return

    mgt, meq, cumq = R[1], R[2], R[3]
    P.step("vector", [("mPL",), ("mhi",), ("mS2",)], [("mgt",)],
           lambda e: e.tensor_scalar(mgt[0:64, :], PL[0:64, :], c64(1), None,
                                     ALU.is_gt))
    P.step("vector", [("mPL",), ("mhi",), ("mrr",)], [("mge",)],
           lambda e: e.tensor_scalar(meq[0:64, :], PL[0:64, :], c64(1), None,
                                     ALU.is_ge))
    P.step("vector", [("mge",), ("mgt",)], [("meq",)],
           lambda e: e.tensor_sub(meq[0:64, :], meq[0:64, :], mgt[0:64, :]))
    P.step("vector", [("mgt",)], [("mgr",)],
           lambda e: e.tensor_reduce(c64(4), mgt[0:64, :], AX, ALU.add))
    P.step("tensor", [("mgr",), "megr"], [("PS", 2)],
           lambda e: e.matmul(PS[2][0:8, 0:1], SM[0:64, s0 + 256:s0 + 264],
                              c64(4), start=True, stop=True))
    P.step("vector", [("PS", 2)], [("mg8b",)],
           lambda e: e.tensor_copy(SM[0:8, s0 + 9:s0 + 10], PS[2][0:8, 0:1]))
    P.step("tensor", [("mg8b",), "me8"], [("PS", 3)],
           lambda e: e.matmul(PS[3][0:64, 0:1], SM[0:8, s0 + 264:s0 + 328],
                              SM[0:8, s0 + 9:s0 + 10], start=True, stop=True))
    P.step("vector", [("PS", 3)], [("mG64",)],
           lambda e: e.tensor_copy(c64(10), PS[3][0:64, 0:1]))
    P.step("vector", [("mG64",)], [("mCm1",)],
           lambda e: e.tensor_scalar_mul(c64(11), c64(10), -1.0))
    P.step("vector", [("mCm1",)], [("mCm1b",)],
           lambda e: e.tensor_scalar(c64(11), c64(11), 2047.0, None, ALU.add))
    P.step("vector", [("meq",)], [("mcq",)],
           lambda e: e.tensor_tensor_scan(cumq[0:64, :], meq[0:64, :],
                                          meq[0:64, :], 0.0, ALU.add,
                                          ALU.bypass))
    P.step("vector", [("mcq",)], [("mrt",)],
           lambda e: e.tensor_copy(c64(13), cumq[0:64, 2047:2048]))
    P.step("tensor", [("mrt",), "mltm"], [("PS", 2)],
           lambda e: e.matmul(PS[2][0:64, 0:1], SM[0:64, s0 + 392:s0 + 456],
                              c64(13), start=True, stop=True))
    P.step("vector", [("PS", 2)], [("moff",)],
           lambda e: e.tensor_copy(c64(14), PS[2][0:64, 0:1]))
    P.step("vector", [("mcq",), ("meq",)], [("mq1",)],
           lambda e: e.tensor_sub(cumq[0:64, :], cumq[0:64, :], meq[0:64, :]))
    P.step("vector", [("mq1",), ("moff",)], [("mq2",)],
           lambda e: e.tensor_scalar(cumq[0:64, :], cumq[0:64, :], c64(14),
                                     None, ALU.add))
    P.step("vector", [("mq2",), ("mCm1b",)], [("mcond",)],
           lambda e: e.tensor_scalar(cumq[0:64, :], cumq[0:64, :], c64(11),
                                     None, ALU.is_le))
    P.step("vector", [("mcond",), ("meq",)], [("mseq",)],
           lambda e: e.tensor_mul(meq[0:64, :], meq[0:64, :], cumq[0:64, :]))
    P.step("vector", [("mseq",), ("mgt",)], [("msel",)],
           lambda e: e.tensor_add(meq[0:64, :], meq[0:64, :], mgt[0:64, :]))
    sel = meq
    dbg_dump(sh, [("msel",)], sel[0:64, :], 2048, "mseld")
    P.step("vector", [("msel",), ("mPL",), ("mcond",)], [("mval",)],
           lambda e: e.tensor_mul(cumq[0:64, :], PL[0:64, :], sel[0:64, :]))
    val = cumq
    P.step("vector", [("msel",), ("mval",)], [("mscn",), ("mR0",)],
           lambda e: e.tensor_tensor_scan(PL[0:64, :], sel[0:64, :],
                                          sel[0:64, :], 0.0, ALU.add,
                                          ALU.bypass))
    scn = PL
    P.step("vector", [("mscn",)], [("mrt2",)],
           lambda e: e.tensor_copy(c64(13), scn[0:64, 2047:2048]))
    P.step("tensor", [("mrt2",), "mltm"], [("PS", 2)],
           lambda e: e.matmul(PS[2][0:64, 0:1], SM[0:64, s0 + 392:s0 + 456],
                              c64(13), start=True, stop=True))
    P.step("vector", [("PS", 2)], [("moff2",)],
           lambda e: e.tensor_copy(c64(14), PS[2][0:64, 0:1]))
    P.step("vector", [("mscn",), ("msel",)], [("mex1",), ("mR0",)],
           lambda e: e.tensor_sub(scn[0:64, :], scn[0:64, :], sel[0:64, :]))
    P.step("vector", [("mex1",), ("moff2",)], [("mex2",), ("mR0",)],
           lambda e: e.tensor_scalar(scn[0:64, :], scn[0:64, :], c64(14),
                                     None, ALU.add))
    dbg_dump(sh, [("mex2",), ("mR0",)], scn[0:64, :], 4096, "mexd")
    P.step("vector", [("mex2",), ("msel",), ("mgr",)], [("mgin",)],
           lambda e: e.tensor_add(mgt[0:64, :], scn[0:64, :], sel[0:64, :]))
    gin = mgt
    if DBG.get("moe_stage") == 3:
        dbg_dump(sh, [("mgin",)], gin[0:64, :], 6144, "mginS3")
        AF32.freea(rBG, [("mPL",), ("mR0",)])
        emit_store_out(sh)
        return

    P.step("sync", [("mgin",)], ["cum_dram"],
           lambda e: e.dma_start(
               out=Dm["cum_dram"][0:128, :].rearrange("(r q) c -> r (q c)",
                                                      q=16),
               in_=gin[0:8, :]), dma=True)
    P.step("vector", [("mgin",)], [("mcb",)],
           lambda e: e.tensor_copy(SM[0:8, s0 + 16:s0 + 32],
                                   gin[0:8, 127:2048:128]))
    P.step("sync", [("mcb",)], ["mcbd"],
           lambda e: e.dma_start(
               out=Dm["moe_row"][0:1, 0:128].rearrange("a (r q) -> (a r) q",
                                                       r=8),
               in_=SM[0:8, s0 + 16:s0 + 32]), dma=True)
    P.step("sync", ["mcbd"], [("mcbc",)],
           lambda e: e.dma_start(
               out=SM[:, s0 + 15:s0 + 16],
               in_=Dm["moe_row"][0:1, 0:128].rearrange("a (p q) -> (a p) q",
                                                       p=128)), dma=True)
    P.step("vector", [("mex2",), "mebs"], [("mrx1",), ("mR0",)],
           lambda e: e.tensor_scalar(scn[0:64, :], scn[0:64, :],
                                     SM[0:64, s0 + 112:s0 + 113], None,
                                     ALU.add))
    P.step("vector", [("mrx1",)], [("mrx2",), ("mR0",)],
           lambda e: e.tensor_scalar(scn[0:64, :], scn[0:64, :], -16384.0,
                                     None, ALU.add))
    P.step("vector", [("mrx2",), ("msel",)], [("mrx3",), ("mR0",)],
           lambda e: e.tensor_mul(scn[0:64, :], scn[0:64, :], sel[0:64, :]))
    P.step("vector", [("mrx3",)], [("mrx4",), ("mR0",)],
           lambda e: e.tensor_scalar(scn[0:64, :], scn[0:64, :], 16384.0,
                                     None, ALU.add))
    P.step("sync", [("mrx4",)], ["mrxd"],
           lambda e: e.dma_start(out=Dm["ridx_dram"][0:64, :],
                                 in_=scn[0:64, :]), dma=True)
    P.step("gpsimd", ["mrxd", "mmyr", "BUFi32"], [("mrst",), ("mgin",)],
           lambda e: e.indirect_dma_start(
               out=gin[0:8, :], out_offset=None,
               in_=Dm["ridx_dram"][0:64, :],
               in_offset=IndirectOffsetOnAxis(
                   i32buf.bitcast(U32)[0:8, 161:162], 0)), dma=True)
    def mkrt():
        def f_(e):
            for a in range(16):
                ins = e.transpose(PS[2][:, a * 8:a * 8 + 8],
                                  gin[0:8, a * 128:(a + 1) * 128],
                                  idf[0:8, 0:8])
            return ins
        return f_
    P.step("tensor", [("mrst",), "idf"], [("PS", 2)], mkrt())
    P.step("vector", [("PS", 2)], [("mrxm",)],
           lambda e: e.tensor_copy(SM[:, s0 + 128:s0 + 256],
                                   PS[2][:, 0:128]))
    P.step("vector", [("mrxm",)], [("mridxi",), "BUFi32"],
           lambda e: e.tensor_copy(i32buf[:, 0:128],
                                   SM[:, s0 + 128:s0 + 256]))
    P.step("sync", [("mval",)], ["evals"],
           lambda e: e.dma_start(
               out=Dm["evals_dram"].rearrange("(r t) c -> r (t c)", r=8),
               in_=val[0:8, :]), dma=True)
    P.step("sync", [("mrx3",)], [("iotar",), ("msel",)],
           lambda e: e.dma_start(out=R[2][0:1, 0:2048], in_=I["iota_row"]),
           dma=True)
    for tch in range(4):
        P.step("tensor", [("iotar",), "onesrow"], [("PS", 1)],
               lambda e, tch=tch: e.matmul(
                   PS[1][:, 0:512], onesrow[0:1, 0:128],
                   R[2][0:1, tch * 512:tch * 512 + 512],
                   start=True, stop=True))
        P.step("vector", [("PS", 1), ("mrxm",)],
               [("mpm", tch), ("mrst",), ("mgin",)],
               lambda e, tch=tch: e.tensor_copy(
                   R[1][:, tch * 512:tch * 512 + 512], PS[1][:, 0:512]))
    P.step("vector", [("mpm", t) for t in range(4)] + [("mcbc",)], [("mcmp",)],
           lambda e: e.tensor_scalar(R[1][:, :], R[1][:, :],
                                     SM[:, s0 + 15:s0 + 16], None, ALU.is_ge))
    for tch in range(4):
        P.step("tensor", [("mcmp",), "consts"], [("PS", 0)],
               lambda e, tch=tch: e.matmul(
                   PS[0][0:1, 0:512], consts[:, 0:1],
                   R[1][:, tch * 512:tch * 512 + 512], start=True, stop=True))
        P.step("vector", [("PS", 0), ("mrxd",)], [("mnf", tch), ("mrx4",)],
               lambda e, tch=tch: e.tensor_copy(
                   R[0][0:1, tch * 512:tch * 512 + 512], PS[0][0:1, 0:512]))
    def mknt():
        def f_(e):
            for f in range(16):
                ins = e.transpose(PS[3][:, f:f + 1],
                                  R[0][0:1, f * 128:(f + 1) * 128],
                                  idf[0:1, 0:1])
            return ins
        return f_
    P.step("tensor", [("mnf", t) for t in range(4)] + ["idf"], [("PS", 3)],
           mknt())
    P.step("vector", [("PS", 3)], [("mnfm",)],
           lambda e: e.tensor_copy(SM[:, s0 + 32:s0 + 48], PS[3][:, 0:16]))
    P.step("vector", [("mnfm",)], [("mnfi",), "BUFi32"],
           lambda e: e.tensor_copy(i32buf[:, 176:192], SM[:, s0 + 32:s0 + 48]))
    gb = [SM[:, s0 + 128:s0 + 256], SM[:, s0 + 640:s0 + 768]]
    for f in range(16):
        sl = f % 2
        P.step("gpsimd", [("mnfi",), "cum_dram", "BUFi32", ("mridxi",)],
               [("mgb", sl)],
               lambda e, f=f, sl=sl: e.indirect_dma_start(
                   out=gb[sl][:, :], out_offset=None,
                   in_=Dm["cum_dram"][0:128, :],
                   in_offset=IndirectOffsetOnAxis(
                       i32buf.bitcast(U32)[:, 176 + f:177 + f], 0)), dma=True)
        P.step("vector", [("mgb", sl), "mtok"], [("mgb", sl)],
               lambda e, f=f, sl=sl: e.tensor_scalar(
                   gb[sl][:, :], gb[sl][:, :], SM[:, s0 + 96 + f:s0 + 97 + f],
                   None, ALU.is_le))
        P.step("vector", [("mgb", sl)], [("mpc", f)],
               lambda e, f=f, sl=sl: e.tensor_reduce(
                   SM[:, s0 + 48 + f:s0 + 49 + f], gb[sl][:, :], AX, ALU.add))
    P.step("vector", [("mpc", f) for f in range(16)] + [("mnfm",)], [("midx",)],
           lambda e: e.scalar_tensor_tensor(
               SM[:, s0 + 64:s0 + 80], SM[:, s0 + 32:s0 + 48], 128.0,
               SM[:, s0 + 48:s0 + 64], ALU.mult, ALU.add))
    P.step("vector", [("midx",)], [("midxi",), "BUFi32"],
           lambda e: e.tensor_copy(i32buf[:, 208:224], SM[:, s0 + 64:s0 + 80]))
    for f in range(16):
        P.step("gpsimd", [("midxi",), "evals", "BUFi32"], [("mvc", f)],
               lambda e, f=f: e.indirect_dma_start(
                   out=SM[:, s0 + 80 + f:s0 + 81 + f], out_offset=None,
                   in_=Dm["evals_dram"][:, :],
                   in_offset=IndirectOffsetOnAxis(
                       i32buf.bitcast(U32)[:, 208 + f:209 + f], 0)), dma=True)

    dbg_dump(sh, [("mhi",), ("mG64",), ("mCm1b",)], SM[:, s0:s0 + 16],
             6144, "msm")
    dbg_dump(sh, [("mnfm",), ("mpc", 15), ("midx",), ("mvc", 15)],
             SM[:, s0 + 32:s0 + 112], 6160, "msm2")
    dbg_dump(sh, [("mridxi",), ("midxi",), ("mnfi",)],
             i32buf.bitcast(F32)[:, 0:224], 6240, "mi32")

    if DBG.get("moe_stage") == 4:
        AF32.freea(rBG, [("mPL",), ("mR0",)])
        emit_store_out(sh)
        return

    rXG, lgXG = AB16.alloc(2048, ["mxg"])
    xg = [AB16.t[:, rXG[0]:rXG[0] + 1024], AB16.t[:, rXG[0] + 1024:rXG[0] + 2048]]
    rXS, lgXS = AB16.alloc(2048, ["mxs"])
    xts = [AB16.t[:, rXS[0]:rXS[0] + 1024], AB16.t[:, rXS[0] + 1024:rXS[0] + 2048]]
    for f in range(16):
        sl = f % 2
        wk = [("mxg", sl)] + (list(lgXG) if f == 0 else [])
        P.step("gpsimd", [("midxi",), "act_ag", "BUFi32"], wk,
               lambda e, f=f, sl=sl: e.indirect_dma_start(
                   out=xg[sl][:, :], out_offset=None,
                   in_=Dm["act_ag"][:, :],
                   in_offset=IndirectOffsetOnAxis(
                       i32buf.bitcast(U32)[:, 208 + f:209 + f], 0)), dma=True)
        for hf in range(2):
            pb = 4 + hf

            def mkx(sl=sl, hf=hf, pb=pb):
                def f_(e):
                    pv = PS[pb].bitcast(BF16)
                    for j in range(4 * hf, 4 * hf + 4):
                        ins = e.transpose(
                            pv[:, (j - 4 * hf) * 128:(j - 4 * hf) * 128 + 128],
                            xg[sl][:, j * 128:(j + 1) * 128], idb[:])
                    return ins
                return f_
            P.step("tensor", [("mxg", sl), "idb"], [("PS", pb)], mkx())
            wk = [("mxs", sl)] + (list(lgXS) if f == 0 and hf == 0 else [])
            P.step("vector", [("PS", pb)], wk,
                   lambda e, sl=sl, hf=hf, pb=pb: e.tensor_copy(
                       xts[sl][:, hf * 512:hf * 512 + 512],
                       PS[pb].bitcast(BF16)[:, 0:512]))
        P.step("sync", [("mxs", sl)], [("mxid", f)],
               lambda e, f=f, sl=sl: e.dma_start(
                   out=Dm["xint_dram"].rearrange("(j p) t -> p j t", p=128)
                   [:, :, f * 128:(f + 1) * 128],
                   in_=xts[sl].rearrange("p (j t) -> p j t", j=NB)), dma=True)
    AB16.freea(rXG, [("mxg", 0), ("mxg", 1)])
    AB16.freea(rXS, [("mxs", 0), ("mxs", 1)])
    AF32.freea(rBG, [("mPL",), ("mS1",), ("mS2",), ("mgt",), ("mge",),
                     ("meq",), ("mseq",), ("msel",), ("mcq",), ("mq1",),
                     ("mq2",), ("mcond",), ("mval",), ("mscn",), ("mex1",),
                     ("mex2",), ("mgin",), ("mrst",), ("mrx1",), ("mrx2",),
                     ("mrx3",), ("mrx4",)] +
               [("mpt", t) for t in range(4)] + [("mpn", t) for t in range(4)] +
               [("mdr", t) for t in range(4)] + [("mrr",)] +
               [("mpm", t) for t in range(4)] + [("mcmp",)] +
               [("mnf", t) for t in range(4)])

    P.step("sync", [], [("mb2r",)] + list(lgBB),
           lambda e: e.dma_start(out=b2bc[0:1, 0:1024], in_=I["b2row"]),
           dma=True)
    for hf in range(2):
        P.step("tensor", [("mb2r",), "onesrow"], [("PS", 1)],
               lambda e, hf=hf: e.matmul(
                   PS[1][:, 0:512], onesrow[0:1, 0:128],
                   b2bc[0:1, hf * 512:hf * 512 + 512], start=True, stop=True))
        P.step("vector", [("PS", 1)], [("mb2", hf)],
               lambda e, hf=hf: e.tensor_copy(
                   b2bc[:, hf * 512:hf * 512 + 512], PS[1][:, 0:512]))
    rHT, lgHT = AB16.alloc(NBH * 512, ["mht"])
    hT = AB16.t[:, rHT[0]:rHT[0] + NBH * 512]
    rXL, lgXL = AB16.alloc(8192, ["mxl"])
    xtl = [AB16.t[:, rXL[0]:rXL[0] + 4096], AB16.t[:, rXL[0] + 4096:rXL[0] + 8192]]
    rW1, lgW1 = AB16.alloc(2048, ["mw1b"])
    w1b = [AB16.t[:, rW1[0]:rW1[0] + 1024], AB16.t[:, rW1[0] + 1024:rW1[0] + 2048]]
    rW2, lgW2 = AB16.alloc(2048, ["mw2b"])
    w2b = [AB16.t[:, rW2[0]:rW2[0] + 1024], AB16.t[:, rW2[0] + 1024:rW2[0] + 2048]]
    first = dict(xl=True, w1=True, w2=True, ht=True, ob=True)
    for tl in range(4):
        wk = [("mxtl", tl % 2)] + (list(lgXL) if first["xl"] else [])
        first["xl"] = False
        P.step("sync", [("mxid", f) for f in range(16)], wk,
               lambda e, tl=tl: e.dma_start(
                   out=xtl[tl % 2].rearrange("p (j t) -> p j t", j=NB),
                   in_=Dm["xint_dram"].rearrange("(j p) t -> p j t", p=128)
                   [:, :, tl * 512:(tl + 1) * 512]), dma=True)
        for fb in range(NBH):
            wk = [("mw1", fb % 2)] + (list(lgW1) if first["w1"] else [])
            first["w1"] = False
            P.step("sync", ["w1e_dram"], wk,
                   lambda e, fb=fb: e.dma_start(
                       out=w1b[fb % 2].rearrange("p (j c) -> p j c", j=NB),
                       in_=Dm["w1e_dram"].rearrange("(j p) c -> p j c", p=128)
                       [:, :, fb * 128:(fb + 1) * 128]), dma=True)

            def mk1(tl=tl, fb=fb):
                def f_(e):
                    for j in range(NB):
                        ins = e.matmul(
                            PS[fb % 2][:, 0:512],
                            w1b[fb % 2][:, j * 128:(j + 1) * 128],
                            xtl[tl % 2][:, j * 512:(j + 1) * 512],
                            start=(j == 0), stop=(j == NB - 1))
                    return ins
                return f_
            P.step("tensor", [("mw1", fb % 2), ("mxtl", tl % 2)],
                   [("PS", fb % 2)], mk1())
            wk = [("mhT", fb)] + (list(lgHT) if first["ht"] else [])
            first["ht"] = False
            P.step("scalar", [("PS", fb % 2), "bb6"], wk,
                   lambda e, fb=fb: e.activation(
                       hT[:, fb * 512:fb * 512 + 512], PS[fb % 2][:, 0:512],
                       AF.Gelu_apprx_tanh,
                       bias=biasbuf[:, OF["B1"] + fb:OF["B1"] + fb + 1]))
        for fb in range(NBH):
            wk = [("mw2", fb % 2)] + (list(lgW2) if first["w2"] else [])
            first["w2"] = False
            P.step("scalar", ["w2e_dram"], wk,
                   lambda e, fb=fb: e.dma_start(
                       out=w2b[fb % 2][:, :],
                       in_=Dm["w2e_dram"][fb * 128:(fb + 1) * 128, :]),
                   dma=True)

            def mk2(fb=fb):
                def f_(e):
                    for tc4 in range(4):
                        for dvh in range(2):
                            ins = e.matmul(
                                PS[tc4 * 2 + dvh][:, 0:512],
                                hT[:, fb * 512 + tc4 * 128:
                                   fb * 512 + tc4 * 128 + 128],
                                w2b[fb % 2][:, dvh * 512:dvh * 512 + 512],
                                start=(fb == 0), stop=(fb == NBH - 1))
                    return ins
                return f_
            P.step("tensor", [("mw2", fb % 2), ("mhT", fb)],
                   [("PS", b) for b in range(8)], mk2())
        for tc4 in range(4):
            osl = tc4 % 2
            for dvh in range(2):
                wk = [("mo", osl, dvh)] + \
                    ([("mzb",)] + list() if first["ob"] else [])
                first["ob"] = False
                P.step("vector", [("PS", tc4 * 2 + dvh), ("mb2", dvh)], wk,
                       lambda e, tc4=tc4, dvh=dvh, osl=osl: e.tensor_add(
                           obuf[osl][:, dvh * 512:dvh * 512 + 512],
                           PS[tc4 * 2 + dvh][:, 0:512],
                           b2bc[:, dvh * 512:dvh * 512 + 512]))
                P.step("vector", [("mo", osl, dvh), ("mvc", tl * 4 + tc4)],
                       [("mo", osl, dvh)],
                       lambda e, tl=tl, tc4=tc4, dvh=dvh, osl=osl:
                       e.tensor_scalar_mul(
                           obuf[osl][:, dvh * 512:dvh * 512 + 512],
                           obuf[osl][:, dvh * 512:dvh * 512 + 512],
                           SM[:, s0 + 80 + tl * 4 + tc4:
                              s0 + 81 + tl * 4 + tc4]))
            P.step("sync", [("mo", osl, 0), ("mo", osl, 1)],
                   [("mob", tl * 4 + tc4)],
                   lambda e, tl=tl, tc4=tc4, osl=osl: e.dma_start(
                       out=Dm["moe_bounce"][(tl * 4 + tc4) * 128:
                                            (tl * 4 + tc4) * 128 + 128, :],
                       in_=obuf[osl][:, :]), dma=True)
    AB16.freea(rHT, [("mhT", fb) for fb in range(NBH)])
    AB16.freea(rXL, [("mxtl", 0), ("mxtl", 1)])
    AB16.freea(rW1, [("mw1", 0), ("mw1", 1)])
    AB16.freea(rW2, [("mw2", 0), ("mw2", 1)])

    P.step("gpsimd", [("mob", a) for a in range(16)], ["moe_ag_c"],
           lambda e: e.collective_compute(
               "AllGather", ALU.bypass,
               replica_groups=[[0, 1, 2, 3, 4, 5, 6, 7]],
               ins=[Dm["moe_bounce"][:, :]], outs=[Dm["moe_ag"][0:NTOKG, :]]),
           own_sem="cc_moe")

    rXI, lgXI = AF32.alloc(2048, ["mxti"])
    xti = [AF32.t[:, rXI[0]:rXI[0] + 1024], AF32.t[:, rXI[0] + 1024:rXI[0] + 2048]]
    rLN, lgLN = AF32.alloc(1024, ["mlnt"])
    lnt = AF32.t[:, rLN[0]:rLN[0] + 1024]
    first_xi = [True]
    for a in range(16):
        sl = a % 2
        for hf in range(2):
            pb = 4 + hf

            def mko(a=a, hf=hf, pb=pb):
                def f_(e):
                    for j in range(4 * hf, 4 * hf + 4):
                        ins = e.transpose(
                            PS[pb][:, (j - 4 * hf) * 128:(j - 4 * hf) * 128 + 128],
                            X[:, j * TOK + a * 128:j * TOK + a * 128 + 128],
                            idf[:])
                    return ins
                return f_
            P.step("tensor", [("X", j) for j in range(4 * hf, 4 * hf + 4)] +
                   ["idf"], [("PS", pb)], mko())
            wk = [("mxt", sl)] + (list(lgXI) if first_xi[0] else [])
            first_xi[0] = False
            P.step("vector", [("PS", pb)], wk,
                   lambda e, sl=sl, hf=hf, pb=pb: e.tensor_copy(
                       xti[sl][:, hf * 512:hf * 512 + 512], PS[pb][:, 0:512]))
        for ee in range(NEXP):
            P.step("gpsimd", ["moe_ag_c", "moe_pad", ("mridxi",), "BUFi32"],
                   [("mxt", sl)],
                   lambda e, a=a, ee=ee, sl=sl: e.indirect_dma_start(
                       out=xti[sl][:, :], out_offset=None,
                       in_=Dm["moe_ag"][:, :],
                       in_offset=IndirectOffsetOnAxis(
                           i32buf.bitcast(U32)[:, a * 8 + ee:a * 8 + ee + 1],
                           0),
                       compute_op=ALU.add), dma=True)
        cb4 = 520 + sl * 4
        P.step("vector", [("mxt", sl)], [("ml4m", sl)],
               lambda e, sl=sl, cb4=cb4: e.tensor_reduce(
                   col(cb4), xti[sl][:, :], AX, ALU.add))
        P.step("vector", [("ml4m", sl)], [("ml4m", sl)],
               lambda e, cb4=cb4: e.tensor_scalar_mul(col(cb4), col(cb4),
                                                      1.0 / D))
        P.step("vector", [("mxt", sl), ("ml4m", sl)], [("mxt", sl)],
               lambda e, sl=sl, cb4=cb4: e.tensor_scalar(
                   xti[sl][:, :], xti[sl][:, :], col(cb4), None,
                   ALU.subtract))
        P.step("scalar", [("mxt", sl)], [("ml4s", sl), ("mlntk",)],
               lambda e, sl=sl, cb4=cb4: e.activation(
                   lnt[:, :], xti[sl][:, :], AF.Square,
                   accum_out=col(cb4 + 1)))
        P.step("vector", [("ml4s", sl)], [("ml4v", sl)],
               lambda e, cb4=cb4: e.tensor_scalar_mul(col(cb4 + 2),
                                                      col(cb4 + 1), 1.0 / D))
        P.step("scalar", [("ml4v", sl), "consts"], [("ml4l", sl)],
               lambda e, cb4=cb4: e.activation(col(cb4 + 2), col(cb4 + 2),
                                               AF.Ln, bias=consts[:, 1:2]))
        P.step("scalar", [("ml4l", sl)], [("ml4r", sl)],
               lambda e, cb4=cb4: e.activation(col(cb4 + 2), col(cb4 + 2),
                                               AF.Exp, scale=-0.5))
        P.step("vector", [("mxt", sl), ("ml4r", sl)], [("mxt", sl)],
               lambda e, sl=sl, cb4=cb4: e.tensor_scalar_mul(
                   xti[sl][:, :], xti[sl][:, :], col(cb4 + 2)))
        P.step("sync", [("mxt", sl)], ["out_hbm"],
               lambda e, a=a, sl=sl: e.dma_start(
                   out=sh["out_hbm"][a * 128:(a + 1) * 128, :],
                   in_=xti[sl][:, :]), dma=True)



def _rope_tables():
    i = np.arange(512, dtype=np.float64)
    theta = np.power(10000.0, -2.0 * (i - 1.0) / D)
    pos = np.arange(8192, dtype=np.float64)
    ang = (pos[:, None] * theta[None, :])
    return (np.cos(ang.astype(np.float32).astype(np.float64)),
            np.sin(ang.astype(np.float32).astype(np.float64)))


def _host_inputs(inputs):
    perm = np.r_[np.arange(0, D, 2), np.arange(1, D, 2)]
    f32 = np.float32
    x = np.asarray(inputs["L0_attn_x"], f32).reshape(NTOKG, D)

    wqT = np.ascontiguousarray(np.asarray(inputs["L0_attn_wq"], f32).T[:, perm])
    wkT = np.ascontiguousarray(np.asarray(inputs["L0_attn_wk"], f32).T[:, perm])
    wvT = np.ascontiguousarray(np.asarray(inputs["L0_attn_wv"], f32).T)
    w1T = np.ascontiguousarray(np.asarray(inputs["L0_ffn_w1"], f32).T)
    w2T = np.ascontiguousarray(np.asarray(inputs["L0_ffn_w2"], f32).T)
    cw = np.asarray(inputs["L0_ffn_conv_w"], f32).reshape(DH, 3)
    convW = np.zeros((128, 3 * NBH), f32)
    for t in range(3):
        convW[:, t * NBH:(t + 1) * NBH] = cw[:, t].reshape(NBH, 128).T
    convB = np.asarray(inputs["L0_ffn_conv_b"], f32).reshape(NBH, 128).T.copy()
    winT = np.ascontiguousarray(np.asarray(inputs["L1_attn_W_in"], f32)[D:, :].T)
    wgT = np.ascontiguousarray(np.asarray(inputs["L1_attn_W_gate"], f32).T)
    woT = np.ascontiguousarray(np.asarray(inputs["L1_attn_W_out"], f32).T)
    b_in = np.asarray(inputs["L1_attn_b_in"], f32)[D:].reshape(NB, 128).T.copy()
    b_g = np.asarray(inputs["L1_attn_b_gate"], f32).reshape(NB, 128).T.copy()
    b_o = np.asarray(inputs["L1_attn_b_out"], f32).reshape(NB, 128).T.copy()
    A_T = np.ascontiguousarray(np.asarray(inputs["L1_attn_A"], f32).T)
    BmT = np.ascontiguousarray(np.asarray(inputs["L1_attn_Bm"], f32).T)
    CmT = np.ascontiguousarray(np.asarray(inputs["L1_attn_Cm"], f32).T)
    Ws = np.ascontiguousarray(np.asarray(inputs["L1_ffn_Ws"], f32))
    bs = np.asarray(inputs["L1_ffn_bs"], f32)
    bst = np.zeros((128, 1), f32)
    bst[:NEXP, 0] = bs
    W1 = np.asarray(inputs["L1_ffn_W1"], f32)
    b1 = np.asarray(inputs["L1_ffn_b1"], f32)
    W2 = np.asarray(inputs["L1_ffn_W2"], f32)
    b2 = np.asarray(inputs["L1_ffn_b2"], f32)

    cos, sin = _rope_tables()
    cos = cos.astype(f32); sin = sin.astype(f32)

    ident_bf = np.eye(128, dtype=ml_dtypes.bfloat16)
    ident_f32 = np.eye(128, dtype=f32)
    consts = np.zeros((128, 8), f32)
    consts[:, 0] = 1.0
    consts[:, 1] = EPS
    consts[:, 2] = 1.0
    egrp = np.zeros((128, NEXP), f32)
    for e in range(NEXP):
        egrp[8 * e:8 * e + 8, e] = 1.0
    e8to128 = np.zeros((NEXP, 128), f32)
    for e in range(NEXP):
        for r in range(8):
            e8to128[e, 8 * e + r] = 1.0
    ltm = np.zeros((128, 128), f32)
    for e in range(NEXP):
        for r1 in range(8):
            for r2 in range(r1 + 1, 8):
                ltm[8 * e + r1, 8 * e + r2] = 1.0
    iota_row = np.arange(TOK, dtype=f32).reshape(1, TOK)
    tokid = np.zeros((128, 16), f32)
    for p in range(128):
        for f in range(16):
            tokid[p, f] = p + 128 * f

    maps = []
    for c in range(NCORES_ALL):
        r = c % 4
        m = dict(
            x_in=x[c * TOK:(c + 1) * TOK], wqT=wqT, wkT=wkT, wvT=wvT,
            ropeC=np.ascontiguousarray(cos[r * TOK:(r + 1) * TOK].T),
            ropeS=np.ascontiguousarray(sin[r * TOK:(r + 1) * TOK].T),
            w1T=w1T, w2T=w2T, convW=convW, convB=convB,
            winT=winT, wgT=wgT, woT=woT, b_in=b_in, b_g=b_g, b_o=b_o,
            A_Tin=A_T, A_nat=np.ascontiguousarray(np.asarray(inputs["L1_attn_A"], f32)),
            BmT=BmT, CmT=CmT, i16rep=np.tile(np.eye(16, dtype=f32), (8, 1)),
            Ws_in=Ws, bs_in=bst,
            W1e=np.ascontiguousarray(W1[c]),
            b1e=b1[c].reshape(NBH, 128).T.copy(),
            W2e=np.ascontiguousarray(W2[c]),
            b2e=b2[c].reshape(NB, 128).T.copy(),
            ident_bf=ident_bf, ident_f32=ident_f32, consts_in=consts,
            egrp=egrp, e8to128=e8to128, ltmask=ltm, tokid=tokid,
            iota_row=iota_row,
            b2row=np.ascontiguousarray(b2[c].reshape(1, D)),
        )
        ebase = np.zeros((128, 1), f32)
        plidx = np.zeros((128, 1), np.uint32)
        myrows = np.zeros((128, 1), np.uint32)
        for eh in range(NEXP):
            te = (eh + c) % NEXP
            ebase[8 * eh:8 * eh + 8, 0] = 2048.0 * te
            for rr in range(8):
                plidx[8 * eh + rr, 0] = 8 * rr + te
            myrows[eh, 0] = 8 * eh + c
        m["ebase"] = ebase
        m["plidx"] = plidx
        m["myrows"] = myrows
        cm = np.zeros((128, 4), f32)
        for cp in range(NCORES_ALL):
            if cp // 4 != c // 4:
                continue
            d = r - 1 - (cp % 4)
            if d < 0 or d > 2:
                continue
            cm[16 * cp:16 * cp + 16, d] = 1.0
        m["cmask"] = cm
        hidx = np.zeros((128, 1), np.uint32)
        hmsk = np.zeros((2, 1), f32)
        if r > 0:
            hidx[0, 0] = 2 * (c - 1) + 1
            hmsk[0, 0] = 1.0
        if r < 3:
            hidx[1, 0] = 2 * (c + 1)
            hmsk[1, 0] = 1.0
        m["halo_idx"] = hidx
        m["halo_msk"] = hmsk
        maps.append(m)
    return maps


NCORES_ALL = 8
_NC_CACHE = {}


def kernel(_phases=(1, 2, 3, 4), **inputs):
    maps = _host_inputs(inputs)
    nc = build_program(phases=tuple(_phases))
    res = run_bass_kernel_spmd(nc, maps, core_ids=list(range(NCORES_ALL)))
    out = np.concatenate([res.results[c]["out_sh"] for c in range(NCORES_ALL)],
                         axis=0)
    return out.reshape(2, 8192, D)


# version 03
# speedup vs baseline: 1.1427x; 1.1171x over previous
import math
from contextlib import ExitStack

import numpy as np
import ml_dtypes

import concourse.bass as bass
import concourse.mybir as mybir
from concourse.bass import IndirectOffsetOnAxis
from concourse.bass_utils import run_bass_kernel_spmd

F32 = mybir.dt.float32
BF16 = mybir.dt.bfloat16
I32 = mybir.dt.int32
U32 = mybir.dt.uint32
U8 = mybir.dt.uint8
AF = mybir.ActivationFunctionType
ALU = mybir.AluOpType
AX = mybir.AxisListType.X

D = 1024
TOK = 2048
NB = D // 128
DH = 4096
NBH = DH // 128
NEXP = 8
KSEL = 2048
NTOKG = 16384
EPS = 1e-5

ENGINES = ("sync", "scalar", "vector", "tensor", "gpsimd")


class Prog:

    def __init__(self, nc, ctx):
        self.nc = nc
        self.ctx = ctx
        self.esem = {e: ctx.enter_context(nc.semaphore(f"E_{e}"))
                     for e in ENGINES}
        self.ecount = {e: 0 for e in ENGINES}
        self.steps = {e: [] for e in ENGINES}
        self.last_w = {}
        self.readers = {}
        self.custom = {}
        self.nsteps = 0

    def _h(self, sid):
        return self.esem[sid] if sid in self.esem else self.custom[sid]

    def step(self, engine, reads, writes, fn, dma=False, own_sem=None):
        waits = {}

        def addw(sid, val):
            waits[sid] = max(waits.get(sid, 0), val)

        for k in reads:
            if k in self.last_w:
                addw(*self.last_w[k])
        for k in writes:
            if k in self.last_w:
                addw(*self.last_w[k])
            for r in self.readers.get(k, ()):
                addw(*r)
        inc_n = 16 if dma else 1
        if own_sem is not None:
            h = self.ctx.enter_context(self.nc.semaphore(own_sem))
            self.custom[own_sem] = h
            sid, self_val = own_sem, inc_n
        else:
            sid = engine
            self.ecount[engine] += inc_n
            self_val = self.ecount[engine]
        self.steps[engine].append((sorted(waits.items()), fn, inc_n, own_sem))
        self.nsteps += 1
        for k in writes:
            self.last_w[k] = (sid, self_val)
            self.readers[k] = []
        for k in reads:
            self.readers.setdefault(k, []).append((sid, self_val))
        return (sid, self_val)

    def emit(self, block, final_keys=()):
        fw = {}
        for k in final_keys:
            sid, v = self.last_w[k]
            fw[sid] = max(fw.get(sid, 0), v)
        self.steps["gpsimd"].append(
            (sorted(fw.items()), None, 0, None))
        for ename in ENGINES:
            items = self.steps[ename]
            if not items:
                continue
            section = getattr(block, ename)

            def make(items=items, ename=ename):
                def body(eng):
                    for waits, fn, inc_n, own in items:
                        for sid, val in waits:
                            eng.wait_ge(self._h(sid), val)
                        if fn is None:
                            continue
                        ins = fn(eng)
                        if inc_n:
                            if own is not None:
                                ins.then_inc(self.custom[own], inc_n)
                            else:
                                ins.then_inc(self.esem[ename], inc_n)
                return body
            section(make())


class Arena:

    def __init__(self, tensor, ncols):
        self.t = tensor
        self.n = ncols
        self.free = [(0, ncols)]
        self.owner_keys = {}

    def alloc(self, cols, keys_hint):
        for i, (s, e) in enumerate(self.free):
            if e - s >= cols:
                if s + cols == e:
                    self.free.pop(i)
                else:
                    self.free[i] = (s + cols, e)
                rng = (s, s + cols)
                legacy = set()
                drop = []
                for r, ks in self.owner_keys.items():
                    if r[0] < rng[1] and rng[0] < r[1]:
                        legacy |= ks
                        drop.append(r)
                for r in drop:
                    del self.owner_keys[r]
                self.owner_keys[rng] = set(keys_hint)
                return rng, legacy
        raise RuntimeError(f"arena OOM: want {cols}, free={self.free}")

    def freea(self, rng, keys=()):
        if rng in self.owner_keys:
            self.owner_keys[rng] = self.owner_keys[rng] | set(keys)
        else:
            self.owner_keys[rng] = set(keys)
        self.free.append(rng)
        self.free.sort()
        out = []
        for s, e in self.free:
            if out and out[-1][1] == s:
                out[-1] = (out[-1][0], e)
            else:
                out.append((s, e))
        self.free = [tuple(x) for x in out]


def cdiv(a, b):
    return (a + b - 1) // b



DBG = {}


def build_program(phases=(1, 2, 3, 4)):
    nc = bass.Bass("TRN2", target_bir_lowering=False)
    ap = lambda t: t.ap()

    I = {}
    def inp(name, shape, dt=F32):
        I[name] = ap(nc.dram_tensor(name, shape, dt, kind="ExternalInput"))
        return I[name]

    x_in = inp("x_in", [TOK, D])
    inp("wqT", [D, D]); inp("wkT", [D, D]); inp("wvT", [D, D])
    inp("ropeC", [512, TOK]); inp("ropeS", [512, TOK])
    inp("w1T", [D, DH]); inp("w2T", [DH, D])
    inp("convW", [128, 3 * NBH]); inp("convB", [128, NBH])
    inp("winT", [D, D]); inp("wgT", [D, D]); inp("woT", [D, D])
    inp("b_in", [128, NB]); inp("b_g", [128, NB]); inp("b_o", [128, NB])
    inp("A_Tin", [16, 16]); inp("A_nat", [16, 16])
    inp("BmT", [D, 16]); inp("CmT", [16, D]); inp("i16rep", [128, 16])
    inp("cmask", [128, 4])
    inp("Ws_in", [D, NEXP]); inp("bs_in", [128, 1])
    inp("W1e", [D, DH]); inp("b1e", [128, NBH])
    inp("W2e", [DH, D]); inp("b2e", [128, NB])
    inp("ident_bf", [128, 128], BF16); inp("ident_f32", [128, 128])
    inp("consts_in", [128, 8])
    inp("egrp", [128, NEXP]); inp("e8to128", [NEXP, 128])
    inp("ltmask", [128, 128])
    inp("halo_idx", [128, 1], U32); inp("halo_msk", [2, 1])
    inp("tokid", [128, 16])
    inp("iota_row", [1, TOK]); inp("b2row", [1, D]); inp("ebase", [128, 1])
    inp("plidx", [128, 1], U32); inp("myrows", [128, 1], U32)

    out_hbm = ap(nc.dram_tensor("out_sh", [TOK, D], F32, kind="ExternalOutput"))
    dbg_hbm = ap(nc.dram_tensor("dbg_out", [128, 8192], F32, kind="ExternalOutput"))

    Dm = {}
    def dram(name, shape, dt):
        Dm[name] = ap(nc.dram_tensor(name, shape, dt))
        return Dm[name]

    dram("ot_spill", [TOK, D], F32)
    dram("qr_spill", [NB * 128, TOK], BF16)
    dram("k_bounce", [D, TOK], BF16)
    dram("v_bounce", [TOK, D], BF16)
    dram("k_ag", [4 * D, TOK], BF16)
    dram("v_ag", [4 * TOK, D], BF16)
    dram("w1q_dram", [D, DH], BF16)
    dram("w2q_dram", [DH, D], BF16)
    dram("halo_bounce", [2, D], F32)
    dram("row_bounce", [1, 4096], F32)
    dram("halo_ag", [16, D], F32)
    dram("state_bounce", [1, 16], F32)
    dram("pow_dram", [16, 32], F32)
    dram("state_ag", [8, 16], F32)
    dram("probs_bounce", [NEXP, TOK], F32)
    dram("probs_ag", [8 * NEXP, TOK], F32)
    dram("act_bounce", [TOK, D], BF16)
    dram("act_ag", [NTOKG, D], BF16)
    dram("moe_bounce", [KSEL, D], BF16)
    dram("moe_ag", [NTOKG + 128, D], BF16)
    dram("cum_dram", [NEXP * 128, 128], F32)
    dram("ce_dram", [NEXP, 128], F32)
    dram("w1e_dram", [D, DH], BF16)
    dram("w2e_dram", [DH, D], BF16)
    dram("xint_dram", [D, KSEL], BF16)
    dram("evals_dram", [NTOKG, 1], F32)
    dram("ridx_dram", [72, TOK], F32)
    dram("moe_row", [1, 4096], F32)

    ctx = ExitStack()
    P = Prog(nc, ctx)
    sbuf = lambda n, s, d: ctx.enter_context(nc.sbuf_tensor(n, s, d))
    psum = lambda n, s, d: ctx.enter_context(nc.psum_tensor(n, s, d))

    X = sbuf("X", [128, NB * TOK], F32)
    st = sbuf("st", [16, 2048], F32)
    idb = sbuf("idb", [128, 128], BF16)
    idf = sbuf("idf", [128, 128], F32)
    consts = sbuf("consts", [128, 8], F32)
    biasbuf = sbuf("biasbuf", [128, 8 * NB + 5 * NBH], F32)
    OF = dict(BIN=0, BG=NB, BO=2 * NB, B2=3 * NB, CB=4 * NB,
              CW=4 * NB + NBH, B1=4 * NB + 4 * NBH)
    AF32 = Arena(sbuf("AF32", [128, 12544], F32), 12544)
    AB16 = Arena(sbuf("AB16", [128, 35072], BF16), 35072)

    sq2 = sbuf("sq2", [128, 2 * 512], F32)
    onesrow = sbuf("onesrow", [1, 128], F32)
    i32buf = sbuf("i32buf", [128, 1024], I32)

    PS = [psum(f"ps{i}", [128, 512], F32) for i in range(8)]

    sh = dict(nc=nc, P=P, I=I, Dm=Dm, X=X, st=st, idb=idb, idf=idf,
              consts=consts, biasbuf=biasbuf, OF=OF, AF32=AF32, AB16=AB16,
              PS=PS, out_hbm=out_hbm, sq2=sq2, onesrow=onesrow, i32buf=i32buf,
              dbg_hbm=dbg_hbm)
    P.step("vector", [], ["onesrow"], lambda e: e.memset(onesrow[:], 1.0))

    rng, legacy = AF32.alloc(4096, ["xtok"])
    xtok = AF32.t[:, rng[0]:rng[0] + 4096]
    for nm, dst, src in (
            ("idb", idb[:], I["ident_bf"]), ("idf", idf[:], I["ident_f32"]),
            ("consts", consts[:], I["consts_in"])):
        P.step("sync", [], [nm], lambda e, d=dst, s=src: e.dma_start(out=d, in_=s), dma=True)
    for nm, sl, src in (
            ("bb0", (OF["BIN"], NB), "b_in"), ("bb1", (OF["BG"], NB), "b_g"),
            ("bb2", (OF["BO"], NB), "b_o"), ("bb3", (OF["B2"], NB), "b2e"),
            ("bb4", (OF["CB"], NBH), "convB"), ("bb5", (OF["CW"], 3 * NBH), "convW"),
            ("bb6", (OF["B1"], NBH), "b1e")):
        P.step("sync", [], [nm],
               lambda e, a=sl[0], n=sl[1], s=src: e.dma_start(
                   out=biasbuf[:, a:a + n], in_=I[s]), dma=True)

    for ch in range(8):
        P.step("sync", (list(legacy) if ch == 0 else []), [("xtok", ch % 2)],
               lambda e, ch=ch: e.dma_start(
                   out=xtok.rearrange("p (a d) -> p a d", a=4)[:, (ch % 2) * 2:(ch % 2) * 2 + 2, :],
                   in_=x_in.rearrange("(a b) d -> b a d", b=128)[:, 2 * ch:2 * ch + 2, :]),
               dma=True)
        for ai in range(2):
            a = ch * 2 + ai
            for j in range(NB):
                k = a * NB + j
                P.step("tensor", [("xtok", ch % 2), "idf"], [("PS", k % 2)],
                       lambda e, ch=ch, ai=ai, j=j, k=k: e.transpose(
                           PS[k % 2][:, 0:128],
                           xtok[:, ((ch % 2) * 2 + ai) * D + 128 * j:
                                ((ch % 2) * 2 + ai) * D + 128 * j + 128], idf[:]))
                P.step("vector", [("PS", k % 2)], [("X", j)],
                       lambda e, a=a, j=j, k=k: e.tensor_copy(
                           X[:, j * TOK + 128 * a: j * TOK + 128 * a + 128],
                           PS[k % 2][:, 0:128]))
    AF32.freea(rng, [("xtok", 0), ("xtok", 1)])

    with ctx:
        with nc.Block() as block:
            if 2 in phases:
                emit_l0_ffn_wquant(sh)
            if 1 in phases:
                emit_l0_attn(sh)
            if 2 in phases:
                drain_wq(sh)
                emit_l0_ffn(sh)
            if 3 in phases:
                emit_l1_attn(sh)
            if 4 in phases:
                emit_moe(sh)
            else:
                emit_debug_out(sh)
            fk = ["out_hbm"] + [("dbg", n) for (n, _, _) in DBG.get("names", [])]
            P.emit(block, final_keys=fk)
    return nc


def emit_debug_out(sh):
    emit_store_out(sh, ln4=False)


def emit_store_out(sh, ln4=False):
    P, X, idf, PS, AF32 = sh["P"], sh["X"], sh["idf"], sh["PS"], sh["AF32"]
    rng, legacy = AF32.alloc(4096, ["xtok2"])
    xtok = AF32.t[:, rng[0]:rng[0] + 4096]
    first = True
    for ch in range(8):
        for ai in range(2):
            a = ch * 2 + ai
            for j in range(NB):
                k = a * NB + j
                P.step("tensor", [("X", j), "idf"], [("PS", 2 + k % 2)],
                       lambda e, a=a, j=j, k=k: e.transpose(
                           PS[2 + k % 2][:, 0:128],
                           X[:, j * TOK + 128 * a: j * TOK + 128 * a + 128], idf[:]))
                w = [("xtok2", ch % 2)] + (list(legacy) if first else [])
                first = False
                P.step("vector", [("PS", 2 + k % 2)], w,
                       lambda e, ch=ch, ai=ai, j=j, k=k: e.tensor_copy(
                           xtok[:, ((ch % 2) * 2 + ai) * D + 128 * j:
                                ((ch % 2) * 2 + ai) * D + 128 * j + 128],
                           PS[2 + k % 2][:, 0:128]))
        P.step("gpsimd", [("xtok2", ch % 2)], ["out_hbm"],
               lambda e, ch=ch: e.dma_start(
                   out=sh["out_hbm"].rearrange("(a b) d -> b a d", b=128)[:, 2 * ch:2 * ch + 2, :],
                   in_=xtok.rearrange("p (a d) -> p a d", a=4)[:, (ch % 2) * 2:(ch % 2) * 2 + 2, :]),
               dma=True)
    AF32.freea(rng)



def wload(sh, tag, dram_ap, nin, nout):
    P, AB16 = sh["P"], sh["AB16"]
    cols = nin * nout * 128
    rng, legacy = AB16.alloc(cols, [tag])
    t = AB16.t[:, rng[0]:rng[0] + cols]
    P.step("gpsimd", [], [tag] + list(legacy),
           lambda e: e.dma_start(
               out=t.rearrange("p (m j c) -> p m j c", m=nout, j=nin),
               in_=dram_ap.rearrange("(j p) (m c) -> p m j c", p=128, c=128)),
           dma=True)
    return t, tag, rng




def dbg_dump(sh, key_reads, src_ap, col0, name):
    if not DBG.get("on"):
        return
    P = sh["P"]
    np_, nw = src_ap.shape[0], src_ap.shape[-1]
    P.step("gpsimd", list(key_reads), [("dbg", name)],
           lambda e: e.dma_start(
               out=sh["dbg_hbm"][0:np_, col0:col0 + nw], in_=src_ap),
           dma=True)
    DBG.setdefault("names", []).append((name, col0, src_ap.shape[-1]))
    P.final_extra = ("dbg", name)


def get_xb(sh):
    P, X, AB16 = sh["P"], sh["X"], sh["AB16"]
    if sh.get("xb_rng") is None:
        rng, legacy = AB16.alloc(NB * TOK, ["xballoc"])
        sh["xb_rng"] = rng
        sh["Xb"] = AB16.t[:, rng[0]:rng[0] + NB * TOK]
        first_extra = list(legacy) + ["xballoc"]
    else:
        first_extra = []
    Xb = sh["Xb"]
    for j in range(NB):
        P.step("vector", [("X", j)], [("Xb", j)] + (first_extra if j == 0 else []),
               lambda e, j=j: e.tensor_copy(Xb[:, j * TOK:(j + 1) * TOK],
                                            X[:, j * TOK:(j + 1) * TOK]))
    return Xb


def free_xb(sh):
    if sh.get("xb_rng") is not None:
        sh["AB16"].freea(sh["xb_rng"], [("Xb", j) for j in range(NB)] + ["xballoc"])
        sh["xb_rng"] = None


def proj(sh, tag, wsb, wkey, evac, nin=NB, nout=NB, tokw=512, src=None,
         srckeyfn=None, psbanks=(4, 5), srctok=TOK, ntch=None):
    P, PS = sh["P"], sh["PS"]
    src = sh["Xb"] if src is None else src
    if srckeyfn is None:
        srckeyfn = lambda j, tch: ("Xb", j)
    ntch = (srctok // tokw) if ntch is None else ntch
    g = 0
    for m in range(nout):
        for tch in range(ntch):
            bank = psbanks[g % 2]
            ps = PS[bank]

            def mk(m=m, tch=tch, ps=ps):
                def f(e):
                    for j in range(nin):
                        ins = e.matmul(
                            ps[:, 0:tokw],
                            wsb[:, (m * nin + j) * 128:(m * nin + j) * 128 + 128],
                            src[:, j * srctok + tch * tokw:
                                j * srctok + tch * tokw + tokw],
                            start=(j == 0), stop=(j == nin - 1))
                    return ins
                return f
            rkeys = [wkey] + [srckeyfn(j, tch) for j in range(nin)]
            P.step("tensor", rkeys, [("PS", bank)], mk())
            evac(m, tch, ps[:, 0:tokw], bank)
            g += 1


def featln(sh, tag, chunk_done=None):
    P, X, st, PS, consts = sh["P"], sh["X"], sh["st"], sh["PS"], sh["consts"]
    sq2, onesrow = sh["sq2"], sh["onesrow"]
    for tch in range(4):
        for j in range(NB):
            k = tch * NB + j
            P.step("scalar", [("X", j)], [(tag + "sq", k % 2), "BUFsq2"],
                   lambda e, j=j, k=k, tch=tch: e.activation(
                       sq2[:, (k % 2) * 512:(k % 2) * 512 + 512],
                       X[:, j * TOK + tch * 512: j * TOK + tch * 512 + 512],
                       AF.Square))
            P.step("tensor", [("X", j), "consts"], [("PS", 4)],
                   lambda e, j=j, tch=tch: e.matmul(
                       PS[4][0:1, 0:512], consts[:, 0:1],
                       X[:, j * TOK + tch * 512: j * TOK + tch * 512 + 512],
                       start=(j == 0), stop=(j == NB - 1)))
            P.step("tensor", [(tag + "sq", k % 2), "consts"], [("PS", 5)],
                   lambda e, j=j, k=k: e.matmul(
                       PS[5][0:1, 0:512], consts[:, 0:1],
                       sq2[:, (k % 2) * 512:(k % 2) * 512 + 512],
                       start=(j == 0), stop=(j == NB - 1)))
        P.step("vector", [("PS", 4)], [(tag, "mu")],
               lambda e: e.tensor_scalar_mul(
                   st[0:1, 1024:1536], PS[4][0:1, 0:512], 1.0 / D))
        P.step("vector", [("PS", 5)], [(tag, "e2")],
               lambda e: e.tensor_scalar_mul(
                   st[0:1, 1536:2048], PS[5][0:1, 0:512], 1.0 / D))
        P.step("vector", [(tag, "mu")], [(tag, "m2")],
               lambda e: e.tensor_mul(st[0:1, 512:1024], st[0:1, 1024:1536],
                                      st[0:1, 1024:1536]))
        P.step("vector", [(tag, "m2"), (tag, "e2")], [(tag, "var")],
               lambda e: e.tensor_sub(st[0:1, 1536:2048], st[0:1, 1536:2048],
                                      st[0:1, 512:1024]))
        P.step("scalar", [(tag, "var"), "consts"], [(tag, "lg")],
               lambda e: e.activation(st[0:1, 512:1024], st[0:1, 1536:2048],
                                      AF.Ln, bias=consts[0:1, 1:2]))
        P.step("scalar", [(tag, "lg")], [(tag, "rstd")],
               lambda e: e.activation(st[0:1, 0:512], st[0:1, 512:1024],
                                      AF.Exp, scale=-0.5))
        P.step("vector", [(tag, "rstd"), (tag, "mu")], [(tag, "b")],
               lambda e: e.tensor_mul(st[0:1, 512:1024], st[0:1, 1024:1536],
                                      st[0:1, 0:512]))
        P.step("tensor", [(tag, "rstd"), "onesrow"], [("PS", 4)],
               lambda e: e.matmul(PS[4][:, 0:512], onesrow[0:1, 0:128],
                                  st[0:1, 0:512], start=True, stop=True))
        P.step("tensor", [(tag, "b"), "onesrow"], [("PS", 5)],
               lambda e: e.matmul(PS[5][:, 0:512], onesrow[0:1, 0:128],
                                  st[0:1, 512:1024], start=True, stop=True))
        for j in range(NB):
            sl = slice(j * TOK + tch * 512, j * TOK + tch * 512 + 512)
            P.step("vector", [("PS", 4)], [("X", j)],
                   lambda e, sl=sl: e.tensor_mul(X[:, sl], X[:, sl],
                                                 PS[4][:, 0:512]))
            P.step("vector", [("PS", 5)], [("X", j)],
                   lambda e, sl=sl: e.tensor_sub(X[:, sl], X[:, sl],
                                                 PS[5][:, 0:512]))
        if chunk_done is not None:
            chunk_done(tch)



def emit_l1_attn(sh):
    P, nc, I, Dm, X, st, PS = (sh["P"], sh["nc"], sh["I"], sh["Dm"], sh["X"],
                               sh["st"], sh["PS"])
    AF32, AB16, consts = sh["AF32"], sh["AB16"], sh["consts"]
    biasbuf, OF = sh["biasbuf"], sh["OF"]

    get_xb(sh)
    win_sb, win_k, win_r = wload(sh, "winsb", I["winT"], NB, NB)
    wg_sb, wg_k, wg_r = wload(sh, "wgsb", I["wgT"], NB, NB)

    rA, lgA = AF32.alloc(768, ["l1small"])
    base = rA[0]
    A_ = AF32.t
    bmsb = A_[:, base:base + 128]
    powB = A_[:, base + 128:base + 336]
    powBT = A_[:, base + 336:base + 544]
    wstack = A_[:, base + 544:base + 560]
    sstack = A_[:, base + 560:base + 561]
    hin = A_[:, base + 561:base + 562]
    i16 = A_[:, base + 576:base + 592]
    cmk = A_[:, base + 592:base + 596]
    prep = A_[:, base + 600:base + 632]
    P.step("sync", [], ["l1bm"] + list(lgA),
           lambda e: e.dma_start(
               out=bmsb.rearrange("p (j c) -> p j c", j=NB),
               in_=I["BmT"].rearrange("(j p) c -> p j c", p=128)), dma=True)
    P.step("sync", [], [("pow", 0)],
           lambda e: e.dma_start(out=powB[0:16, 0:16], in_=I["A_Tin"]), dma=True)
    P.step("sync", [], [("powT", 0)],
           lambda e: e.dma_start(out=powBT[0:16, 0:16], in_=I["A_nat"]), dma=True)
    P.step("sync", [], ["i16"],
           lambda e: e.dma_start(out=i16[:], in_=I["i16rep"]), dma=True)
    P.step("sync", [], ["cmk"],
           lambda e: e.dma_start(out=cmk[:], in_=I["cmask"]), dma=True)

    for s in range(12):
        P.step("tensor", [("pow", s), ("powT", s)], [("PS", 6)],
               lambda e, s=s: e.matmul(PS[6][0:16, 0:16],
                                       powBT[0:16, s * 16:s * 16 + 16],
                                       powB[0:16, s * 16:s * 16 + 16],
                                       start=True, stop=True))
        P.step("vector", [("PS", 6)], [("pow", s + 1)],
               lambda e, s=s: e.tensor_copy(
                   powB[0:16, (s + 1) * 16:(s + 1) * 16 + 16], PS[6][0:16, 0:16]))
        P.step("tensor", [("pow", s), ("powT", s)], [("PS", 7)],
               lambda e, s=s: e.matmul(PS[7][0:16, 0:16],
                                       powB[0:16, s * 16:s * 16 + 16],
                                       powBT[0:16, s * 16:s * 16 + 16],
                                       start=True, stop=True))
        P.step("vector", [("PS", 7)], [("powT", s + 1)],
               lambda e, s=s: e.tensor_copy(
                   powBT[0:16, (s + 1) * 16:(s + 1) * 16 + 16], PS[7][0:16, 0:16]))

    rU, lgU = AF32.alloc(2048, ["U16z"])
    U16 = AF32.t[:, rU[0]:rU[0] + 2048]
    rG, lgG = AF32.alloc(8192, ["gtile"])
    gtile = AF32.t[:, rG[0]:rG[0] + 4096]
    itile = AF32.t[:, rG[0] + 4096:rG[0] + 8192]
    lgG = list(lgG)

    for tch in range(4):
        g = 0
        for m in range(NB):
            for wsb, wkey, which in ((wg_sb, wg_k, "g"), (win_sb, win_k, "i")):
                bank = (4, 5)[g % 2]

                def mk(m=m, wsb=wsb, bank=bank, tch=tch):
                    def f(e):
                        for j in range(NB):
                            ins = e.matmul(
                                PS[bank][:, 0:512],
                                wsb[:, (m * NB + j) * 128:(m * NB + j) * 128 + 128],
                                sh["Xb"][:, j * TOK + tch * 512:
                                         j * TOK + tch * 512 + 512],
                                start=(j == 0), stop=(j == NB - 1))
                        return ins
                    return f
                P.step("tensor", [wkey] + [("Xb", j) for j in range(NB)],
                       [("PS", bank)], mk())
                if which == "g":
                    P.step("scalar", [("PS", bank), "bb1"],
                           [("gt", m)] + (lgG if (m, tch) == (0, 0) else []),
                           lambda e, m=m, bank=bank: e.activation(
                               gtile[:, m * 512:m * 512 + 512],
                               PS[bank][:, 0:512], AF.Sigmoid,
                               bias=biasbuf[:, OF["BG"] + m:OF["BG"] + m + 1]))
                else:
                    P.step("scalar", [("PS", bank), "bb0"], [("it", m)],
                           lambda e, m=m, bank=bank: e.activation(
                               itile[:, m * 512:m * 512 + 512],
                               PS[bank][:, 0:512], AF.Identity,
                               bias=biasbuf[:, OF["BIN"] + m:OF["BIN"] + m + 1]))
                g += 1
        for m in range(NB):
            P.step("vector", [("gt", m), ("it", m)], [("ut", m)],
                   lambda e, m=m: e.tensor_mul(itile[:, m * 512:m * 512 + 512],
                                               itile[:, m * 512:m * 512 + 512],
                                               gtile[:, m * 512:m * 512 + 512]))

        def mkub(tch=tch):
            def f(e):
                for j in range(NB):
                    ins = e.matmul(PS[6][0:16, 0:512],
                                   bmsb[:, j * 16:j * 16 + 16],
                                   itile[:, j * 512:j * 512 + 512],
                                   start=(j == 0), stop=(j == NB - 1))
                return ins
            return f
        P.step("tensor", [("ut", j) for j in range(NB)] + ["l1bm"],
               [("PS", 6)], mkub())
        P.step("vector", [("PS", 6)],
               [("U16", tch)] + (list(lgU) if tch == 0 else []),
               lambda e, tch=tch: e.tensor_copy(
                   U16[0:16, tch * 512:tch * 512 + 512], PS[6][0:16, 0:512]))

    AF32.freea(rG, [(k, m) for k in ("gt", "it", "ut") for m in range(NB)])
    AB16.freea(win_r, [win_k])
    AB16.freea(wg_r, [wg_k])
    UKEYS = [("U16", t) for t in range(4)]

    for s in range(11):
        shift = 1 << s
        spans = []
        end = 2048
        while end > shift:
            spans.append((max(shift, end - 512), end))
            end = max(shift, end - 512)
        for (start, end) in spans:
            P.step("tensor", [("pow", s)] + UKEYS, [("PS", 6)],
                   lambda e, s=s, start=start, end=end, shift=shift: e.matmul(
                       PS[6][0:16, 0:end - start],
                       powB[0:16, s * 16:s * 16 + 16],
                       U16[0:16, start - shift:end - shift],
                       start=True, stop=True))
            P.step("vector", [("PS", 6)], UKEYS,
                   lambda e, start=start, end=end: e.tensor_add(
                       U16[0:16, start:end], U16[0:16, start:end],
                       PS[6][0:16, 0:end - start]))

    P.step("sync", UKEYS, ["state_bounce"],
           lambda e: e.dma_start(out=Dm["state_bounce"][:, :],
                                 in_=U16[0:16, 2047:2048]), dma=True)
    P.step("gpsimd", ["state_bounce"], ["state_ag"],
           lambda e: e.collective_compute(
               "AllGather", ALU.bypass,
               replica_groups=[[0, 1, 2, 3, 4, 5, 6, 7]],
               ins=[Dm["state_bounce"][:, :]], outs=[Dm["state_ag"][:, :]]),
           own_sem="cc_state")
    P.step("sync", ["state_ag"], ["sstack"],
           lambda e: e.dma_start(
               out=sstack[:, 0:1],
               in_=Dm["state_ag"].rearrange("a (b c) -> (a b) c", c=1)),
           dma=True)
    P.step("sync", [("pow", 11), ("pow", 12)], ["pow_dram"],
           lambda e: e.dma_start(out=Dm["pow_dram"][0:16, 0:32],
                                 in_=powB[0:16, 176:208]), dma=True)
    for r in range(8):
        P.step("sync", ["pow_dram"], [("powrep", r)],
               lambda e, r=r: e.dma_start(
                   out=prep[16 * r:16 * r + 16, 0:32],
                   in_=Dm["pow_dram"][0:16, 0:32]), dma=True)
    P.step("vector", ["i16", "cmk"], [("ws", 0)],
           lambda e: e.tensor_scalar_mul(wstack[:], i16[:], cmk[:, 0:1]))
    P.step("vector", [("ws", 0)] + [("powrep", r) for r in range(8)], [("ws", 1)],
           lambda e: e.scalar_tensor_tensor(
               wstack[:], prep[:, 0:16], cmk[:, 1:2], wstack[:],
               ALU.mult, ALU.add))
    P.step("vector", [("ws", 1)], [("ws", 2)],
           lambda e: e.scalar_tensor_tensor(
               wstack[:], prep[:, 16:32], cmk[:, 2:3], wstack[:],
               ALU.mult, ALU.add))
    P.step("tensor", [("ws", 2), "sstack"], [("PS", 7)],
           lambda e: e.matmul(PS[7][0:16, 0:1], wstack[:], sstack[:, 0:1],
                              start=True, stop=True))
    P.step("vector", [("PS", 7)], ["hin"],
           lambda e: e.tensor_copy(hin[0:16, 0:1], PS[7][0:16, 0:1]))

    rC, lgC = AF32.alloc(2048, ["C16z"])
    C16 = AF32.t[:, rC[0]:rC[0] + 2048]
    P.step("vector", [], [("C16", -1)] + list(lgC),
           lambda e: e.memset(C16[0:16, :], 0.0))
    P.step("tensor", [("pow", 0), "hin", ("C16", -1)], [("PS", 7)],
           lambda e: e.matmul(PS[7][0:16, 0:1], powB[0:16, 0:16],
                              hin[0:16, 0:1], start=True, stop=True))
    P.step("vector", [("PS", 7), ("C16", -1)], [("C16", 0)],
           lambda e: e.tensor_copy(C16[0:16, 0:1], PS[7][0:16, 0:1]))
    for s in range(11):
        shift = 1 << s
        lo, hi = shift, min(2048, 2 * shift)
        for start in range(lo, hi, 512):
            end = min(start + 512, hi)
            P.step("tensor", [("pow", s), ("C16", s)], [("PS", 7)],
                   lambda e, s=s, start=start, end=end, shift=shift: e.matmul(
                       PS[7][0:16, 0:end - start],
                       powB[0:16, s * 16:s * 16 + 16],
                       C16[0:16, start - shift:end - shift],
                       start=True, stop=True))
            P.step("vector", [("PS", 7), ("C16", s)], [("C16", s + 1)],
                   lambda e, start=start, end=end: e.tensor_add(
                       C16[0:16, start:end], C16[0:16, start:end],
                       PS[7][0:16, 0:end - start]))
    P.step("vector", [("C16", 11)] + UKEYS, UKEYS,
           lambda e: e.tensor_add(U16[0:16, :], U16[0:16, :], C16[0:16, :]))
    AF32.freea(rC, [("C16", s) for s in range(-1, 12)])

    rH, lgH = AB16.alloc(2048, ["Hbz"])
    Hb = AB16.t[:, rH[0]:rH[0] + 2048]
    P.step("vector", UKEYS, ["Hb"] + list(lgH),
           lambda e: e.tensor_copy(Hb[0:16, :], U16[0:16, :]))
    rcm, lgcm = AB16.alloc(1024, ["cmsbz"])
    cmsb = AB16.t[:, rcm[0]:rcm[0] + 1024]
    P.step("gpsimd", [], ["cmsb"] + list(lgcm),
           lambda e: e.dma_start(out=cmsb[0:16, :], in_=I["CmT"][:, :]), dma=True)
    wo_sb, wo_k, wo_r = wload(sh, "wosb", I["woT"], NB, NB)
    ryt, lgyt = AB16.alloc(NB * 512, ["yTz"])
    yTc = AB16.t[:, ryt[0]:ryt[0] + NB * 512]
    first_yt = [True]
    for tch in range(4):
        for m in range(NB):
            P.step("tensor", ["cmsb", "Hb"], [("PS", (4, 5)[m % 2])],
                   lambda e, m=m, tch=tch: e.matmul(
                       PS[(4, 5)[m % 2]][:, 0:512],
                       cmsb[0:16, m * 128:m * 128 + 128],
                       Hb[0:16, tch * 512:tch * 512 + 512],
                       start=True, stop=True))
            w = [("yTc", m)] + (list(lgyt) if first_yt[0] else [])
            first_yt[0] = False
            P.step("vector", [("PS", (4, 5)[m % 2])], w,
                   lambda e, m=m: e.tensor_copy(
                       yTc[:, m * 512:m * 512 + 512],
                       PS[(4, 5)[m % 2]][:, 0:512]))
        for m in range(NB):
            bank = (6, 7)[m % 2]

            def mkwo(m=m, bank=bank):
                def f(e):
                    for j in range(NB):
                        ins = e.matmul(
                            PS[bank][:, 0:512],
                            wo_sb[:, (m * NB + j) * 128:(m * NB + j) * 128 + 128],
                            yTc[:, j * 512:j * 512 + 512],
                            start=(j == 0), stop=(j == NB - 1))
                    return ins
                return f
            P.step("tensor", [wo_k] + [("yTc", j) for j in range(NB)],
                   [("PS", bank)], mkwo())
            sl = slice(m * TOK + tch * 512, m * TOK + tch * 512 + 512)
            P.step("vector", [("PS", bank), ("X", m), "bb2"], [("X", m)],
                   lambda e, m=m, sl=sl, bank=bank: e.scalar_tensor_tensor(
                       X[:, sl], PS[bank][:, 0:512],
                       biasbuf[:, OF["BO"] + m:OF["BO"] + m + 1],
                       X[:, sl], ALU.add, ALU.add))
    AF32.freea(rU, [("U16", t) for t in range(4)])
    AF32.freea(rA, [("pow", s) for s in range(13)] +
               [("powT", s) for s in range(13)] +
               ["l1bm", "i16", "cmk", "sstack", "hin",
                ("ws", 0), ("ws", 1), ("ws", 2)] +
               [("powrep", r) for r in range(8)])
    AB16.freea(rH, ["Hb"])
    AB16.freea(rcm, ["cmsb"])
    AB16.freea(ryt, [("yTc", m) for m in range(NB)])
    AB16.freea(wo_r, [wo_k])

    featln(sh, "ln3")


def emit_l0_attn(sh):
    P, nc, I, Dm, X, PS = sh["P"], sh["nc"], sh["I"], sh["Dm"], sh["X"], sh["PS"]
    AF32, AB16, sq2 = sh["AF32"], sh["AB16"], sh["sq2"]
    idb, idf = sh["idb"], sh["idf"]

    Xb = get_xb(sh)

    rS, lgS = AF32.alloc(8192, ["qkslab"])
    qsl = AF32.t[:, rS[0]:rS[0] + 8192]
    rT, lgT = AF32.alloc(2048, ["ropetmp"])
    rt1 = AF32.t[:, rT[0]:rT[0] + 512]
    rt2 = AF32.t[:, rT[0] + 512:rT[0] + 1024]
    rtc = AF32.t[:, rT[0] + 1024:rT[0] + 1536]
    rts = AF32.t[:, rT[0] + 1536:rT[0] + 2048]
    rQKB, lgQKB = AB16.alloc(4096, ["qkb"])
    qkb = AB16.t[:, rQKB[0]:rQKB[0] + 4096]
    first_tbl = [True]
    first_cast = [True]
    first_qs = [True]

    def qkproj(tg, wname, spill, spkey):
        wsb, wkey, wrng = wload(sh, tg + "w", I[wname], NB, NB)
        g = 0
        for H in range(2):
            for m in range(NB):
                for tc in range(2):
                    bank = (4, 5)[g % 2]
                    g += 1
                    tch = 2 * H + tc

                    def mkp(m=m, tch=tch, bank=bank):
                        def f(e):
                            for j in range(NB):
                                ins = e.matmul(
                                    PS[bank][:, 0:512],
                                    wsb[:, (m * NB + j) * 128:
                                        (m * NB + j) * 128 + 128],
                                    Xb[:, j * TOK + tch * 512:
                                       j * TOK + tch * 512 + 512],
                                    start=(j == 0), stop=(j == NB - 1))
                            return ins
                        return f
                    P.step("tensor", [wkey] + [("Xb", j) for j in range(NB)],
                           [("PS", bank)], mkp())
                    wk = [("qs", m, tc)]
                    if first_qs[0]:
                        wk += list(lgS)
                        first_qs[0] = False
                    P.step("vector", [("PS", bank)], wk,
                           lambda e, m=m, tc=tc, bank=bank: e.tensor_copy(
                               qsl[:, m * 1024 + tc * 512:
                                   m * 1024 + tc * 512 + 512],
                               PS[bank][:, 0:512]))
                pump_wq(sh)
            for hh in range(4):
                for q4 in range(2):
                    c0 = hh * 1024 + q4 * 512
                    te = qsl[:, c0:c0 + 512]
                    to = qsl[:, 4096 + c0:4096 + c0 + 512]
                    kTE = [("qs", hh, q4)]
                    kTO = [("qs", hh + 4, q4)]
                    lt = list(lgT) if first_tbl[0] else []
                    first_tbl[0] = False
                    t0c = H * 1024 + q4 * 512
                    P.step("sync", [], [("RTC",)] + lt,
                           lambda e, hh=hh, t0c=t0c: e.dma_start(
                               out=rtc[:, :],
                               in_=I["ropeC"][hh * 128:(hh + 1) * 128,
                                              t0c:t0c + 512]), dma=True)
                    P.step("sync", [], [("RTS",)],
                           lambda e, hh=hh, t0c=t0c: e.dma_start(
                               out=rts[:, :],
                               in_=I["ropeS"][hh * 128:(hh + 1) * 128,
                                              t0c:t0c + 512]), dma=True)
                    P.step("vector", kTE + [("RTC",)], [("RT1",)],
                           lambda e, te=te: e.tensor_mul(rt1, te, rtc))
                    P.step("vector", kTE + [("RTS",)], [("RT2",)],
                           lambda e, te=te: e.tensor_mul(rt2, te, rts))
                    P.step("vector", kTO + [("RTS",)], kTE,
                           lambda e, te=te, to=to: e.tensor_mul(te, to, rts))
                    P.step("vector", [("RT1",)] + kTE, kTE,
                           lambda e, te=te: e.tensor_add(te, te, rt1))
                    P.step("vector", kTO + [("RTC",)], [("RT1",)],
                           lambda e, to=to: e.tensor_mul(rt1, to, rtc))
                    P.step("vector", [("RT1",), ("RT2",)], kTO,
                           lambda e, to=to: e.tensor_sub(to, rt1, rt2))
            for j in range(NB):
                slot = j % 2
                wk = [("qkb", slot)] + (list(lgQKB) if first_cast[0] else [])
                first_cast[0] = False
                P.step("vector", [("qs", j, 0), ("qs", j, 1)], wk,
                       lambda e, j=j, slot=slot: e.tensor_copy(
                           qkb[:, slot * 2048:slot * 2048 + 1024],
                           qsl[:, j * 1024:(j + 1) * 1024]))
                P.step("sync", [("qkb", slot)], [(spkey, j, H)],
                       lambda e, j=j, slot=slot, spill=spill, H=H: e.dma_start(
                           out=spill[j * 128:(j + 1) * 128,
                                     H * 1024:H * 1024 + 1024],
                           in_=qkb[:, slot * 2048:slot * 2048 + 1024]),
                       dma=True)
        AB16.freea(wrng, [wkey])

    qkproj("k", "wkT", Dm["k_bounce"], "ksp")
    P.step("gpsimd", [("ksp", j, H) for j in range(NB) for H in range(2)],
           ["k_ag"],
           lambda e: e.collective_compute(
               "AllGather", ALU.bypass,
               replica_groups=[[0, 1, 2, 3], [4, 5, 6, 7]],
               ins=[Dm["k_bounce"][:, :]], outs=[Dm["k_ag"][:, :]]),
           own_sem="cc_k")
    AB16.freea(rQKB, [("qkb", 0), ("qkb", 1)])

    wvsb, wvk, wvr = wload(sh, "vw", I["wvT"], NB, NB)
    rVS, lgVS = AB16.alloc(4096, ["vstage"])
    vstage = AB16.t[:, rVS[0]:rVS[0] + 4096]
    rVT, lgVT = AB16.alloc(2048, ["vtok"])
    vtok = AB16.t[:, rVT[0]:rVT[0] + 2048]
    first_vs, first_vt = [True], [True]
    for tch in range(4):
        for m in range(NB):
            bank = (4, 5)[m % 2]

            def mkv(m=m, tch=tch, bank=bank):
                def f(e):
                    for j in range(NB):
                        ins = e.matmul(
                            PS[bank][:, 0:512],
                            wvsb[:, (m * NB + j) * 128:(m * NB + j) * 128 + 128],
                            Xb[:, j * TOK + tch * 512:j * TOK + tch * 512 + 512],
                            start=(j == 0), stop=(j == NB - 1))
                    return ins
                return f
            P.step("tensor", [wvk] + [("Xb", j) for j in range(NB)],
                   [("PS", bank)], mkv())
            pump_wq(sh)
            wk = [("vst", m)] + (list(lgVS) if first_vs[0] else [])
            first_vs[0] = False
            P.step("scalar", [("PS", bank)], wk,
                   lambda e, m=m, bank=bank: e.activation(
                       vstage[:, m * 512:m * 512 + 512], PS[bank][:, 0:512],
                       AF.Copy))
        for a in range(4):
            pb = (6, 7)[a % 2]

            def mkt(a=a, pb=pb):
                def f(e):
                    pv = PS[pb].bitcast(BF16)
                    for m in range(NB):
                        ins = e.transpose(
                            pv[:, m * 128:(m + 1) * 128],
                            vstage[:, m * 512 + a * 128:m * 512 + a * 128 + 128],
                            idb[:])
                    return ins
                return f
            P.step("tensor", [("vst", m) for m in range(NB)] + ["idb"],
                   [("PS", pb)], mkt())
            slot = a % 2
            wk = [("vtok", slot)] + (list(lgVT) if first_vt[0] else [])
            first_vt[0] = False
            P.step("vector", [("PS", pb)], wk,
                   lambda e, pb=pb, slot=slot: e.tensor_copy(
                       vtok[:, slot * 1024:slot * 1024 + 1024],
                       PS[pb].bitcast(BF16)[:, 0:1024]))
            P.step("sync", [("vtok", slot)], [("vsp", tch, a)],
                   lambda e, tch=tch, a=a, slot=slot: e.dma_start(
                       out=Dm["v_bounce"].rearrange("(a p) c -> p a c", p=128)
                       [:, tch * 4 + a:tch * 4 + a + 1, :],
                       in_=vtok.rearrange("p (s c) -> p s c", s=2)
                       [:, slot:slot + 1, :]), dma=True)
    P.step("gpsimd", [("vsp", t, a) for t in range(4) for a in range(4)],
           ["v_ag"],
           lambda e: e.collective_compute(
               "AllGather", ALU.bypass,
               replica_groups=[[0, 1, 2, 3], [4, 5, 6, 7]],
               ins=[Dm["v_bounce"][:, :]], outs=[Dm["v_ag"][:, :]]),
           own_sem="cc_v")
    AB16.freea(wvr, [wvk])
    AB16.freea(rVS, [("vst", m) for m in range(NB)])
    AB16.freea(rVT, [("vtok", 0), ("vtok", 1)])

    rQKB, lgQKB = AB16.alloc(4096, ["qkb"])
    qkb = AB16.t[:, rQKB[0]:rQKB[0] + 4096]
    first_cast[0] = True
    qkproj("q", "wqT", Dm["qr_spill"], "qsp")
    AF32.freea(rS, [("qs", m, t) for m in range(NB) for t in range(2)])
    AF32.freea(rT, [("RT1",), ("RT2",), ("RTC",), ("RTS",)])
    AB16.freea(rQKB, [("qkb", 0), ("qkb", 1)])
    free_xb(sh)

    Xv = X.bitcast(BF16)
    kslab = Xv[:, 0:16384]
    vslab = Xv[:, 16384:32768]
    rQR, lgQR = AB16.alloc(16384, ["qres"])
    qres = AB16.t[:, rQR[0]:rQR[0] + 16384]
    rSC, lgSC = AB16.alloc(4096, ["scsl"])
    scsl = [AB16.t[:, rSC[0]:rSC[0] + 2048],
            AB16.t[:, rSC[0] + 2048:rSC[0] + 4096]]
    rAT, lgAT = AB16.alloc(1024, ["atmp"])
    atmp = [AB16.t[:, rAT[0]:rAT[0] + 512], AB16.t[:, rAT[0] + 512:rAT[0] + 1024]]
    rO, lgO = AF32.alloc(6144, ["otile"])
    oti = [AF32.t[:, rO[0]:rO[0] + 1024], AF32.t[:, rO[0] + 1024:rO[0] + 2048]]
    otr = [AF32.t[:, rO[0] + 2048:rO[0] + 3072], AF32.t[:, rO[0] + 3072:rO[0] + 4096]]
    orb = [AF32.t[:, rO[0] + 4096:rO[0] + 5120], AF32.t[:, rO[0] + 5120:rO[0] + 6144]]
    first_sc, first_at, first_ot = [True], [True], [True]
    AVB = {(0, 0): 2, (0, 1): 3, (1, 0): 6, (1, 1): 7}

    P.step("sync", [("qsp", j, H) for j in range(NB) for H in range(2)],
           ["qres"] + list(lgQR),
           lambda e: e.dma_start(
               out=qres.rearrange("p (j t) -> p j t", j=NB),
               in_=Dm["qr_spill"].rearrange("(j p) t -> p j t", p=128)),
           dma=True)

    for qt in range(4):
        P.step("sync", ["k_ag"],
               [("kslab",), ("X", 0), ("X", 1), ("X", 2), ("X", 3)],
               lambda e, qt=qt: e.dma_start(
                   out=kslab.rearrange("p (j u) -> p j u", j=NB),
                   in_=Dm["k_ag"][qt * 1024:(qt + 1) * 1024, :]
                   .rearrange("(j p) t -> p j t", p=128)), dma=True)
        P.step("scalar", ["v_ag"],
               [("vslab",), ("X", 4), ("X", 5), ("X", 6), ("X", 7)],
               lambda e, qt=qt: e.dma_start(
                   out=vslab.rearrange("p (g c) -> p g c", g=16),
                   in_=Dm["v_ag"][qt * 2048:(qt + 1) * 2048, :]
                   .rearrange("(g p) c -> p g c", p=128)), dma=True)
        for p in range(8):
            q0 = p * 256
            pump_wq(sh, 2)
            if qt >= 1:
                for T in (0, 1):
                    t = 2 * p + T
                    P.step("gpsimd", [("ots", t)], [("OTR", T)],
                           lambda e, t=t, T=T: e.dma_start(
                               out=otr[T][:, :],
                               in_=Dm["ot_spill"][t * 128:(t + 1) * 128, :]),
                           dma=True)
            for kc in range(4):
                for T in (0, 1):
                    bank = T

                    def mks(kc=kc, T=T, bank=bank, q0=q0):
                        def f(e):
                            for j in range(NB):
                                ins = e.matmul(
                                    PS[bank][:, 0:512],
                                    qres[:, j * 2048 + q0 + T * 128:
                                         j * 2048 + q0 + T * 128 + 128],
                                    kslab[:, j * 2048 + kc * 512:
                                          j * 2048 + kc * 512 + 512],
                                    start=(j == 0), stop=(j == NB - 1))
                            return ins
                        return f
                    P.step("tensor", [("qres",), ("kslab",)],
                           [("PS", bank)], mks())
                    wk = [("sc", T, kc)] + (list(lgSC) if first_sc[0] else [])
                    first_sc[0] = False
                    P.step("scalar", [("PS", bank)], wk,
                           lambda e, T=T, kc=kc, bank=bank: e.activation(
                               scsl[T][:, kc * 512:kc * 512 + 512],
                               PS[bank][:, 0:512], AF.Copy))
                    P.step("vector", [("sc", T, kc)], [("fmx", T, kc)],
                           lambda e, T=T, kc=kc: e.tensor_reduce(
                               sq2[:, T * 4 + kc:T * 4 + kc + 1],
                               scsl[T][:, kc * 512:kc * 512 + 512], AX, ALU.max))
            for T in (0, 1):
                P.step("vector", [("fmx", T, kc) for kc in range(4)], [("fM", T)],
                       lambda e, T=T: e.tensor_reduce(
                           sq2[:, 16 + T:17 + T], sq2[:, T * 4:T * 4 + 4],
                           AX, ALU.max))
                P.step("vector", [("fM", T)], [("fnm", T)],
                       lambda e, T=T: e.tensor_scalar_mul(
                           sq2[:, 18 + T:19 + T], sq2[:, 16 + T:17 + T],
                           -1.0 / 32.0))
                for kc in range(4):
                    tb = kc % 2
                    wk = [("fat", tb), ("fds", T, kc)] + \
                        (list(lgAT) if first_at[0] else [])
                    first_at[0] = False
                    P.step("scalar", [("sc", T, kc), ("fnm", T)], wk,
                           lambda e, T=T, kc=kc, tb=tb: e.activation(
                               atmp[tb], scsl[T][:, kc * 512:kc * 512 + 512],
                               AF.Exp, scale=1.0 / 32.0,
                               bias=sq2[:, 18 + T:19 + T],
                               accum_out=sq2[:, 20 + T * 4 + kc:
                                             21 + T * 4 + kc]))
                    pb = 4 + kc % 2

                    def mkt(tb=tb, pb=pb):
                        def f(e):
                            pv = PS[pb].bitcast(BF16)
                            for i in range(4):
                                ins = e.transpose(
                                    pv[:, i * 128:(i + 1) * 128],
                                    atmp[tb][:, i * 128:(i + 1) * 128], idb[:])
                            return ins
                        return f
                    P.step("tensor", [("fat", tb), "idb"], [("PS", pb)], mkt())
                    P.step("vector", [("PS", pb)], [("sc", T, kc)],
                           lambda e, T=T, kc=kc, pb=pb: e.tensor_copy(
                               scsl[T][:, kc * 512:kc * 512 + 512],
                               PS[pb].bitcast(BF16)[:, 0:512]))
                P.step("vector", [("fds", T, kc) for kc in range(4)], [("fD", T)],
                       lambda e, T=T: e.tensor_reduce(
                           sq2[:, 36 + T:37 + T], sq2[:, 20 + T * 4:24 + T * 4],
                           AX, ALU.add))
            for T in (0, 1):
                for dvh in (0, 1):
                    bank = AVB[(T, dvh)]
                    for gg in range(2):
                        def mka(T=T, dvh=dvh, bank=bank, gg=gg):
                            def f(e):
                                for g in range(gg * 8, gg * 8 + 8):
                                    ins = e.matmul(
                                        PS[bank][:, 0:512],
                                        scsl[T][:, g * 128:(g + 1) * 128],
                                        vslab[:, g * 1024 + dvh * 512:
                                              g * 1024 + dvh * 512 + 512],
                                        start=(g == 0), stop=(g == 15))
                                return ins
                            return f
                        P.step("tensor",
                               [("sc", T, kc) for kc in range(gg * 2, gg * 2 + 2)]
                               + [("vslab",)], [("PS", bank)], mka())
            for T in (0, 1):
                t = 2 * p + T
                cb = 72 + T * 8
                if qt == 0:
                    for dvh in (0, 1):
                        wk = [("OT", T, dvh)] + \
                            (list(lgO) if first_ot[0] else [])
                        first_ot[0] = False
                        P.step("vector", [("PS", AVB[(T, dvh)])], wk,
                               lambda e, T=T, dvh=dvh: e.tensor_copy(
                                   oti[T][:, dvh * 512:dvh * 512 + 512],
                                   PS[AVB[(T, dvh)]][:, 0:512]))
                    P.step("vector", [("fM", T)], [("frm", t)],
                           lambda e, T=T, t=t: e.tensor_copy(
                               sq2[:, 40 + t:41 + t], sq2[:, 16 + T:17 + T]))
                    P.step("vector", [("fD", T)], [("frd", t)],
                           lambda e, T=T, t=t: e.tensor_copy(
                               sq2[:, 56 + t:57 + t], sq2[:, 36 + T:37 + T]))
                else:
                    P.step("vector", [("frm", t), ("fM", T)], [("fmm", T)],
                           lambda e, t=t, cb=cb, T=T: e.tensor_tensor(
                               sq2[:, cb:cb + 1], sq2[:, 40 + t:41 + t],
                               sq2[:, 16 + T:17 + T], ALU.max))
                    P.step("vector", [("fmm", T), ("frm", t)], [("fe1a", T)],
                           lambda e, t=t, cb=cb: e.tensor_sub(
                               sq2[:, cb + 1:cb + 2], sq2[:, 40 + t:41 + t],
                               sq2[:, cb:cb + 1]))
                    P.step("scalar", [("fe1a", T)], [("fe1", T)],
                           lambda e, cb=cb: e.activation(
                               sq2[:, cb + 1:cb + 2], sq2[:, cb + 1:cb + 2],
                               AF.Exp, scale=1.0 / 32.0))
                    P.step("vector", [("fmm", T), ("fM", T)], [("fe2a", T)],
                           lambda e, cb=cb, T=T: e.tensor_sub(
                               sq2[:, cb + 2:cb + 3], sq2[:, 16 + T:17 + T],
                               sq2[:, cb:cb + 1]))
                    P.step("scalar", [("fe2a", T)], [("fe2", T)],
                           lambda e, cb=cb: e.activation(
                               sq2[:, cb + 2:cb + 3], sq2[:, cb + 2:cb + 3],
                               AF.Exp, scale=1.0 / 32.0))
                    P.step("vector", [("fmm", T)], [("frm", t)],
                           lambda e, t=t, cb=cb: e.tensor_copy(
                               sq2[:, 40 + t:41 + t], sq2[:, cb:cb + 1]))
                    P.step("vector", [("fe1", T), ("frd", t)], [("fda", T)],
                           lambda e, t=t, cb=cb: e.tensor_mul(
                               sq2[:, cb + 3:cb + 4], sq2[:, 56 + t:57 + t],
                               sq2[:, cb + 1:cb + 2]))
                    P.step("vector", [("fe2", T), ("fD", T)], [("fdb", T)],
                           lambda e, cb=cb, T=T: e.tensor_mul(
                               sq2[:, cb + 5:cb + 6], sq2[:, 36 + T:37 + T],
                               sq2[:, cb + 2:cb + 3]))
                    P.step("vector", [("fda", T), ("fdb", T)], [("frd", t)],
                           lambda e, t=t, cb=cb: e.tensor_add(
                               sq2[:, 56 + t:57 + t], sq2[:, cb + 3:cb + 4],
                               sq2[:, cb + 5:cb + 6]))
                    for dvh in (0, 1):
                        bank = AVB[(T, dvh)]
                        P.step("vector", [("PS", bank), ("fe2", T)],
                               [("OT", T, dvh)],
                               lambda e, bank=bank, dvh=dvh, T=T, cb=cb:
                               e.tensor_scalar_mul(
                                   oti[T][:, dvh * 512:dvh * 512 + 512],
                                   PS[bank][:, 0:512], sq2[:, cb + 2:cb + 3]))
                        P.step("vector", [("OTR", T), ("fe1", T)],
                               [("OTR", T)],
                               lambda e, dvh=dvh, T=T, cb=cb:
                               e.tensor_scalar_mul(
                                   otr[T][:, dvh * 512:dvh * 512 + 512],
                                   otr[T][:, dvh * 512:dvh * 512 + 512],
                                   sq2[:, cb + 1:cb + 2]))
                        P.step("vector", [("OTR", T)], [("OT", T, dvh)],
                               lambda e, dvh=dvh, T=T: e.tensor_add(
                                   oti[T][:, dvh * 512:dvh * 512 + 512],
                                   oti[T][:, dvh * 512:dvh * 512 + 512],
                                   otr[T][:, dvh * 512:dvh * 512 + 512]))
                if qt == 3:
                    P.step("vector", [("frd", t)], [("fr", T)],
                           lambda e, t=t, cb=cb: e.reciprocal(
                               sq2[:, cb + 4:cb + 5], sq2[:, 56 + t:57 + t]))
                    P.step("sync", [], [("xrw", T)],
                           lambda e, t=t, T=T: e.dma_start(
                               out=orb[T][:, :],
                               in_=sh["I"]["x_in"][t * 128:(t + 1) * 128, :]),
                           dma=True)
                    for dvh in (0, 1):
                        P.step("vector", [("fr", T)], [("OT", T, dvh)],
                               lambda e, dvh=dvh, T=T, cb=cb:
                               e.tensor_scalar_mul(
                                   oti[T][:, dvh * 512:dvh * 512 + 512],
                                   oti[T][:, dvh * 512:dvh * 512 + 512],
                                   sq2[:, cb + 4:cb + 5]))
                    P.step("vector", [("xrw", T), ("OT", T, 0), ("OT", T, 1)],
                           [("OT", T, 0), ("OT", T, 1)],
                           lambda e, T=T: e.tensor_add(
                               oti[T][:, :], oti[T][:, :], orb[T][:, :]))
                P.step("gpsimd", [("OT", T, 0), ("OT", T, 1)], [("ots", t)],
                       lambda e, t=t, T=T: e.dma_start(
                           out=Dm["ot_spill"][t * 128:(t + 1) * 128, :],
                           in_=oti[T][:, :]), dma=True)

    for a in range(16):
        sl = a % 2
        P.step("sync", [("ots", a)], [("orb2", sl), ("xrw", sl)],
               lambda e, a=a, sl=sl: e.dma_start(
                   out=orb[sl][:, :],
                   in_=Dm["ot_spill"][a * 128:(a + 1) * 128, :]), dma=True)
        for j in range(NB):
            pb = 4 + j % 2
            P.step("tensor", [("orb2", sl), "idf"], [("PS", pb)],
                   lambda e, j=j, pb=pb, sl=sl: e.transpose(
                       PS[pb][:, 0:128], orb[sl][:, j * 128:j * 128 + 128],
                       idf[:]))
            P.step("vector", [("PS", pb)],
                   [("X", j), ("kslab",) if j < 4 else ("vslab",)],
                   lambda e, j=j, pb=pb, a=a: e.tensor_copy(
                       X[:, j * TOK + a * 128:j * TOK + a * 128 + 128],
                       PS[pb][:, 0:128]))

    AB16.freea(rQR, [("qres",)])
    AB16.freea(rSC, [("sc", T, kc) for T in (0, 1) for kc in range(4)])
    AB16.freea(rAT, [("fat", 0), ("fat", 1)])
    AF32.freea(rO, [("OT", s, d) for s in (0, 1) for d in (0, 1)] +
               [("OTR", 0), ("OTR", 1), ("orb2", 0), ("orb2", 1),
                ("xrw", 0), ("xrw", 1)])
    featln(sh, "ln1")



def emit_l0_ffn_wquant(sh):
    P, I, Dm = sh["P"], sh["I"], sh["Dm"]
    AF32, AB16 = sh["AF32"], sh["AB16"]
    onesrow, biasbuf = sh["onesrow"], sh["biasbuf"]
    wqs = biasbuf
    rWT, lgWT = AF32.alloc(2048, ["wqtile2"])
    wqt = AF32.t[:, rWT[0]:rWT[0] + 2048]
    rSG, lgSG = AB16.alloc(4096, ["wq_sg2"])
    sgn = AB16.t[:, rSG[0]:rSG[0] + 2048]
    qout = AB16.t[:, rSG[0] + 2048:rSG[0] + 4096]
    RB = Dm["row_bounce"]
    th = []

    def chunks(wn):
        out = []
        if wn == "w1":
            for t in range(8):
                for hh in range(2):
                    out.append((I["w1T"][t * 128:(t + 1) * 128,
                                         hh * 2048:hh * 2048 + 2048],
                                Dm["w1q_dram"][t * 128:(t + 1) * 128,
                                               hh * 2048:hh * 2048 + 2048],
                                2048))
        else:
            for t in range(32):
                out.append((I["w2T"][t * 128:(t + 1) * 128, :],
                            Dm["w2q_dram"][t * 128:(t + 1) * 128, :], 1024))
        return out

    for io, (wn, total) in enumerate((("w1", D * DH), ("w2", DH * D))):
        for ci, (sap, dap, w) in enumerate(chunks(wn)):
            def p1(sap=sap, w=w, first=(io, ci) == (0, 0), ci=ci, wn=wn):
                wk = [("wqt2",)] + (list(lgWT) if first else [])
                P.step("gpsimd", [], wk,
                       lambda e: e.dma_start(out=wqt[:, 0:w], in_=sap),
                       dma=True)
                P.step("vector", [("wqt2",)], [("wq1c", wn)],
                       lambda e: e.tensor_reduce(wqs[:, 193:194], wqt[:, 0:w],
                                                 AX, ALU.add,
                                                 apply_absolute_value=True))
                if ci == 0:
                    P.step("vector", [("wq1c", wn)], [("wq1a", wn)],
                           lambda e: e.tensor_copy(wqs[:, 192:193],
                                                   wqs[:, 193:194]))
                else:
                    P.step("vector", [("wq1c", wn), ("wq1a", wn)],
                           [("wq1a", wn)],
                           lambda e: e.tensor_add(wqs[:, 192:193],
                                                  wqs[:, 192:193],
                                                  wqs[:, 193:194]))
            th.append(p1)

        def stats(wn=wn, total=total, io=io):
            P.step("gpsimd", [("wq1a", wn)], [("wqpd", wn)],
                   lambda e: e.dma_start(
                       out=RB[0:1, 2304:2432].rearrange("a (p c) -> (a p) c",
                                                        p=128),
                       in_=wqs[:, 192:193]), dma=True)
            P.step("gpsimd", [("wqpd", wn)], [("wqpr", wn), ("wqt2",)],
                   lambda e: e.dma_start(out=wqt[0:1, 0:128],
                                         in_=RB[0:1, 2304:2432]), dma=True)
            P.step("vector", [("wqpr", wn)], [("wqm", wn)],
                   lambda e: e.tensor_reduce(wqs[0:1, 195:196],
                                             wqt[0:1, 0:128], AX, ALU.add))
            P.step("vector", [("wqm", wn)], [("wqm", wn)],
                   lambda e: e.tensor_scalar_mul(wqs[0:1, 195:196],
                                                 wqs[0:1, 195:196],
                                                 1.0 / total))
            P.step("vector", [("wqm", wn)], [("wqm3", wn)],
                   lambda e: e.tensor_scalar_max(wqs[0:1, 195:196],
                                                 wqs[0:1, 195:196], 1e-5))
            P.step("vector", [("wqm3", wn)], [("wth", wn)],
                   lambda e: e.tensor_scalar_mul(wqs[0:1, 196:197],
                                                 wqs[0:1, 195:196], 0.5))
            P.step("vector", [("wqm3", wn)], [("wsc", wn)],
                   lambda e: e.tensor_scalar_mul(wqs[0:1, 197 + io:198 + io],
                                                 wqs[0:1, 195:196],
                                                 1.0 / 127.0))
            P.step("vector", [("wth", wn), "onesrow"],
                   [("wthr2", wn), ("wqt2",)],
                   lambda e: e.tensor_scalar_mul(wqt[0:1, 128:256],
                                                 onesrow[0:1, 0:128],
                                                 wqs[0:1, 196:197]))
            P.step("gpsimd", [("wthr2", wn)], [("wthd", wn)],
                   lambda e: e.dma_start(out=RB[0:1, 2432:2560],
                                         in_=wqt[0:1, 128:256]), dma=True)
            P.step("gpsimd", [("wthd", wn)], [("wthb", wn)],
                   lambda e: e.dma_start(
                       out=wqs[:, 194:195],
                       in_=RB[0:1, 2432:2560].rearrange("a (p c) -> (a p) c",
                                                        p=128)), dma=True)
        th.append(stats)

        for ci, (sap, dap, w) in enumerate(chunks(wn)):
            def p2(sap=sap, dap=dap, w=w, wn=wn,
                   first=(io, ci) == (0, 0)):
                P.step("gpsimd", [("wthb", wn)], [("wqt2",)],
                       lambda e: e.dma_start(out=wqt[:, 0:w], in_=sap),
                       dma=True)
                wk = [("wsg2",)] + (list(lgSG) if first else [])
                P.step("scalar", [("wqt2",)], wk,
                       lambda e: e.activation(sgn[:, 0:w], wqt[:, 0:w],
                                              AF.Sign))
                P.step("scalar", [("wsg2",)], [("wab2",), ("wqt2",)],
                       lambda e: e.activation(wqt[:, 0:w], wqt[:, 0:w],
                                              AF.Abs))
                P.step("vector", [("wab2",), ("wthb", wn)],
                       [("wge2",), ("wqt2",)],
                       lambda e: e.tensor_scalar(wqt[:, 0:w], wqt[:, 0:w],
                                                 wqs[:, 194:195], None,
                                                 ALU.is_ge))
                P.step("vector", [("wge2",), ("wsg2",)], [("wqo2",)],
                       lambda e: e.tensor_mul(qout[:, 0:w], wqt[:, 0:w],
                                              sgn[:, 0:w]))
                P.step("gpsimd", [("wqo2",)], [("wqdr", wn)],
                       lambda e: e.dma_start(out=dap, in_=qout[:, 0:w]),
                       dma=True)
            th.append(p2)

    def fin():
        AF32.freea(rWT, [("wqt2",)])
        AB16.freea(rSG, [("wsg2",), ("wqo2",)])
    th.append(fin)
    sh["wqthunks"] = th


def pump_wq(sh, n=1):
    th = sh.get("wqthunks")
    if not th:
        return
    for _ in range(n):
        if not th:
            break
        th.pop(0)()


def drain_wq(sh):
    th = sh.get("wqthunks")
    while th:
        th.pop(0)()


def emit_l0_ffn(sh):
    P, nc, I, Dm, X, st, PS = (sh["P"], sh["nc"], sh["I"], sh["Dm"], sh["X"],
                               sh["st"], sh["PS"])
    AF32, AB16, consts, onesrow = sh["AF32"], sh["AB16"], sh["consts"], sh["onesrow"]
    biasbuf, OF, idf, idb, i32buf = (sh["biasbuf"], sh["OF"], sh["idf"],
                                     sh["idb"], sh["i32buf"])
    free_xb(sh)

    rRB, lgRB = AF32.alloc(2944, ["rowB"])
    rowB = AF32.t[:, rRB[0]:rRB[0] + 2944]
    rWS, lgWS = AF32.alloc(8192, ["wq_tile"])
    wtile = AF32.t[:, rWS[0]:rWS[0] + 4096]
    rowA = AF32.t[:, rWS[0] + 4096:rWS[0] + 8192]
    racc = sh["sq2"]
    wqs = biasbuf



    for j in range(NB):
        P.step("sync", [("X", j)], [("hb", j)],
               lambda e, j=j: e.dma_start(
                   out=Dm["halo_bounce"][0:1, j * 128:(j + 1) * 128],
                   in_=X[:, j * TOK:j * TOK + 1]), dma=True)
        P.step("sync", [("X", j)], [("hb2", j)],
               lambda e, j=j: e.dma_start(
                   out=Dm["halo_bounce"][1:2, j * 128:(j + 1) * 128],
                   in_=X[:, j * TOK + 2047:j * TOK + 2048]), dma=True)
    P.step("gpsimd",
           [("hb", j) for j in range(NB)] + [("hb2", j) for j in range(NB)],
           ["halo_ag"],
           lambda e: e.collective_compute(
               "AllGather", ALU.bypass, replica_groups=[[0, 1, 2, 3, 4, 5, 6, 7]],
               ins=[Dm["halo_bounce"][:, :]], outs=[Dm["halo_ag"][:, :]]),
           own_sem="cc_halo")
    rHG, lgHG = AF32.alloc(1024, ["hgath"])
    hg = AF32.t[:, rHG[0]:rHG[0] + 1024]
    P.step("sync", [], ["hidx", "BUFi32"],
           lambda e: e.dma_start(out=i32buf.bitcast(U32)[:, 0:1],
                                 in_=I["halo_idx"]), dma=True)
    P.step("gpsimd", ["halo_ag", "hidx", "BUFi32"], [("hg",)] + list(lgHG),
           lambda e: e.indirect_dma_start(
               out=hg[:, :], out_offset=None, in_=Dm["halo_ag"][:, :],
               in_offset=IndirectOffsetOnAxis(i32buf.bitcast(U32)[:, 0:1], 0)),
           dma=True)
    P.step("sync", [], ["hmsk"],
           lambda e: e.dma_start(out=st[0:2, 8:9], in_=I["halo_msk"]), dma=True)
    P.step("vector", [("hg",)], [("hamax",)],
           lambda e: e.tensor_reduce(st[0:2, 4:5], hg[0:2, :], AX, ALU.max,
                                     apply_absolute_value=True))
    P.step("vector", [("hamax",)], [("hamaxc",)],
           lambda e: e.tensor_scalar_max(st[0:2, 5:6], st[0:2, 4:5], 1e-5))
    P.step("vector", [("hamaxc",)], [("hrec",)],
           lambda e: e.reciprocal(st[0:2, 6:7], st[0:2, 5:6]))
    P.step("vector", [("hrec",), "hmsk"], [("hsq",)],
           lambda e: e.scalar_tensor_tensor(st[0:2, 7:8], st[0:2, 6:7], 127.0,
                                            st[0:2, 8:9], ALU.mult, ALU.mult))

    rXQ, lgXQ = AB16.alloc(NB * 2050, ["xq"])
    xq = AB16.t[:, rXQ[0]:rXQ[0] + NB * 2050]
    XQ0 = rXQ[0]
    rHQ, lgHQ = AB16.alloc(1024, ["hq"])
    hq = AB16.t[:, rHQ[0]:rHQ[0] + 1024]
    P.step("vector", [("hsq",), ("hg",)], [("hqi",), "BUFi32"],
           lambda e: e.tensor_scalar(i32buf[0:2, 0:1024], hg[0:2, :],
                                     st[0:2, 7:8], None, ALU.mult))
    P.step("vector", [("hqi",)], [("hq",)] + list(lgHQ) + list(lgHG),
           lambda e: e.tensor_copy(hq[0:2, :], i32buf[0:2, 0:1024]))
    AF32.freea(rHG, [("hg",)])

    macc = wtile[:, 0:2048]
    t0 = wtile[:, 2048:4096]
    t0b = rowA[:, 0:2048]
    P.step("scalar", [("X", 0), "BUFwt"],
           [("mt", 6), "BUFwt"] + list(lgWS) + list(lgRB),
           lambda e: e.activation(macc, X[:, 0:2048], AF.Abs))
    for j in range(1, NB):
        buf = t0 if j % 2 else t0b
        bk = "BUFwt" if j % 2 else "BUFrowA"
        P.step("scalar", [("X", j)], [("mta", j % 2), bk],
               lambda e, j=j, buf=buf: e.activation(
                   buf, X[:, j * TOK:j * TOK + 2048], AF.Abs))
        P.step("vector", [("mta", j % 2), ("mt", 6), bk], [("mt", 6), ("mtt", j % 2)],
               lambda e, buf=buf: e.tensor_max(macc, macc, buf))
    for a in range(16):
        P.step("tensor", [("mt", 6), "idf"], [("PS", 3)],
               lambda e, a=a: e.transpose(PS[3][:, 0:128],
                                          macc[:, a * 128:a * 128 + 128], idf[:]))
        P.step("vector", [("PS", 3)], [("amx", a), "BUFsq2"],
               lambda e, a=a: e.tensor_reduce(racc[:, 4 + a:5 + a],
                                              PS[3][:, 0:128], AX, ALU.max))
    P.step("tensor", [("amx", a) for a in range(16)] + ["idf"], [("PS", 3)],
           lambda e: e.transpose(PS[3][0:16, 0:128], racc[:, 4:20], idf[:]))
    P.step("vector", [("PS", 3), "BUFwt"], [("amxT",), "BUFwt"],
           lambda e: e.tensor_copy(t0[0:16, 0:128], PS[3][0:16, 0:128]))
    P.step("sync", [("amxT",), "BUFwt"], ["amxd"],
           lambda e: e.dma_start(out=Dm["row_bounce"][0:1, 0:2048],
                                 in_=t0[0:16, 0:128]), dma=True)
    P.step("sync", ["amxd", "BUFrowA"], [("amaxrow",), "BUFrowA"],
           lambda e: e.dma_start(out=rowA[0:1, 0:2048],
                                 in_=Dm["row_bounce"][0:1, 0:2048]), dma=True)
    P.step("vector", [("amaxrow",), "BUFrowA"], [("amaxrow",), "BUFrowA"],
           lambda e: e.tensor_scalar_max(rowA[0:1, 0:2048], rowA[0:1, 0:2048],
                                         1e-5))
    P.step("vector", [("amaxrow",), "BUFrowA"], [("srow",), "BUFrowA"],
           lambda e: e.reciprocal(rowA[0:1, 2048:4096], rowA[0:1, 0:2048]))
    P.step("vector", [("srow",), "BUFrowA"], [("srow",), "BUFrowA"],
           lambda e: e.tensor_scalar_mul(rowA[0:1, 2048:4096],
                                         rowA[0:1, 2048:4096], 127.0))
    P.step("vector", [("amaxrow",), ("wsc", "w1"), "BUFrowA"], [("s1core",), "BUFrowA"],
           lambda e: e.tensor_scalar(rowB[0:1, 1:2049], rowA[0:1, 0:2048],
                                     wqs[0:1, 197:198], None, ALU.mult))
    P.step("tensor", [("hamaxc",), "idf"], [("PS", 3)],
           lambda e: e.transpose(PS[3][0:1, 0:2], st[0:2, 5:6], idf[0:2, 0:2]))
    P.step("vector", [("PS", 3), ("wsc", "w1")], [("hs1row",)],
           lambda e: e.tensor_scalar(st[0:1, 10:12], PS[3][0:1, 0:2],
                                     wqs[0:1, 197:198], None, ALU.mult))
    P.step("vector", [("hs1row",)], [("s1halo",)],
           lambda e: e.tensor_copy(rowB[0:1, 0:1], st[0:1, 10:11]))
    P.step("vector", [("hs1row",)], [("s1halo2",)],
           lambda e: e.tensor_copy(rowB[0:1, 2049:2050], st[0:1, 11:12]))
    for tch in range(4):
        P.step("tensor", [("srow",), "onesrow", "BUFrowA"], [("PS", 2)],
               lambda e, tch=tch: e.matmul(
                   PS[2][:, 0:512], onesrow[0:1, 0:128],
                   rowA[0:1, 2048 + tch * 512:2048 + tch * 512 + 512],
                   start=True, stop=True))
        sbuf_ = t0 if tch % 2 else t0b
        sbk = "BUFwt" if tch % 2 else "BUFrowA"
        P.step("vector", [("PS", 2), sbk], [("sbc", tch), sbk],
               lambda e, sbuf_=sbuf_: e.tensor_copy(sbuf_[:, 0:512], PS[2][:, 0:512]))
        for j in range(NB):
            P.step("vector", [("sbc", tch), ("X", j), sbk], [("xqi", j, tch), "BUFi32"],
                   lambda e, j=j, tch=tch, sbuf_=sbuf_: e.tensor_mul(
                       i32buf[:, 0:512],
                       X[:, j * TOK + tch * 512:j * TOK + tch * 512 + 512],
                       sbuf_[:, 0:512]))
            P.step("vector", [("xqi", j, tch)],
                   [("xq", j, tch)] + (list(lgXQ) if (j, tch) == (0, 0) else []),
                   lambda e, j=j, tch=tch: e.tensor_copy(
                       xq[:, j * 2050 + 1 + tch * 512:j * 2050 + 1 + tch * 512 + 512],
                       i32buf[:, 0:512]))
    for j in range(NB):
        P.step("tensor", [("hq",), "idb"], [("PS", 3)],
               lambda e, j=j: e.transpose(
                   PS[3].bitcast(BF16)[0:128, 0:2], hq[0:2, j * 128:(j + 1) * 128],
                   idb[0:2, 0:2]))
        P.step("vector", [("PS", 3)], [("xqh", j)],
               lambda e, j=j: e.tensor_copy(
                   xq[:, j * 2050:j * 2050 + 2050:2049],
                   PS[3].bitcast(BF16)[:, 0:2]))
    AB16.freea(rHQ, [("hq",)])
    AF32.freea(rWS, [("wqt", w, p) for w in ("w1", "w2") for p in (0, 1)] +
               ["BUFwt", "BUFrowA", ("amaxrow",), ("srow",), ("mt", 6),
                ("mta", 0), ("mta", 1), ("mtt", 0), ("mtt", 1), ("amxT",)] +
               [("sbc", tc) for tc in range(4)])

    XQKEYS = ([("xq", j, tc) for j in range(NB) for tc in range(4)] +
              [("xqh", j) for j in range(NB)])
    rH, lgH = AF32.alloc(NBH * 258, ["hext"])
    hext = AF32.t[:, rH[0]:rH[0] + NBH * 258]
    rsm, lgsm = AF32.alloc(1312, ["smalls"])
    s1bc = AF32.t[:, rsm[0]:rsm[0] + 258]
    s2bc = AF32.t[:, rsm[0] + 258:rsm[0] + 514]
    s2obc = AF32.t[:, rsm[0] + 514:rsm[0] + 770]
    tmpev = AF32.t[:, rsm[0] + 770:rsm[0] + 1282]
    sgbuf = [i32buf.bitcast(F32)[:, 512:768], i32buf.bitcast(F32)[:, 768:1024]]
    rCV, lgCV = AB16.alloc(NBH * 256, ["cvo"])
    cvo = AB16.t[:, rCV[0]:rCV[0] + NBH * 256]
    xq2 = cvo
    rw1s, lgw1s = AB16.alloc(2 * NB * 128, ["w1s"])
    w1s = [AB16.t[:, rw1s[0] + i * NB * 128:rw1s[0] + (i + 1) * NB * 128]
           for i in range(2)]
    rw2s, lgw2s = AB16.alloc(2 * 16 * 128, ["w2s"])
    w2s = [AB16.t[:, rw2s[0] + i * 16 * 128:rw2s[0] + (i + 1) * 16 * 128]
           for i in range(2)]
    first = dict(h=True, c=True, q2=True, w1=True, w2=True, sm=True)
    CW0, CW1, CW2 = OF["CW"], OF["CW"] + NBH, OF["CW"] + 2 * NBH

    for t in range(8):
        off = t * 256
        P.step("tensor", [("s1core",), ("s1halo",), ("s1halo2",), "onesrow"],
               [("PS", 2)],
               lambda e, off=off: e.matmul(
                   PS[2][:, 0:258], onesrow[0:1, 0:128],
                   rowB[0:1, off:off + 258], start=True, stop=True))
        P.step("vector", [("PS", 2)],
               [("s1bc",)] + (list(lgsm) if first["sm"] else []),
               lambda e: e.tensor_copy(s1bc[:, 0:258], PS[2][:, 0:258]))
        first["sm"] = False
        for fb in range(NBH):
            bank = (0, 1)[fb % 2]
            P.step("scalar", [("wqdr", "w1"), ("w1c", fb % 2)],
                   [("w1l", fb % 2)] + (list(lgw1s) if first["w1"] else []),
                   lambda e, fb=fb: e.dma_start(
                       out=w1s[fb % 2].rearrange("p (j c) -> p j c", j=NB),
                       in_=Dm["w1q_dram"].rearrange("(j p) q -> p j q", p=128)
                       [:, :, fb * 128:(fb + 1) * 128]), dma=True)
            first["w1"] = False

            def mk1(t=t, fb=fb, bank=bank):
                def f(e):
                    for j in range(NB):
                        ins = e.matmul(
                            PS[bank][:, 0:258],
                            w1s[fb % 2][:, j * 128:(j + 1) * 128],
                            xq[:, j * 2050 + t * 256:j * 2050 + t * 256 + 258],
                            start=(j == 0), stop=(j == NB - 1))
                    return ins
                return f
            P.step("tensor", [("w1l", fb % 2)] + XQKEYS,
                   [("PS", bank), ("w1c", fb % 2)], mk1())
            P.step("vector", [("PS", bank), ("s1bc",)],
                   [("hx", fb)] + (list(lgH) if first["h"] else []),
                   lambda e, fb=fb, bank=bank: e.tensor_mul(
                       hext[:, fb * 258:fb * 258 + 258], PS[bank][:, 0:258],
                       s1bc[:, 0:258]))
            first["h"] = False
        if t == 7:
            dbg_dump(sh, [("hx", 0)], hext[:, 0:258], 3328, "hext0_t7")
            dbg_dump(sh, [("s1bc",)], s1bc[:, 0:258], 3600, "s1bc_t7")
            dbg_dump(sh, [("xq", 0, 3), ("xqh", 0)],
                     sh["AB16"].t[:, XQ0 + 1536:XQ0 + 2050], 3872, "xqtail")
        for fb in range(NBH):
            cs = slice(fb * 256, fb * 256 + 256)
            P.step("vector", [("hx", fb), "bb5", "bb4"],
                   [("cv", fb)] + (list(lgCV) if first["c"] else []),
                   lambda e, fb=fb, cs=cs: e.tensor_scalar(
                       cvo[:, cs], hext[:, fb * 258:fb * 258 + 256],
                       biasbuf[:, CW0 + fb:CW0 + fb + 1],
                       biasbuf[:, OF["CB"] + fb:OF["CB"] + fb + 1],
                       ALU.mult, ALU.add))
            first["c"] = False
            P.step("vector", [("cv", fb)], [("cv", fb)],
                   lambda e, fb=fb, cs=cs: e.scalar_tensor_tensor(
                       cvo[:, cs], hext[:, fb * 258 + 1:fb * 258 + 257],
                       biasbuf[:, CW1 + fb:CW1 + fb + 1], cvo[:, cs],
                       ALU.mult, ALU.add))
            P.step("vector", [("cv", fb)], [("cv", fb)],
                   lambda e, fb=fb, cs=cs: e.scalar_tensor_tensor(
                       cvo[:, cs], hext[:, fb * 258 + 2:fb * 258 + 258],
                       biasbuf[:, CW2 + fb:CW2 + fb + 1], cvo[:, cs],
                       ALU.mult, ALU.add))
            P.step("scalar", [("cv", fb), ("sgt", fb % 2)], [("sg", fb % 2), "BUFi32"],
                   lambda e, fb=fb, cs=cs: e.activation(
                       sgbuf[fb % 2], cvo[:, cs], AF.Sigmoid))
            P.step("vector", [("sg", fb % 2), ("cv", fb)],
                   [("cv", fb), ("sgt", fb % 2)],
                   lambda e, fb=fb, cs=cs: e.tensor_mul(
                       cvo[:, cs], cvo[:, cs], sgbuf[fb % 2]))
        mx = sh["sq2"][:, 0:256]
        mabs2 = [sh["sq2"][:, 256:512], sh["sq2"][:, 644:900]]
        P.step("scalar", [("cv", 0), "BUFsq2"], [("mx",), "BUFsq2"],
               lambda e: e.activation(mx, cvo[:, 0:256], AF.Abs))
        for fb in range(1, NBH):
            P.step("scalar", [("cv", fb), ("mabs", fb % 2)], [("mabs", fb % 2)],
                   lambda e, fb=fb: e.activation(
                       mabs2[fb % 2], cvo[:, fb * 256:fb * 256 + 256], AF.Abs))
            P.step("vector", [("mabs", fb % 2), ("mx",)], [("mx",), ("mabs", fb % 2)],
                   lambda e, fb=fb: e.tensor_max(mx, mx, mabs2[fb % 2]))
        for a in range(2):
            P.step("tensor", [("mx",), "idf"], [("PS", 3)],
                   lambda e, a=a: e.transpose(PS[3][:, 0:128],
                                              mx[:, a * 128:a * 128 + 128], idf[:]))
            P.step("vector", [("PS", 3)], [("am2t", a), "BUFsq2"],
                   lambda e, a=a: e.tensor_reduce(sh["sq2"][:, 512 + a:513 + a],
                                                  PS[3][:, 0:128], AX, ALU.max))
        P.step("tensor", [("am2t", 0), ("am2t", 1), "idf"], [("PS", 3)],
               lambda e: e.transpose(PS[3][0:2, 0:128], sh["sq2"][:, 512:514],
                                     idf[:]))
        P.step("vector", [("PS", 3)], [("am2T",), "BUFsq2"],
               lambda e: e.tensor_copy(sh["sq2"][0:2, 516:644], PS[3][0:2, 0:128]))
        P.step("sync", [("am2T",)], ["am2d"],
               lambda e: e.dma_start(out=Dm["row_bounce"][0:1, 2048:2304],
                                     in_=sh["sq2"][0:2, 516:644]), dma=True)
        P.step("sync", ["am2d"], [("am2row",)],
               lambda e: e.dma_start(out=rowB[0:1, 2080:2336],
                                     in_=Dm["row_bounce"][0:1, 2048:2304]),
               dma=True)
        P.step("vector", [("am2row",)], [("am2row",)],
               lambda e: e.tensor_scalar_max(rowB[0:1, 2080:2336],
                                             rowB[0:1, 2080:2336], 1e-5))
        P.step("vector", [("am2row",)], [("s2row",)],
               lambda e: e.reciprocal(rowB[0:1, 2336:2592], rowB[0:1, 2080:2336]))
        P.step("vector", [("s2row",)], [("s2row",)],
               lambda e: e.tensor_scalar_mul(rowB[0:1, 2336:2592],
                                             rowB[0:1, 2336:2592], 127.0))
        P.step("vector", [("am2row",), ("wsc", "w2")], [("s2orow",)],
               lambda e: e.tensor_scalar(rowB[0:1, 2592:2848], rowB[0:1, 2080:2336],
                                         wqs[0:1, 198:199], None, ALU.mult))
        P.step("tensor", [("s2row",), "onesrow"], [("PS", 2)],
               lambda e: e.matmul(PS[2][:, 0:256], onesrow[0:1, 0:128],
                                  rowB[0:1, 2336:2592], start=True, stop=True))
        P.step("vector", [("PS", 2)], [("s2bc",)],
               lambda e: e.tensor_copy(s2bc[:, 0:256], PS[2][:, 0:256]))
        P.step("tensor", [("s2orow",), "onesrow"], [("PS", 2)],
               lambda e: e.matmul(PS[2][:, 0:256], onesrow[0:1, 0:128],
                                  rowB[0:1, 2592:2848], start=True, stop=True))
        P.step("vector", [("PS", 2)], [("s2obc",)],
               lambda e: e.tensor_copy(s2obc[:, 0:256], PS[2][:, 0:256]))
        if t == 7:
            dbg_dump(sh, [("am2row",)], rowB[0:1, 2080:2336], 0, "am2row")
            dbg_dump(sh, [("s2orow",)], rowB[0:1, 2592:2848], 256, "s2orow")
            dbg_dump(sh, [("s2bc",)], s2bc[:, 0:256], 512, "s2bc")
        for fb in range(NBH):
            cs = slice(fb * 256, fb * 256 + 256)
            P.step("vector", [("cv", fb), ("s2bc",)], [("q2i", fb), "BUFi32"],
                   lambda e, fb=fb, cs=cs: e.tensor_tensor(
                       i32buf[:, 0:256], cvo[:, cs], s2bc[:, 0:256], ALU.mult))
            P.step("vector", [("q2i", fb)], [("cv", fb)],
                   lambda e, fb=fb: e.tensor_copy(
                       xq2[:, fb * 256:fb * 256 + 256], i32buf[:, 0:256]))
        if t == 7:
            dbg_dump(sh, [("cv", fb) for fb in range(NBH)],
                     cvo[:, 0:2048], 1024, "xq2_t0")
        for m in range(NB):
            bank = (4, 5)[m % 2]
            for hf in range(2):
                P.step("sync", [("wqdr", "w2"), ("w2c", hf)],
                       [("w2l", hf)] + (list(lgw2s) if first["w2"] else []),
                       lambda e, m=m, hf=hf: e.dma_start(
                           out=w2s[hf].rearrange("p (k c) -> p k c", k=16),
                           in_=Dm["w2q_dram"].rearrange("(k p) q -> p k q", p=128)
                           [:, hf * 16:hf * 16 + 16, m * 128:(m + 1) * 128]),
                       dma=True)
                first["w2"] = False

                def mk2(m=m, bank=bank, hf=hf):
                    def f(e):
                        for kk in range(16):
                            ins = e.matmul(
                                PS[bank][:, 0:256],
                                w2s[hf][:, kk * 128:(kk + 1) * 128],
                                xq2[:, (hf * 16 + kk) * 256:(hf * 16 + kk) * 256 + 256],
                                start=(hf == 0 and kk == 0),
                                stop=(hf == 1 and kk == 15))
                        return ins
                    return f
                P.step("tensor", [("w2l", hf)] + [("cv", fb) for fb in range(NBH)],
                       [("PS", bank), ("w2c", hf)], mk2())
            P.step("vector", [("PS", bank), ("s2obc",)], [("tmpev", m % 2)],
                   lambda e, bank=bank, m=m: e.tensor_mul(
                       tmpev[:, (m % 2) * 256:(m % 2) * 256 + 256],
                       PS[bank][:, 0:256], s2obc[:, 0:256]))
            if t == 7 and m == 0:
                dbg_dump(sh, [("tmpev", m % 2)],
                         tmpev[:, 0:256], 3072, "ffnout_m0t0")
            P.step("vector", [("tmpev", m % 2), ("X", m)], [("X", m)],
                   lambda e, m=m, t=t: e.tensor_add(
                       X[:, m * TOK + t * 256:m * TOK + t * 256 + 256],
                       X[:, m * TOK + t * 256:m * TOK + t * 256 + 256],
                       tmpev[:, (m % 2) * 256:(m % 2) * 256 + 256]))

    AF32.freea(rH, [("hx", fb) for fb in range(NBH)])
    AF32.freea(rsm, [("s1bc",), ("s2bc",), ("s2obc",), ("tmpev", 0), ("tmpev", 1)])
    AF32.freea(rRB, [("s1core",), ("s1halo",), ("s1halo2",), ("am2row",),
                     ("s2row",), ("s2orow",), ("wsc", "w1"), ("wsc", "w2")])
    AB16.freea(rCV, [("cv", fb) for fb in range(NBH)] + [("mx",)])
    AB16.freea(rw1s, [("w1l", 0), ("w1l", 1), ("w1c", 0), ("w1c", 1)])
    AB16.freea(rw2s, [("w2l", 0), ("w2l", 1), ("w2c", 0), ("w2c", 1)])
    AB16.freea(rXQ, [("xq", j, tc) for j in range(NB) for tc in range(4)] +
               [("xqh", j) for j in range(NB)])

    featln(sh, "ln2")





def emit_moe(sh):
    P, nc, I, Dm, X, PS = sh["P"], sh["nc"], sh["I"], sh["Dm"], sh["X"], sh["PS"]
    AF32, AB16, sq2 = sh["AF32"], sh["AB16"], sh["sq2"]
    idb, idf, consts, onesrow = sh["idb"], sh["idf"], sh["consts"], sh["onesrow"]
    i32buf, biasbuf, OF = sh["i32buf"], sh["biasbuf"], sh["OF"]
    NIT = 36
    free_xb(sh)

    if not DBG.get("no_wcast"):
        rWB_, lgWB_ = AB16.alloc(4096, ["mwcb"])
        wcb = [AB16.t[:, rWB_[0]:rWB_[0] + 2048],
               AB16.t[:, rWB_[0] + 2048:rWB_[0] + 4096]]
        wjobs = []
        for t in range(8):
            for h in range(2):
                wjobs.append(("w1e_dram",
                              I["W1e"][t * 128:(t + 1) * 128,
                                       h * 2048:h * 2048 + 2048],
                              Dm["w1e_dram"][t * 128:(t + 1) * 128,
                                             h * 2048:h * 2048 + 2048], 2048))
        for t in range(32):
            wjobs.append(("w2e_dram",
                          I["W2e"][t * 128:(t + 1) * 128, :],
                          Dm["w2e_dram"][t * 128:(t + 1) * 128, :], 1024))
        for i, (dk, sap, dap, w) in enumerate(wjobs):
            sl = i % 2
            wk = [("mwcb", sl)] + (list(lgWB_) if i == 0 else [])
            P.step("gpsimd", [], wk,
                   lambda e, sap=sap, sl=sl, w=w: e.dma_start(
                       out=wcb[sl][:, 0:w], in_=sap), dma=True)
            P.step("sync", [("mwcb", sl)], [dk],
                   lambda e, dap=dap, sl=sl, w=w: e.dma_start(
                       out=dap, in_=wcb[sl][:, 0:w]), dma=True)
        AB16.freea(rWB_, [("mwcb", 0), ("mwcb", 1)])

    rSM, lgSM = AF32.alloc(1024, ["msm"])
    s0 = rSM[0]
    SM = AF32.t
    col = lambda c: SM[:, s0 + c:s0 + c + 1]
    c64 = lambda c: SM[0:64, s0 + c:s0 + c + 1]
    rBG, lgBG = AF32.alloc(8192, ["mbg"])
    b0 = rBG[0]
    R = [AF32.t[:, b0 + i * 2048:b0 + (i + 1) * 2048] for i in range(4)]
    rOB, lgOB = AF32.alloc(2048, ["mob"])
    obuf = [AF32.t[:, rOB[0]:rOB[0] + 1024], AF32.t[:, rOB[0] + 1024:rOB[0] + 2048]]
    rBB, lgBB = AF32.alloc(1024, ["mb2"])
    b2bc = AF32.t[:, rBB[0]:rBB[0] + 1024]

    rZB, lgZB = AB16.alloc(1024, ["mzbb"])
    zbb = AB16.t[:, rZB[0]:rZB[0] + 1024]
    P.step("vector", [], [("mzb",)] + list(lgZB) + list(lgOB),
           lambda e: e.memset(zbb[:, :], 0.0))
    P.step("scalar", [("mzb",)], ["moe_pad"],
           lambda e: e.dma_start(out=Dm["moe_ag"][NTOKG:NTOKG + 128, :],
                                 in_=zbb[:, :]), dma=True)
    AB16.freea(rZB, [("mzb",)])

    for nm, c, src, npart in (("megr", 256, "egrp", 128), ("me8", 264, "e8to128", 8),
                              ("mltm", 392, "ltmask", 128), ("mws", 576, "Ws_in", None),
                              ("mtok", 96, "tokid", 128), ("mbs", 12, "bs_in", 128),
                              ("mebs", 112, "ebase", 128)):
        if nm == "mws":
            P.step("sync", [], [nm] + (list(lgSM) if nm == "megr" else []),
                   lambda e: e.dma_start(
                       out=SM[:, s0 + 576:s0 + 640].rearrange(
                           "p (j ee) -> p j ee", j=NB),
                       in_=I["Ws_in"].rearrange("(j p) ee -> p j ee", p=128)),
                   dma=True)
        else:
            w = src
            nc_ = {"egrp": 8, "e8to128": 128, "ltmask": 128, "tokid": 16,
                   "bs_in": 1, "ebase": 1}[src]
            P.step("sync", [], [nm] + (list(lgSM) if nm == "megr" else []),
                   lambda e, c=c, w=w, npart=npart, nc_=nc_: e.dma_start(
                       out=SM[0:npart, s0 + c:s0 + c + nc_], in_=I[w]), dma=True)

    probsT = R[1]
    drow = R[2][0:1, :]
    rrow = R[3][0:1, :]
    for tch in range(4):
        def mkl(tch=tch):
            def f(e):
                for j in range(NB):
                    ins = e.matmul(
                        PS[0][0:8, 0:512],
                        SM[:, s0 + 576 + j * 8:s0 + 576 + j * 8 + 8],
                        X[:, j * TOK + tch * 512:j * TOK + tch * 512 + 512],
                        start=(j == 0), stop=(j == NB - 1))
                return ins
            return f
        P.step("tensor", ["mws"] + [("X", j) for j in range(NB)],
               [("PS", 0)], mkl())
        wk = [("mpt0", tch)] + (list(lgBG) if tch == 0 else [])
        P.step("vector", [("PS", 0)], wk,
               lambda e, tch=tch: e.tensor_copy(
                   probsT[0:8, tch * 512:tch * 512 + 512], PS[0][0:8, 0:512]))
        P.step("scalar", [("mpt0", tch), "mbs"], [("mpt", tch)],
               lambda e, tch=tch: e.activation(
                   probsT[0:8, tch * 512:tch * 512 + 512],
                   probsT[0:8, tch * 512:tch * 512 + 512],
                   AF.Exp, bias=SM[0:8, s0 + 12:s0 + 13]))
        P.step("tensor", [("mpt", tch), "consts"], [("PS", 1)],
               lambda e, tch=tch: e.matmul(
                   PS[1][0:1, 0:512], consts[0:8, 0:1],
                   probsT[0:8, tch * 512:tch * 512 + 512],
                   start=True, stop=True))
        P.step("vector", [("PS", 1)], [("mdr", tch)],
               lambda e, tch=tch: e.tensor_copy(
                   drow[:, tch * 512:tch * 512 + 512], PS[1][0:1, 0:512]))
    P.step("vector", [("mdr", t) for t in range(4)], [("mrr",)],
           lambda e: e.reciprocal(rrow[:, :], drow[:, :]))
    for tch in range(4):
        P.step("tensor", [("mrr",), "onesrow"], [("PS", 1)],
               lambda e, tch=tch: e.matmul(
                   PS[1][0:8, 0:512], onesrow[0:1, 0:8],
                   rrow[:, tch * 512:tch * 512 + 512], start=True, stop=True))
        P.step("vector", [("PS", 1), ("mpt", tch)], [("mpn", tch)],
               lambda e, tch=tch: e.tensor_mul(
                   probsT[0:8, tch * 512:tch * 512 + 512],
                   probsT[0:8, tch * 512:tch * 512 + 512], PS[1][0:8, 0:512]))
    P.step("sync", [("mpn", t) for t in range(4)], ["probs_bounce"],
           lambda e: e.dma_start(out=Dm["probs_bounce"][:, :],
                                 in_=probsT[0:8, :]), dma=True)
    P.step("gpsimd", ["probs_bounce"], ["probs_ag"],
           lambda e: e.collective_compute(
               "AllGather", ALU.bypass,
               replica_groups=[[0, 1, 2, 3, 4, 5, 6, 7]],
               ins=[Dm["probs_bounce"][:, :]], outs=[Dm["probs_ag"][:, :]]),
           own_sem="cc_probs")
    if not DBG.get("no_actexp"):
        rXT, lgXT = AB16.alloc(2048, ["mxtb"])
        xtb = [AB16.t[:, rXT[0]:rXT[0] + 1024], AB16.t[:, rXT[0] + 1024:rXT[0] + 2048]]
        for a in range(16):
            sl = a % 2
            for hf in range(2):
                pb = 4 + hf

                def mkt(a=a, hf=hf, pb=pb):
                    def f(e):
                        for j in range(4 * hf, 4 * hf + 4):
                            ins = e.transpose(
                                PS[pb][:, (j - 4 * hf) * 128:(j - 4 * hf) * 128 + 128],
                                X[:, j * TOK + a * 128:j * TOK + a * 128 + 128],
                                idf[:])
                        return ins
                    return f
                P.step("tensor", [("X", j) for j in range(4 * hf, 4 * hf + 4)] +
                       ["idf"], [("PS", pb)], mkt())
                wk = [("mxtb", sl)] + (list(lgXT) if a == 0 and hf == 0 else [])
                P.step("vector", [("PS", pb)], wk,
                       lambda e, sl=sl, hf=hf, pb=pb: e.tensor_copy(
                           xtb[sl][:, hf * 512:hf * 512 + 512], PS[pb][:, 0:512]))
            P.step("sync", [("mxtb", sl)], [("mab", a)],
                   lambda e, a=a, sl=sl: e.dma_start(
                       out=Dm["act_bounce"][a * 128:(a + 1) * 128, :],
                       in_=xtb[sl][:, :]), dma=True)
        P.step("gpsimd", [("mab", a) for a in range(16)], ["act_ag"],
               lambda e: e.collective_compute(
                   "AllGather", ALU.bypass,
                   replica_groups=[[0, 1, 2, 3, 4, 5, 6, 7]],
                   ins=[Dm["act_bounce"][:, :]], outs=[Dm["act_ag"][:, :]]),
               own_sem="cc_act")
    if not DBG.get("no_actexp"):
        AB16.freea(rXT, [("mxtb", 0), ("mxtb", 1)])

    PL = R[0]
    P.step("sync", [], ["mpli", "BUFi32"],
           lambda e: e.dma_start(out=i32buf.bitcast(U32)[:, 160:161],
                                 in_=I["plidx"]), dma=True)
    P.step("sync", [], ["mmyr", "BUFi32"],
           lambda e: e.dma_start(out=i32buf.bitcast(U32)[:, 161:162],
                                 in_=I["myrows"]), dma=True)
    P.step("gpsimd", ["probs_ag", "mpli", "BUFi32"], [("mPL",)],
           lambda e: e.indirect_dma_start(
               out=PL[0:64, :], out_offset=None,
               in_=Dm["probs_ag"][:, :],
               in_offset=IndirectOffsetOnAxis(
                   i32buf.bitcast(U32)[0:64, 160:161], 0)), dma=True)

    dbg_dump(sh, [("mPL",), ("mR0",)], PL[0:64, :], 0, "mPLd")
    if DBG.get("moe_stage") == 1:
        AF32.freea(rBG, [("mPL",), ("mR0",)])
        emit_store_out(sh)
        return

    SS = R[1]
    P.step("vector", [], [("mlo",)] + list(lgSM),
           lambda e: e.memset(c64(0), 0.0))
    P.step("vector", [], [("mhi",)] + list(lgSM),
           lambda e: e.memset(c64(1), 1.0))
    for it in range(NIT):
        P.step("vector", [("mlo",), ("mhi",)], [("mmid",)],
               lambda e: e.tensor_add(c64(2), c64(0), c64(1)))
        P.step("vector", [("mmid",)], [("mmid",)],
               lambda e: e.tensor_scalar_mul(c64(2), c64(2), 0.5))
        P.step("vector", [("mmid",)], [("mnmid",)],
               lambda e: e.tensor_scalar_mul(c64(3), c64(2), -1.0))
        P.step("scalar", [("mPL",), ("mnmid",)], [("mS1",), ("mc1",)],
               lambda e: e.activation(SS[0:64, :], PL[0:64, :], AF.Sign,
                                      bias=c64(3), accum_out=c64(4)))
        P.step("scalar", [("mS1",)], [("mS2",), ("mc2",)],
               lambda e: e.activation(SS[0:64, :], SS[0:64, :], AF.Abs,
                                      accum_out=c64(5)))
        P.step("vector", [("mc1",), ("mc2",)], [("mcnt",)],
               lambda e: e.tensor_add(c64(6), c64(4), c64(5)))
        P.step("vector", [("mcnt",)], [("mcnt",)],
               lambda e: e.tensor_scalar_mul(c64(6), c64(6), 0.5))
        P.step("tensor", [("mcnt",), "megr"], [("PS", 2)],
               lambda e: e.matmul(PS[2][0:8, 0:1],
                                  SM[0:64, s0 + 256:s0 + 264], c64(6),
                                  start=True, stop=True))
        P.step("vector", [("PS", 2)], [("mg8",)],
               lambda e: e.tensor_copy(SM[0:8, s0 + 9:s0 + 10],
                                       PS[2][0:8, 0:1]))
        P.step("tensor", [("mg8",), "me8"], [("PS", 3)],
               lambda e: e.matmul(PS[3][0:64, 0:1],
                                  SM[0:8, s0 + 264:s0 + 328],
                                  SM[0:8, s0 + 9:s0 + 10],
                                  start=True, stop=True))
        P.step("vector", [("PS", 3)], [("mg64",)],
               lambda e: e.tensor_copy(c64(10), PS[3][0:64, 0:1]))
        P.step("vector", [("mg64",)], [("mfl",)],
               lambda e: e.tensor_scalar(c64(7), c64(10), 2047.5, None,
                                         ALU.is_ge))
        P.step("vector", [("mmid",), ("mlo",)], [("mt1",)],
               lambda e: e.tensor_sub(c64(8), c64(2), c64(0)))
        P.step("vector", [("mt1",), ("mfl",)], [("mt1b",)],
               lambda e: e.tensor_mul(c64(8), c64(8), c64(7)))
        P.step("vector", [("mt1b",), ("mlo",)], [("mlo",)],
               lambda e: e.tensor_add(c64(0), c64(0), c64(8)))
        P.step("vector", [("mhi",), ("mmid",)], [("mt2",)],
               lambda e: e.tensor_sub(c64(9), c64(1), c64(2)))
        P.step("vector", [("mt2",), ("mfl",)], [("mt2b",)],
               lambda e: e.tensor_mul(c64(9), c64(9), c64(7)))
        P.step("vector", [("mt2b",), ("mmid",)], [("mhi",)],
               lambda e: e.tensor_add(c64(1), c64(2), c64(9)))

    if DBG.get("moe_stage") == 2:
        dbg_dump(sh, [("mhi",), ("mlo",)], SM[:, s0:s0 + 16], 6144, "msmS2")
        AF32.freea(rBG, [("mPL",), ("mR0",)])
        emit_store_out(sh)
        return

    mgt, meq, cumq = R[1], R[2], R[3]
    P.step("vector", [("mPL",), ("mhi",), ("mS2",)], [("mgt",)],
           lambda e: e.tensor_scalar(mgt[0:64, :], PL[0:64, :], c64(1), None,
                                     ALU.is_gt))
    P.step("vector", [("mPL",), ("mhi",), ("mrr",)], [("mge",)],
           lambda e: e.tensor_scalar(meq[0:64, :], PL[0:64, :], c64(1), None,
                                     ALU.is_ge))
    P.step("vector", [("mge",), ("mgt",)], [("meq",)],
           lambda e: e.tensor_sub(meq[0:64, :], meq[0:64, :], mgt[0:64, :]))
    P.step("vector", [("mgt",)], [("mgr",)],
           lambda e: e.tensor_reduce(c64(4), mgt[0:64, :], AX, ALU.add))
    P.step("tensor", [("mgr",), "megr"], [("PS", 2)],
           lambda e: e.matmul(PS[2][0:8, 0:1], SM[0:64, s0 + 256:s0 + 264],
                              c64(4), start=True, stop=True))
    P.step("vector", [("PS", 2)], [("mg8b",)],
           lambda e: e.tensor_copy(SM[0:8, s0 + 9:s0 + 10], PS[2][0:8, 0:1]))
    P.step("tensor", [("mg8b",), "me8"], [("PS", 3)],
           lambda e: e.matmul(PS[3][0:64, 0:1], SM[0:8, s0 + 264:s0 + 328],
                              SM[0:8, s0 + 9:s0 + 10], start=True, stop=True))
    P.step("vector", [("PS", 3)], [("mG64",)],
           lambda e: e.tensor_copy(c64(10), PS[3][0:64, 0:1]))
    P.step("vector", [("mG64",)], [("mCm1",)],
           lambda e: e.tensor_scalar_mul(c64(11), c64(10), -1.0))
    P.step("vector", [("mCm1",)], [("mCm1b",)],
           lambda e: e.tensor_scalar(c64(11), c64(11), 2047.0, None, ALU.add))
    P.step("vector", [("meq",)], [("mcq",)],
           lambda e: e.tensor_tensor_scan(cumq[0:64, :], meq[0:64, :],
                                          meq[0:64, :], 0.0, ALU.add,
                                          ALU.bypass))
    P.step("vector", [("mcq",)], [("mrt",)],
           lambda e: e.tensor_copy(c64(13), cumq[0:64, 2047:2048]))
    P.step("tensor", [("mrt",), "mltm"], [("PS", 2)],
           lambda e: e.matmul(PS[2][0:64, 0:1], SM[0:64, s0 + 392:s0 + 456],
                              c64(13), start=True, stop=True))
    P.step("vector", [("PS", 2)], [("moff",)],
           lambda e: e.tensor_copy(c64(14), PS[2][0:64, 0:1]))
    P.step("vector", [("mcq",), ("meq",)], [("mq1",)],
           lambda e: e.tensor_sub(cumq[0:64, :], cumq[0:64, :], meq[0:64, :]))
    P.step("vector", [("mq1",), ("moff",)], [("mq2",)],
           lambda e: e.tensor_scalar(cumq[0:64, :], cumq[0:64, :], c64(14),
                                     None, ALU.add))
    P.step("vector", [("mq2",), ("mCm1b",)], [("mcond",)],
           lambda e: e.tensor_scalar(cumq[0:64, :], cumq[0:64, :], c64(11),
                                     None, ALU.is_le))
    P.step("vector", [("mcond",), ("meq",)], [("mseq",)],
           lambda e: e.tensor_mul(meq[0:64, :], meq[0:64, :], cumq[0:64, :]))
    P.step("vector", [("mseq",), ("mgt",)], [("msel",)],
           lambda e: e.tensor_add(meq[0:64, :], meq[0:64, :], mgt[0:64, :]))
    sel = meq
    dbg_dump(sh, [("msel",)], sel[0:64, :], 2048, "mseld")
    P.step("vector", [("msel",), ("mPL",), ("mcond",)], [("mval",)],
           lambda e: e.tensor_mul(cumq[0:64, :], PL[0:64, :], sel[0:64, :]))
    val = cumq
    P.step("vector", [("msel",), ("mval",)], [("mscn",), ("mR0",)],
           lambda e: e.tensor_tensor_scan(PL[0:64, :], sel[0:64, :],
                                          sel[0:64, :], 0.0, ALU.add,
                                          ALU.bypass))
    scn = PL
    P.step("vector", [("mscn",)], [("mrt2",)],
           lambda e: e.tensor_copy(c64(13), scn[0:64, 2047:2048]))
    P.step("tensor", [("mrt2",), "mltm"], [("PS", 2)],
           lambda e: e.matmul(PS[2][0:64, 0:1], SM[0:64, s0 + 392:s0 + 456],
                              c64(13), start=True, stop=True))
    P.step("vector", [("PS", 2)], [("moff2",)],
           lambda e: e.tensor_copy(c64(14), PS[2][0:64, 0:1]))
    P.step("vector", [("mscn",), ("msel",)], [("mex1",), ("mR0",)],
           lambda e: e.tensor_sub(scn[0:64, :], scn[0:64, :], sel[0:64, :]))
    P.step("vector", [("mex1",), ("moff2",)], [("mex2",), ("mR0",)],
           lambda e: e.tensor_scalar(scn[0:64, :], scn[0:64, :], c64(14),
                                     None, ALU.add))
    dbg_dump(sh, [("mex2",), ("mR0",)], scn[0:64, :], 4096, "mexd")
    P.step("vector", [("mex2",), ("msel",), ("mgr",)], [("mgin",)],
           lambda e: e.tensor_add(mgt[0:64, :], scn[0:64, :], sel[0:64, :]))
    gin = mgt
    if DBG.get("moe_stage") == 3:
        dbg_dump(sh, [("mgin",)], gin[0:64, :], 6144, "mginS3")
        AF32.freea(rBG, [("mPL",), ("mR0",)])
        emit_store_out(sh)
        return

    P.step("sync", [("mgin",)], ["cum_dram"],
           lambda e: e.dma_start(
               out=Dm["cum_dram"][0:128, :].rearrange("(r q) c -> r (q c)",
                                                      q=16),
               in_=gin[0:8, :]), dma=True)
    P.step("vector", [("mgin",)], [("mcb",)],
           lambda e: e.tensor_copy(SM[0:8, s0 + 16:s0 + 32],
                                   gin[0:8, 127:2048:128]))
    P.step("sync", [("mcb",)], ["mcbd"],
           lambda e: e.dma_start(
               out=Dm["moe_row"][0:1, 0:128].rearrange("a (r q) -> (a r) q",
                                                       r=8),
               in_=SM[0:8, s0 + 16:s0 + 32]), dma=True)
    P.step("sync", ["mcbd"], [("mcbc",)],
           lambda e: e.dma_start(
               out=SM[:, s0 + 15:s0 + 16],
               in_=Dm["moe_row"][0:1, 0:128].rearrange("a (p q) -> (a p) q",
                                                       p=128)), dma=True)
    P.step("vector", [("mex2",), "mebs"], [("mrx1",), ("mR0",)],
           lambda e: e.tensor_scalar(scn[0:64, :], scn[0:64, :],
                                     SM[0:64, s0 + 112:s0 + 113], None,
                                     ALU.add))
    P.step("vector", [("mrx1",)], [("mrx2",), ("mR0",)],
           lambda e: e.tensor_scalar(scn[0:64, :], scn[0:64, :], -16384.0,
                                     None, ALU.add))
    P.step("vector", [("mrx2",), ("msel",)], [("mrx3",), ("mR0",)],
           lambda e: e.tensor_mul(scn[0:64, :], scn[0:64, :], sel[0:64, :]))
    P.step("vector", [("mrx3",)], [("mrx4",), ("mR0",)],
           lambda e: e.tensor_scalar(scn[0:64, :], scn[0:64, :], 16384.0,
                                     None, ALU.add))
    P.step("sync", [("mrx4",)], ["mrxd"],
           lambda e: e.dma_start(out=Dm["ridx_dram"][0:64, :],
                                 in_=scn[0:64, :]), dma=True)
    P.step("gpsimd", ["mrxd", "mmyr", "BUFi32"], [("mrst",), ("mgin",)],
           lambda e: e.indirect_dma_start(
               out=gin[0:8, :], out_offset=None,
               in_=Dm["ridx_dram"][0:64, :],
               in_offset=IndirectOffsetOnAxis(
                   i32buf.bitcast(U32)[0:8, 161:162], 0)), dma=True)
    def mkrt():
        def f_(e):
            for a in range(16):
                ins = e.transpose(PS[2][:, a * 8:a * 8 + 8],
                                  gin[0:8, a * 128:(a + 1) * 128],
                                  idf[0:8, 0:8])
            return ins
        return f_
    P.step("tensor", [("mrst",), "idf"], [("PS", 2)], mkrt())
    P.step("vector", [("PS", 2)], [("mrxm",)],
           lambda e: e.tensor_copy(SM[:, s0 + 128:s0 + 256],
                                   PS[2][:, 0:128]))
    P.step("vector", [("mrxm",)], [("mridxi",), "BUFi32"],
           lambda e: e.tensor_copy(i32buf[:, 0:128],
                                   SM[:, s0 + 128:s0 + 256]))
    P.step("sync", [("mval",)], ["evals"],
           lambda e: e.dma_start(
               out=Dm["evals_dram"].rearrange("(r t) c -> r (t c)", r=8),
               in_=val[0:8, :]), dma=True)
    P.step("sync", [("mrx3",)], [("iotar",), ("msel",)],
           lambda e: e.dma_start(out=R[2][0:1, 0:2048], in_=I["iota_row"]),
           dma=True)
    for tch in range(4):
        P.step("tensor", [("iotar",), "onesrow"], [("PS", 1)],
               lambda e, tch=tch: e.matmul(
                   PS[1][:, 0:512], onesrow[0:1, 0:128],
                   R[2][0:1, tch * 512:tch * 512 + 512],
                   start=True, stop=True))
        P.step("vector", [("PS", 1), ("mrxm",)],
               [("mpm", tch), ("mrst",), ("mgin",)],
               lambda e, tch=tch: e.tensor_copy(
                   R[1][:, tch * 512:tch * 512 + 512], PS[1][:, 0:512]))
    P.step("vector", [("mpm", t) for t in range(4)] + [("mcbc",)], [("mcmp",)],
           lambda e: e.tensor_scalar(R[1][:, :], R[1][:, :],
                                     SM[:, s0 + 15:s0 + 16], None, ALU.is_ge))
    for tch in range(4):
        P.step("tensor", [("mcmp",), "consts"], [("PS", 0)],
               lambda e, tch=tch: e.matmul(
                   PS[0][0:1, 0:512], consts[:, 0:1],
                   R[1][:, tch * 512:tch * 512 + 512], start=True, stop=True))
        P.step("vector", [("PS", 0), ("mrxd",)], [("mnf", tch), ("mrx4",)],
               lambda e, tch=tch: e.tensor_copy(
                   R[0][0:1, tch * 512:tch * 512 + 512], PS[0][0:1, 0:512]))
    def mknt():
        def f_(e):
            for f in range(16):
                ins = e.transpose(PS[3][:, f:f + 1],
                                  R[0][0:1, f * 128:(f + 1) * 128],
                                  idf[0:1, 0:1])
            return ins
        return f_
    P.step("tensor", [("mnf", t) for t in range(4)] + ["idf"], [("PS", 3)],
           mknt())
    P.step("vector", [("PS", 3)], [("mnfm",)],
           lambda e: e.tensor_copy(SM[:, s0 + 32:s0 + 48], PS[3][:, 0:16]))
    P.step("vector", [("mnfm",)], [("mnfi",), "BUFi32"],
           lambda e: e.tensor_copy(i32buf[:, 176:192], SM[:, s0 + 32:s0 + 48]))
    gb = [SM[:, s0 + 128:s0 + 256], SM[:, s0 + 640:s0 + 768]]
    for f in range(16):
        sl = f % 2
        P.step("gpsimd", [("mnfi",), "cum_dram", "BUFi32", ("mridxi",)],
               [("mgb", sl)],
               lambda e, f=f, sl=sl: e.indirect_dma_start(
                   out=gb[sl][:, :], out_offset=None,
                   in_=Dm["cum_dram"][0:128, :],
                   in_offset=IndirectOffsetOnAxis(
                       i32buf.bitcast(U32)[:, 176 + f:177 + f], 0)), dma=True)
        P.step("vector", [("mgb", sl), "mtok"], [("mgb", sl)],
               lambda e, f=f, sl=sl: e.tensor_scalar(
                   gb[sl][:, :], gb[sl][:, :], SM[:, s0 + 96 + f:s0 + 97 + f],
                   None, ALU.is_le))
        P.step("vector", [("mgb", sl)], [("mpc", f)],
               lambda e, f=f, sl=sl: e.tensor_reduce(
                   SM[:, s0 + 48 + f:s0 + 49 + f], gb[sl][:, :], AX, ALU.add))
    P.step("vector", [("mpc", f) for f in range(16)] + [("mnfm",)], [("midx",)],
           lambda e: e.scalar_tensor_tensor(
               SM[:, s0 + 64:s0 + 80], SM[:, s0 + 32:s0 + 48], 128.0,
               SM[:, s0 + 48:s0 + 64], ALU.mult, ALU.add))
    P.step("vector", [("midx",)], [("midxi",), "BUFi32"],
           lambda e: e.tensor_copy(i32buf[:, 208:224], SM[:, s0 + 64:s0 + 80]))
    for f in range(16):
        P.step("gpsimd", [("midxi",), "evals", "BUFi32"], [("mvc", f)],
               lambda e, f=f: e.indirect_dma_start(
                   out=SM[:, s0 + 80 + f:s0 + 81 + f], out_offset=None,
                   in_=Dm["evals_dram"][:, :],
                   in_offset=IndirectOffsetOnAxis(
                       i32buf.bitcast(U32)[:, 208 + f:209 + f], 0)), dma=True)

    dbg_dump(sh, [("mhi",), ("mG64",), ("mCm1b",)], SM[:, s0:s0 + 16],
             6144, "msm")
    dbg_dump(sh, [("mnfm",), ("mpc", 15), ("midx",), ("mvc", 15)],
             SM[:, s0 + 32:s0 + 112], 6160, "msm2")
    dbg_dump(sh, [("mridxi",), ("midxi",), ("mnfi",)],
             i32buf.bitcast(F32)[:, 0:224], 6240, "mi32")

    if DBG.get("moe_stage") == 4:
        AF32.freea(rBG, [("mPL",), ("mR0",)])
        emit_store_out(sh)
        return

    rXG, lgXG = AB16.alloc(2048, ["mxg"])
    xg = [AB16.t[:, rXG[0]:rXG[0] + 1024], AB16.t[:, rXG[0] + 1024:rXG[0] + 2048]]
    rXS, lgXS = AB16.alloc(2048, ["mxs"])
    xts = [AB16.t[:, rXS[0]:rXS[0] + 1024], AB16.t[:, rXS[0] + 1024:rXS[0] + 2048]]
    for f in range(16):
        sl = f % 2
        wk = [("mxg", sl)] + (list(lgXG) if f == 0 else [])
        P.step("gpsimd", [("midxi",), "act_ag", "BUFi32"], wk,
               lambda e, f=f, sl=sl: e.indirect_dma_start(
                   out=xg[sl][:, :], out_offset=None,
                   in_=Dm["act_ag"][:, :],
                   in_offset=IndirectOffsetOnAxis(
                       i32buf.bitcast(U32)[:, 208 + f:209 + f], 0)), dma=True)
        for hf in range(2):
            pb = 4 + hf

            def mkx(sl=sl, hf=hf, pb=pb):
                def f_(e):
                    pv = PS[pb].bitcast(BF16)
                    for j in range(4 * hf, 4 * hf + 4):
                        ins = e.transpose(
                            pv[:, (j - 4 * hf) * 128:(j - 4 * hf) * 128 + 128],
                            xg[sl][:, j * 128:(j + 1) * 128], idb[:])
                    return ins
                return f_
            P.step("tensor", [("mxg", sl), "idb"], [("PS", pb)], mkx())
            wk = [("mxs", sl)] + (list(lgXS) if f == 0 and hf == 0 else [])
            P.step("vector", [("PS", pb)], wk,
                   lambda e, sl=sl, hf=hf, pb=pb: e.tensor_copy(
                       xts[sl][:, hf * 512:hf * 512 + 512],
                       PS[pb].bitcast(BF16)[:, 0:512]))
        P.step("sync", [("mxs", sl)], [("mxid", f)],
               lambda e, f=f, sl=sl: e.dma_start(
                   out=Dm["xint_dram"].rearrange("(j p) t -> p j t", p=128)
                   [:, :, f * 128:(f + 1) * 128],
                   in_=xts[sl].rearrange("p (j t) -> p j t", j=NB)), dma=True)
    AB16.freea(rXG, [("mxg", 0), ("mxg", 1)])
    AB16.freea(rXS, [("mxs", 0), ("mxs", 1)])
    AF32.freea(rBG, [("mPL",), ("mS1",), ("mS2",), ("mgt",), ("mge",),
                     ("meq",), ("mseq",), ("msel",), ("mcq",), ("mq1",),
                     ("mq2",), ("mcond",), ("mval",), ("mscn",), ("mex1",),
                     ("mex2",), ("mgin",), ("mrst",), ("mrx1",), ("mrx2",),
                     ("mrx3",), ("mrx4",)] +
               [("mpt", t) for t in range(4)] + [("mpn", t) for t in range(4)] +
               [("mdr", t) for t in range(4)] + [("mrr",)] +
               [("mpm", t) for t in range(4)] + [("mcmp",)] +
               [("mnf", t) for t in range(4)])

    P.step("sync", [], [("mb2r",)] + list(lgBB),
           lambda e: e.dma_start(out=b2bc[0:1, 0:1024], in_=I["b2row"]),
           dma=True)
    for hf in range(2):
        P.step("tensor", [("mb2r",), "onesrow"], [("PS", 1)],
               lambda e, hf=hf: e.matmul(
                   PS[1][:, 0:512], onesrow[0:1, 0:128],
                   b2bc[0:1, hf * 512:hf * 512 + 512], start=True, stop=True))
        P.step("vector", [("PS", 1)], [("mb2", hf)],
               lambda e, hf=hf: e.tensor_copy(
                   b2bc[:, hf * 512:hf * 512 + 512], PS[1][:, 0:512]))
    rHT, lgHT = AB16.alloc(NBH * 512, ["mht"])
    hT = AB16.t[:, rHT[0]:rHT[0] + NBH * 512]
    rXL, lgXL = AB16.alloc(8192, ["mxl"])
    xtl = [AB16.t[:, rXL[0]:rXL[0] + 4096], AB16.t[:, rXL[0] + 4096:rXL[0] + 8192]]
    rW1, lgW1 = AB16.alloc(2048, ["mw1b"])
    w1b = [AB16.t[:, rW1[0]:rW1[0] + 1024], AB16.t[:, rW1[0] + 1024:rW1[0] + 2048]]
    rOBB, lgOBB = AB16.alloc(2048, ["mobb"])
    obb = [AB16.t[:, rOBB[0]:rOBB[0] + 1024],
           AB16.t[:, rOBB[0] + 1024:rOBB[0] + 2048]]
    rW2, lgW2 = AB16.alloc(2048, ["mw2b"])
    w2b = [AB16.t[:, rW2[0]:rW2[0] + 1024], AB16.t[:, rW2[0] + 1024:rW2[0] + 2048]]
    first = dict(xl=True, w1=True, w2=True, ht=True, ob=True, obb=True)
    for tl in range(4):
        wk = [("mxtl", tl % 2)] + (list(lgXL) if first["xl"] else [])
        first["xl"] = False
        P.step("sync", [("mxid", f) for f in range(16)], wk,
               lambda e, tl=tl: e.dma_start(
                   out=xtl[tl % 2].rearrange("p (j t) -> p j t", j=NB),
                   in_=Dm["xint_dram"].rearrange("(j p) t -> p j t", p=128)
                   [:, :, tl * 512:(tl + 1) * 512]), dma=True)
        for fb in range(NBH):
            wk = [("mw1", fb % 2)] + (list(lgW1) if first["w1"] else [])
            first["w1"] = False
            P.step("sync", ["w1e_dram"], wk,
                   lambda e, fb=fb: e.dma_start(
                       out=w1b[fb % 2].rearrange("p (j c) -> p j c", j=NB),
                       in_=Dm["w1e_dram"].rearrange("(j p) c -> p j c", p=128)
                       [:, :, fb * 128:(fb + 1) * 128]), dma=True)

            def mk1(tl=tl, fb=fb):
                def f_(e):
                    for j in range(NB):
                        ins = e.matmul(
                            PS[fb % 2][:, 0:512],
                            w1b[fb % 2][:, j * 128:(j + 1) * 128],
                            xtl[tl % 2][:, j * 512:(j + 1) * 512],
                            start=(j == 0), stop=(j == NB - 1))
                    return ins
                return f_
            P.step("tensor", [("mw1", fb % 2), ("mxtl", tl % 2)],
                   [("PS", fb % 2)], mk1())
            wk = [("mhT", fb)] + (list(lgHT) if first["ht"] else [])
            first["ht"] = False
            P.step("scalar", [("PS", fb % 2), "bb6"], wk,
                   lambda e, fb=fb: e.activation(
                       hT[:, fb * 512:fb * 512 + 512], PS[fb % 2][:, 0:512],
                       AF.Gelu_apprx_tanh,
                       bias=biasbuf[:, OF["B1"] + fb:OF["B1"] + fb + 1]))
        for fb in range(NBH):
            wk = [("mw2", fb % 2)] + (list(lgW2) if first["w2"] else [])
            first["w2"] = False
            P.step("scalar", ["w2e_dram"], wk,
                   lambda e, fb=fb: e.dma_start(
                       out=w2b[fb % 2][:, :],
                       in_=Dm["w2e_dram"][fb * 128:(fb + 1) * 128, :]),
                   dma=True)

            def mk2(fb=fb):
                def f_(e):
                    for tc4 in range(4):
                        for dvh in range(2):
                            ins = e.matmul(
                                PS[tc4 * 2 + dvh][:, 0:512],
                                hT[:, fb * 512 + tc4 * 128:
                                   fb * 512 + tc4 * 128 + 128],
                                w2b[fb % 2][:, dvh * 512:dvh * 512 + 512],
                                start=(fb == 0), stop=(fb == NBH - 1))
                    return ins
                return f_
            P.step("tensor", [("mw2", fb % 2), ("mhT", fb)],
                   [("PS", b) for b in range(8)], mk2())
        for tc4 in range(4):
            osl = tc4 % 2
            for dvh in range(2):
                wk = [("mo", osl, dvh)] + \
                    ([("mzb",)] + list() if first["ob"] else [])
                first["ob"] = False
                P.step("vector", [("PS", tc4 * 2 + dvh), ("mb2", dvh)], wk,
                       lambda e, tc4=tc4, dvh=dvh, osl=osl: e.tensor_add(
                           obuf[osl][:, dvh * 512:dvh * 512 + 512],
                           PS[tc4 * 2 + dvh][:, 0:512],
                           b2bc[:, dvh * 512:dvh * 512 + 512]))
                P.step("vector", [("mo", osl, dvh), ("mvc", tl * 4 + tc4)],
                       [("mo", osl, dvh)],
                       lambda e, tl=tl, tc4=tc4, dvh=dvh, osl=osl:
                       e.tensor_scalar_mul(
                           obuf[osl][:, dvh * 512:dvh * 512 + 512],
                           obuf[osl][:, dvh * 512:dvh * 512 + 512],
                           SM[:, s0 + 80 + tl * 4 + tc4:
                              s0 + 81 + tl * 4 + tc4]))
            wk = [("mobb", osl)] + (list(lgOBB) if first["obb"] else [])
            first["obb"] = False
            P.step("vector", [("mo", osl, 0), ("mo", osl, 1)], wk,
                   lambda e, osl=osl: e.tensor_copy(obb[osl][:, :],
                                                    obuf[osl][:, :]))
            P.step("sync", [("mobb", osl)], [("mob", tl * 4 + tc4)],
                   lambda e, tl=tl, tc4=tc4, osl=osl: e.dma_start(
                       out=Dm["moe_bounce"][(tl * 4 + tc4) * 128:
                                            (tl * 4 + tc4) * 128 + 128, :],
                       in_=obb[osl][:, :]), dma=True)
    AB16.freea(rHT, [("mhT", fb) for fb in range(NBH)])
    AB16.freea(rXL, [("mxtl", 0), ("mxtl", 1)])
    AB16.freea(rW1, [("mw1", 0), ("mw1", 1)])
    AB16.freea(rW2, [("mw2", 0), ("mw2", 1)])
    AB16.freea(rOBB, [("mobb", 0), ("mobb", 1)])

    P.step("gpsimd", [("mob", a) for a in range(16)], ["moe_ag_c"],
           lambda e: e.collective_compute(
               "AllGather", ALU.bypass,
               replica_groups=[[0, 1, 2, 3, 4, 5, 6, 7]],
               ins=[Dm["moe_bounce"][:, :]], outs=[Dm["moe_ag"][0:NTOKG, :]]),
           own_sem="cc_moe")

    rXI, lgXI = AF32.alloc(2048, ["mxti"])
    xti = [AF32.t[:, rXI[0]:rXI[0] + 1024], AF32.t[:, rXI[0] + 1024:rXI[0] + 2048]]
    rGS, lgGS = AB16.alloc(2048, ["mgst"])
    gstg = [AB16.t[:, rGS[0]:rGS[0] + 1024], AB16.t[:, rGS[0] + 1024:rGS[0] + 2048]]
    first_gs = [True]
    rLN, lgLN = AF32.alloc(1024, ["mlnt"])
    lnt = AF32.t[:, rLN[0]:rLN[0] + 1024]
    first_xi = [True]
    for a in range(16):
        sl = a % 2
        for hf in range(2):
            pb = 4 + hf

            def mko(a=a, hf=hf, pb=pb):
                def f_(e):
                    for j in range(4 * hf, 4 * hf + 4):
                        ins = e.transpose(
                            PS[pb][:, (j - 4 * hf) * 128:(j - 4 * hf) * 128 + 128],
                            X[:, j * TOK + a * 128:j * TOK + a * 128 + 128],
                            idf[:])
                    return ins
                return f_
            P.step("tensor", [("X", j) for j in range(4 * hf, 4 * hf + 4)] +
                   ["idf"], [("PS", pb)], mko())
            wk = [("mxt", sl)] + (list(lgXI) if first_xi[0] else [])
            first_xi[0] = False
            P.step("vector", [("PS", pb)], wk,
                   lambda e, sl=sl, hf=hf, pb=pb: e.tensor_copy(
                       xti[sl][:, hf * 512:hf * 512 + 512], PS[pb][:, 0:512]))
        for ee in range(NEXP):
            gsl = ee % 2
            wk = [("mgst", gsl)] + (list(lgGS) if first_gs[0] else [])
            first_gs[0] = False
            P.step("gpsimd", ["moe_ag_c", "moe_pad", ("mridxi",), "BUFi32"],
                   wk,
                   lambda e, a=a, ee=ee, gsl=gsl: e.indirect_dma_start(
                       out=gstg[gsl][:, :], out_offset=None,
                       in_=Dm["moe_ag"][:, :],
                       in_offset=IndirectOffsetOnAxis(
                           i32buf.bitcast(U32)[:, a * 8 + ee:a * 8 + ee + 1],
                           0)), dma=True)
            P.step("vector", [("mgst", gsl)], [("mxt", sl)],
                   lambda e, sl=sl, gsl=gsl: e.tensor_add(
                       xti[sl][:, :], xti[sl][:, :], gstg[gsl][:, :]))
        cb4 = 520 + sl * 4
        P.step("vector", [("mxt", sl)], [("ml4m", sl)],
               lambda e, sl=sl, cb4=cb4: e.tensor_reduce(
                   col(cb4), xti[sl][:, :], AX, ALU.add))
        P.step("vector", [("ml4m", sl)], [("ml4m", sl)],
               lambda e, cb4=cb4: e.tensor_scalar_mul(col(cb4), col(cb4),
                                                      1.0 / D))
        P.step("vector", [("mxt", sl), ("ml4m", sl)], [("mxt", sl)],
               lambda e, sl=sl, cb4=cb4: e.tensor_scalar(
                   xti[sl][:, :], xti[sl][:, :], col(cb4), None,
                   ALU.subtract))
        P.step("scalar", [("mxt", sl)], [("ml4s", sl), ("mlntk",)],
               lambda e, sl=sl, cb4=cb4: e.activation(
                   lnt[:, :], xti[sl][:, :], AF.Square,
                   accum_out=col(cb4 + 1)))
        P.step("vector", [("ml4s", sl)], [("ml4v", sl)],
               lambda e, cb4=cb4: e.tensor_scalar_mul(col(cb4 + 2),
                                                      col(cb4 + 1), 1.0 / D))
        P.step("scalar", [("ml4v", sl), "consts"], [("ml4l", sl)],
               lambda e, cb4=cb4: e.activation(col(cb4 + 2), col(cb4 + 2),
                                               AF.Ln, bias=consts[:, 1:2]))
        P.step("scalar", [("ml4l", sl)], [("ml4r", sl)],
               lambda e, cb4=cb4: e.activation(col(cb4 + 2), col(cb4 + 2),
                                               AF.Exp, scale=-0.5))
        P.step("vector", [("mxt", sl), ("ml4r", sl)], [("mxt", sl)],
               lambda e, sl=sl, cb4=cb4: e.tensor_scalar_mul(
                   xti[sl][:, :], xti[sl][:, :], col(cb4 + 2)))
        P.step("sync", [("mxt", sl)], ["out_hbm"],
               lambda e, a=a, sl=sl: e.dma_start(
                   out=sh["out_hbm"][a * 128:(a + 1) * 128, :],
                   in_=xti[sl][:, :]), dma=True)



def _rope_tables():
    i = np.arange(512, dtype=np.float64)
    theta = np.power(10000.0, -2.0 * (i - 1.0) / D)
    pos = np.arange(8192, dtype=np.float64)
    ang = (pos[:, None] * theta[None, :])
    return (np.cos(ang.astype(np.float32).astype(np.float64)),
            np.sin(ang.astype(np.float32).astype(np.float64)))


def _host_inputs(inputs):
    perm = np.r_[np.arange(0, D, 2), np.arange(1, D, 2)]
    f32 = np.float32
    x = np.asarray(inputs["L0_attn_x"], f32).reshape(NTOKG, D)

    wqT = np.ascontiguousarray(np.asarray(inputs["L0_attn_wq"], f32).T[:, perm])
    wkT = np.ascontiguousarray(np.asarray(inputs["L0_attn_wk"], f32).T[:, perm])
    wvT = np.ascontiguousarray(np.asarray(inputs["L0_attn_wv"], f32).T)
    w1T = np.ascontiguousarray(np.asarray(inputs["L0_ffn_w1"], f32).T)
    w2T = np.ascontiguousarray(np.asarray(inputs["L0_ffn_w2"], f32).T)
    cw = np.asarray(inputs["L0_ffn_conv_w"], f32).reshape(DH, 3)
    convW = np.zeros((128, 3 * NBH), f32)
    for t in range(3):
        convW[:, t * NBH:(t + 1) * NBH] = cw[:, t].reshape(NBH, 128).T
    convB = np.asarray(inputs["L0_ffn_conv_b"], f32).reshape(NBH, 128).T.copy()
    winT = np.ascontiguousarray(np.asarray(inputs["L1_attn_W_in"], f32)[D:, :].T)
    wgT = np.ascontiguousarray(np.asarray(inputs["L1_attn_W_gate"], f32).T)
    woT = np.ascontiguousarray(np.asarray(inputs["L1_attn_W_out"], f32).T)
    b_in = np.asarray(inputs["L1_attn_b_in"], f32)[D:].reshape(NB, 128).T.copy()
    b_g = np.asarray(inputs["L1_attn_b_gate"], f32).reshape(NB, 128).T.copy()
    b_o = np.asarray(inputs["L1_attn_b_out"], f32).reshape(NB, 128).T.copy()
    A_T = np.ascontiguousarray(np.asarray(inputs["L1_attn_A"], f32).T)
    BmT = np.ascontiguousarray(np.asarray(inputs["L1_attn_Bm"], f32).T)
    CmT = np.ascontiguousarray(np.asarray(inputs["L1_attn_Cm"], f32).T)
    Ws = np.ascontiguousarray(np.asarray(inputs["L1_ffn_Ws"], f32))
    bs = np.asarray(inputs["L1_ffn_bs"], f32)
    bst = np.zeros((128, 1), f32)
    bst[:NEXP, 0] = bs
    W1 = np.asarray(inputs["L1_ffn_W1"], f32)
    b1 = np.asarray(inputs["L1_ffn_b1"], f32)
    W2 = np.asarray(inputs["L1_ffn_W2"], f32)
    b2 = np.asarray(inputs["L1_ffn_b2"], f32)

    cos, sin = _rope_tables()
    cos = cos.astype(f32); sin = sin.astype(f32)

    ident_bf = np.eye(128, dtype=ml_dtypes.bfloat16)
    ident_f32 = np.eye(128, dtype=f32)
    consts = np.zeros((128, 8), f32)
    consts[:, 0] = 1.0
    consts[:, 1] = EPS
    consts[:, 2] = 1.0
    egrp = np.zeros((128, NEXP), f32)
    for e in range(NEXP):
        egrp[8 * e:8 * e + 8, e] = 1.0
    e8to128 = np.zeros((NEXP, 128), f32)
    for e in range(NEXP):
        for r in range(8):
            e8to128[e, 8 * e + r] = 1.0
    ltm = np.zeros((128, 128), f32)
    for e in range(NEXP):
        for r1 in range(8):
            for r2 in range(r1 + 1, 8):
                ltm[8 * e + r1, 8 * e + r2] = 1.0
    iota_row = np.arange(TOK, dtype=f32).reshape(1, TOK)
    tokid = np.zeros((128, 16), f32)
    for p in range(128):
        for f in range(16):
            tokid[p, f] = p + 128 * f

    maps = []
    for c in range(NCORES_ALL):
        r = c % 4
        m = dict(
            x_in=x[c * TOK:(c + 1) * TOK], wqT=wqT, wkT=wkT, wvT=wvT,
            ropeC=np.ascontiguousarray(cos[r * TOK:(r + 1) * TOK].T),
            ropeS=np.ascontiguousarray(sin[r * TOK:(r + 1) * TOK].T),
            w1T=w1T, w2T=w2T, convW=convW, convB=convB,
            winT=winT, wgT=wgT, woT=woT, b_in=b_in, b_g=b_g, b_o=b_o,
            A_Tin=A_T, A_nat=np.ascontiguousarray(np.asarray(inputs["L1_attn_A"], f32)),
            BmT=BmT, CmT=CmT, i16rep=np.tile(np.eye(16, dtype=f32), (8, 1)),
            Ws_in=Ws, bs_in=bst,
            W1e=np.ascontiguousarray(W1[c]),
            b1e=b1[c].reshape(NBH, 128).T.copy(),
            W2e=np.ascontiguousarray(W2[c]),
            b2e=b2[c].reshape(NB, 128).T.copy(),
            ident_bf=ident_bf, ident_f32=ident_f32, consts_in=consts,
            egrp=egrp, e8to128=e8to128, ltmask=ltm, tokid=tokid,
            iota_row=iota_row,
            b2row=np.ascontiguousarray(b2[c].reshape(1, D)),
        )
        ebase = np.zeros((128, 1), f32)
        plidx = np.zeros((128, 1), np.uint32)
        myrows = np.zeros((128, 1), np.uint32)
        for eh in range(NEXP):
            te = (eh + c) % NEXP
            ebase[8 * eh:8 * eh + 8, 0] = 2048.0 * te
            for rr in range(8):
                plidx[8 * eh + rr, 0] = 8 * rr + te
            myrows[eh, 0] = 8 * eh + c
        m["ebase"] = ebase
        m["plidx"] = plidx
        m["myrows"] = myrows
        cm = np.zeros((128, 4), f32)
        for cp in range(NCORES_ALL):
            if cp // 4 != c // 4:
                continue
            d = r - 1 - (cp % 4)
            if d < 0 or d > 2:
                continue
            cm[16 * cp:16 * cp + 16, d] = 1.0
        m["cmask"] = cm
        hidx = np.zeros((128, 1), np.uint32)
        hmsk = np.zeros((2, 1), f32)
        if r > 0:
            hidx[0, 0] = 2 * (c - 1) + 1
            hmsk[0, 0] = 1.0
        if r < 3:
            hidx[1, 0] = 2 * (c + 1)
            hmsk[1, 0] = 1.0
        m["halo_idx"] = hidx
        m["halo_msk"] = hmsk
        maps.append(m)
    return maps


NCORES_ALL = 8
_NC_CACHE = {}


def kernel(_phases=(1, 2, 3, 4), **inputs):
    maps = _host_inputs(inputs)
    nc = build_program(phases=tuple(_phases))
    res = run_bass_kernel_spmd(nc, maps, core_ids=list(range(NCORES_ALL)))
    out = np.concatenate([res.results[c]["out_sh"] for c in range(NCORES_ALL)],
                         axis=0)
    return out.reshape(2, 8192, D)
